# version 12
# baseline (speedup 1.0000x reference)
import contextlib
import numpy as np
import concourse.bass as bass
from concourse.bass_utils import run_bass_kernel_spmd
import concourse.mybir as mybir

F32 = mybir.dt.float32
BF16 = mybir.dt.bfloat16
I32 = mybir.dt.int32
AF = mybir.ActivationFunctionType
ALU = mybir.AluOpType
AX = mybir.AxisListType

MAGIC = 12582912.0


class Sem:
    __slots__ = ("h", "v")
    def __init__(self, h):
        self.h = h
        self.v = 0


class P:
    ENGS = ("g", "t", "v", "s", "y")

    def __init__(self, nc):
        self.nc = nc
        self.qs = {e: [] for e in self.ENGS}
        self.stack = contextlib.ExitStack()
        self._semno = 0
        self._all_sems = {}
        self._free = {}

    def sem(self, name=None, tag="c"):
        free = self._free.setdefault(tag, [])
        if free:
            return free.pop()
        self._semno += 1
        h = self.stack.enter_context(self.nc.semaphore(name or f"sm{self._semno}"))
        s = Sem(h)
        self._all_sems.setdefault(tag, []).append(s)
        return s

    def recycle(self, keep=()):
        keep = set(id(k) for k in keep)
        self._free = {t: [s for s in lst if id(s) not in keep]
                      for t, lst in self._all_sems.items()}

    def o(self, eng):
        def reg(fn):
            self.qs[eng].append(fn)
        return reg

    def finalize(self):
        with self.nc.Block() as block:
            block.gpsimd(lambda e: [f(e) for f in self.qs["g"]])
            block.tensor(lambda e: [f(e) for f in self.qs["t"]])
            block.vector(lambda e: [f(e) for f in self.qs["v"]])
            block.scalar(lambda e: [f(e) for f in self.qs["s"]])
            block.sync(lambda e: [f(e) for f in self.qs["y"]])
        self.stack.close()


def cdiv(a, b):
    return (a + b - 1) // b


def fm_proj(p, *, w_dram, n_fc, n_k, x_rhs, ttile, wbufs, psum, slot_banks,
            evac, ev, x_waits=(), rank1=None, w_guard=(), tee=None, w_hw=False,
            w_hw_rearr=False, shared=None):
    if w_hw:
        wtag = ['hw', 'hw', 'hw', 'hw']
        wq = ['y', 'y', 'y', 'y']
    else:
        wtag = ['sw'] * 4
        wq = ['g'] * 4
    nslot = len(slot_banks)
    nw = len(wbufs)
    if shared is None:
        mmk = p.sem()
        wls = [p.sem(tag=wtag[i]) for i in range(nw)]
    else:
        mmk, wls = shared
    mb = mmk.v
    wtag = wtag[:nw]
    wq = wq[:nw]
    wlb = [w.v for w in wls]
    n_nt = cdiv(ttile, 512)
    ev0 = ev.v
    tes = [p.sem(tag='hw') for _ in range(nw)] if tee is not None else None
    teb = [t.v for t in tes] if tes else None

    for qname in sorted(set(wq)):
        slots_q = [i for i in range(nw) if wq[i] == qname]

        @p.o(qname)
        def _(g, wg=list(w_guard), slots_q=slots_q):
            for s, val in wg:
                g.wait_ge(s.h, val)
            for fc in range(n_fc):
                if fc % nw not in slots_q:
                    continue
                if fc - nw >= 0:
                    g.wait_ge(mmk.h, mb + fc - nw + 1)
                if tes is not None and fc >= nw:
                    g.wait_ge(tes[fc % nw].h, teb[fc % nw] + 16 * (fc // nw))
                if w_hw:
                    src = w_dram(fc)
                    dst = wbufs[fc % nw][:, 0:n_k * 128]
                    if w_hw_rearr:
                        dst = dst.rearrange("pp (kc j) -> pp kc j", j=128)
                else:
                    src = w_dram(fc).rearrange("(kc pp) j -> pp kc j", pp=128)
                    dst = wbufs[fc % nw][:, 0:n_k * 128].rearrange("pp (kc j) -> pp kc j", j=128)
                g.dma_start(out=dst, in_=src).then_inc(wls[fc % nw].h, 16)
    for i in range(nw):
        wls[i].v += 16 * ((n_fc + nw - 1 - i) // nw)

    if tee is not None:
        @p.o("y")
        def _(y):
            for fc in range(n_fc):
                y.wait_ge(wls[fc % nw].h, wlb[fc % nw] + 16 * (fc // nw + 1))
                y.dma_start(out=tee[fc],
                            in_=wbufs[fc % nw][:, 0:n_k * 128]).then_inc(tes[fc % nw].h, 16)
        for i in range(nw):
            tes[i].v += 16 * ((n_fc + nw - 1 - i) // nw)
    tee_done = ([(t, t.v) for t in tes] if tes is not None else None)

    @p.o("t")
    def _(t, xw=list(x_waits)):
        for s, val in xw:
            t.wait_ge(s.h, val)
        for fc in range(n_fc):
            t.wait_ge(wls[fc % nw].h, wlb[fc % nw] + 16 * (fc // nw + 1))
            if ev0 + fc - nslot + 1 > 0:
                t.wait_ge(ev.h, ev0 + fc - nslot + 1)
            wb = wbufs[fc % nw]
            b0 = slot_banks[fc % nslot] * 512
            for nt in range(n_nt):
                tn = min(512, ttile - nt * 512)
                last_nt = (nt == n_nt - 1)
                ps = psum[:, b0 + nt * 512: b0 + nt * 512 + tn]
                for k in range(n_k):
                    mi = t.matmul(ps, wb[:, k * 128:(k + 1) * 128], x_rhs(k, nt * 512, tn),
                                  start=(k == 0), stop=(k == n_k - 1 and rank1 is None))
                    if last_nt and rank1 is None and k == n_k - 1:
                        mi.then_inc(mmk.h, 1)
                if rank1 is not None:
                    mi = t.matmul(ps, lhs1 := rank1[0], rank1[1](fc, nt * 512, tn), start=False, stop=True)
                    if last_nt:
                        mi.then_inc(mmk.h, 1)
    mmk.v += n_fc

    for fc in range(n_fc):
        b0 = slot_banks[fc % nslot] * 512
        ps = psum[:, b0: b0 + ttile]
        evac(fc, ps, [(mmk, mb + fc + 1)], ev)
    ev.v = ev0 + n_fc
    return tee_done


def tok_proj(p, *, w_dram, d_in, d_out, x_lhs, n_tc, wbuf, psum, slot_banks,
             evac, ev, x_waits=(), wprep=None):
    wl = p.sem(tag='sw')
    mmk = p.sem()
    mb = mmk.v
    n_k = d_in // 128
    n_nt = cdiv(d_out, 512)
    nslot = len(slot_banks)
    ev0 = ev.v

    @p.o("g")
    def _(g):
        for k in range(n_k):
            g.dma_start(out=wbuf[:, k * d_out:(k + 1) * d_out],
                        in_=w_dram[k * 128:(k + 1) * 128, :]).then_inc(wl.h, 16)
    wl.v += 16 * n_k
    wready = [(wl, wl.v)]
    if wprep is not None:
        wready = wprep(wready)

    @p.o("t")
    def _(t, xw=list(x_waits) + wready):
        for s, val in xw:
            t.wait_ge(s.h, val)
        for tc in range(n_tc):
            if tc >= nslot:
                t.wait_ge(ev.h, ev0 + tc - nslot + 1)
            b0 = slot_banks[tc % nslot] * 512
            for k in range(n_k):
                for nt in range(n_nt):
                    w = min(512, d_out - nt * 512)
                    ps = psum[:, b0 + nt * 512: b0 + nt * 512 + w]
                    mi = t.matmul(ps, x_lhs(k, tc), wbuf[:, k * d_out + nt * 512: k * d_out + nt * 512 + w],
                                  start=(k == 0), stop=(k == n_k - 1))
                    if k == n_k - 1 and nt == n_nt - 1:
                        mi.then_inc(mmk.h, 1)
    mmk.v += n_tc

    for tc in range(n_tc):
        b0 = slot_banks[tc % nslot] * 512
        ps = psum[:, b0: b0 + d_out]
        evac(tc, ps, [(mmk, mb + tc + 1)], ev)
    ev.v = ev0 + n_tc


def build_identity(p, idn_bf, io_i32, iom_f32, done):
    @p.o("g")
    def _(g):
        g.iota(io_i32[:, :], pattern=[[1, 128]], base=0, channel_multiplier=-1).then_inc(done.h, 1)
    done.v += 1

    @p.o("v")
    def _(v, w=done.v):
        v.wait_ge(done.h, w)
        v.tensor_single_scalar(iom_f32[:, :], io_i32[:, :], 0.0, ALU.is_equal).then_inc(done.h, 1)
        v.wait_ge(done.h, w + 1)
        v.tensor_copy(idn_bf[:, :], iom_f32[:, :]).then_inc(done.h, 1)
    done.v += 2


def ln_chunk(p, *, a_ap, b_ap, tmp, trash_bf, sc, n_f, waits, u, done,
             out_f32=None, out_bf=None, eps=1e-5, absmax=None):
    b = u.v

    @p.o("v")
    def _(v, w=list(waits) + [(done, done.v)], b=b):
        for s, val in w:
            v.wait_ge(s.h, val)
        if b_ap is not None:
            v.tensor_tensor(tmp, a_ap, b_ap, ALU.add).then_inc(u.h, 1)
        else:
            v.tensor_copy(tmp, a_ap).then_inc(u.h, 1)
        v.wait_ge(u.h, b + 1)
        v.reduce_sum(sc[:, 0:1], tmp, axis=AX.X).then_inc(u.h, 1)
        v.wait_ge(u.h, b + 2)
        v.tensor_scalar(sc[:, 1:2], sc[:, 0:1], 1.0 / n_f, None, ALU.mult).then_inc(u.h, 1)
        v.wait_ge(u.h, b + 3)
        v.tensor_scalar(tmp, tmp, sc[:, 1:2], None, ALU.subtract).then_inc(u.h, 1)
        v.wait_ge(u.h, b + 4)
        v.tensor_tensor(trash_bf, tmp, tmp, ALU.mult).then_inc(u.h, 1)
        v.wait_ge(u.h, b + 5)
        v.reduce_sum(sc[:, 2:3], trash_bf, axis=AX.X).then_inc(u.h, 1)
        v.wait_ge(u.h, b + 6)
        v.tensor_scalar(sc[:, 2:3], sc[:, 2:3], 1.0 / n_f, eps, ALU.mult, ALU.add).then_inc(u.h, 1)

    @p.o("s")
    def _(s, b=b):
        s.wait_ge(u.h, b + 7)
        s.activation(sc[:, 3:4], sc[:, 2:3], AF.Sqrt).then_inc(u.h, 1)

    d0 = done.v
    nout = (out_f32 is not None) + (out_bf is not None)

    @p.o("v")
    def _(v, b=b, d0=d0, nout=nout):
        v.wait_ge(u.h, b + 8)
        v.reciprocal(sc[:, 3:4], sc[:, 3:4]).then_inc(u.h, 1)
        v.wait_ge(u.h, b + 9)
        if out_f32 is not None:
            v.tensor_scalar(out_f32, tmp, sc[:, 3:4], None, ALU.mult).then_inc(done.h, 1)
        if out_bf is not None:
            v.tensor_scalar(out_bf, tmp, sc[:, 3:4], None, ALU.mult).then_inc(done.h, 1)
        if absmax is not None:
            v.wait_ge(done.h, d0 + nout)
            v.reduce_max(absmax, out_f32 if out_f32 is not None else out_bf,
                         axis=AX.X, apply_absolute_value=True).then_inc(done.h, 1)
    u.v = b + 9
    done.v = d0 + nout + (absmax is not None)


def transpose_tiles(p, *, src, dst, n, idn, psum_bf, slots, waits, ev):
    tk = p.sem()
    b_tk = tk.v
    nslot = len(slots)
    ev0 = ev.v

    @p.o("t")
    def _(t, w=list(waits)):
        for s, val in w:
            t.wait_ge(s.h, val)
        for i in range(n):
            if i >= nslot - 1:
                t.wait_ge(ev.h, ev0 + i - nslot + 2)
            ps = psum_bf[:, slots[i % nslot]: slots[i % nslot] + 128]
            t.transpose(ps, src(i), idn[:, :]).then_inc(tk.h, 1)
    tk.v += n

    @p.o("v")
    def _(v):
        for i in range(n):
            v.wait_ge(tk.h, b_tk + i + 1)
            ps = psum_bf[:, slots[i % nslot]: slots[i % nslot] + 128]
            v.tensor_copy(dst(i), ps).then_inc(ev.h, 1)
    ev.v = ev0 + n


TRAN_SLOTS = [4 * 1024, 5 * 1024, 6 * 1024, 7 * 1024]


def ffn0_phase(p, h, sb, psum, psum_bf, x_ready, TT=512, x2fm=None, wc_done=None,
               v_hook=None):
    NTILE = 2048 // TT
    NCH = TT // 128
    x_fm = sb['x_fm']
    a = sb['ff_a']
    bb = sb['ff_b']
    a1 = sb['ff_a1']
    w8 = sb['w8']
    w32 = sb['w32']
    w3o = sb['w3o']
    w3tok = sb['w3tok']
    xs_f32 = sb['xs_f32']
    x2s = sb['x2s']
    idn = sb['idn']

    svl = p.sem(tag='sw')
    svecs = ['s1c', 's2c', 'h1sc', 'h2sc', 's3c']

    @p.o("g")
    def _(g):
        for nm in svecs:
            g.dma_start(out=sb['svec_' + nm][:, 0:h[nm].shape[1]], in_=h[nm][:, :]).then_inc(svl.h, 16)
    svl.v += 80
    svlf = svl.v
    sv = {nm: sb['svec_' + nm] for nm in svecs}

    u = p.sem()
    ln_done = p.sem()
    sxs = [p.sem(tag='hw'), p.sem(tag='hw')]
    sos = [p.sem(tag='hw'), p.sem(tag='hw')]
    tokb = h.get('x_tokb')
    sob = [p.sem(tag='hw') for _ in range(4)]
    b_sob = [s.v for s in sob]
    ln_marks = []
    prev_tail = []
    ev_w1 = p.sem(); ev_a = p.sem(); ev_b = p.sem(); ev_c = p.sem()
    ev_w3 = p.sem(); ev_tr = p.sem()
    sh_w1 = (p.sem(), [p.sem(tag='hw'), p.sem(tag='hw')])
    sh_w2 = (p.sem(), [p.sem(tag='hw'), p.sem(tag='hw')])
    sh_h1 = (p.sem(), [p.sem(tag='hw') for _ in range(4)])
    sh_h2 = (p.sem(), [p.sem(tag='hw') for _ in range(4)])
    sh_w3 = (p.sem(), [p.sem(tag='hw') for _ in range(4)])

    for tt in range(NTILE):
        t0 = tt * TT

        def evac_w1(fc, ps, waits, ev):
            @p.o("s")
            def _(s, w=list(waits), fc=fc, ps=ps):
                for sm, val in w:
                    s.wait_ge(sm.h, val)
                s.wait_ge(svl.h, svlf)
                s.activation(a[:, fc * TT:(fc + 1) * TT], ps,
                             AF.Silu, scale=sv['s1c'][:, fc:fc + 1]).then_inc(ev.h, 1)
        fm_proj(p, w_dram=lambda fc: h['w1c'][fc], n_fc=32, n_k=8,
                x_rhs=lambda k, o, n, t0=t0: x_fm[:, k * 2048 + t0 + o: k * 2048 + t0 + o + n],
                ttile=TT, wbufs=w8[0:2], psum=psum, slot_banks=[0, 1] if TT == 512 else [0, 2],
                evac=evac_w1, ev=ev_w1, w_hw=True, shared=sh_w1,
                w_guard=(wc_done['w1c'] if (wc_done and tt == 0) else []),
                x_waits=[x_ready] + [w for w in prev_tail if w[0].h is not None][:2])

        def evac_w2(fc, ps, waits, ev, ew0=ev_w1.v - 32):
            @p.o("v")
            def _(v, w=list(waits), fc=fc, ps=ps, ew0=ew0):
                for sm, val in w:
                    v.wait_ge(sm.h, val)
                v.wait_ge(svl.h, svlf)
                v.wait_ge(ev_w1.h, ew0 + fc + 1)
                v.scalar_tensor_tensor(a[:, fc * TT:(fc + 1) * TT], ps,
                                       sv['s2c'][:, fc:fc + 1],
                                       a[:, fc * TT:(fc + 1) * TT],
                                       ALU.mult, ALU.mult).then_inc(ev.h, 1)
        ea_base = ev_a.v
        fm_proj(p, w_dram=lambda fc: h['w2c'][fc], n_fc=32, n_k=8,
                x_rhs=lambda k, o, n, t0=t0: x_fm[:, k * 2048 + t0 + o: k * 2048 + t0 + o + n],
                ttile=TT, wbufs=w8[2:4], psum=psum, slot_banks=[2, 3] if TT == 512 else [4, 6],
                evac=evac_w2, ev=ev_a, w_hw=True, shared=sh_w2,
                w_guard=(wc_done['w2c'] if (wc_done and tt == 0) else []),
                x_waits=[x_ready] + prev_tail[-1:])

        def evac_h1(fc, ps, waits, ev):
            @p.o("s")
            def _(s, w=list(waits), fc=fc, ps=ps):
                for sm, val in w:
                    s.wait_ge(sm.h, val)
                s.wait_ge(svl.h, svlf)
                s.activation(bb[:, fc * TT:(fc + 1) * TT], ps, AF.Gelu,
                             scale=sv['h1sc'][:, fc:fc + 1]).then_inc(ev.h, 1)
        fm_proj(p, w_dram=lambda fc: h['h1c'][fc], n_fc=32, n_k=32,
                x_rhs=lambda k, o, n: a[:, k * TT + o: k * TT + o + n],
                ttile=TT, wbufs=w32, psum=psum, slot_banks=[4, 5] if TT == 512 else [0, 2],
                evac=evac_h1, ev=ev_b, w_hw=True, shared=sh_h1,
                x_waits=[(ev_a, ea_base + 32)],
                w_guard=(([(ev_w3, ev_w3.v)] if tt > 0 else []) +
                         (wc_done['h1c'] if (wc_done and tt == 0) else [])))

        eb_base = ev_b.v - 32
        def evac_h2(fc, ps, waits, ev):
            @p.o("s")
            def _(s, w=list(waits), fc=fc, ps=ps):
                for sm, val in w:
                    s.wait_ge(sm.h, val)
                s.wait_ge(svl.h, svlf)
                s.activation(a[:, fc * TT:(fc + 1) * TT], ps, AF.Gelu,
                             scale=sv['h2sc'][:, fc:fc + 1]).then_inc(ev.h, 1)
        fm_proj(p, w_dram=lambda fc: h['h2c'][fc], n_fc=32, n_k=32,
                x_rhs=lambda k, o, n: bb[:, k * TT + o: k * TT + o + n],
                ttile=TT, wbufs=w32, psum=psum, slot_banks=[6, 7] if TT == 512 else [4, 6],
                evac=evac_h2, ev=ev_c, w_hw=True, shared=sh_h2,
                x_waits=[(ev_b, eb_base + 32)],
                w_guard=[(ev_b, eb_base + 32)] +
                        (wc_done['h2c'] if (wc_done and tt == 0) else []))

        ec_base = ev_c.v - 32
        def evac_w3(fc, ps, waits, ev):
            @p.o("v")
            def _(v, w=list(waits), fc=fc, ps=ps):
                for sm, val in w:
                    v.wait_ge(sm.h, val)
                v.wait_ge(svl.h, svlf)
                v.tensor_scalar(w3o[:, fc * TT:(fc + 1) * TT], ps,
                                sv['s3c'][:, fc:fc + 1], None, ALU.mult).then_inc(ev.h, 1)
        fm_proj(p, w_dram=lambda fc: h['w3c'][fc], n_fc=8, n_k=32,
                x_rhs=lambda k, o, n: a[:, k * TT + o: k * TT + o + n],
                ttile=TT, wbufs=w32, psum=psum, slot_banks=[0, 1],
                evac=evac_w3, ev=ev_w3, w_hw=True, shared=sh_w3,
                x_waits=[(ev_c, ec_base + 32)] + ([(ev_tr, ev_tr.v)] if tt > 0 else []),
                w_guard=[(ev_c, ec_base + 32)] +
                        (wc_done['w3c'] if (wc_done and tt == 0) else []))

        ew3_0 = ev_w3.v
        n_tr = 8 * NCH
        transpose_tiles(
            p,
            src=lambda i: w3o[:, (i % 8) * TT + (i // 8) * 128: (i % 8) * TT + (i // 8) * 128 + 128],
            dst=lambda i: w3tok[:, (i // 8) * 1024 + (i % 8) * 128: (i // 8) * 1024 + (i % 8) * 128 + 128],
            n=n_tr, idn=idn, psum_bf=psum_bf, slots=TRAN_SLOTS,
            waits=[(ev_w3, ew3_0)] + ([(ln_done, ln_done.v)] if tt > 0 else []), ev=ev_tr)
        etr_base = ev_tr.v - n_tr

        for c8 in range(NCH):
            tc = tt * NCH + c8
            sx = sxs[tc % 2]
            @p.o("y")
            def _(y, tc=tc, need=(ln_marks[tc - 2] if tc >= 2 else 0), sx=sx):
                if tc >= 2:
                    y.wait_ge(ln_done.h, need)
                y.dma_start(out=xs_f32[:, (tc % 2) * 1024:(tc % 2) * 1024 + 1024],
                            in_=h['x_tok'][tc * 128:(tc + 1) * 128, :]).then_inc(sx.h, 16)
            sx.v += 16
            d_before = ln_done.v
            ln_chunk(p,
                     a_ap=xs_f32[:, (tc % 2) * 1024:(tc % 2) * 1024 + 1024],
                     b_ap=w3tok[:, c8 * 1024:(c8 + 1) * 1024],
                     tmp=sb['ln_tmp'][:, :], trash_bf=sb['ln_tr'][:, :], sc=sb['ln_sc'],
                     n_f=1024,
                     waits=[(sx, sx.v), (ev_tr, etr_base + n_tr)] +
                           ([(sos[tc % 2], sos[tc % 2].v)] if tc >= 2 else []) +
                           ([(sob[tc % 4], sob[tc % 4].v)]
                            if (tokb is not None and tc >= 4) else []),
                     u=u, done=ln_done,
                     out_f32=x2s[:, (tc % 2) * 1024:(tc % 2) * 1024 + 1024],
                     out_bf=(sb['x2b'][:, (tc % 4) * 1024:(tc % 4) * 1024 + 1024]
                             if (x2fm is not None or tokb is not None) else None))
            so = sos[tc % 2]
            @p.o("y")
            def _(y, tc=tc, w=d_before + 1, so=so):
                y.wait_ge(ln_done.h, w)
                y.dma_start(out=h['x2_tok'][tc * 128:(tc + 1) * 128, :],
                            in_=x2s[:, (tc % 2) * 1024:(tc % 2) * 1024 + 1024]).then_inc(so.h, 16)
            so.v += 16
            if tokb is not None:
                @p.o("y")
                def _(y, tc=tc, w=d_before + 2, so2=sob[tc % 4]):
                    y.wait_ge(ln_done.h, w)
                    y.dma_start(out=tokb[tc * 128:(tc + 1) * 128, :],
                                in_=sb['x2b'][:, (tc % 4) * 1024:(tc % 4) * 1024 + 1024]
                                ).then_inc(so2.h, 16)
                sob[tc % 4].v += 16
            ln_marks.append(ln_done.v)
            if x2fm is not None:
                x2_fm_t, ev_x2 = x2fm
                transpose_tiles(
                    p,
                    src=lambda i, tc=tc: sb['x2b'][:, (tc % 2) * 1024 + i * 128:(tc % 2) * 1024 + i * 128 + 128],
                    dst=lambda i, tc=tc: x2_fm_t[:, i * 2048 + tc * 128: i * 2048 + tc * 128 + 128],
                    n=8, idn=idn, psum_bf=psum_bf, slots=TRAN_SLOTS,
                    waits=[(ln_done, ln_done.v)], ev=ev_x2)
        prev_tail = [(ev_w3, ev_w3.v), (ev_tr, ev_tr.v)]
        if v_hook is not None:
            v_hook(tt)

    @p.o("y")
    def _(y, w0=sos[0].v, w1=sos[1].v):
        y.wait_ge(sos[0].h, w0)
        y.wait_ge(sos[1].h, w1)
    return ln_done, sos, (sob, b_sob)


from concourse.bass_isa import ReduceOp
from concourse.library_config import attnmlp as LIB_ATTNMLP


def ffn1_wprep(p, h, sb, psum_bf, done):
    wt = sb['pw_a']
    sts = [sb['pw_c0'], sb['pw_c1']]
    scs = sb['pw_s']
    idn = sb['idn']
    ld = p.sem(tag='hw')
    stv = p.sem()
    sto = [p.sem(tag='hw'), p.sem(tag='hw')]
    tk = p.sem()
    te = p.sem()
    bld, bsv, btk, bte = ld.v, stv.v, tk.v, te.v
    bst = [sto[0].v, sto[1].v]
    NI = 6

    def _store1(g, fc):
        dst = h['w1q_blk'][fc].rearrange("(kc pp) j -> pp kc j", pp=128)
        src2 = sts[fc % 2][:, 1024:2048].rearrange("pp (kc j) -> pp kc j", j=128)
        g.dma_start(out=dst, in_=src2).then_inc(sto[fc % 2].h, 16)

    @p.o("y")
    def _(g):
        for fc in range(32):
            if fc:
                g.wait_ge(stv.h, bsv + NI * fc)
            if fc >= 2:
                g.wait_ge(te.h, bte + 8 * (fc - 1))
                _store1(g, fc - 2)
            g.dma_start(out=wt[:, 0:1024],
                        in_=h['W1nT'][fc * 128:(fc + 1) * 128, :]).then_inc(ld.h, 16)
        g.wait_ge(te.h, bte + 8 * 31)
        _store1(g, 30)
        g.wait_ge(te.h, bte + 8 * 32)
        _store1(g, 31)
    ld.v += 16 * 32

    @p.o("v")
    def _(v):
        for fc in range(32):
            v.wait_ge(ld.h, bld + 16 * (fc + 1))
            if fc >= 2:
                v.wait_ge(tk.h, btk + 8 * (fc - 1))
            v.reduce_max(scs[:, 36:37], wt[:, 0:1024], axis=AX.X,
                         apply_absolute_value=True).then_inc(stv.h, 1)
            v.wait_ge(stv.h, bsv + NI * fc + 1)
            v.tensor_scalar(scs[:, 36:37], scs[:, 36:37], 1e-8, None, ALU.max).then_inc(stv.h, 1)
            v.wait_ge(stv.h, bsv + NI * fc + 2)
            v.reciprocal(scs[:, 37:38], scs[:, 36:37]).then_inc(stv.h, 1)
            v.wait_ge(stv.h, bsv + NI * fc + 3)
            v.tensor_scalar(scs[:, fc:fc + 1], scs[:, 37:38], 127.0, None, ALU.mult).then_inc(stv.h, 1)
            v.wait_ge(stv.h, bsv + NI * fc + 4)
            v.tensor_scalar(wt[:, 0:1024], wt[:, 0:1024], scs[:, fc:fc + 1], MAGIC,
                            ALU.mult, ALU.add).then_inc(stv.h, 1)
            v.wait_ge(stv.h, bsv + NI * fc + 5)
            v.tensor_scalar(sts[fc % 2][:, 0:1024], wt[:, 0:1024], MAGIC, None,
                            ALU.subtract).then_inc(stv.h, 1)
    stv.v += NI * 32

    @p.o("t")
    def _(t):
        for fc in range(32):
            t.wait_ge(stv.h, bsv + NI * (fc + 1))
            for kc in range(8):
                i = fc * 8 + kc
                if i >= 4:
                    t.wait_ge(te.h, bte + i - 3)
                ps = psum_bf[:, TRAN_SLOTS[i % 4]: TRAN_SLOTS[i % 4] + 128]
                t.transpose(ps, sts[fc % 2][:, kc * 128: kc * 128 + 128], idn[:, :]).then_inc(tk.h, 1)
    tk.v += 32 * 8

    @p.o("s")
    def _(s):
        for fc in range(32):
            for kc in range(8):
                i = fc * 8 + kc
                s.wait_ge(tk.h, btk + i + 1)
                if fc >= 2 and kc == 0:
                    s.wait_ge(sto[fc % 2].h, bst[fc % 2] + 16 * (fc // 2))
                ps = psum_bf[:, TRAN_SLOTS[i % 4]: TRAN_SLOTS[i % 4] + 128]
                s.copy(sts[fc % 2][:, 1024 + kc * 128: 1024 + kc * 128 + 128],
                       ps).then_inc(te.h, 1)
    te.v += 32 * 8

    sto[0].v += 16 * 16
    sto[1].v += 16 * 16

    rm = scs[:, 38:39]
    rr = scs[:, 39:40]
    B2 = stv.v
    ld2 = ld.v
    st2 = [sto[0].v, sto[1].v]

    def _store2(g, k):
        dst = h['w2q_blk'][:, k * 128:(k + 1) * 128, :].rearrange("fc pp j -> pp fc j")
        src2 = sts[k % 2][:, 0:1024].rearrange("pp (fc j) -> pp fc j", j=128)
        g.dma_start(out=dst, in_=src2).then_inc(sto[k % 2].h, 16)

    @p.o("y")
    def _(g):
        for k in range(32):
            if k:
                g.wait_ge(stv.h, B2 + 6 * k)
                _store2(g, k - 1)
            g.dma_start(out=wt[:, 0:1024], in_=h['W2n'][k * 128:(k + 1) * 128, :]).then_inc(ld.h, 16)
        g.wait_ge(stv.h, B2 + 6 * 32)
        _store2(g, 31)
    ld.v += 16 * 32

    @p.o("v")
    def _(v, tev=te.v):
        for k in range(32):
            v.wait_ge(ld.h, ld2 + 16 * (k + 1))
            if k >= 2:
                v.wait_ge(sto[k % 2].h, st2[k % 2] + 16 * (k // 2))
            else:
                v.wait_ge(sto[k % 2].h, st2[k % 2])
                v.wait_ge(te.h, tev)
            v.reduce_max(rm, wt[:, 0:1024], axis=AX.X, apply_absolute_value=True).then_inc(stv.h, 1)
            v.wait_ge(stv.h, B2 + 6 * k + 1)
            v.tensor_scalar(rm, rm, 1e-8, None, ALU.max).then_inc(stv.h, 1)
            v.wait_ge(stv.h, B2 + 6 * k + 2)
            v.reciprocal(rr, rm).then_inc(stv.h, 1)
            v.wait_ge(stv.h, B2 + 6 * k + 3)
            v.tensor_scalar(wt[:, 0:1024], wt[:, 0:1024], rr, 127.0, ALU.mult, ALU.mult).then_inc(stv.h, 1)
            v.wait_ge(stv.h, B2 + 6 * k + 4)
            v.tensor_scalar(wt[:, 0:1024], wt[:, 0:1024], MAGIC, MAGIC, ALU.add, ALU.subtract).then_inc(stv.h, 1)
            v.wait_ge(stv.h, B2 + 6 * k + 5)
            v.tensor_scalar(sts[k % 2][:, 0:1024], wt[:, 0:1024], rm, 1.0 / 127.0,
                            ALU.mult, ALU.mult).then_inc(stv.h, 1)
    stv.v = B2 + 6 * 32

    @p.o("y")
    def _(g, a0=sto[0].v + 16 * 16, a1=sto[1].v + 16 * 16, b=stv.v):
        g.wait_ge(sto[0].h, a0)
        g.wait_ge(sto[1].h, a1)
        g.wait_ge(stv.h, b)
        g.dma_start(out=h['prepflag'][0:1, 0:1], in_=scs[0:1, 0:1]).then_inc(done.h, 16)
    sto[0].v += 16 * 16
    sto[1].v += 16 * 16
    done.v += 16


def ffn1_wprep_quant(p, h, sb, done):
    wt = sb['pw_a'][:, 0:1024]
    sts = [sb['pw_c0'], sb['pw_c1']]
    scs = sb['pw_s']
    ld = p.sem(tag='sw')
    stv = p.sem()
    sto = [p.sem(tag='sw'), p.sem(tag='sw')]
    bld, bsv = ld.v, stv.v
    bst = [sto[0].v, sto[1].v]
    NI = 6

    def _emit_g1():
        @p.o("g")
        def _(g):
            for fc in range(32):
                if fc:
                    g.wait_ge(stv.h, bsv + NI * fc)
                if fc >= 2:
                    g.dma_start(out=h['w1qn'][fc - 2],
                                in_=sts[fc % 2][:, 0:1024]).then_inc(sto[fc % 2].h, 16)
                g.dma_start(out=wt[:, :],
                            in_=h['W1nT'][fc * 128:(fc + 1) * 128, :]).then_inc(ld.h, 16)
            g.wait_ge(stv.h, bsv + NI * 31)
            g.dma_start(out=h['w1qn'][30], in_=sts[0][:, 0:1024]).then_inc(sto[0].h, 16)
            g.wait_ge(stv.h, bsv + NI * 32)
            g.dma_start(out=h['w1qn'][31], in_=sts[1][:, 0:1024]).then_inc(sto[1].h, 16)
        ld.v += 16 * 32
        sto[0].v += 16 * 16
        sto[1].v += 16 * 16

    def _piece1(lo, hi):
        @p.o("v")
        def _(v):
            for fc in range(lo, hi):
                v.wait_ge(ld.h, bld + 16 * (fc + 1))
                if fc >= 2:
                    v.wait_ge(sto[fc % 2].h, bst[fc % 2] + 16 * (fc // 2))
                v.reduce_max(scs[:, 36:37], wt[:, :], axis=AX.X,
                             apply_absolute_value=True).then_inc(stv.h, 1)
                v.wait_ge(stv.h, bsv + NI * fc + 1)
                v.tensor_scalar(scs[:, 36:37], scs[:, 36:37], 1e-8, None, ALU.max).then_inc(stv.h, 1)
                v.wait_ge(stv.h, bsv + NI * fc + 2)
                v.reciprocal(scs[:, 37:38], scs[:, 36:37]).then_inc(stv.h, 1)
                v.wait_ge(stv.h, bsv + NI * fc + 3)
                v.tensor_scalar(scs[:, fc:fc + 1], scs[:, 37:38], 127.0, None, ALU.mult).then_inc(stv.h, 1)
                v.wait_ge(stv.h, bsv + NI * fc + 4)
                v.tensor_scalar(wt[:, :], wt[:, :], scs[:, fc:fc + 1], MAGIC,
                                ALU.mult, ALU.add).then_inc(stv.h, 1)
                v.wait_ge(stv.h, bsv + NI * fc + 5)
                v.tensor_scalar(sts[fc % 2][:, 0:1024], wt[:, :], MAGIC, None,
                                ALU.subtract).then_inc(stv.h, 1)
    stv.v += NI * 32

    rm = scs[:, 38:39]
    rr = scs[:, 39:40]
    B2 = bsv + NI * 32
    ld2 = bld + 16 * 32
    st2 = [bst[0] + 16 * 16, bst[1] + 16 * 16]

    def _store2(g, k):
        dst = h['w2q_blk'][:, :, k * 128:(k + 1) * 128].rearrange("fc pp j -> pp fc j")
        src2 = sts[k % 2][:, 0:1024].rearrange("pp (fc j) -> pp fc j", j=128)
        g.dma_start(out=dst, in_=src2).then_inc(sto[k % 2].h, 16)

    def _emit_g2():
        @p.o("g")
        def _(g):
            for k in range(32):
                if k:
                    g.wait_ge(stv.h, B2 + 6 * k)
                    _store2(g, k - 1)
                g.dma_start(out=wt[:, :], in_=h['W2n'][k * 128:(k + 1) * 128, :]).then_inc(ld.h, 16)
            g.wait_ge(stv.h, B2 + 6 * 32)
            _store2(g, 31)
        ld.v += 16 * 32

    def _piece2(lo, hi):
        @p.o("v")
        def _(v):
            for k in range(lo, hi):
                v.wait_ge(ld.h, ld2 + 16 * (k + 1))
                if k >= 2:
                    v.wait_ge(sto[k % 2].h, st2[k % 2] + 16 * (k // 2))
                else:
                    v.wait_ge(sto[k % 2].h, st2[k % 2])
                v.reduce_max(rm, wt[:, :], axis=AX.X, apply_absolute_value=True).then_inc(stv.h, 1)
                v.wait_ge(stv.h, B2 + 6 * k + 1)
                v.tensor_scalar(rm, rm, 1e-8, None, ALU.max).then_inc(stv.h, 1)
                v.wait_ge(stv.h, B2 + 6 * k + 2)
                v.reciprocal(rr, rm).then_inc(stv.h, 1)
                v.wait_ge(stv.h, B2 + 6 * k + 3)
                v.tensor_scalar(wt[:, :], wt[:, :], rr, 127.0, ALU.mult, ALU.mult).then_inc(stv.h, 1)
                v.wait_ge(stv.h, B2 + 6 * k + 4)
                v.tensor_scalar(wt[:, :], wt[:, :], MAGIC, MAGIC, ALU.add, ALU.subtract).then_inc(stv.h, 1)
                v.wait_ge(stv.h, B2 + 6 * k + 5)
                v.tensor_scalar(sts[k % 2][:, 0:1024], wt[:, :], rm, 1.0 / 127.0,
                                ALU.mult, ALU.mult).then_inc(stv.h, 1)
    stv.v = B2 + 6 * 32

    def _emit_sentinel():
        @p.o("g")
        def _(g, a0=bst[0] + 16 * 32, a1=bst[1] + 16 * 32, b=B2 + 6 * 32):
            g.wait_ge(sto[0].h, a0)
            g.wait_ge(sto[1].h, a1)
            g.wait_ge(stv.h, b)
            g.dma_start(out=h['prepflag'][0:1, 0:1], in_=scs[0:1, 0:1]).then_inc(done.h, 16)
        sto[0].v += 16 * 16
        sto[1].v += 16 * 16
        done.v += 16

    def emit_g():
        _emit_g1()
        _emit_g2()
        _emit_sentinel()
    return ([lambda: _piece1(0, 16), lambda: _piece1(16, 32),
             lambda: _piece2(0, 16), lambda: _piece2(16, 32)], emit_g)


def ffn1_blockify(p, h, sb, psum_bf, idn, quant_done, done):
    stv2 = sb['pw_b'][:, 0:2048].bitcast(BF16)
    sts = [stv2[:, 0:1024], stv2[:, 1024:2048]]
    evb = sb['pw_b'][:, 2048:3072].bitcast(BF16)
    evs = [evb[:, 0:1024], evb[:, 1024:2048]]
    ld = [p.sem(tag='sw'), p.sem(tag='sw')]
    sto = [p.sem(tag='sw'), p.sem(tag='sw')]
    tk = p.sem()
    te = p.sem()
    bld = [ld[0].v, ld[1].v]
    btk, bte = tk.v, te.v
    bst = [sto[0].v, sto[1].v]

    @p.o("g")
    def _(g, qd=(quant_done, quant_done.v)):
        g.wait_ge(qd[0].h, qd[1])
        for fc in range(32):
            if fc >= 2:
                g.wait_ge(tk.h, btk + 8 * (fc - 1))
            g.dma_start(out=sts[fc % 2][:, :],
                        in_=h['w1qn'][fc]).then_inc(ld[fc % 2].h, 16)
            if fc >= 2:
                g.wait_ge(te.h, bte + 8 * (fc - 1))
                g.dma_start(out=h['w1q_blk'][fc - 2],
                            in_=evs[fc % 2][:, :]).then_inc(sto[fc % 2].h, 16)
        g.wait_ge(te.h, bte + 8 * 31)
        g.dma_start(out=h['w1q_blk'][30], in_=evs[0][:, :]).then_inc(sto[0].h, 16)
        g.wait_ge(te.h, bte + 8 * 32)
        g.dma_start(out=h['w1q_blk'][31], in_=evs[1][:, :]).then_inc(sto[1].h, 16)
        g.wait_ge(sto[0].h, bst[0] + 16 * 16)
        g.wait_ge(sto[1].h, bst[1] + 16 * 16)
        g.dma_start(out=h['prepflag'][0:1, 2:3], in_=sb['pw_s'][0:1, 0:1]).then_inc(done.h, 16)
    ld[0].v += 16 * 16
    ld[1].v += 16 * 16

    @p.o("t")
    def _(t):
        for fc in range(32):
            t.wait_ge(ld[fc % 2].h, bld[fc % 2] + 16 * (fc // 2 + 1))
            for kc in range(8):
                i = fc * 8 + kc
                if i >= 4:
                    t.wait_ge(te.h, bte + i - 3)
                ps = psum_bf[:, TRAN_SLOTS[i % 4]: TRAN_SLOTS[i % 4] + 128]
                t.transpose(ps, sts[fc % 2][:, kc * 128: kc * 128 + 128],
                            idn[:, :]).then_inc(tk.h, 1)
    tk.v += 32 * 8

    @p.o("s")
    def _(s):
        for fc in range(32):
            for kc in range(8):
                i = fc * 8 + kc
                s.wait_ge(tk.h, btk + i + 1)
                if fc >= 2 and kc == 0:
                    s.wait_ge(sto[fc % 2].h, bst[fc % 2] + 16 * (fc // 2))
                ps = psum_bf[:, TRAN_SLOTS[i % 4]: TRAN_SLOTS[i % 4] + 128]
                s.copy(evs[fc % 2][:, kc * 128: kc * 128 + 128], ps).then_inc(te.h, 1)
    te.v += 32 * 8

    sto[0].v += 16 * 16
    sto[1].v += 16 * 16
    done.v += 16


def allreduce_max_scalar(p, *, local_11, bounce_in, bounce_out, rg, out_11,
                         tmp18, waits, done):
    vz = p.sem()
    cc = p.sem(tag='co')
    dmx = p.sem(tag='sw')
    b_vz, b_cc, b_dmx = vz.v, cc.v, dmx.v

    @p.o("v")
    def _(v, w=list(waits)):
        for s, val in w:
            v.wait_ge(s.h, val)
        v.tensor_copy(tmp18[0:1, 0:8], local_11.broadcast_to([1, 8])).then_inc(vz.h, 1)
    vz.v += 1

    @p.o("g")
    def _(g):
        g.wait_ge(vz.h, b_vz + 1)
        g.dma_start(out=bounce_in[:, :], in_=tmp18[0:1, 0:8]).then_inc(dmx.h, 16)
        g.wait_ge(dmx.h, b_dmx + 16)
        g.collective_compute("AllReduce", ALU.max, replica_groups=rg,
                             ins=[bounce_in.ap().opt()],
                             outs=[bounce_out.ap().opt()]).then_inc(cc.h, 1)
        g.wait_ge(cc.h, b_cc + 1)
        g.dma_start(out=out_11, in_=bounce_out[0:1, 0:1]).then_inc(done.h, 16)
    dmx.v += 16
    cc.v += 1
    done.v += 16


def bcast_row(p, *, onesf, row, out, psum, bank, waits, done):
    mmv = p.sem()
    b_mm = mmv.v

    @p.o("t")
    def _(t, w=list(waits)):
        for s, val in w:
            t.wait_ge(s.h, val)
        t.matmul(psum[:, bank * 512: bank * 512 + row.shape[1]], onesf, row,
                 start=True, stop=True).then_inc(mmv.h, 1)
    mmv.v += 1

    @p.o("v")
    def _(v):
        v.wait_ge(mmv.h, b_mm + 1)
        v.tensor_copy(out, psum[:, bank * 512: bank * 512 + row.shape[1]]).then_inc(done.h, 1)
    done.v += 1


def colmax_to_scalar(p, *, col, dram_flat, flat128, out_11, waits, done):
    dmx = p.sem(tag='sw')
    b_dmx = dmx.v

    @p.o("g")
    def _(g, w=list(waits)):
        for s, val in w:
            g.wait_ge(s.h, val)
        g.dma_start(out=dram_flat[0:128], in_=col,
                    allow_slow_non_contiguous=True).then_inc(dmx.h, 16)
        g.wait_ge(dmx.h, b_dmx + 16)
        g.dma_start(out=flat128[0:1, 0:128], in_=dram_flat[0:128]).then_inc(dmx.h, 16)
    dmx.v += 32

    @p.o("v")
    def _(v):
        v.wait_ge(dmx.h, b_dmx + 32)
        v.reduce_max(out_11, flat128[0:1, 0:128], axis=AX.X).then_inc(done.h, 1)
    done.v += 1


def ffn1_phase(p, h, sb, psum, psum_bf, x_ready, rg, prep_done, TT=512, xamax=None):
    NTILE = 2048 // TT
    NCH = TT // 128
    x3f = sb['x3f']
    fhid = sb['fhid']
    w8 = sb['w8']
    w32 = sb['w32']
    wo = sb['w3o']
    wotok = sb['w3tok']
    xs_f32 = sb['xs_f32']
    x2s = sb['x2s']
    idn = sb['idn']
    qs = sb['qs']
    q1 = sb['q1']
    scs = sb['pw_s']
    ep1 = sb['ep1']
    binc = sb['binc']
    boutc = sb['boutc']
    hmv = sb['hmv']
    onesf = sb['onesf']

    svl = p.sem(tag='sw')

    @p.o("g")
    def _(g):
        g.dma_start(out=binc[:, :], in_=h['b_in_c'][:, :]).then_inc(svl.h, 16)
        g.dma_start(out=boutc[:, :], in_=h['b_out_c'][:, :]).then_inc(svl.h, 16)
    svl.v += 32
    svlf = svl.v

    cz = p.sem()

    @p.o("v")
    def _(v):
        v.memset(onesf[0:1, 0:128], 1.0).then_inc(cz.h, 1)
    cz.v += 1
    czf = cz.v

    if xamax is None:
        am = p.sem()
        b_am = am.v

        @p.o("v")
        def _(v, xw=[x_ready]):
            for s, val in xw:
                v.wait_ge(s.h, val)
            for kc in range(8):
                v.reduce_max(qs[:, 8 + kc:9 + kc], x3f[:, kc * 2048:(kc + 1) * 2048],
                             axis=AX.X, apply_absolute_value=True).then_inc(am.h, 1)
            v.wait_ge(am.h, b_am + 8)
            v.reduce_max(qs[:, 0:1], qs[:, 8:16], axis=AX.X).then_inc(am.h, 1)
        am.v += 9

        colmax_to_scalar(p, col=qs[:, 0:1], dram_flat=h['flat_sc'][0],
                         flat128=sb['q1x'], out_11=q1[0:1, 0:1],
                         waits=[(am, am.v)], done=am)
        arx = p.sem(tag='sw')
        allreduce_max_scalar(p, local_11=q1[0:1, 0:1], bounce_in=h['arm1'],
                             bounce_out=h['arm1o'], rg=rg, out_11=q1[0:1, 1:2],
                             tmp18=sb['tmp18'], waits=[(am, am.v)], done=arx)
        amx_src = None
    else:
        amx_src, arx = xamax

    sq = p.sem()
    b_sq = sq.v
    off = 1 if amx_src is not None else 0

    @p.o("v")
    def _(v, w=arx.v, amx_src=amx_src):
        v.wait_ge(arx.h, w)
        if amx_src is not None:
            v.tensor_copy(q1[0:1, 1:2], amx_src).then_inc(sq.h, 1)
            v.wait_ge(sq.h, b_sq + 1)
        v.tensor_scalar(q1[0:1, 1:2], q1[0:1, 1:2], 1e-8, None, ALU.max).then_inc(sq.h, 1)
        v.wait_ge(sq.h, b_sq + off + 1)
        v.reciprocal(q1[0:1, 2:3], q1[0:1, 1:2]).then_inc(sq.h, 1)
        v.wait_ge(sq.h, b_sq + off + 2)
        v.tensor_scalar(q1[0:1, 2:3], q1[0:1, 2:3], 32767.0, None, ALU.mult).then_inc(sq.h, 1)
        v.wait_ge(sq.h, b_sq + off + 3)
        v.tensor_scalar(q1[0:1, 3:4], q1[0:1, 1:2], 1.0 / 32767.0, None, ALU.mult).then_inc(sq.h, 1)
    sq.v = b_sq + off + 4

    bc1 = p.sem()
    bcast_row(p, onesf=onesf[0:1, 0:128], row=q1[0:1, 2:4], out=qs[:, 2:4],
              psum=psum, bank=0, waits=[(sq, sq.v), (cz, czf)], done=bc1)

    @p.o("v")
    def _(v, w=prep_done.v, b=bc1.v, xw=[x_ready]):
        v.wait_ge(prep_done.h, w)
        v.wait_ge(bc1.h, b)
        for s, val in xw:
            v.wait_ge(s.h, val)
        v.reciprocal(ep1[:, :], scs[:, 0:32]).then_inc(sq.h, 1)
        v.wait_ge(sq.h, b_sq + off + 5)
        v.tensor_scalar(ep1[:, :], ep1[:, :], qs[:, 3:4], None, ALU.mult).then_inc(sq.h, 1)
        tmpf = sb['pw_a']
        for kc in range(8):
            v.wait_ge(sq.h, b_sq + off + 6 + 2 * kc)
            v.tensor_scalar(tmpf[:, 0:2048], x3f[:, kc * 2048:(kc + 1) * 2048],
                            qs[:, 2:3], MAGIC, ALU.mult, ALU.add).then_inc(sq.h, 1)
            v.wait_ge(sq.h, b_sq + off + 7 + 2 * kc)
            v.tensor_scalar(x3f[:, kc * 2048:(kc + 1) * 2048], tmpf[:, 0:2048],
                            MAGIC, None, ALU.subtract).then_inc(sq.h, 1)
    sq.v = b_sq + off + 6 + 16
    xq_ready = (sq, sq.v)

    u = p.sem()
    ln_done = p.sem()
    sxs = [p.sem(tag='hw'), p.sem(tag='hw')]
    sos = [p.sem(tag='hw'), p.sem(tag='hw')]
    hsp_st = [p.sem(tag='hw'), p.sem(tag='hw')]
    ln_marks = []
    ev_h = p.sem()
    ev_wo = p.sem()
    ev_tr = p.sem()
    hmx = p.sem()
    sh_s1 = (p.sem(), [p.sem(tag='hw'), p.sem(tag='hw')])
    sh_s2 = (p.sem(), [p.sem(tag='hw') for _ in range(4)])
    prev_tail = []

    for tt in range(NTILE):
        t0 = tt * TT

        def evac_g(fc, ps, waits, ev, tt=tt):
            @p.o("s")
            def _(s, w=list(waits), fc=fc, ps=ps):
                for sm, val in w:
                    s.wait_ge(sm.h, val)
                s.wait_ge(svl.h, svlf)
                s.activation(fhid[:, fc * TT:(fc + 1) * TT], ps, AF.Gelu,
                             scale=ep1[:, fc:fc + 1], bias=binc[:, fc:fc + 1]).then_inc(ev.h, 1)

            @p.o("v")
            def _(v, w=ev_h.v + (fc + 1), fc=fc):
                v.wait_ge(ev_h.h, w)
                v.reduce_max(hmv[:, fc:fc + 1], fhid[:, fc * TT:(fc + 1) * TT],
                             axis=AX.X, apply_absolute_value=True).then_inc(hmx.h, 1)
        fm_proj(p, w_dram=lambda fc: h['w1q_blk'][fc],
                w_hw=True, shared=sh_s1, n_fc=32, n_k=8,
                w_guard=([(prep_done, prep_done.v)] if tt == 0 else []),
                x_rhs=lambda k, o, n, t0=t0: x3f[:, k * 2048 + t0 + o: k * 2048 + t0 + o + n],
                ttile=TT, wbufs=w8[0:2], psum=psum, slot_banks=[0, 1] if TT == 512 else [0, 2],
                evac=evac_g, ev=ev_h, x_waits=[xq_ready] + prev_tail)
        hb = hmx.v
        hmx.v += 32

        @p.o("v")
        def _(v, hb=hb, tt=tt):
            v.wait_ge(hmx.h, hb + 32)
            if tt == 0:
                v.reduce_max(qs[:, 8:9], hmv[:, :], axis=AX.X).then_inc(hmx.h, 1)
            else:
                v.reduce_max(qs[:, 9:10], hmv[:, :], axis=AX.X).then_inc(hmx.h, 1)
                v.wait_ge(hmx.h, hb + 33)
                v.tensor_tensor(qs[:, 8:9], qs[:, 8:9], qs[:, 9:10], ALU.max).then_inc(hmx.h, 1)
        hmx.v += 1 if tt == 0 else 2

        @p.o("y")
        def _(y, w=ev_h.v, tt=tt):
            y.wait_ge(ev_h.h, w)
            y.dma_start(out=h['hsp'][tt], in_=fhid[:, :]).then_inc(hsp_st[tt % 2].h, 16)
        hsp_st[tt % 2].v += 16
        prev_tail = [(hsp_st[tt % 2], hsp_st[tt % 2].v), (hmx, hmx.v)]

    colmax_to_scalar(p, col=qs[:, 8:9], dram_flat=h['flat_sc'][1],
                     flat128=sb['q1x'], out_11=q1[0:1, 4:5],
                     waits=[(hmx, hmx.v)], done=hmx)
    arh = p.sem(tag='sw')
    allreduce_max_scalar(p, local_11=q1[0:1, 4:5], bounce_in=h['arm2'],
                         bounce_out=h['arm2o'], rg=rg, out_11=q1[0:1, 5:6],
                         tmp18=sb['tmp18'], waits=[(hmx, hmx.v)], done=arh)

    @p.o("v")
    def _(v, w=arh.v, b=sq.v):
        v.wait_ge(arh.h, w)
        v.tensor_scalar(q1[0:1, 5:6], q1[0:1, 5:6], 1e-8, None, ALU.max).then_inc(sq.h, 1)
        v.wait_ge(sq.h, b + 1)
        v.reciprocal(q1[0:1, 6:7], q1[0:1, 5:6]).then_inc(sq.h, 1)
        v.wait_ge(sq.h, b + 2)
        v.tensor_scalar(q1[0:1, 6:7], q1[0:1, 6:7], 32767.0, None, ALU.mult).then_inc(sq.h, 1)
        v.wait_ge(sq.h, b + 3)
        v.tensor_scalar(q1[0:1, 7:8], q1[0:1, 5:6], 1.0 / 32767.0, None, ALU.mult).then_inc(sq.h, 1)
    sq.v += 4
    bc2 = p.sem()
    bcast_row(p, onesf=onesf[0:1, 0:128], row=q1[0:1, 6:8], out=qs[:, 6:8],
              psum=psum, bank=0, waits=[(sq, sq.v)], done=bc2)
    s_h_ready = (bc2, bc2.v)

    for tt in range(NTILE):
        rb = p.sem(tag='hw')

        @p.o("y")
        def _(y, tt=tt, w=(ev_wo.v if tt else 0), rb=rb,
              spill_w=[(s, s.v) for s in hsp_st] + [(hmx, hmx.v)]):
            if tt:
                y.wait_ge(ev_wo.h, w)
            else:
                for s, val in spill_w:
                    y.wait_ge(s.h, val)
            y.dma_start(out=fhid[:, :], in_=h['hsp'][tt]).then_inc(rb.h, 16)
        rb.v += 16

        NKC = (32 * TT) // 2048

        @p.o("v")
        def _(v, rb=rb, rbb_=rb.v, b=sq.v, shr=s_h_ready):
            v.wait_ge(rb.h, rbb_)
            v.wait_ge(shr[0].h, shr[1])
            tmpf = sb['pw_a']
            for kc in range(NKC):
                v.wait_ge(sq.h, b + 2 * kc)
                v.tensor_scalar(tmpf[:, 0:2048], fhid[:, kc * 2048:(kc + 1) * 2048],
                                qs[:, 6:7], MAGIC, ALU.mult, ALU.add).then_inc(sq.h, 1)
                v.wait_ge(sq.h, b + 2 * kc + 1)
                v.tensor_scalar(fhid[:, kc * 2048:(kc + 1) * 2048], tmpf[:, 0:2048],
                                MAGIC, None, ALU.subtract).then_inc(sq.h, 1)
        hq_ready = (sq, sq.v + 2 * NKC)
        sq.v += 2 * NKC

        def evac_o(fc, ps, waits, ev):
            @p.o("v")
            def _(v, w=list(waits), fc=fc, ps=ps):
                for sm, val in w:
                    v.wait_ge(sm.h, val)
                v.wait_ge(svl.h, svlf)
                v.tensor_scalar(wo[:, fc * TT:(fc + 1) * TT], ps, qs[:, 7:8],
                                boutc[:, fc:fc + 1], ALU.mult, ALU.add).then_inc(ev.h, 1)
        fm_proj(p, w_dram=lambda fc: h['w2q_blk'][fc],
                w_hw=True, shared=sh_s2, n_fc=8, n_k=32,
                w_guard=([(prep_done, prep_done.v)] if tt == 0 else []),
                x_rhs=lambda k, o, n: fhid[:, k * TT + o: k * TT + o + n],
                ttile=TT, wbufs=w32, psum=psum, slot_banks=[0, 1] if TT == 512 else [0, 2],
                evac=evac_o, ev=ev_wo, x_waits=[hq_ready])

        ewo0 = ev_wo.v
        n_tr = 8 * NCH
        transpose_tiles(
            p,
            src=lambda i: wo[:, (i % 8) * TT + (i // 8) * 128: (i % 8) * TT + (i // 8) * 128 + 128],
            dst=lambda i: wotok[:, (i // 8) * 1024 + (i % 8) * 128: (i // 8) * 1024 + (i % 8) * 128 + 128],
            n=n_tr, idn=idn, psum_bf=psum_bf, slots=TRAN_SLOTS,
            waits=[(ev_wo, ewo0)] + ([(ln_done, ln_done.v)] if tt else []), ev=ev_tr)
        etr_base = ev_tr.v - n_tr

        for c8 in range(NCH):
            tc = tt * NCH + c8
            sx = sxs[tc % 2]

            @p.o("y")
            def _(y, tc=tc, need=(ln_marks[tc - 2] if tc >= 2 else 0), sx=sx):
                if tc >= 2:
                    y.wait_ge(ln_done.h, need)
                y.dma_start(out=xs_f32[:, (tc % 2) * 1024:(tc % 2) * 1024 + 1024],
                            in_=h['x_tok'][tc * 128:(tc + 1) * 128, :]).then_inc(sx.h, 16)
            sx.v += 16
            d_before = ln_done.v
            ln_chunk(p,
                     a_ap=xs_f32[:, (tc % 2) * 1024:(tc % 2) * 1024 + 1024],
                     b_ap=wotok[:, c8 * 1024:(c8 + 1) * 1024],
                     tmp=sb['ln_tmp'][:, :], trash_bf=sb['ln_tr'][:, :], sc=sb['ln_sc'],
                     n_f=1024,
                     waits=[(sx, sx.v), (ev_tr, etr_base + n_tr)] +
                           ([(sos[tc % 2], sos[tc % 2].v)] if tc >= 2 else []),
                     u=u, done=ln_done,
                     out_f32=x2s[:, (tc % 2) * 1024:(tc % 2) * 1024 + 1024])
            so = sos[tc % 2]

            @p.o("y")
            def _(y, tc=tc, w=d_before + 1, so=so):
                y.wait_ge(ln_done.h, w)
                y.dma_start(out=h['x4_tok'][tc * 128:(tc + 1) * 128, :],
                            in_=x2s[:, (tc % 2) * 1024:(tc % 2) * 1024 + 1024]).then_inc(so.h, 16)
            so.v += 16
            ln_marks.append(ln_done.v)

    @p.o("y")
    def _(y, w0=sos[0].v, w1=sos[1].v):
        y.wait_ge(sos[0].h, w0)
        y.wait_ge(sos[1].h, w1)
    return ln_done, sos


def attn1_phase(p, h, sb, psum, psum_bf, x_ready, rg, absmax_col=None):
    x2f = sb['x2f']
    xt2 = sb['xt2']
    qf = sb['qf']
    kf = sb['kf']
    vaug = sb['vaug']
    aout = sb['aout']
    pbuf = sb['pbuf']
    tau = sb['tau']
    rrow = sb['rrow']
    wq3 = sb['wq3']
    wv = sb['wv']
    wos = sb['wos']
    onesf = sb['onesf']

    agv = p.sem(tag='sw')
    cc = p.sem(tag='co')
    b_agv = agv.v

    @p.o("g")
    def _(g, xw=[x_ready]):
        for s, val in xw:
            g.wait_ge(s.h, val)
        g.dma_start(out=h['agx_in'].ap().rearrange("(fc pp t) -> pp fc t", fc=8, pp=128),
                    in_=x2f[:, :].rearrange("pp (fc t) -> pp fc t", t=2048)).then_inc(agv.h, 16)
        g.wait_ge(agv.h, b_agv + 16)
        g.collective_compute("AllGather", ALU.bypass, replica_groups=rg,
                             ins=[h['agx_in'].ap().opt()],
                             outs=[h['agx_out'].ap().opt()]).then_inc(cc.h, 1)
    agv.v += 16
    cc.v += 1

    cl = p.sem(tag='sw')

    @p.o("g")
    def _(g, agw=agv.v):
        g.wait_ge(agv.h, agw)
        for j in range(4):
            g.dma_start(out=tau[:, j * 512:(j + 1) * 512], in_=h['tauj'][j]).then_inc(cl.h, 16)
        for i in range(4):
            dst = wq3[:, i * 1024: i * 1024 + 1024].rearrange("pp (kc j) -> pp kc j", j=128)
            g.dma_start(out=dst, in_=h['Wqb'][i].rearrange("(kc pp) j -> pp kc j", pp=128)).then_inc(cl.h, 16)
        for i in range(4):
            dst = wq3[:, (4 + i) * 1024: (4 + i) * 1024 + 1024].rearrange("pp (kc j) -> pp kc j", j=128)
            g.dma_start(out=dst, in_=h['Wkb'][i].rearrange("(kc pp) j -> pp kc j", pp=128)).then_inc(cl.h, 16)
        g.dma_start(out=wv[:, :].rearrange("pp (kc j) -> pp kc j", j=512),
                    in_=h['Wvb'].rearrange("(kc pp) j -> pp kc j", pp=128)).then_inc(cl.h, 16)
        g.dma_start(out=sb['bo_c'][:, :], in_=h['bo_c'][:, :]).then_inc(cl.h, 16)
    cl.v += 16 * 14

    vz = p.sem()
    b_vz = vz.v

    @p.o("v")
    def _(v):
        v.memset(vaug[:, :], 0.0).then_inc(vz.h, 1)
        v.wait_ge(vz.h, b_vz + 1)
        ones_ap = vaug[:, :].rearrange("pp (tc hh s) -> pp tc hh s", tc=32, hh=8)[:, :, :, 64:65]
        v.memset(ones_ap, 1.0).then_inc(vz.h, 1)
    vz.v += 2

    xl = [p.sem(tag='sw'), p.sem(tag='sw')]
    mmq = p.sem()
    evq = p.sem()
    b_xl = [xl[0].v, xl[1].v]
    b_mmq, b_evq = mmq.v, evq.v

    @p.o("g")
    def _(g, ccv=cc.v):
        g.wait_ge(cc.h, ccv)
        for tt in range(4):
            if tt >= 2:
                g.wait_ge(mmq.h, b_mmq + 16 * (tt - 1))
            src = h['agx_out'].ap().rearrange("r (fc pp t) -> r fc pp t", fc=8, pp=128)
            blk, tloc = (tt * 1024) // 2048, (tt * 1024) % 2048
            g.dma_start(out=xt2[tt % 2][:, :].rearrange("pp (fc t) -> pp fc t", t=1024),
                        in_=src[blk, :, :, tloc:tloc + 1024].rearrange("fc pp t -> pp fc t")
                        ).then_inc(xl[tt % 2].h, 16)
    xl[0].v += 32
    xl[1].v += 32

    @p.o("t")
    def _(t, clv=cl.v, vzv=vz.v):
        t.wait_ge(cl.h, clv)
        t.wait_ge(vz.h, vzv)
        n = 0
        for tt in range(4):
            t.wait_ge(xl[tt % 2].h, b_xl[tt % 2] + 16 * (tt // 2 + 1))
            x2t = xt2[tt % 2]
            for pi in range(2):
                for fc in range(4):
                    if n >= 2:
                        t.wait_ge(evq.h, b_evq + n - 1)
                    b0 = (n % 2) * 1024
                    for nt in range(2):
                        ps = psum[:, b0 + nt * 512: b0 + nt * 512 + 512]
                        for k in range(8):
                            mi = t.matmul(ps, wq3[:, (pi * 4 + fc) * 1024 + k * 128:
                                                  (pi * 4 + fc) * 1024 + k * 128 + 128],
                                          x2t[:, k * 1024 + nt * 512: k * 1024 + nt * 512 + 512],
                                          start=(k == 0), stop=(k == 7))
                        if nt == 1:
                            mi.then_inc(mmq.h, 1)
                    n += 1
            for tc8 in range(8):
                if n >= 2:
                    t.wait_ge(evq.h, b_evq + n - 1)
                b0 = (n % 2) * 1024
                ps = psum[:, b0: b0 + 512]
                for k in range(8):
                    mi = t.matmul(ps, x2t[:, k * 1024 + tc8 * 128: k * 1024 + tc8 * 128 + 128],
                                  wv[:, k * 512: k * 512 + 512],
                                  start=(k == 0), stop=(k == 7))
                mi.then_inc(mmq.h, 1)
                n += 1
    mmq.v += 64

    n = 0
    for tt in range(4):
        for pi in range(2):
            for fc in range(4):
                @p.o("v")
                def _(v, n=n, tt=tt, pi=pi, fc=fc):
                    v.wait_ge(mmq.h, b_mmq + n + 1)
                    dstbuf = qf if pi == 0 else kf
                    v.tensor_copy(dstbuf[:, fc * 4096 + tt * 1024: fc * 4096 + tt * 1024 + 1024],
                                  psum[:, (n % 2) * 1024: (n % 2) * 1024 + 1024]).then_inc(evq.h, 1)
                n += 1
        for tc8 in range(8):
            @p.o("v")
            def _(v, n=n, tt=tt, tc8=tc8):
                v.wait_ge(mmq.h, b_mmq + n + 1)
                tc = tt * 8 + tc8
                dst = vaug[:, tc * 528: tc * 528 + 528].rearrange("pp (hh s) -> pp hh s", s=66)[:, :, 0:64]
                v.tensor_copy(dst, psum[:, (n % 2) * 1024: (n % 2) * 1024 + 512]
                              .rearrange("pp (hh d) -> pp hh d", d=64)).then_inc(evq.h, 1)
            n += 1
    evq.v += 64
    qkv_done = (evq, evq.v)

    sc_mm = p.sem()
    sc_ex = p.sem()
    sc_ms = p.sem()
    sc_av = p.sem()
    sc_ep = p.sem()
    ep_bc = p.sem()
    ep_rr = p.sem()
    b_smm, b_sex, b_sav, b_sep = sc_mm.v, sc_ex.v, sc_av.v, sc_ep.v
    b_epbc = ep_bc.v

    step = 0
    epn = 0
    AVB = [4, 5, 6]
    pend_av = [None]
    pend_ep = [None]

    def emit_pend_av():
        if pend_av[0] is not None:
            p.o("t")(pend_av[0])
            pend_av[0] = None

    def emit_pend_ep():
        if pend_ep[0] is not None:
            p.o("t")(pend_ep[0])
            pend_ep[0] = None

    for h8 in range(8):
        for qt in range(8):
            avb = AVB[(h8 * 8 + qt) % 3]
            first = True
            for g_ in range(qt + 1):
                for half in range(2):
                    st = step % 2
                    b0 = st * 1024
                    kt0 = g_ * 4 + half * 2

                    @p.o("t")
                    def _(t, h8=h8, qt=qt, st=st, b0=b0, kt0=kt0,
                          step=step, qkv=qkv_done):
                        if step == 0:
                            t.wait_ge(qkv[0].h, qkv[1])
                        if step >= 2:
                            t.wait_ge(sc_ex.h, b_sex + step - 1)
                        for i in range(2):
                            kt = kt0 + i
                            ps = psum[:, b0 + i * 512: b0 + i * 512 + 512]
                            mi = t.matmul(ps,
                                          kf[(h8 % 2) * 64:(h8 % 2) * 64 + 64,
                                             (h8 // 2) * 4096 + kt * 128:(h8 // 2) * 4096 + kt * 128 + 128],
                                          qf[(h8 % 2) * 64:(h8 % 2) * 64 + 64,
                                             (h8 // 2) * 4096 + qt * 512:(h8 // 2) * 4096 + qt * 512 + 512],
                                          start=True, stop=True)
                            if i == 1:
                                mi.then_inc(sc_mm.h, 1)
                    sc_mm.v += 1
                    emit_pend_av()
                    emit_pend_ep()

                    @p.o("s")
                    def _(s, step=step, st=st, b0=b0):
                        s.wait_ge(sc_mm.h, b_smm + step + 1)
                        if step >= 2:
                            s.wait_ge(sc_av.h, b_sav + step - 1)
                        s.activation(pbuf[:, (step % 2) * 1024:(step % 2) * 1024 + 1024],
                                     psum[:, b0: b0 + 1024], AF.Exp,
                                     scale=0.125).then_inc(sc_ex.h, 1)
                    sc_ex.v += 1

                    if g_ == qt:
                        @p.o("v")
                        def _(v, step=step, half=half):
                            v.wait_ge(sc_ex.h, b_sex + step + 1)
                            for i in range(2):
                                v.tensor_tensor(pbuf[:, (step % 2) * 1024 + i * 512:
                                                     (step % 2) * 1024 + i * 512 + 512],
                                                pbuf[:, (step % 2) * 1024 + i * 512:
                                                     (step % 2) * 1024 + i * 512 + 512],
                                                tau[:, (half * 2 + i) * 512:(half * 2 + i) * 512 + 512],
                                                ALU.mult).then_inc(sc_ms.h, 1)
                        sc_ms.v += 2

                    def _av(t, step=step, avb=avb, h8=h8, first=first,
                            msv=sc_ms.v, exv=sc_ex.v, kt0=kt0, epn=epn):
                        t.wait_ge(sc_ex.h, exv)
                        t.wait_ge(sc_ms.h, msv)
                        if first and epn >= 3:
                            t.wait_ge(sc_ep.h, b_sep + 2 * epn - 4)
                        for i in range(2):
                            tc = kt0 + i
                            mi = t.matmul(psum[0:66, avb * 512: avb * 512 + 512],
                                          vaug[:, tc * 528 + (h8 % 8) * 66: tc * 528 + (h8 % 8) * 66 + 66],
                                          pbuf[:, (step % 2) * 1024 + i * 512:
                                               (step % 2) * 1024 + i * 512 + 512],
                                          start=(first and i == 0), stop=False,
                                          skip_group_check=True)
                            if i == 1:
                                mi.then_inc(sc_av.h, 1)
                    pend_av[0] = _av
                    sc_av.v += 1
                    first = False
                    step += 1

            @p.o("v")
            def _(v, avb=avb, avv=sc_av.v, epn=epn):
                v.wait_ge(sc_av.h, avv)
                v.wait_ge(ep_bc.h, b_epbc + epn)
                v.reciprocal(rrow[0:1, 0:512],
                             psum[64:65, avb * 512: avb * 512 + 512]
                             ).then_inc(ep_rr.h, 1)
            ep_rr.v += 1

            def _bc(t, rrv=ep_rr.v, epn=epn):
                t.wait_ge(ep_rr.h, rrv)
                if epn > 0:
                    t.wait_ge(sc_ep.h, b_sep + 2 * epn - 1)
                t.matmul(psum[0:64, 7 * 512: 7 * 512 + 512], onesf[0:1, 0:64],
                         rrow[0:1, 0:512], start=True, stop=True).then_inc(ep_bc.h, 1)
            pend_ep[0] = _bc
            ep_bc.v += 1

            @p.o("v")
            def _(v, avb=avb, bcv=ep_bc.v, h8=h8, qt=qt, sv=sc_ep.v):
                v.wait_ge(ep_bc.h, bcv)
                v.wait_ge(sc_ep.h, sv)
                v.tensor_copy(sb['rbb'][0:64, 0:512],
                              psum[0:64, 7 * 512: 7 * 512 + 512]).then_inc(sc_ep.h, 1)
                v.wait_ge(sc_ep.h, sv + 1)
                v.tensor_tensor(aout[(h8 % 2) * 64:(h8 % 2) * 64 + 64,
                                     (h8 // 2) * 4096 + qt * 512:(h8 // 2) * 4096 + qt * 512 + 512],
                                psum[0:64, avb * 512: avb * 512 + 512],
                                sb['rbb'][0:64, 0:512], ALU.mult).then_inc(sc_ep.h, 1)
            sc_ep.v += 2
            epn += 1
    emit_pend_av()
    emit_pend_ep()

    attn_done = (sc_ep, sc_ep.v)

    wl = p.sem(tag='sw')

    @p.o("g")
    def _(g):
        for fc in range(8):
            dst = wos[:, fc * 512: fc * 512 + 512].rearrange("pp (kc j) -> pp kc j", j=128)
            g.dma_start(out=dst, in_=h['Wob'][fc].rearrange("(kc pp) j -> pp kc j", pp=128)).then_inc(wl.h, 16)
    wl.v += 16 * 8

    mmo = p.sem()
    evo = p.sem()
    sto = [p.sem(tag='hw'), p.sem(tag='hw')]
    b_mmo, b_evo = mmo.v, evo.v
    wst = sb['wst']

    @p.o("t")
    def _(t, wlv=wl.v, ad=attn_done):
        t.wait_ge(wl.h, wlv)
        t.wait_ge(ad[0].h, ad[1])
        n = 0
        for tt in range(4):
            for fc in range(8):
                if n >= 2:
                    t.wait_ge(evo.h, b_evo + n - 1)
                b0 = (n % 2) * 1024
                for nt in range(2):
                    ps = psum[:, b0 + nt * 512: b0 + nt * 512 + 512]
                    for k in range(4):
                        mi = t.matmul(ps, wos[:, fc * 512 + k * 128: fc * 512 + k * 128 + 128],
                                      aout[:, k * 4096 + tt * 1024 + nt * 512:
                                           k * 4096 + tt * 1024 + nt * 512 + 512],
                                      start=(k == 0), stop=(k == 3))
                    if nt == 1:
                        mi.then_inc(mmo.h, 1)
                n += 1
    mmo.v += 32

    n = 0
    for tt in range(4):
        for fc in range(8):
            @p.o("v")
            def _(v, n=n, stw=(sto[n % 2], sto[n % 2].v)):
                v.wait_ge(mmo.h, b_mmo + n + 1)
                if n >= 2:
                    v.wait_ge(stw[0].h, stw[1])
                v.tensor_copy(wst[:, (n % 2) * 1024:(n % 2) * 1024 + 1024],
                              psum[:, (n % 2) * 1024:(n % 2) * 1024 + 1024]).then_inc(evo.h, 1)

            @p.o("y")
            def _(y, n=n, tt=tt, fc=fc):
                y.wait_ge(evo.h, b_evo + n + 1)
                blk, tloc = (tt * 1024) // 2048, (tt * 1024) % 2048
                y.dma_start(out=h['rs_in'][blk, fc * 128:(fc + 1) * 128, tloc:tloc + 1024],
                            in_=wst[:, (n % 2) * 1024:(n % 2) * 1024 + 1024]).then_inc(sto[n % 2].h, 16)
            sto[n % 2].v += 16
            n += 1
    evo.v += 32

    ccr = p.sem(tag='co')

    @p.o("g")
    def _(g, a=sto[0].v, b=sto[1].v):
        g.wait_ge(sto[0].h, a)
        g.wait_ge(sto[1].h, b)
        g.collective_compute("ReduceScatter", ALU.add, replica_groups=rg,
                             ins=[h['rs_in'].ap().opt()],
                             outs=[h['rs_out'].ap().opt()]).then_inc(ccr.h, 1)
    ccr.v += 1

    rl = p.sem(tag='hw')
    bov = p.sem()
    wotok = sb['wotok']
    wofm = sb['wofm']

    @p.o("y")
    def _(y, c=ccr.v):
        y.wait_ge(ccr.h, c)
        y.dma_start(out=wofm[:, :].rearrange("pp (fc t) -> pp fc t", t=2048),
                    in_=h['rs_out'].ap().rearrange("(fc pp t) -> pp fc t", fc=8, pp=128)
                    ).then_inc(rl.h, 16)
    rl.v += 16

    @p.o("v")
    def _(v, r=rl.v, c=cl.v):
        v.wait_ge(rl.h, r)
        v.wait_ge(cl.h, c)
        for fc in range(8):
            v.tensor_scalar(wofm[:, fc * 2048:(fc + 1) * 2048], wofm[:, fc * 2048:(fc + 1) * 2048],
                            sb['bo_c'][:, fc:fc + 1], None, ALU.add).then_inc(bov.h, 1)
    bov.v += 8

    ev_tr = p.sem()
    transpose_tiles(
        p,
        src=lambda i: wofm[:, (i % 8) * 2048 + (i // 8) * 128: (i % 8) * 2048 + (i // 8) * 128 + 128],
        dst=lambda i: wotok[:, (i // 8) * 1024 + (i % 8) * 128: (i // 8) * 1024 + (i % 8) * 128 + 128],
        n=128, idn=sb['idn'], psum_bf=psum_bf, slots=TRAN_SLOTS,
        waits=[(bov, bov.v)], ev=ev_tr)

    u = p.sem()
    ln_done = p.sem()
    sxs = [p.sem(tag='hw'), p.sem(tag='hw')]
    sos = [p.sem(tag='hw'), p.sem(tag='hw')]
    tokb = h.get('x_tokb')
    bfb = sb['aout'][:, 0:4096]
    sob = [p.sem(tag='hw') for _ in range(4)]
    b_sob = [s.v for s in sob]
    ln_marks = []
    for tc in range(16):
        sx = sxs[tc % 2]

        @p.o("y")
        def _(y, tc=tc, need=(ln_marks[tc - 2] if tc >= 2 else 0), sx=sx):
            if tc >= 2:
                y.wait_ge(ln_done.h, need)
            y.dma_start(out=sb['xs_f32'][:, (tc % 2) * 1024:(tc % 2) * 1024 + 1024],
                        in_=h['x_tok'][tc * 128:(tc + 1) * 128, :]).then_inc(sx.h, 16)
        sx.v += 16
        d_before = ln_done.v
        ln_chunk(p,
                 a_ap=sb['xs_f32'][:, (tc % 2) * 1024:(tc % 2) * 1024 + 1024],
                 b_ap=wotok[:, tc * 1024:(tc + 1) * 1024],
                 tmp=sb['ln_tmp'][:, :], trash_bf=sb['ln_tr'][:, :], sc=sb['ln_sc'],
                 n_f=1024,
                 waits=[(sx, sx.v), (ev_tr, ev_tr.v)] +
                       ([(sos[tc % 2], sos[tc % 2].v)] if tc >= 2 else []) +
                       ([(sob[tc % 4], sob[tc % 4].v)]
                        if (tokb is not None and tc >= 4) else []),
                 u=u, done=ln_done,
                 out_f32=sb['x2s'][:, (tc % 2) * 1024:(tc % 2) * 1024 + 1024],
                 out_bf=(bfb[:, (tc % 4) * 1024:(tc % 4) * 1024 + 1024]
                         if tokb is not None else None),
                 absmax=(absmax_col[:, tc:tc + 1] if absmax_col is not None else None))
        so = sos[tc % 2]

        @p.o("y")
        def _(y, tc=tc, w=d_before + 1, so=so):
            y.wait_ge(ln_done.h, w)
            y.dma_start(out=h['x3_tok'][tc * 128:(tc + 1) * 128, :],
                        in_=sb['x2s'][:, (tc % 2) * 1024:(tc % 2) * 1024 + 1024]).then_inc(so.h, 16)
        so.v += 16
        if tokb is not None:
            @p.o("y")
            def _(y, tc=tc, w=d_before + 2, so2=sob[tc % 4]):
                y.wait_ge(ln_done.h, w)
                y.dma_start(out=tokb[tc * 128:(tc + 1) * 128, :],
                            in_=bfb[:, (tc % 4) * 1024:(tc % 4) * 1024 + 1024]
                            ).then_inc(so2.h, 16)
            sob[tc % 4].v += 16
        ln_marks.append(ln_done.v)

    @p.o("y")
    def _(y, w0=sos[0].v, w1=sos[1].v):
        y.wait_ge(sos[0].h, w0)
        y.wait_ge(sos[1].h, w1)
    return ln_done, sos, (sob, b_sob)


def mamba_phase(p, h, sb, psum, psum_bf, rg, g_hook=None):
    nc = p.nc
    xt0 = sb['xt0']
    xbcp = sb['xbcp']
    xtok = sb['xtok']
    ztok = sb['ztok']
    bcf = sb['bcf']
    vfull = sb['vfull']
    dtraw = sb['dtraw']
    dwork = sb['dwork']
    cumf = sb['cumf']
    w16 = sb['w16']
    mbig = sb['mbig']
    m0sb = sb['m0sb']
    btok = sb['btok']
    hst = sb['hst']
    hbb = sb['hbb']
    xtt = sb['xtt']
    szb = sb['szb']
    yt1 = sb['yt1']
    ssq = sb['ssq']
    wdg = sb['wdg']
    dide = sb['dide']
    bmask = sb['bmask']
    bigl = sb['bigl']
    idtl = sb['idtl']
    trilm = sb['trilm']
    onesf = sb['onesf']
    idn = sb['idn']
    iom = sb['iom']
    io_i32 = sb['io_i32']
    BIG = 32768.0

    cz = p.sem()
    cl = p.sem(tag='sw')

    @p.o("g")
    def _(g):
        for nm, d in [('cwx', 'cwx'), ('cwbc', 'cwbc'), ('cbx', 'cbx'), ('cbbc', 'cbbc'),
                      ('dtb', 'dtb'), ('alog', 'alog'), ('drep', 'drep'), ('nwc', 'nwc')]:
            g.dma_start(out=sb[nm][:, 0:h[d].shape[-1]], in_=h[d][:, :]).then_inc(cl.h, 16)
    cl.v += 16 * 8

    b_cz = cz.v

    @p.o("v")
    def _(v, clv=cl.v):
        v.tensor_single_scalar(sb['trf'][:, 0:128], io_i32[:, :], 0, ALU.is_ge).then_inc(cz.h, 1)
        v.wait_ge(cz.h, b_cz + 1)
        v.tensor_copy(trilm[:, :], sb['trf'][:, 0:128]).then_inc(cz.h, 1)
        v.wait_ge(cz.h, b_cz + 2)
        v.tensor_single_scalar(sb['trf'][:, 128:256], io_i32[:, :], 0, ALU.is_le).then_inc(cz.h, 1)
        v.wait_ge(cz.h, b_cz + 3)
        v.tensor_scalar(bigl[:, :], sb['trf'][:, 128:256], BIG, None, ALU.mult).then_inc(cz.h, 1)
        v.wait_ge(cz.h, b_cz + 4)
        v.tensor_copy(idtl[:, :].rearrange("pp (r j) -> pp r j", r=16),
                      idn[:, :].unsqueeze(1).broadcast_to([128, 16, 128])).then_inc(cz.h, 1)
        v.wait_ge(cz.h, b_cz + 5)
        v.wait_ge(cl.h, clv)
        for hh in range(16):
            v.tensor_scalar(dide[:, hh * 128:(hh + 1) * 128], idn[:, :],
                            sb['drep'][:, hh:hh + 1], None, ALU.mult).then_inc(cz.h, 1)
        v.wait_ge(cz.h, b_cz + 21)
        for fc in range(8):
            for tap in range(4):
                v.tensor_scalar(wdg[:, (fc * 4 + tap) * 128:(fc * 4 + tap) * 128 + 128],
                                iom[:, :], sb['cwx'][:, fc * 4 + tap:fc * 4 + tap + 1],
                                None, ALU.mult).then_inc(cz.h, 1)
        for fc in range(2):
            for tap in range(4):
                v.tensor_scalar(wdg[:, (32 + fc * 4 + tap) * 128:(32 + fc * 4 + tap) * 128 + 128],
                                iom[:, :], sb['cwbc'][:, fc * 4 + tap:fc * 4 + tap + 1],
                                None, ALU.mult).then_inc(cz.h, 1)
        v.wait_ge(cz.h, b_cz + 61)
        v.memset(xbcp[:, :].rearrange("pp (fc j) -> pp fc j", j=1027)[:, :, 0:3], 0.0).then_inc(cz.h, 1)
        v.wait_ge(cz.h, b_cz + 62)
        v.memset(hst[:, :], 0.0).then_inc(cz.h, 1)
        v.wait_ge(cz.h, b_cz + 63)
        v.memset(hbb[:, :], 0.0).then_inc(cz.h, 1)
        v.wait_ge(cz.h, b_cz + 64)
        v.memset(sb['dones'][0:16, 0:128], 1.0).then_inc(cz.h, 1)
        v.wait_ge(cz.h, b_cz + 65)
        v.tensor_single_scalar(sb['w16'][:, 80:81], io_i32[:, 0:1], -127,
                               ALU.is_equal).then_inc(cz.h, 1)
        v.wait_ge(cz.h, b_cz + 66)
        v.memset(cumf[0:1, :], 0.0).then_inc(cz.h, 1)
    cz.v += 67

    @p.o("g")
    def _(g, w=cz.v):
        g.wait_ge(cz.h, w)
        g.iota(sb['bio'][0:16, 0:2048], pattern=[[1, 2048]], base=0,
               channel_multiplier=-128).then_inc(cz.h, 1)
    cz.v += 1

    @p.o("v")
    def _(v, w=cz.v):
        v.wait_ge(cz.h, w)
        v.tensor_single_scalar(bmask[0:16, :], sb['bio'][0:16, 0:2048], 0, ALU.is_ge).then_inc(cz.h, 1)
        v.wait_ge(cz.h, w + 1)
        v.tensor_single_scalar(sb['trf'][0:16, 0:2048], sb['bio'][0:16, 0:2048], 128,
                               ALU.is_lt).then_inc(cz.h, 1)
        v.wait_ge(cz.h, w + 2)
        v.tensor_tensor(bmask[0:16, :], bmask[0:16, :], sb['trf'][0:16, 0:2048],
                        ALU.mult).then_inc(cz.h, 1)
        v.wait_ge(cz.h, w + 3)
        v.engine_nop().then_inc(cz.h, 1)
    cz.v += 4

    sA = p.sem()

    @p.o("s")
    def _(s, clv=cl.v):
        s.wait_ge(cl.h, clv)
        s.activation(sb['acol'][0:16, 0:1], sb['alog'][0:16, 0:1], AF.Exp).then_inc(sA.h, 1)
    sA.v += 1

    @p.o("v")
    def _(v, w=sA.v):
        v.wait_ge(sA.h, w)
        v.tensor_scalar(sb['acol'][0:16, 0:1], sb['acol'][0:16, 0:1], -1.0, None,
                        ALU.mult).then_inc(sA.h, 1)
    sA.v += 1
    consts_done = (cz, cz.v)

    wzl = p.sem(tag='sw')

    @p.o("g")
    def _(g):
        for k in range(8):
            src_ = h['wzb'].ap().rearrange("fc (kc pp) j -> kc pp fc j", pp=128)[k]
            g.dma_start(out=sb['wzr'][:, k * 1024:(k + 1) * 1024]
                        .rearrange("pp (fc j) -> pp fc j", j=128), in_=src_).then_inc(wzl.h, 16)
        for i in range(10):
            hsrc = h['wxsb'][i] if i < 8 else h['wbcb'][i - 8]
            dst = sb['wxsbcr'][:, i * 1024:(i + 1) * 1024].rearrange("pp (kc j) -> pp kc j", j=128)
            g.dma_start(out=dst, in_=hsrc.rearrange("(kc pp) j -> pp kc j", pp=128)).then_inc(wzl.h, 16)
        for k in range(8):
            g.dma_start(out=sb['wdtr'][:, k * 16:(k + 1) * 16],
                        in_=h['wdt'][k * 128:(k + 1) * 128, :]).then_inc(wzl.h, 16)
    wzl.v += 16 * 26
    weights_ready = (wzl, wzl.v)

    x0l = p.sem(tag='sw')
    mmw = p.sem()
    evw = p.sem()
    hsm = p.sem()
    mmc = p.sem()
    evc = p.sem()
    tkx = p.sem()
    tex = p.sem()
    dtv = p.sem()
    mmk = p.sem()
    ckv = p.sem()
    cks = p.sem()
    spz = p.sem(tag='hw')
    vsp = p.sem(tag='hw')

    ssq_marks = []
    prev_tile_guard = [consts_done]
    tile_cks = []

    for tt in range(4):
        T0 = tt * 1024

        @p.o("g")
        def _(g, tt=tt, T0=T0, w=mmw.v, wz=weights_ready):
            if tt == 0:
                g.wait_ge(wz[0].h, wz[1])
            else:
                g.wait_ge(mmw.h, w)
            g.dma_start(out=xt0[:, :].rearrange("pp (kc t) -> pp kc t", t=1024),
                        in_=h['x0fm'].ap().rearrange("(kc pp) t -> pp kc t", pp=128)
                        [:, :, 3 + T0: 3 + T0 + 1024]).then_inc(x0l.h, 16)
        x0l.v += 16
        if g_hook is not None:
            g_hook(tt)

        if tt > 0:
            @p.o("v")
            def _(v, w=hsm.v, tt=tt):
                v.wait_ge(hsm.h, w)
                v.tensor_copy(xbcp[:, :].rearrange("pp (fc j) -> pp fc j", j=1027)[:, :, 0:3],
                              sb['hstash'][:, (tt % 2) * 30:(tt % 2) * 30 + 30]
                              .rearrange("pp (fc j) -> pp fc j", j=3)).then_inc(hsm.h, 1)
            hsm.v += 1

        base_mm = mmw.v
        base_ev = evw.v

        @p.o("t")
        def _(t, tt=tt, xv=x0l.v, hv=hsm.v, base_ev=base_ev, guards=list(prev_tile_guard)):
            t.wait_ge(x0l.h, xv)
            t.wait_ge(hsm.h, hv)
            for s_, val in guards:
                t.wait_ge(s_.h, val)
            n = 0
            for fc in range(10):
                if base_ev + n - 1 > 0:
                    t.wait_ge(evw.h, base_ev + n - 1)
                b0 = (n % 2) * 1024
                for nt in range(2):
                    ps = psum[:, b0 + nt * 512: b0 + nt * 512 + 512]
                    for k in range(8):
                        mi = t.matmul(ps, sb['wxsbcr'][:, fc * 1024 + k * 128: fc * 1024 + k * 128 + 128],
                                      xt0[:, k * 1024 + nt * 512: k * 1024 + nt * 512 + 512],
                                      start=(k == 0), stop=(k == 7))
                    if nt == 1:
                        mi.then_inc(mmw.h, 1)
                n += 1
            for tc in range(8):
                t.wait_ge(evw.h, base_ev + n - 1)
                b0 = (n % 2) * 1024
                for nt in range(2):
                    ps = psum[:, b0 + nt * 512: b0 + nt * 512 + 512]
                    for k in range(8):
                        mi = t.matmul(ps, xt0[:, k * 1024 + tc * 128: k * 1024 + tc * 128 + 128],
                                      sb['wzr'][:, k * 1024 + nt * 512: k * 1024 + nt * 512 + 512],
                                      start=(k == 0), stop=(k == 7))
                    if nt == 1:
                        mi.then_inc(mmw.h, 1)
                n += 1
            t.wait_ge(evw.h, base_ev + n - 1)
            b0 = (n % 2) * 1024
            for nt in range(2):
                ps = psum[0:16, b0 + nt * 512: b0 + nt * 512 + 512]
                for k in range(8):
                    mi = t.matmul(ps, sb['wdtr'][:, k * 16:(k + 1) * 16],
                                  xt0[:, k * 1024 + nt * 512: k * 1024 + nt * 512 + 512],
                                  start=(k == 0), stop=(k == 7))
                if nt == 1:
                    mi.then_inc(mmw.h, 1)
        mmw.v += 19

        for n in range(19):
            if n < 10:
                @p.o("v")
                def _(v, n=n, tt=tt, w=base_mm + n + 1, sw=hsm.v):
                    v.wait_ge(mmw.h, w)
                    v.wait_ge(hsm.h, sw)
                    v.tensor_copy(xbcp[:, n * 1027 + 3: n * 1027 + 3 + 1024],
                                  psum[:, (n % 2) * 1024:(n % 2) * 1024 + 1024]).then_inc(evw.h, 1)

                @p.o("v")
                def _(v, n=n, tt=tt, w=base_ev + n + 1):
                    v.wait_ge(evw.h, w)
                    v.tensor_copy(sb['hstash'][:, ((tt + 1) % 2) * 30 + n * 3: ((tt + 1) % 2) * 30 + n * 3 + 3],
                                  xbcp[:, n * 1027 + 1024: n * 1027 + 1027]).then_inc(hsm.h, 1)
            elif n < 18:
                @p.o("v")
                def _(v, n=n, w=base_mm + n + 1):
                    v.wait_ge(mmw.h, w)
                    v.tensor_copy(ztok[:, (n - 10) * 1024:(n - 9) * 1024],
                                  psum[:, (n % 2) * 1024:(n % 2) * 1024 + 1024]).then_inc(evw.h, 1)
            else:
                @p.o("v")
                def _(v, n=n, tt=tt, w=base_mm + n + 1,
                      ckw=(tile_cks[tt - 2] if tt >= 2 else 0)):
                    v.wait_ge(mmw.h, w)
                    if tt >= 2:
                        v.wait_ge(cks.h, ckw)
                    v.tensor_copy(dtraw[0:16, (tt % 2) * 1024:(tt % 2) * 1024 + 1024],
                                  psum[0:16, (n % 2) * 1024:(n % 2) * 1024 + 1024]).then_inc(evw.h, 1)
        evw.v += 19
        hsm.v += 10

        base_cm = mmc.v
        base_ce = evc.v

        @p.o("t")
        def _(t, tt=tt, base_ev=base_ev, base_ce=base_ce, sw=hsm.v):
            for fc in range(10):
                t.wait_ge(evw.h, base_ev + fc + 1)
                t.wait_ge(hsm.h, sw)
                if base_ce + fc - 1 > 0:
                    t.wait_ge(evc.h, base_ce + fc - 1)
                b0 = 4 * 512 + (fc % 2) * 1024
                for nt in range(2):
                    ps = psum[:, b0 + nt * 512: b0 + nt * 512 + 512]
                    for tap in range(4):
                        mi = t.matmul(ps, wdg[:, (fc * 4 + tap) * 128:(fc * 4 + tap) * 128 + 128],
                                      xbcp[:, fc * 1027 + nt * 512 + tap: fc * 1027 + nt * 512 + tap + 512],
                                      start=(tap == 0), stop=(tap == 3))
                    if nt == 1:
                        mi.then_inc(mmc.h, 1)
        mmc.v += 10

        for fc in range(10):
            @p.o("s")
            def _(s, fc=fc, tt=tt, w=base_cm + fc + 1, clv=cl.v):
                s.wait_ge(mmc.h, w)
                s.wait_ge(cl.h, clv)
                b0 = 4 * 512 + (fc % 2) * 1024
                bias = (sb['cbx'][:, fc:fc + 1] if fc < 8 else sb['cbbc'][:, fc - 8:fc - 7])
                s.activation(xbcp[:, fc * 1027 + 3: fc * 1027 + 3 + 1024] if fc < 8 else
                             bcf[:, (fc - 8) * 1024: (fc - 8) * 1024 + 1024],
                             psum[:, b0: b0 + 1024], AF.Silu, bias=bias).then_inc(evc.h, 1)
        evc.v += 10

        base_tk = tkx.v
        base_te = tex.v

        @p.o("t")
        def _(t, tt=tt, w=base_ev + 19, base_ce=base_ce, base_te=base_te):
            t.wait_ge(evw.h, w)
            for i in range(64):
                fc, tc = i // 8, i % 8
                t.wait_ge(evc.h, base_ce + 10)
                if i >= 4:
                    t.wait_ge(tex.h, base_te + i - 3)
                ps = psum_bf[:, TRAN_SLOTS[i % 4]: TRAN_SLOTS[i % 4] + 128]
                t.transpose(ps, xbcp[:, fc * 1027 + 3 + tc * 128: fc * 1027 + 3 + tc * 128 + 128],
                            idn[:, :]).then_inc(tkx.h, 1)
        tkx.v += 64

        for i in range(64):
            fc, tc = i // 8, i % 8

            @p.o("v")
            def _(v, i=i, fc=fc, tc=tc, w=base_tk + i + 1):
                v.wait_ge(tkx.h, w)
                ps = psum_bf[:, TRAN_SLOTS[i % 4]: TRAN_SLOTS[i % 4] + 128]
                v.tensor_copy(xtok[:, tc * 1024 + fc * 128: tc * 1024 + fc * 128 + 128],
                              ps).then_inc(tex.h, 1)
        tex.v += 64
        xs_t_done = (tex, tex.v)

        for c8 in range(8):
            c = tt * 8 + c8
            _mamba_chunk(p, h, sb, psum, psum_bf, tt, c8, c,
                         mmk, ckv, cks, spz, dtv, xs_t_done,
                         (evc, base_ce + 10), (evw, base_ev + 19), BIG)

        tile_cks.append(cks.v)

        @p.o("y")
        def _(y, tt=tt, w=ckv.v):
            y.wait_ge(ckv.h, w)
            y.dma_start(out=h['vsp'][tt], in_=vfull[:, :]).then_inc(vsp.h, 16)
        vsp.v += 16
        ssq_marks.append(ckv.v)

        prev_tile_guard = [(tex, tex.v), (evc, evc.v), (ckv, ckv.v), (vsp, vsp.v)]

    chunks_done = (ckv, ckv.v)
    return dict(chunks_done=chunks_done, ssq_marks=ssq_marks, vsp=vsp, spz=spz,
                ckv=ckv, cks=cks, mmk=mmk)


def _mamba_chunk(p, h, sb, psum, psum_bf, tt, c8, c, mmk, ckv, cks, spz, dtv,
                 xs_t_done, conv_done, win_done, BIG):
    dwork = sb['dwork']
    dtraw = sb['dtraw']
    w16 = sb['w16']
    wbt = sb['wbt']
    mbig = sb['mbig']
    m0sb = sb['m0sb']
    btok = sb['btok']
    xtt = sb['xtt']
    szb = sb['szb']
    yt1 = sb['yt1']
    vtrash = sb['vtrash']
    bcf = sb['bcf']
    xtok = sb['xtok']
    ztok = sb['ztok']
    hst = sb['hst']
    hbb = sb['hbb']
    vfull = sb['vfull']
    ssq = sb['ssq']
    cumf = sb['cumf']
    cum2048 = sb['cum2048']
    onesf = sb['onesf']
    idn = sb['idn']
    sl = 0

    V0 = ckv.v
    S0 = cks.v
    M0v = mmk.v
    D0 = dtv.v
    Z0 = spz.v

    @p.o("s")
    def _(s, c=c, S0=S0, V0=V0, wd=win_done):
        if c > 0:
            s.wait_ge(ckv.h, V0)
        s.wait_ge(wd[0].h, wd[1])
        s.activation(dwork[0:16, 0:128],
                     dtraw[0:16, (tt % 2) * 1024 + c8 * 128:(tt % 2) * 1024 + c8 * 128 + 128],
                     AF.Exp, bias=sb['dtb'][0:16, 0:1]).then_inc(cks.h, 1)
    cks.v += 1

    @p.o("v")
    def _(v, S0=S0):
        v.wait_ge(cks.h, S0 + 1)
        v.tensor_scalar(dwork[0:16, 0:128], dwork[0:16, 0:128], 1.0, None,
                        ALU.add).then_inc(cks.h, 1)
    cks.v += 1

    @p.o("s")
    def _(s, S0=S0):
        s.wait_ge(cks.h, S0 + 2)
        s.activation(dwork[0:16, 0:128], dwork[0:16, 0:128], AF.Ln).then_inc(cks.h, 1)
        s.wait_ge(cks.h, S0 + 3)
        s.activation(dwork[0:16, 128:256], dwork[0:16, 0:128], AF.Ln).then_inc(cks.h, 1)
    cks.v += 2

    @p.o("v")
    def _(v, S0=S0, V0=V0, sa=None):
        v.wait_ge(cks.h, S0 + 4)
        v.tensor_scalar(dwork[0:16, 128:256], dwork[0:16, 128:256], -60.0, None,
                        ALU.max).then_inc(ckv.h, 1)
        v.wait_ge(ckv.h, V0 + 1)
        v.tensor_scalar(dwork[0:16, 256:384], dwork[0:16, 0:128],
                        sb['acol'][0:16, 0:1], None, ALU.mult).then_inc(ckv.h, 1)
        v.wait_ge(ckv.h, V0 + 2)
        v.tensor_tensor_scan(dwork[0:16, 384:512], sb['dones'][0:16, 0:128],
                             dwork[0:16, 256:384], 0.0, ALU.mult, ALU.add).then_inc(ckv.h, 1)
        v.wait_ge(ckv.h, V0 + 3)
        v.tensor_sub(dwork[0:16, 512:640], dwork[0:16, 128:256],
                     dwork[0:16, 384:512]).then_inc(ckv.h, 1)
        v.wait_ge(ckv.h, V0 + 4)
        v.tensor_scalar(dwork[0:16, 640:768], dwork[0:16, 512:640], -BIG, None,
                        ALU.add).then_inc(ckv.h, 1)
    ckv.v += 5

    @p.o("t")
    def _(t, V0=V0, M0v=M0v, xtd=xs_t_done, c=c):
        t.wait_ge(ckv.h, V0 + 5)
        if c > 0:
            t.wait_ge(ckv.h, V0)
        t.transpose(psum[:, 1024:1040], dwork[0:16, 384:512], sb['iom'][0:16, 0:16]).then_inc(mmk.h, 1)
        t.transpose(psum[:, 1040:1056], dwork[0:16, 512:640], sb['iom'][0:16, 0:16]).then_inc(mmk.h, 1)
    mmk.v += 2

    @p.o("v")
    def _(v, M0v=M0v, V0=V0):
        v.wait_ge(mmk.h, M0v + 2)
        v.tensor_copy(w16[:, 0:16], psum[:, 1024:1040]).then_inc(ckv.h, 1)
        v.tensor_copy(w16[:, 16:32], psum[:, 1040:1056]).then_inc(ckv.h, 1)
    ckv.v += 2

    @p.o("t")
    def _(t, V0=V0):
        t.wait_ge(ckv.h, V0 + 6)
        t.matmul(psum[0:1, 1552:1568], w16[:, 80:81], w16[:, 0:16],
                 start=True, stop=True).then_inc(mmk.h, 1)
    mmk.v += 1

    @p.o("v")
    def _(v, M0v=M0v, V0=V0):
        v.tensor_tensor(cum2048[0:16, :].rearrange("p (hh t) -> p hh t", t=128),
                        sb['bmask'][0:16, :].rearrange("p (hh t) -> p hh t", t=128),
                        dwork[0:16, 384:512].unsqueeze(1).broadcast_to([16, 16, 128]),
                        ALU.mult).then_inc(ckv.h, 1)
        v.wait_ge(mmk.h, M0v + 3)
        v.tensor_copy(cumf[0:1, 0:16], psum[0:1, 1552:1568]).then_inc(ckv.h, 1)
    ckv.v += 2

    @p.o("t")
    def _(t, V0=V0):
        t.wait_ge(ckv.h, V0 + 9)
        t.matmul(psum[:, 1536:1552], onesf[0:1, 0:128], cumf[0:1, 0:16],
                 start=True, stop=True).then_inc(mmk.h, 1)
    mmk.v += 1

    @p.o("v")
    def _(v, M0v=M0v, V0=V0):
        v.wait_ge(mmk.h, M0v + 4)
        v.tensor_copy(w16[:, 32:48], psum[:, 1536:1552]).then_inc(ckv.h, 1)
        v.wait_ge(ckv.h, V0 + 10)
        v.tensor_add(w16[:, 64:80], w16[:, 16:32], w16[:, 32:48]).then_inc(ckv.h, 1)
    ckv.v += 2

    @p.o("s")
    def _(s, V0=V0, S0=S0):
        s.wait_ge(ckv.h, V0 + 11)
        s.activation(wbt[:, 0:16], w16[:, 0:16], AF.Exp).then_inc(cks.h, 1)
        s.activation(w16[:, 48:64], w16[:, 32:48], AF.Exp).then_inc(cks.h, 1)
        s.activation(wbt[:, 16:32], w16[:, 64:80], AF.Exp).then_inc(cks.h, 1)
    cks.v += 3

    @p.o("t")
    def _(t, c=c, M0v=M0v, V0=V0, cd=conv_done):
        t.wait_ge(cd[0].h, cd[1])
        if c > 0:
            t.wait_ge(ckv.h, V0)
        t.matmul(psum[:, 0:128], bcf[:, (c % 8) * 128:((c % 8) + 1) * 128],
                 bcf[:, 1024 + (c % 8) * 128: 1024 + (c % 8) * 128 + 128],
                 start=True, stop=True).then_inc(mmk.h, 1)
        t.transpose(psum_bf[:, 2 * 1024 + 512: 2 * 1024 + 512 + 128],
                    bcf[:, (c % 8) * 128:((c % 8) + 1) * 128], idn[:, :]).then_inc(mmk.h, 1)
    mmk.v += 2

    @p.o("v")
    def _(v, M0v=M0v, V0=V0, sl=sl):
        v.wait_ge(mmk.h, M0v + 5)
        v.tensor_tensor(m0sb[:, sl * 128:(sl + 1) * 128], psum[:, 0:128],
                        sb['trilm'][:, :], ALU.mult).then_inc(ckv.h, 1)
        v.wait_ge(mmk.h, M0v + 6)
        v.tensor_copy(btok[:, sl * 128:(sl + 1) * 128],
                      psum_bf[:, 2 * 1024 + 512: 2 * 1024 + 512 + 128]).then_inc(ckv.h, 1)
    ckv.v += 2

    @p.o("t")
    def _(t, V0=V0, M0v=M0v):
        t.wait_ge(ckv.h, V0 + 13)
        for nt in range(4):
            t.matmul(psum[:, nt * 512:(nt + 1) * 512], sb['dones'][0:16, 0:128],
                     cum2048[0:16, nt * 512:(nt + 1) * 512], start=True, stop=False,
                     skip_group_check=True)
            t.matmul(psum[:, nt * 512:(nt + 1) * 512], dwork[0:16, 640:768],
                     sb['bmask'][0:16, nt * 512:(nt + 1) * 512], start=False, stop=False,
                     skip_group_check=True)
            mi = t.matmul(psum[:, nt * 512:(nt + 1) * 512], sb['bigl'][:, :],
                          sb['idtl'][:, nt * 512:(nt + 1) * 512], start=False, stop=True,
                          skip_group_check=True)
            if nt == 3:
                mi.then_inc(mmk.h, 1)
    mmk.v += 1

    @p.o("s")
    def _(s, M0v=M0v, S0=S0, sl=sl, c=c, V0=V0):
        s.wait_ge(mmk.h, M0v + 7)
        if c > 0:
            s.wait_ge(mmk.h, 10 * c - 1)
        s.activation(mbig[:, sl * 2048:(sl + 1) * 2048], psum[:, 0:2048], AF.Exp).then_inc(cks.h, 1)
    cks.v += 1

    @p.o("v")
    def _(v, S0=S0, V0=V0, sl=sl, c=c):
        v.wait_ge(cks.h, S0 + 8)
        v.wait_ge(ckv.h, V0 + 12)
        v.tensor_tensor(mbig[:, sl * 2048:(sl + 1) * 2048].rearrange("pp (hh j) -> pp hh j", j=128),
                        mbig[:, sl * 2048:(sl + 1) * 2048].rearrange("pp (hh j) -> pp hh j", j=128),
                        m0sb[:, sl * 128:(sl + 1) * 128].unsqueeze(1).broadcast_to([128, 16, 128]),
                        ALU.mult).then_inc(ckv.h, 1)
        v.wait_ge(ckv.h, V0 + 14)
        v.tensor_tensor(mbig[:, sl * 2048:(sl + 1) * 2048], mbig[:, sl * 2048:(sl + 1) * 2048],
                        sb['dide'][:, :], ALU.add).then_inc(ckv.h, 1)
        v.wait_ge(cks.h, S0 + 7)
        if c > 0:
            v.wait_ge(mmk.h, 10 * c)
        v.tensor_tensor(xtt[:, sl * 1024:(sl + 1) * 1024].rearrange("pp (hh d) -> pp hh d", d=64),
                        xtok[:, (c % 8) * 1024:((c % 8) + 1) * 1024].rearrange("pp (hh d) -> pp hh d", d=64),
                        wbt[:, 16:32].unsqueeze(2).broadcast_to([128, 16, 64]),
                        ALU.mult).then_inc(ckv.h, 1)
    ckv.v += 3

    @p.o("s")
    def _(s, S0=S0, c=c, V0=V0, sl=sl, wd=win_done):
        if c > 0:
            s.wait_ge(ckv.h, V0)
        s.wait_ge(wd[0].h, wd[1])
        s.activation(szb[:, sl * 1024:(sl + 1) * 1024],
                     ztok[:, (c % 8) * 1024:((c % 8) + 1) * 1024], AF.Silu).then_inc(cks.h, 1)
    cks.v += 1

    @p.o("t")
    def _(t, V0=V0, M0v=M0v, sl=sl, c=c, xtd=xs_t_done):
        t.wait_ge(xtd[0].h, xtd[1])
        t.wait_ge(ckv.h, V0 + 13)
        for nt in range(2):
            mi = t.matmul(psum[:, (6 + nt) * 512:(7 + nt) * 512],
                          bcf[:, 1024 + (c % 8) * 128: 1024 + (c % 8) * 128 + 128],
                          hbb[:, nt * 512:(nt + 1) * 512], start=True, stop=True,
                          skip_group_check=True)
        mi.then_inc(mmk.h, 1)
        t.wait_ge(ckv.h, V0 + 15)
        for hh in range(16):
            mi = t.matmul(psum[:, 2048 + hh * 64: 2048 + hh * 64 + 64],
                          mbig[:, sl * 2048 + hh * 128: sl * 2048 + hh * 128 + 128],
                          xtok[:, (c % 8) * 1024 + hh * 64:(c % 8) * 1024 + hh * 64 + 64],
                          start=(hh == 0 or hh == 8), stop=(hh == 7 or hh == 15),
                          skip_group_check=True)
        mi.then_inc(mmk.h, 1)
        t.wait_ge(ckv.h, V0 + 16)
        for nt in range(2):
            mi = t.matmul(psum[:, nt * 512:(nt + 1) * 512], btok[:, sl * 128:(sl + 1) * 128],
                          xtt[:, sl * 1024 + nt * 512: sl * 1024 + nt * 512 + 512],
                          start=True, stop=True, skip_group_check=True)
        mi.then_inc(mmk.h, 1)
    mmk.v += 3

    @p.o("v")
    def _(v, V0=V0, M0v=M0v, S0=S0, sl=sl, c=c, c8=c8):
        v.wait_ge(mmk.h, M0v + 9)
        v.wait_ge(cks.h, S0 + 5)
        v.tensor_tensor(yt1[:, :].rearrange("pp (hh d) -> pp hh d", d=64),
                        psum[:, 3072:4096].rearrange("pp (hh d) -> pp hh d", d=64),
                        wbt[:, 0:16].unsqueeze(2).broadcast_to([128, 16, 64]),
                        ALU.mult).then_inc(ckv.h, 1)
        v.wait_ge(ckv.h, V0 + 17)
        v.tensor_tensor(yt1[:, :], yt1[:, :], psum[:, 2048:3072], ALU.add).then_inc(ckv.h, 1)
        v.wait_ge(ckv.h, V0 + 18)
        v.wait_ge(cks.h, S0 + 9)
        v.tensor_tensor(vfull[:, c8 * 1024:(c8 + 1) * 1024], yt1[:, :],
                        szb[:, sl * 1024:(sl + 1) * 1024], ALU.mult).then_inc(ckv.h, 1)
        v.wait_ge(ckv.h, V0 + 19)
        v.tensor_tensor(sb['vtrash'][:, :], vfull[:, c8 * 1024:(c8 + 1) * 1024],
                        vfull[:, c8 * 1024:(c8 + 1) * 1024], ALU.mult).then_inc(ckv.h, 1)
        v.wait_ge(ckv.h, V0 + 20)
        v.reduce_sum(sb['ssq'][:, c:c + 1], sb['vtrash'][:, :], axis=AX.X).then_inc(ckv.h, 1)
        v.wait_ge(cks.h, S0 + 6)
        v.tensor_tensor(yt1[:, :].rearrange("pp (hh d) -> pp hh d", d=64),
                        hst[:, :].rearrange("pp (hh d) -> pp hh d", d=64),
                        w16[:, 48:64].unsqueeze(2).broadcast_to([128, 16, 64]),
                        ALU.mult).then_inc(ckv.h, 1)
        v.wait_ge(ckv.h, V0 + 22)
        v.wait_ge(mmk.h, M0v + 10)
        v.tensor_tensor(hst[:, :], yt1[:, :], psum[:, 0:1024], ALU.add).then_inc(ckv.h, 1)
        v.wait_ge(ckv.h, V0 + 23)
        v.tensor_copy(hbb[:, :], hst[:, :]).then_inc(ckv.h, 1)
    ckv.v += 8


def mamba_tail(p, h, sb, psum, psum_bf, rg, mres):
    ckv = mres['ckv']
    vsp = mres['vsp']
    ssq = sb['ssq']
    vfull = sb['vfull']
    idn = sb['idn']
    onesf = sb['onesf']

    ag = p.sem(tag='sw')
    cc = p.sem(tag='co')
    b_ag, b_cc = ag.v, cc.v

    @p.o("g")
    def _(g, w=mres['chunks_done']):
        g.wait_ge(w[0].h, w[1])
        g.dma_start(out=h['ag2_in'].ap().rearrange("(c s) -> s c", s=128),
                    in_=ssq[:, 0:32], allow_slow_non_contiguous=True).then_inc(ag.h, 16)
        g.wait_ge(ag.h, b_ag + 16)
        g.collective_compute("AllGather", ALU.bypass, replica_groups=rg,
                             ins=[h['ag2_in'].ap().opt()],
                             outs=[h['ag2_out'].ap().opt()]).then_inc(cc.h, 1)
        g.wait_ge(cc.h, b_cc + 1)
        g.dma_start(out=ssq[:, 32:64], in_=h['ag2_out'].ap()[0].rearrange("(c s) -> s c", s=128),
                    allow_slow_non_contiguous=True).then_inc(ag.h, 16)
        g.dma_start(out=ssq[:, 64:96], in_=h['ag2_out'].ap()[1].rearrange("(c s) -> s c", s=128),
                    allow_slow_non_contiguous=True).then_inc(ag.h, 16)
    ag.v += 48
    cc.v += 1

    rq = p.sem()
    b_rq = rq.v

    @p.o("v")
    def _(v, w=ag.v):
        v.wait_ge(ag.h, w)
        v.tensor_add(ssq[:, 0:32], ssq[:, 32:64], ssq[:, 64:96]).then_inc(rq.h, 1)
        v.wait_ge(rq.h, b_rq + 1)
        v.tensor_scalar(ssq[:, 0:32], ssq[:, 0:32], 1.0 / 2048.0, 1e-5, ALU.mult,
                        ALU.add).then_inc(rq.h, 1)
    rq.v += 2

    @p.o("s")
    def _(s, w=rq.v):
        s.wait_ge(rq.h, w)
        s.activation(ssq[:, 32:64], ssq[:, 0:32], AF.Sqrt).then_inc(rq.h, 1)
    rq.v += 1

    @p.o("v")
    def _(v, w=rq.v):
        v.wait_ge(rq.h, w)
        v.reciprocal(ssq[:, 32:64], ssq[:, 32:64]).then_inc(rq.h, 1)
    rq.v += 1
    rsq_ready = (rq, rq.v)

    wol = p.sem(tag='sw')

    @p.o("g")
    def _(g):
        for fc in range(8):
            dst = sb['wob'][:, fc * 1024:(fc + 1) * 1024].rearrange("pp (kc j) -> pp kc j", j=128)
            g.dma_start(out=dst, in_=h['woutb'][fc].rearrange("(kc pp) j -> pp kc j", pp=128)).then_inc(wol.h, 16)
    wol.v += 16 * 8

    vrl = p.sem(tag='hw')
    vnv = p.sem()
    tk2 = p.sem()
    te2 = p.sem()
    mm2 = p.sem()
    ev2 = p.sem()
    st2 = [p.sem(tag='hw'), p.sem(tag='hw')]

    for tt in range(4):
        @p.o("y")
        def _(y, tt=tt, w=vsp.v, w2=(te2, te2.v)):
            y.wait_ge(vsp.h, w)
            if tt:
                y.wait_ge(w2[0].h, w2[1])
            y.dma_start(out=vfull[:, :], in_=h['vsp'][tt]).then_inc(vrl.h, 16)
        vrl.v += 16

        @p.o("v")
        def _(v, tt=tt, w=vrl.v, rv=rsq_ready):
            v.wait_ge(vrl.h, w)
            v.wait_ge(rv[0].h, rv[1])
            for c8 in range(8):
                v.tensor_scalar(vfull[:, c8 * 1024:(c8 + 1) * 1024],
                                vfull[:, c8 * 1024:(c8 + 1) * 1024],
                                ssq[:, 32 + tt * 8 + c8: 33 + tt * 8 + c8], None,
                                ALU.mult).then_inc(vnv.h, 1)
        vnv.v += 8

        base_tk = tk2.v
        base_te = te2.v

        @p.o("t")
        def _(t, tt=tt, w=vnv.v, base_te=base_te):
            t.wait_ge(vnv.h, w)
            for i in range(64):
                c8, fc = i // 8, i % 8
                if i >= 4:
                    t.wait_ge(te2.h, base_te + i - 3)
                ps = psum_bf[:, TRAN_SLOTS[i % 4]: TRAN_SLOTS[i % 4] + 128]
                t.transpose(ps, vfull[:, c8 * 1024 + fc * 128: c8 * 1024 + fc * 128 + 128],
                            idn[:, :]).then_inc(tk2.h, 1)
        tk2.v += 64

        for i in range(64):
            c8, fc = i // 8, i % 8

            @p.o("v")
            def _(v, i=i, c8=c8, fc=fc, w=base_tk + i + 1, clv=None):
                v.wait_ge(tk2.h, w)
                ps = psum_bf[:, TRAN_SLOTS[i % 4]: TRAN_SLOTS[i % 4] + 128]
                v.tensor_scalar(sb['vfm'][:, fc * 1024 + c8 * 128: fc * 1024 + c8 * 128 + 128],
                                ps, sb['nwc'][:, fc:fc + 1], None, ALU.mult).then_inc(te2.h, 1)
        te2.v += 64

        base_mm = mm2.v
        base_ev = ev2.v

        @p.o("t")
        def _(t, tt=tt, w=te2.v, wv=wol.v, base_ev=base_ev):
            t.wait_ge(te2.h, w)
            t.wait_ge(wol.h, wv)
            for fc in range(8):
                if base_ev + fc - 1 > 0:
                    t.wait_ge(ev2.h, base_ev + fc - 1)
                b0 = (fc % 2) * 1024
                for nt in range(2):
                    ps = psum[:, b0 + nt * 512: b0 + nt * 512 + 512]
                    for k in range(8):
                        mi = t.matmul(ps, sb['wob'][:, fc * 1024 + k * 128: fc * 1024 + k * 128 + 128],
                                      sb['vfm'][:, k * 1024 + nt * 512: k * 1024 + nt * 512 + 512],
                                      start=(k == 0), stop=(k == 7))
                    if nt == 1:
                        mi.then_inc(mm2.h, 1)
        mm2.v += 8

        for fc in range(8):
            @p.o("v")
            def _(v, fc=fc, w=base_mm + fc + 1, sw=(st2[fc % 2], st2[fc % 2].v)):
                v.wait_ge(mm2.h, w)
                v.wait_ge(sw[0].h, sw[1])
                v.tensor_copy(sb['wst2'][:, (fc % 2) * 1024:(fc % 2) * 1024 + 1024],
                              psum[:, (fc % 2) * 1024:(fc % 2) * 1024 + 1024]).then_inc(ev2.h, 1)

            @p.o("y")
            def _(y, fc=fc, tt=tt, w=base_ev + fc + 1):
                y.wait_ge(ev2.h, w)
                blk, tloc = (tt * 1024) // 2048, (tt * 1024) % 2048
                y.dma_start(out=h['rs2_in'][blk, fc * 128:(fc + 1) * 128, tloc:tloc + 1024],
                            in_=sb['wst2'][:, (fc % 2) * 1024:(fc % 2) * 1024 + 1024]).then_inc(st2[fc % 2].h, 16)
            st2[fc % 2].v += 16
        ev2.v += 8

    ccr = p.sem(tag='co')

    @p.o("g")
    def _(g, a=st2[0].v, b=st2[1].v):
        g.wait_ge(st2[0].h, a)
        g.wait_ge(st2[1].h, b)
        g.collective_compute("ReduceScatter", ALU.add, replica_groups=rg,
                             ins=[h['rs2_in'].ap().opt()],
                             outs=[h['rs2_out'].ap().opt()]).then_inc(ccr.h, 1)
    ccr.v += 1

    rl = p.sem(tag='hw')

    @p.o("y")
    def _(y, c=ccr.v):
        y.wait_ge(ccr.h, c)
        y.dma_start(out=sb['ofm'][:, :].rearrange("pp (fc t) -> pp fc t", t=2048),
                    in_=h['rs2_out'].ap().rearrange("(fc pp t) -> pp fc t", fc=8, pp=128)
                    ).then_inc(rl.h, 16)
    rl.v += 16

    ev_tr = p.sem()
    transpose_tiles(
        p,
        src=lambda i: sb['ofm'][:, (i % 8) * 2048 + (i // 8) * 128: (i % 8) * 2048 + (i // 8) * 128 + 128],
        dst=lambda i: sb['outok'][:, (i // 8) * 1024 + (i % 8) * 128: (i // 8) * 1024 + (i % 8) * 128 + 128],
        n=128, idn=idn, psum_bf=psum_bf, slots=TRAN_SLOTS,
        waits=[(rl, rl.v)], ev=ev_tr)

    u = p.sem()
    ln_done = p.sem()
    rms2 = p.sem()
    sxs = [p.sem(tag='hw'), p.sem(tag='hw')]
    sos = [p.sem(tag='hw'), p.sem(tag='hw')]
    tokb = h.get('x_tokb')
    bfb = sb['ofm'][:, 0:4096]
    sob = [p.sem(tag='hw') for _ in range(4)]
    b_sob = [s.v for s in sob]
    ln_marks = []
    for tc in range(16):
        sx = sxs[tc % 2]

        @p.o("y")
        def _(y, tc=tc, need=(ln_marks[tc - 2] if tc >= 2 else 0), sx=sx):
            if tc >= 2:
                y.wait_ge(ln_done.h, need)
            y.dma_start(out=sb['xs_f32'][:, (tc % 2) * 1024:(tc % 2) * 1024 + 1024],
                        in_=h['x0tokh'][tc * 128:(tc + 1) * 128, :]).then_inc(sx.h, 16)
        sx.v += 16

        R0 = rms2.v

        @p.o("v")
        def _(v, tc=tc, w=ev_tr.v, R0=R0, sxv=sx.v, sx=sx):
            v.wait_ge(ev_tr.h, w)
            v.wait_ge(sx.h, sxv)
            ot = sb['outok'][:, tc * 1024:(tc + 1) * 1024]
            v.tensor_tensor(sb['vtrash'][:, :], ot, ot, ALU.mult).then_inc(rms2.h, 1)
            v.wait_ge(rms2.h, R0 + 1)
            v.reduce_sum(sb['ln_sc'][:, 4:5], sb['vtrash'][:, :], axis=AX.X).then_inc(rms2.h, 1)
            v.wait_ge(rms2.h, R0 + 2)
            v.tensor_scalar(sb['ln_sc'][:, 4:5], sb['ln_sc'][:, 4:5], 1.0 / 1024.0, 1e-5,
                            ALU.mult, ALU.add).then_inc(rms2.h, 1)
        rms2.v += 3

        @p.o("s")
        def _(s, R0=R0):
            s.wait_ge(rms2.h, R0 + 3)
            s.activation(sb['ln_sc'][:, 5:6], sb['ln_sc'][:, 4:5], AF.Sqrt).then_inc(rms2.h, 1)
        rms2.v += 1

        @p.o("v")
        def _(v, tc=tc, R0=R0, uv=u.v):
            v.wait_ge(rms2.h, R0 + 4)
            v.reciprocal(sb['ln_sc'][:, 5:6], sb['ln_sc'][:, 5:6]).then_inc(rms2.h, 1)
            v.wait_ge(rms2.h, R0 + 5)
            v.wait_ge(u.h, uv)
            ot = sb['outok'][:, tc * 1024:(tc + 1) * 1024]
            v.scalar_tensor_tensor(sb['ln_tmp'][:, :], ot, sb['ln_sc'][:, 5:6],
                                   sb['xs_f32'][:, (tc % 2) * 1024:(tc % 2) * 1024 + 1024],
                                   ALU.mult, ALU.add).then_inc(rms2.h, 1)
        rms2.v += 2

        d_before = ln_done.v
        ln_chunk(p,
                 a_ap=sb['ln_tmp'][:, :],
                 b_ap=sb['xs_f32'][:, (tc % 2) * 1024:(tc % 2) * 1024 + 1024],
                 tmp=sb['ln_tmp2'][:, :], trash_bf=sb['ln_tr'][:, :], sc=sb['ln_sc'],
                 n_f=1024,
                 waits=[(rms2, rms2.v)] +
                       ([(sos[tc % 2], sos[tc % 2].v)] if tc >= 2 else []) +
                       ([(sob[tc % 4], sob[tc % 4].v)]
                        if (tokb is not None and tc >= 4) else []),
                 u=u, done=ln_done,
                 out_f32=sb['x2s'][:, (tc % 2) * 1024:(tc % 2) * 1024 + 1024],
                 out_bf=(bfb[:, (tc % 4) * 1024:(tc % 4) * 1024 + 1024]
                         if tokb is not None else None))
        so = sos[tc % 2]

        @p.o("y")
        def _(y, tc=tc, w=d_before + 1, so=so):
            y.wait_ge(ln_done.h, w)
            y.dma_start(out=h['x1_tok'][tc * 128:(tc + 1) * 128, :],
                        in_=sb['x2s'][:, (tc % 2) * 1024:(tc % 2) * 1024 + 1024]).then_inc(so.h, 16)
        so.v += 16
        if tokb is not None:
            @p.o("y")
            def _(y, tc=tc, w=d_before + 2, so2=sob[tc % 4]):
                y.wait_ge(ln_done.h, w)
                y.dma_start(out=tokb[tc * 128:(tc + 1) * 128, :],
                            in_=bfb[:, (tc % 4) * 1024:(tc % 4) * 1024 + 1024]
                            ).then_inc(so2.h, 16)
            sob[tc % 4].v += 16
        ln_marks.append(ln_done.v)

    @p.o("y")
    def _(y, w0=sos[0].v, w1=sos[1].v):
        y.wait_ge(sos[0].h, w0)
        y.wait_ge(sos[1].h, w1)
    return ln_done, sos, (sob, b_sob)


def barrier(p, waits):
    for eng in ("g", "t", "v", "s", "y"):
        @p.o(eng)
        def _(q, waits=list(waits)):
            for s, v in waits:
                q.wait_ge(s.h, v)


def fm_from_tok(p, *, tok_dram, stage, xfm, psum_bf, idn, waits, slots=None,
                tok_bf=None, bf_waits=()):
    if slots is None:
        slots = TRAN_SLOTS
    idn = idn[:, :]
    tk = p.sem()
    te = p.sem()
    b_tk, b_te = tk.v, te.v

    if tok_bf is not None:
        lds = [p.sem(tag='hw') for _ in range(4)]
        b_ld = [s.v for s in lds]

        @p.o("y")
        def _(y, waits=list(waits), bf_waits=list(bf_waits)):
            for s, v in waits:
                y.wait_ge(s.h, v)
            for g in range(4):
                for s, v in bf_waits[g] if bf_waits else []:
                    y.wait_ge(s.h, v)
                for j in range(4):
                    tc = g * 4 + j
                    y.dma_start(out=stage[:, tc * 1024:(tc + 1) * 1024],
                                in_=tok_bf[tc * 128:(tc + 1) * 128, :]).then_inc(lds[g].h, 16)
        for g in range(4):
            lds[g].v += 64

        @p.o("t")
        def _(t, waits=list(waits)):
            for s, v in waits:
                t.wait_ge(s.h, v)
            for i in range(128):
                tc, kc = i // 8, i % 8
                if kc == 0:
                    t.wait_ge(lds[tc // 4].h, b_ld[tc // 4] + 64)
                if i >= 4:
                    t.wait_ge(te.h, b_te + i - 3)
                ps = psum_bf[:, slots[i % 4]: slots[i % 4] + 128]
                t.transpose(ps, stage[:, tc * 1024 + kc * 128: tc * 1024 + kc * 128 + 128],
                            idn).then_inc(tk.h, 1)
        tk.v += 128
    else:
        ld = p.sem(tag='sw')

        @p.o("g")
        def _(g, waits=list(waits)):
            for s, v in waits:
                g.wait_ge(s.h, v)
            for tc in range(16):
                g.dma_start(out=stage[:, tc * 1024:(tc + 1) * 1024],
                            in_=tok_dram[tc * 128:(tc + 1) * 128, :]).then_inc(ld.h, 16)
        ld.v += 16 * 16

        @p.o("t")
        def _(t, w=ld.v, waits=list(waits)):
            for s, v in waits:
                t.wait_ge(s.h, v)
            t.wait_ge(ld.h, w)
            for i in range(128):
                tc, kc = i // 8, i % 8
                if i >= 4:
                    t.wait_ge(te.h, b_te + i - 3)
                ps = psum_bf[:, slots[i % 4]: slots[i % 4] + 128]
                t.transpose(ps, stage[:, tc * 1024 + kc * 128: tc * 1024 + kc * 128 + 128],
                            idn).then_inc(tk.h, 1)
        tk.v += 128

    for i in range(128):
        tc, kc = i // 8, i % 8

        @p.o("v")
        def _(v, i=i, tc=tc, kc=kc, w=b_tk + i + 1):
            v.wait_ge(tk.h, w)
            ps = psum_bf[:, slots[i % 4]: slots[i % 4] + 128]
            v.tensor_copy(xfm[:, kc * 2048 + tc * 128: kc * 2048 + tc * 128 + 128],
                          ps).then_inc(te.h, 1)
    te.v += 128
    return (te, te.v)


N_CORES = 8
POOL_N = 104832


class Carver:
    def __init__(self, pool):
        self.pool = pool
        self.o = 0

    def take(self, n_el, dt=BF16, parts=128):
        mult = 1 if dt == BF16 else 2
        nb = n_el * mult
        o = self.o
        self.o = o + ((nb + 31) // 32) * 32
        assert self.o <= POOL_N, f"pool overflow {self.o}"
        sl = self.pool[:, o:o + nb]
        if dt == BF16:
            v = sl
        else:
            v = sl.bitcast(dt)
        if parts != 128:
            v = v[0:parts, :]
        return v


def carve_mamba(pool):
    c = Carver(pool)
    sb = {}
    A1 = c.take(20608)
    sb['xt0'] = A1[:, 0:8192]
    sb['xbcp'] = A1[:, 8192:18462]
    sb['ofm'] = A1[:, 0:16384]
    sb['wst2'] = A1[:, 18560:20608]
    A2 = c.take(18432)
    sb['wzr'] = A2[:, 0:8192]
    sb['wxsbcr'] = A2[:, 8192:18432]
    sb['wob'] = A2[:, 0:8192]
    sb['vfm'] = A2[:, 8192:16384]
    sb['outok'] = A2[:, 0:16384]
    sb['wdtr'] = c.take(128)
    xtok_t = c.take(8192)
    sb['xtok'] = xtok_t
    xv = xtok_t.bitcast(F32)
    sb['xs_f32'] = xv[:, 0:2048]
    sb['x2s'] = xv[:, 2048:4096]
    ztok_t = c.take(8192)
    sb['ztok'] = ztok_t
    zv = ztok_t.bitcast(F32)
    sb['ln_tmp'] = zv[:, 0:1024]
    sb['ln_tmp2'] = zv[:, 1024:2048]
    sb['ln_tr'] = ztok_t[:, 4096:5120]
    sb['bcf'] = c.take(2048)
    vfull_t = c.take(8192)
    sb['vfull'] = vfull_t
    sb['trf'] = vfull_t.bitcast(F32)
    sb['bio'] = vfull_t.bitcast(I32)[0:16, 0:2048]
    sb['dtraw'] = c.take(2048, parts=16)
    sb['wcst'] = c.take(2048)
    sb['dwork'] = c.take(768, F32, parts=16)
    sb['cumf'] = c.take(2048, F32, parts=1)
    sb['cum2048'] = c.take(2048, F32, parts=16)
    sb['w16'] = c.take(96, F32)
    sb['wbt'] = c.take(32)
    sb['mbig'] = c.take(2048)
    sb['m0sb'] = c.take(128)
    sb['btok'] = c.take(128)
    sb['hst'] = c.take(1024, F32)
    sb['hbb'] = c.take(1024)
    sb['xtt'] = c.take(1024)
    sb['szb'] = c.take(1024)
    sb['yt1'] = c.take(1024, F32)
    sb['vtrash'] = c.take(1024)
    sb['ssq'] = c.take(96, F32)
    sb['wdg'] = c.take(5120)
    sb['dide'] = c.take(2048)
    sb['bmask'] = c.take(2048, F32, parts=16)
    sb['bigl'] = c.take(128)
    sb['idtl'] = c.take(2048)
    sb['trilm'] = c.take(128)
    sb['hstash'] = c.take(64)[:, 0:60]
    sb['acol'] = c.take(16, F32, parts=16)[:, 0:1]
    sb['dones'] = c.take(128, F32, parts=16)
    sb['ln_sc'] = c.take(8, F32)
    for nm, shp in [('cwx', (128, 32)), ('cwbc', (128, 8)), ('cbx', (128, 8)),
                    ('cbbc', (128, 2)), ('dtb', (16, 1)), ('alog', (16, 1)),
                    ('drep', (128, 16)), ('nwc', (128, 8))]:
        sb[nm] = c.take(max(shp[1], 8), F32, parts=shp[0])[:, 0:shp[1]]
    return sb, c.o


def carve_ffn0(pool, TT):
    c = Carver(pool)
    sb = {}
    sb['x_fm'] = c.take(16384)
    sb['ff_a'] = c.take(32 * TT)
    sb['ff_b'] = c.take(32 * TT)
    sb['ff_a1'] = c.take(2 * TT)
    sb['w8'] = [c.take(8 * 128) for _ in range(4)]
    sb['w32'] = [c.take(32 * 128) for _ in range(4)]
    sb['w3o'] = c.take(8 * TT)
    sb['w3tok'] = c.take((TT // 128) * 1024)
    sb['xs_f32'] = c.take(2 * 1024, F32)
    sb['ln_tmp'] = c.take(1024, F32)
    sb['ln_sc'] = c.take(8, F32)
    sb['ln_tr'] = c.take(1024)
    sb['x2s'] = c.take(2 * 1024, F32)
    sb['x2b'] = c.take(4 * 1024)
    for nm, w in [('s1c', 32), ('s2c', 32), ('h1sc', 32), ('h2sc', 32), ('s3c', 8)]:
        sb['svec_' + nm] = c.take(w, F32)
    sb['pw_a'] = c.take(1024, F32)
    sb['pw_c0'] = c.take(1024)
    sb['pw_c1'] = c.take(1024)
    return sb, c.o


def carve_attn1(pool):
    c = Carver(pool)
    sb = {}
    sb['x2f'] = c.take(8 * 2048)
    att_a = c.take(16384)
    sb['xt2'] = [att_a[:, 0:8192], att_a[:, 8192:16384]]
    sb['aout'] = att_a
    sb['qf'] = c.take(4 * 4096)
    sb['kf'] = c.take(4 * 4096)
    sb['vaug'] = c.take(32 * 528)
    sb['pbuf'] = c.take(2 * 1024)
    sb['tau'] = c.take(4 * 512)
    sb['rrow'] = c.take(512, F32, parts=1)
    sb['rbb'] = c.take(512, parts=64)
    sb['wq3'] = sb['x2f'][:, 0:8192]
    sb['wv'] = sb['x2f'][:, 8192:12288]
    sb['wos'] = c.take(8 * 512)
    sb['wst'] = sb['pbuf']
    sb['wofm'] = sb['qf']
    sb['wotok'] = sb['kf']
    sb['bo_c'] = c.take(8, F32)
    sb['xs_f32'] = c.take(2 * 1024, F32)
    sb['ln_tmp'] = c.take(1024, F32)
    sb['ln_sc'] = c.take(8, F32)
    sb['ln_tr'] = c.take(1024)
    sb['x2s'] = c.take(2 * 1024, F32)
    return sb, c.o


def carve_ffn1(pool, TT):
    c = Carver(pool)
    sb = {}
    sb['x3f'] = c.take(8 * 2048)
    sb['fhid'] = c.take(32 * TT)
    sb['w8'] = [c.take(8 * 128) for _ in range(2)]
    sb['w32'] = [c.take(32 * 128) for _ in range(4)]
    sb['w3o'] = c.take(8 * TT)
    sb['w3tok'] = c.take((TT // 128) * 1024)
    sb['xs_f32'] = c.take(2 * 1024, F32)
    sb['ln_tmp'] = c.take(1024, F32)
    sb['ln_sc'] = c.take(8, F32)
    sb['ln_tr'] = c.take(1024)
    sb['x2s'] = c.take(2 * 1024, F32)
    sb['pw_a'] = c.take(4096, F32)
    sb['pw_b'] = c.take(4096, F32)
    sb['pw_c0'] = c.take(4096)
    sb['pw_c1'] = c.take(4096)
    sb['pw_s'] = c.take(40, F32)
    sb['qs'] = c.take(16, F32)
    sb['ep1'] = c.take(32, F32)
    sb['binc'] = c.take(32, F32)
    sb['boutc'] = c.take(8, F32)
    sb['hmv'] = c.take(32, F32)
    sb['tmp18'] = c.take(8, F32, parts=1)
    sb['q1'] = c.take(16, F32, parts=1)
    sb['q1x'] = c.take(128, F32, parts=1)
    return sb, c.o


def build(two_core=False, TT0=512, TT1=512):
    rgp = [[0, 1]] if two_core else [[0, 1], [2, 3], [4, 5], [6, 7]]
    rg8 = [[0, 1]] if two_core else [list(range(N_CORES))]
    nc = bass.Bass("TRN2", target_bir_lowering=False)
    d = {}
    EI, EO, IN = "ExternalInput", "ExternalOutput", "Internal"
    d['x0fm'] = nc.dram_tensor("x0fm", [1024, 4099], F32, kind=EI)
    d['x0tokh'] = nc.dram_tensor("x0tokh", [2048, 1024], F32, kind=EI)
    for nm, shp in [('wzb', [8, 1024, 128]), ('wxsb', [8, 1024, 128]),
                    ('wbcb', [2, 1024, 128]), ('wdt', [1024, 16]),
                    ('cwx', [128, 32]), ('cwbc', [128, 8]), ('cbx', [128, 8]),
                    ('cbbc', [128, 2]), ('dtb', [16, 1]), ('alog', [16, 1]),
                    ('drep', [128, 16]), ('nwc', [128, 8]),
                    ('woutb', [8, 1024, 128])]:
        d[nm] = nc.dram_tensor(nm, shp, F32, kind=EI)
    d['x1_tok'] = nc.dram_tensor("x1_tok", [2048, 1024], F32, kind=IN)
    d['x1_tokb'] = nc.dram_tensor("x1_tokb", [2048, 1024], BF16, kind=IN)
    d['wcflag'] = nc.dram_tensor("wcflag", [1, 4], F32, kind=IN)
    d['x2_tokb'] = nc.dram_tensor("x2_tokb", [2048, 1024], BF16, kind=IN)
    d['x3_tokb'] = nc.dram_tensor("x3_tokb", [2048, 1024], BF16, kind=IN)
    d['cfl'] = nc.dram_tensor("cfl", [2048], F32, kind=IN)
    d['vsp'] = nc.dram_tensor("vsp", [4, 128, 8192], BF16, kind=IN)
    d['ag2_in'] = nc.dram_tensor("ag2_in", [4096], F32, kind=IN)
    d['ag2_out'] = nc.dram_tensor("ag2_out", [2, 4096], F32, kind=IN)
    d['rs2_in'] = nc.dram_tensor("rs2_in", [2, 1024, 2048], BF16, kind=IN)
    d['rs2_out'] = nc.dram_tensor("rs2_out", [1024 * 2048], BF16, kind=IN)
    for nm, shp in [('w1b', [32, 1024, 128]), ('w2b', [32, 1024, 128]),
                    ('h1b', [32, 4096, 128]), ('h2b', [32, 4096, 128]),
                    ('w3b', [8, 4096, 128]), ('s1c', [128, 32]), ('s2c', [128, 32]),
                    ('h1sc', [128, 32]), ('h2sc', [128, 32]), ('s3c', [128, 8])]:
        d[nm] = nc.dram_tensor(nm, shp, F32, kind=EI)
    d['x2_tok'] = nc.dram_tensor("x2_tok", [2048, 1024], F32, kind=IN)
    d['w1c'] = nc.dram_tensor("w1c", [32, 128, 1024], BF16, kind=IN)
    d['w2c'] = nc.dram_tensor("w2c", [32, 128, 1024], BF16, kind=IN)
    d['h1c'] = nc.dram_tensor("h1c", [32, 128, 4096], BF16, kind=IN)
    d['h2c'] = nc.dram_tensor("h2c", [32, 128, 4096], BF16, kind=IN)
    d['w3c'] = nc.dram_tensor("w3c", [8, 128, 4096], BF16, kind=IN)
    for nm, shp in [('Wqb', [4, 1024, 128]), ('Wkb', [4, 1024, 128]),
                    ('Wvb', [1024, 512]), ('Wob', [8, 512, 128]),
                    ('bo_c', [128, 8]), ('tauj', [4, 128, 512])]:
        d[nm] = nc.dram_tensor(nm, shp, F32, kind=EI)
    d['x3_tok'] = nc.dram_tensor("x3_tok", [2048, 1024], F32, kind=IN)
    d['agx_in'] = nc.dram_tensor("agx_in", [1024 * 2048], BF16, kind=IN)
    d['agx_out'] = nc.dram_tensor("agx_out", [2, 1024 * 2048], BF16, kind=IN)
    d['rs_in'] = nc.dram_tensor("rs_in", [2, 1024, 2048], BF16, kind=IN)
    d['rs_out'] = nc.dram_tensor("rs_out", [1024 * 2048], BF16, kind=IN)
    for nm, shp in [('W1nT', [4096, 1024]), ('W2n', [4096, 1024]),
                    ('b_in_c', [128, 32]), ('b_out_c', [128, 8])]:
        d[nm] = nc.dram_tensor(nm, shp, F32, kind=EI)
    d['x4_tok'] = nc.dram_tensor("x4_tok", [2048, 1024], F32, kind=EO)
    d['w1q_blk'] = nc.dram_tensor("w1q_blk", [32, 128, 1024], BF16, kind=IN)
    d['w1qn'] = nc.dram_tensor("w1qn", [32, 128, 1024], BF16, kind=IN)
    d['w2q_blk'] = nc.dram_tensor("w2q_blk", [8, 128, 4096], BF16, kind=IN)
    d['hsp'] = nc.dram_tensor("hsp", [2048 // TT1, 128, 32 * TT1], BF16, kind=IN)
    d['flat_sc'] = nc.dram_tensor("flat_sc", [2, 128], F32, kind=IN)
    d['prepflag'] = nc.dram_tensor("prepflag", [1, 4], F32, kind=IN)
    for nm in ('arm1', 'arm1o', 'arm2', 'arm2o'):
        d[nm] = nc.dram_tensor(nm, [1, 8], F32, kind=IN)

    p = P(nc)
    pool = nc.alloc_sbuf_tensor("pool", [128, POOL_N], BF16)
    amx = nc.alloc_sbuf_tensor("amx", [128, 18], F32)
    q1p = nc.alloc_sbuf_tensor("q1p", [1, 12], F32)
    q1xp = nc.alloc_sbuf_tensor("q1xp", [1, 128], F32)
    scsP = nc.alloc_sbuf_tensor("scsP", [128, 40], F32)
    idn = nc.alloc_sbuf_tensor("idn", [128, 128], BF16)
    io_i32 = nc.alloc_sbuf_tensor("io_i32", [128, 128], I32)
    iom = nc.alloc_sbuf_tensor("iom", [128, 128], F32)
    onesf = nc.alloc_sbuf_tensor("onesf", [1, 128], F32)
    psum = nc.alloc_psum_tensor("ps", [128, 4096], F32)
    psum_bf = psum[:, :].bitcast(BF16)

    cst = p.sem()
    build_identity(p, idn, io_i32, iom, cst)
    ov = p.sem()

    @p.o("v")
    def _(v, cv=cst.v):
        v.wait_ge(cst.h, cv)
        v.memset(onesf[0:1, 0:128], 1.0).then_inc(ov.h, 1)
    ov.v += 1

    sbM, oM = carve_mamba(pool)
    sbM['onesf'] = onesf
    sbM['idn'] = idn
    sbM['iom'] = iom
    sbM['io_i32'] = io_i32
    hM = {k: d[k] for k in ('x0fm', 'x0tokh', 'wzb', 'wxsb', 'wbcb', 'wdt',
                            'cwx', 'cwbc', 'cbx', 'cbbc', 'dtb', 'alog', 'drep',
                            'nwc', 'woutb', 'x1_tok', 'cfl', 'vsp', 'ag2_in',
                            'ag2_out', 'rs2_in', 'rs2_out')}
    hM['x_tokb'] = d['x1_tokb']

    wc_stage = sbM['wcst']
    wc_sg = [p.sem(tag='sw'), p.sem(tag='sw')]
    wc_ss = [p.sem(tag='hw'), p.sem(tag='hw')]
    wc_bg = [s.v for s in wc_sg]
    wc_bs = [s.v for s in wc_ss]

    def _wc_jobs(src_t, dst_t, nblk, nkc):
        out = []
        for fc in range(nblk):
            for q in range(nkc // 8):
                out.append((src_t[fc].rearrange("(kc pp) j -> pp kc j", pp=128)
                            [:, q * 8:(q + 1) * 8, :],
                            dst_t[fc][:, q * 1024:(q + 1) * 1024]))
        return out
    wc_sets = [[],
               _wc_jobs(d['w1b'], d['w1c'], 32, 8) + _wc_jobs(d['w2b'], d['w2c'], 32, 8),
               _wc_jobs(d['h1b'], d['h1c'], 32, 32),
               _wc_jobs(d['h2b'], d['h2c'], 32, 32) + _wc_jobs(d['w3b'], d['w3c'], 8, 32)]
    wc_state = {'ji': 0}
    wc_done = {}

    def wc_hook(tt):
        jobs = wc_sets[tt]
        if not jobs:
            return
        base = wc_state['ji']

        @p.o("g")
        def _(g, jobs=jobs, base=base):
            for k, (srcA, dstA) in enumerate(jobs):
                i = base + k
                sl = i % 2
                if i >= 2:
                    g.wait_ge(wc_ss[sl].h, wc_bs[sl] + 16 * (i // 2))
                g.dma_start(out=wc_stage[:, sl * 1024:(sl + 1) * 1024]
                            .rearrange("pp (kc j) -> pp kc j", j=128),
                            in_=srcA).then_inc(wc_sg[sl].h, 16)

        @p.o("y")
        def _(y, jobs=jobs, base=base):
            for k, (srcA, dstA) in enumerate(jobs):
                i = base + k
                sl = i % 2
                y.wait_ge(wc_sg[sl].h, wc_bg[sl] + 16 * (i // 2 + 1))
                y.dma_start(out=dstA,
                            in_=wc_stage[:, sl * 1024:(sl + 1) * 1024]).then_inc(wc_ss[sl].h, 16)
        for k in range(len(jobs)):
            i = base + k
            wc_sg[i % 2].v += 16
            wc_ss[i % 2].v += 16
        wc_state['ji'] += len(jobs)
        sd = p.sem(tag='hw')

        @p.o("y")
        def _(y, w0=wc_ss[0].v, w1=wc_ss[1].v, sd=sd, tt=tt):
            y.wait_ge(wc_ss[0].h, w0)
            y.wait_ge(wc_ss[1].h, w1)
            y.dma_start(out=d['wcflag'][0:1, tt:tt + 1],
                        in_=onesf[0:1, 0:1]).then_inc(sd.h, 16)
        sd.v += 16
        names = [(), ('w1c', 'w2c'), ('h1c',), ('h2c', 'w3c')][tt]
        for nm in names:
            wc_done[nm] = [(sd, sd.v)]
        wc_state.setdefault('sds', []).append(sd)

    mres = mamba_phase(p, hM, sbM, psum, psum_bf, rgp, g_hook=wc_hook)
    lnM, sosM, bfM = mamba_tail(p, hM, sbM, psum, psum_bf, rgp, mres)
    barM = ([(lnM, lnM.v), (sosM[0], sosM[0].v), (sosM[1], sosM[1].v)] +
            [(s, s.v) for s in bfM[0]] +
            [(s, s.v) for s in wc_state['sds']])
    barrier(p, barM)
    p.recycle()

    sbF0, oF0 = carve_ffn0(pool, TT0)
    sbF0['idn'] = idn
    hF0 = {'x_tok': d['x1_tok'], 'x2_tok': d['x2_tok']}
    for k in ('w1b', 'w2b', 'h1b', 'h2b', 'w3b', 's1c', 's2c', 'h1sc', 'h2sc', 's3c',
              'w1c', 'w2c', 'h1c', 'h2c', 'w3c'):
        hF0[k] = d[k]
    xr0 = fm_from_tok(p, tok_dram=d['x1_tok'], stage=sbF0['ff_a'][:, 0:16384],
                         xfm=sbF0['x_fm'], psum_bf=psum_bf, idn=idn, waits=barM,
                         tok_bf=d['x1_tokb'],
                         bf_waits=[[(bfM[0][s], bfM[1][s] + 16 * (g + 1))
                      for s in range(4)] for g in range(4)])
    hF0['x_tokb'] = d['x2_tokb']
    sbF0['pw_s'] = scsP
    hQ = {k: d[k] for k in ('w1qn', 'W1nT', 'W2n', 'w2q_blk', 'prepflag')}
    done_q = p.sem(tag='sw')
    q_pieces, q_emit_g = ffn1_wprep_quant(p, hQ, sbF0, done_q)
    ln0, sos0, bf0 = ffn0_phase(p, hF0, sbF0, psum, psum_bf, xr0, TT=TT0, x2fm=None,
                                   wc_done=wc_done,
                                   v_hook=lambda tt: q_pieces[tt]())
    q_emit_g()
    bar0 = ([(ln0, ln0.v), (sos0[0], sos0[0].v), (sos0[1], sos0[1].v)] +
            [(s, s.v) for s in bf0[0]] + [(done_q, done_q.v)])
    barrier(p, bar0)
    p.recycle()

    sbA, oA = carve_attn1(pool)
    sbA['idn'] = idn
    sbA['onesf'] = onesf
    hA = {'x_tok': d['x2_tok'], 'x3_tok': d['x3_tok']}
    for k in ('Wqb', 'Wkb', 'Wvb', 'Wob', 'bo_c', 'tauj', 'agx_in', 'agx_out',
              'rs_in', 'rs_out'):
        hA[k] = d[k]
    xr1 = fm_from_tok(p, tok_dram=d['x2_tok'], stage=sbA['qf'][:, 0:16384],
                         xfm=sbA['x2f'], psum_bf=psum_bf, idn=idn, waits=bar0,
                         tok_bf=d['x2_tokb'],
                         bf_waits=[[(bf0[0][s], bf0[1][s] + 16 * (g + 1))
                      for s in range(4)] for g in range(4)])
    hA['x_tokb'] = d['x3_tokb']
    lnA, sosA, bfA = attn1_phase(p, hA, sbA, psum, psum_bf, xr1, rgp,
                               absmax_col=amx[:, 0:16])

    amr = p.sem()
    b_amr = amr.v

    @p.o("v")
    def _(v, w=lnA.v):
        v.wait_ge(lnA.h, w)
        v.reduce_max(amx[:, 16:17], amx[:, 0:16], axis=mybir.AxisListType.X).then_inc(amr.h, 1)
    amr.v += 1
    amc = p.sem(tag='sw')
    b_amc = amc.v

    @p.o("g")
    def _(g, w=amr.v):
        g.wait_ge(amr.h, w)
        g.dma_start(out=d['flat_sc'][0], in_=amx[:, 16:17],
                    allow_slow_non_contiguous=True).then_inc(amc.h, 16)
        g.wait_ge(amc.h, b_amc + 16)
        g.dma_start(out=q1xp[0:1, 0:128], in_=d['flat_sc'][0]).then_inc(amc.h, 16)
    amc.v += 32

    @p.o("v")
    def _(v, w=amc.v, b=b_amr):
        v.wait_ge(amc.h, w)
        v.reduce_max(q1p[0:1, 0:1], q1xp[0:1, 0:128],
                     axis=mybir.AxisListType.X).then_inc(amr.h, 1)
        v.wait_ge(amr.h, b + 2)
        v.tensor_copy(q1p[0:1, 2:10],
                      q1p[0:1, 0:1].broadcast_to([1, 8])).then_inc(amr.h, 1)
    amr.v += 2
    cc1 = p.sem(tag='co')

    @p.o("g")
    def _(g, w=amr.v, b=b_amc):
        g.wait_ge(amr.h, w)
        g.dma_start(out=d['arm1'].ap(), in_=q1p[0:1, 2:10]).then_inc(amc.h, 16)
        g.wait_ge(amc.h, b + 48)
        g.collective_compute("AllReduce", mybir.AluOpType.max, replica_groups=rg8,
                             ins=[d['arm1'].ap().opt()],
                             outs=[d['arm1o'].ap().opt()]).then_inc(cc1.h, 1)
    amc.v += 16
    cc1.v += 1

    barA = ([(lnA, lnA.v), (sosA[0], sosA[0].v), (sosA[1], sosA[1].v)] +
            [(s, s.v) for s in bfA[0]])
    barrier(p, barA)
    p.recycle(keep=[amr, amc, cc1])

    sbF1, oF1 = carve_ffn1(pool, TT1)
    sbF1['idn'] = idn
    sbF1['onesf'] = onesf
    hF1 = {'x_tok': d['x3_tok'], 'x4_tok': d['x4_tok']}
    for k in ('W1nT', 'W2n', 'b_in_c', 'b_out_c', 'w1q_blk', 'w1qn', 'w2q_blk',
              'hsp', 'flat_sc', 'prepflag', 'arm1', 'arm1o', 'arm2', 'arm2o'):
        hF1[k] = d[k]
    xr2 = fm_from_tok(p, tok_dram=d['x3_tok'], stage=sbF1['fhid'][:, 0:16384],
                         xfm=sbF1['x3f'], psum_bf=psum_bf, idn=idn, waits=barA,
                         slots=[0, 1024, 2048, 3072],
                         tok_bf=d['x3_tokb'],
                         bf_waits=[[(bfA[0][s], bfA[1][s] + 16 * (g + 1))
                      for s in range(4)] for g in range(4)])
    sbF1['pw_s'] = scsP
    prep_done = p.sem(tag='sw')
    ffn1_blockify(p, hF1, sbF1, psum_bf, idn, done_q, prep_done)

    arxF = p.sem(tag='hw')

    @p.o("y")
    def _(y, w=cc1.v):
        y.wait_ge(cc1.h, w)
        y.dma_start(out=q1p[0:1, 1:2],
                    in_=d['arm1o'].ap()[0:1, 0:1]).then_inc(arxF.h, 16)
    arxF.v += 16
    ln1, sos1 = ffn1_phase(p, hF1, sbF1, psum, psum_bf, xr2, rg8, prep_done,
                              TT=TT1, xamax=(q1p[0:1, 1:2], arxF))
    p.finalize()
    return nc


def colsplit(s):
    return np.ascontiguousarray(np.asarray(s, np.float32).reshape(-1, 128).T)


def blocked(wT, n_fc):
    d_in, d_out = wT.shape
    return np.ascontiguousarray(wT.reshape(d_in, n_fc, 128).transpose(1, 0, 2))


def stage_core(ins, c):
    b, hh = c // 2, c % 2
    x = np.asarray(ins['L0_attn_x'][b])
    W_in = np.asarray(ins['L0_attn_W_in'])
    conv_w = np.asarray(ins['L0_attn_conv_w'])
    conv_b = np.asarray(ins['L0_attn_conv_b'])
    DI = 2048
    zcols = W_in[:, hh * 1024:(hh + 1) * 1024]
    xscols = W_in[:, DI + hh * 1024: DI + (hh + 1) * 1024]
    bccols = W_in[:, 2 * DI: 2 * DI + 256]
    dtcols = W_in[:, 4352 + hh * 16: 4352 + (hh + 1) * 16]
    cw_xs = conv_w[hh * 1024:(hh + 1) * 1024]
    cw_bc = conv_w[2048:2304]
    cwx = cw_xs.reshape(8, 128, 4).transpose(1, 0, 2).reshape(128, 32)
    cwbc = cw_bc.reshape(2, 128, 4).transpose(1, 0, 2).reshape(128, 8)
    m = {
        'x0fm': np.concatenate([np.zeros((1024, 3), np.float32),
                                np.ascontiguousarray(x.T)], axis=1),
        'x0tokh': np.ascontiguousarray(x[hh * 2048:(hh + 1) * 2048]),
        'wzb': blocked(zcols, 8), 'wxsb': blocked(xscols, 8),
        'wbcb': blocked(bccols, 2), 'wdt': np.ascontiguousarray(dtcols),
        'cwx': np.ascontiguousarray(cwx), 'cwbc': np.ascontiguousarray(cwbc),
        'cbx': colsplit(conv_b[hh * 1024:(hh + 1) * 1024]),
        'cbbc': colsplit(conv_b[2048:2304]),
        'dtb': np.asarray(ins['L0_attn_dt_bias'])[hh * 16:(hh + 1) * 16].reshape(16, 1),
        'alog': np.asarray(ins['L0_attn_A_log'])[hh * 16:(hh + 1) * 16].reshape(16, 1),
        'drep': np.tile(np.asarray(ins['L0_attn_D'])[hh * 16:(hh + 1) * 16], (128, 1)),
        'nwc': colsplit(np.asarray(ins['L0_attn_norm_w'])[hh * 1024:(hh + 1) * 1024]),
        'woutb': blocked(np.asarray(ins['L0_attn_W_out'])[hh * 1024:(hh + 1) * 1024, :], 8),
    }
    m.update({
        'w1b': blocked(np.asarray(ins['L0_ffn_w1_w']).T, 32),
        'w2b': blocked(np.asarray(ins['L0_ffn_w2_w']).T, 32),
        'h1b': blocked(np.asarray(ins['L0_ffn_h1_w']).T, 32),
        'h2b': blocked(np.asarray(ins['L0_ffn_h2_w']).T, 32),
        'w3b': blocked(np.asarray(ins['L0_ffn_w3_w']).T, 8),
        's1c': colsplit(ins['L0_ffn_w1_s']), 's2c': colsplit(ins['L0_ffn_w2_s']),
        'h1sc': colsplit(ins['L0_ffn_h1_s']), 'h2sc': colsplit(ins['L0_ffn_h2_s']),
        's3c': colsplit(ins['L0_ffn_w3_s']),
    })
    Wq = np.asarray(ins['L1_attn_Wq'])
    Wk = np.asarray(ins['L1_attn_Wk'])
    Wv = np.asarray(ins['L1_attn_Wv'])
    Wo = np.asarray(ins['L1_attn_Wo'])
    tauj = np.zeros((4, 128, 512), np.float32)
    for j in range(4):
        for kk in range(128):
            tauj[j, kk, 128 * j + kk:] = 1.0
    m.update({
        'Wqb': blocked(Wq[:, hh * 512:(hh + 1) * 512], 4),
        'Wkb': blocked(Wk[:, hh * 512:(hh + 1) * 512], 4),
        'Wvb': np.ascontiguousarray(Wv[:, hh * 512:(hh + 1) * 512]),
        'Wob': blocked(Wo[hh * 512:(hh + 1) * 512, :], 8),
        'bo_c': colsplit(ins['L1_attn_bo']),
        'tauj': tauj,
    })
    m.update({
        'W1nT': np.ascontiguousarray(np.asarray(ins['L1_ffn_W_in']).T),
        'W2n': np.ascontiguousarray(np.asarray(ins['L1_ffn_W_out'])),
        'b_in_c': colsplit(ins['L1_ffn_b_in']),
        'b_out_c': colsplit(ins['L1_ffn_b_out']),
    })
    return m




_NC_CACHE = {}


def _get_nc():
    if 'nc' not in _NC_CACHE:
        _NC_CACHE['nc'] = build(two_core=False)
    return _NC_CACHE['nc']


def kernel(**inputs):
    ins = {k: np.asarray(v) for k, v in inputs.items()}
    in_maps = [stage_core(ins, c) for c in range(N_CORES)]
    nc = _get_nc()
    res = run_bass_kernel_spmd(nc, in_maps, core_ids=list(range(N_CORES)))
    out = np.empty((4, 4096, 1024), np.float32)
    for c in range(N_CORES):
        b, hh = c // 2, c % 2
        out[b, hh * 2048:(hh + 1) * 2048] = res.results[c]['x4_tok']
    return out


# version 13 (final)
# speedup vs baseline: 1.0128x; 1.0128x over previous
import contextlib
import numpy as np
import concourse.bass as bass
from concourse.bass_utils import run_bass_kernel_spmd
import concourse.mybir as mybir

F32 = mybir.dt.float32
BF16 = mybir.dt.bfloat16
I32 = mybir.dt.int32
AF = mybir.ActivationFunctionType
ALU = mybir.AluOpType
AX = mybir.AxisListType

MAGIC = 12582912.0


class Sem:
    __slots__ = ("h", "v")
    def __init__(self, h):
        self.h = h
        self.v = 0


class P:
    ENGS = ("g", "t", "v", "s", "y")

    def __init__(self, nc):
        self.nc = nc
        self.qs = {e: [] for e in self.ENGS}
        self.stack = contextlib.ExitStack()
        self._semno = 0
        self._all_sems = {}
        self._free = {}

    def sem(self, name=None, tag="c"):
        free = self._free.setdefault(tag, [])
        if free:
            return free.pop()
        self._semno += 1
        h = self.stack.enter_context(self.nc.semaphore(name or f"sm{self._semno}"))
        s = Sem(h)
        self._all_sems.setdefault(tag, []).append(s)
        return s

    def recycle(self, keep=()):
        keep = set(id(k) for k in keep)
        self._free = {t: [s for s in lst if id(s) not in keep]
                      for t, lst in self._all_sems.items()}

    def o(self, eng):
        def reg(fn):
            self.qs[eng].append(fn)
        return reg

    def finalize(self):
        with self.nc.Block() as block:
            block.gpsimd(lambda e: [f(e) for f in self.qs["g"]])
            block.tensor(lambda e: [f(e) for f in self.qs["t"]])
            block.vector(lambda e: [f(e) for f in self.qs["v"]])
            block.scalar(lambda e: [f(e) for f in self.qs["s"]])
            block.sync(lambda e: [f(e) for f in self.qs["y"]])
        self.stack.close()


def cdiv(a, b):
    return (a + b - 1) // b


def fm_proj(p, *, w_dram, n_fc, n_k, x_rhs, ttile, wbufs, psum, slot_banks,
            evac, ev, x_waits=(), rank1=None, w_guard=(), tee=None, w_hw=False,
            w_hw_rearr=False, shared=None):
    if w_hw:
        wtag = ['hw', 'hw', 'hw', 'hw']
        wq = ['y', 'y', 'y', 'y']
    else:
        wtag = ['sw'] * 4
        wq = ['g'] * 4
    nslot = len(slot_banks)
    nw = len(wbufs)
    if shared is None:
        mmk = p.sem()
        wls = [p.sem(tag=wtag[i]) for i in range(nw)]
    else:
        mmk, wls = shared
    mb = mmk.v
    wtag = wtag[:nw]
    wq = wq[:nw]
    wlb = [w.v for w in wls]
    n_nt = cdiv(ttile, 512)
    ev0 = ev.v
    tes = [p.sem(tag='hw') for _ in range(nw)] if tee is not None else None
    teb = [t.v for t in tes] if tes else None

    for qname in sorted(set(wq)):
        slots_q = [i for i in range(nw) if wq[i] == qname]

        @p.o(qname)
        def _(g, wg=list(w_guard), slots_q=slots_q):
            for s, val in wg:
                g.wait_ge(s.h, val)
            for fc in range(n_fc):
                if fc % nw not in slots_q:
                    continue
                if fc - nw >= 0:
                    g.wait_ge(mmk.h, mb + fc - nw + 1)
                if tes is not None and fc >= nw:
                    g.wait_ge(tes[fc % nw].h, teb[fc % nw] + 16 * (fc // nw))
                if w_hw:
                    src = w_dram(fc)
                    dst = wbufs[fc % nw][:, 0:n_k * 128]
                    if w_hw_rearr:
                        dst = dst.rearrange("pp (kc j) -> pp kc j", j=128)
                else:
                    src = w_dram(fc).rearrange("(kc pp) j -> pp kc j", pp=128)
                    dst = wbufs[fc % nw][:, 0:n_k * 128].rearrange("pp (kc j) -> pp kc j", j=128)
                g.dma_start(out=dst, in_=src).then_inc(wls[fc % nw].h, 16)
    for i in range(nw):
        wls[i].v += 16 * ((n_fc + nw - 1 - i) // nw)

    if tee is not None:
        @p.o("y")
        def _(y):
            for fc in range(n_fc):
                y.wait_ge(wls[fc % nw].h, wlb[fc % nw] + 16 * (fc // nw + 1))
                y.dma_start(out=tee[fc],
                            in_=wbufs[fc % nw][:, 0:n_k * 128]).then_inc(tes[fc % nw].h, 16)
        for i in range(nw):
            tes[i].v += 16 * ((n_fc + nw - 1 - i) // nw)
    tee_done = ([(t, t.v) for t in tes] if tes is not None else None)

    @p.o("t")
    def _(t, xw=list(x_waits)):
        for s, val in xw:
            t.wait_ge(s.h, val)
        for fc in range(n_fc):
            t.wait_ge(wls[fc % nw].h, wlb[fc % nw] + 16 * (fc // nw + 1))
            if ev0 + fc - nslot + 1 > 0:
                t.wait_ge(ev.h, ev0 + fc - nslot + 1)
            wb = wbufs[fc % nw]
            b0 = slot_banks[fc % nslot] * 512
            for nt in range(n_nt):
                tn = min(512, ttile - nt * 512)
                last_nt = (nt == n_nt - 1)
                ps = psum[:, b0 + nt * 512: b0 + nt * 512 + tn]
                for k in range(n_k):
                    mi = t.matmul(ps, wb[:, k * 128:(k + 1) * 128], x_rhs(k, nt * 512, tn),
                                  start=(k == 0), stop=(k == n_k - 1 and rank1 is None))
                    if last_nt and rank1 is None and k == n_k - 1:
                        mi.then_inc(mmk.h, 1)
                if rank1 is not None:
                    mi = t.matmul(ps, lhs1 := rank1[0], rank1[1](fc, nt * 512, tn), start=False, stop=True)
                    if last_nt:
                        mi.then_inc(mmk.h, 1)
    mmk.v += n_fc

    for fc in range(n_fc):
        b0 = slot_banks[fc % nslot] * 512
        ps = psum[:, b0: b0 + ttile]
        evac(fc, ps, [(mmk, mb + fc + 1)], ev)
    ev.v = ev0 + n_fc
    return tee_done


def tok_proj(p, *, w_dram, d_in, d_out, x_lhs, n_tc, wbuf, psum, slot_banks,
             evac, ev, x_waits=(), wprep=None):
    wl = p.sem(tag='sw')
    mmk = p.sem()
    mb = mmk.v
    n_k = d_in // 128
    n_nt = cdiv(d_out, 512)
    nslot = len(slot_banks)
    ev0 = ev.v

    @p.o("g")
    def _(g):
        for k in range(n_k):
            g.dma_start(out=wbuf[:, k * d_out:(k + 1) * d_out],
                        in_=w_dram[k * 128:(k + 1) * 128, :]).then_inc(wl.h, 16)
    wl.v += 16 * n_k
    wready = [(wl, wl.v)]
    if wprep is not None:
        wready = wprep(wready)

    @p.o("t")
    def _(t, xw=list(x_waits) + wready):
        for s, val in xw:
            t.wait_ge(s.h, val)
        for tc in range(n_tc):
            if tc >= nslot:
                t.wait_ge(ev.h, ev0 + tc - nslot + 1)
            b0 = slot_banks[tc % nslot] * 512
            for k in range(n_k):
                for nt in range(n_nt):
                    w = min(512, d_out - nt * 512)
                    ps = psum[:, b0 + nt * 512: b0 + nt * 512 + w]
                    mi = t.matmul(ps, x_lhs(k, tc), wbuf[:, k * d_out + nt * 512: k * d_out + nt * 512 + w],
                                  start=(k == 0), stop=(k == n_k - 1))
                    if k == n_k - 1 and nt == n_nt - 1:
                        mi.then_inc(mmk.h, 1)
    mmk.v += n_tc

    for tc in range(n_tc):
        b0 = slot_banks[tc % nslot] * 512
        ps = psum[:, b0: b0 + d_out]
        evac(tc, ps, [(mmk, mb + tc + 1)], ev)
    ev.v = ev0 + n_tc


def build_identity(p, idn_bf, io_i32, iom_f32, done):
    @p.o("g")
    def _(g):
        g.iota(io_i32[:, :], pattern=[[1, 128]], base=0, channel_multiplier=-1).then_inc(done.h, 1)
    done.v += 1

    @p.o("v")
    def _(v, w=done.v):
        v.wait_ge(done.h, w)
        v.tensor_single_scalar(iom_f32[:, :], io_i32[:, :], 0.0, ALU.is_equal).then_inc(done.h, 1)
        v.wait_ge(done.h, w + 1)
        v.tensor_copy(idn_bf[:, :], iom_f32[:, :]).then_inc(done.h, 1)
    done.v += 2


def ln_chunk(p, *, a_ap, b_ap, tmp, trash_bf, sc, n_f, waits, u, done,
             out_f32=None, out_bf=None, eps=1e-5, absmax=None):
    b = u.v

    @p.o("v")
    def _(v, w=list(waits) + [(done, done.v)], b=b):
        for s, val in w:
            v.wait_ge(s.h, val)
        if b_ap is not None:
            v.tensor_tensor(tmp, a_ap, b_ap, ALU.add).then_inc(u.h, 1)
        else:
            v.tensor_copy(tmp, a_ap).then_inc(u.h, 1)
        v.wait_ge(u.h, b + 1)
        v.reduce_sum(sc[:, 0:1], tmp, axis=AX.X).then_inc(u.h, 1)
        v.wait_ge(u.h, b + 2)
        v.tensor_scalar(sc[:, 1:2], sc[:, 0:1], 1.0 / n_f, None, ALU.mult).then_inc(u.h, 1)
        v.wait_ge(u.h, b + 3)
        v.tensor_scalar(tmp, tmp, sc[:, 1:2], None, ALU.subtract).then_inc(u.h, 1)
        v.wait_ge(u.h, b + 4)
        v.tensor_tensor(trash_bf, tmp, tmp, ALU.mult).then_inc(u.h, 1)
        v.wait_ge(u.h, b + 5)
        v.reduce_sum(sc[:, 2:3], trash_bf, axis=AX.X).then_inc(u.h, 1)
        v.wait_ge(u.h, b + 6)
        v.tensor_scalar(sc[:, 2:3], sc[:, 2:3], 1.0 / n_f, eps, ALU.mult, ALU.add).then_inc(u.h, 1)

    @p.o("s")
    def _(s, b=b):
        s.wait_ge(u.h, b + 7)
        s.activation(sc[:, 3:4], sc[:, 2:3], AF.Sqrt).then_inc(u.h, 1)

    d0 = done.v
    nout = (out_f32 is not None) + (out_bf is not None)

    @p.o("v")
    def _(v, b=b, d0=d0, nout=nout):
        v.wait_ge(u.h, b + 8)
        v.reciprocal(sc[:, 3:4], sc[:, 3:4]).then_inc(u.h, 1)
        v.wait_ge(u.h, b + 9)
        if out_f32 is not None:
            v.tensor_scalar(out_f32, tmp, sc[:, 3:4], None, ALU.mult).then_inc(done.h, 1)
        if out_bf is not None:
            v.tensor_scalar(out_bf, tmp, sc[:, 3:4], None, ALU.mult).then_inc(done.h, 1)
        if absmax is not None:
            v.wait_ge(done.h, d0 + nout)
            v.reduce_max(absmax, out_f32 if out_f32 is not None else out_bf,
                         axis=AX.X, apply_absolute_value=True).then_inc(done.h, 1)
    u.v = b + 9
    done.v = d0 + nout + (absmax is not None)


def transpose_tiles(p, *, src, dst, n, idn, psum_bf, slots, waits, ev):
    tk = p.sem()
    b_tk = tk.v
    nslot = len(slots)
    ev0 = ev.v

    @p.o("t")
    def _(t, w=list(waits)):
        for s, val in w:
            t.wait_ge(s.h, val)
        for i in range(n):
            if i >= nslot - 1:
                t.wait_ge(ev.h, ev0 + i - nslot + 2)
            ps = psum_bf[:, slots[i % nslot]: slots[i % nslot] + 128]
            t.transpose(ps, src(i), idn[:, :]).then_inc(tk.h, 1)
    tk.v += n

    @p.o("v")
    def _(v):
        for i in range(n):
            v.wait_ge(tk.h, b_tk + i + 1)
            ps = psum_bf[:, slots[i % nslot]: slots[i % nslot] + 128]
            v.tensor_copy(dst(i), ps).then_inc(ev.h, 1)
    ev.v = ev0 + n


TRAN_SLOTS = [4 * 1024, 5 * 1024, 6 * 1024, 7 * 1024]


def ffn0_phase(p, h, sb, psum, psum_bf, x_ready, TT=512, x2fm=None, wc_done=None,
               v_hook=None):
    NTILE = 2048 // TT
    NCH = TT // 128
    x_fm = sb['x_fm']
    a = sb['ff_a']
    bb = sb['ff_b']
    a1 = sb['ff_a1']
    w8 = sb['w8']
    w32 = sb['w32']
    w3o = sb['w3o']
    w3tok = sb['w3tok']
    xs_f32 = sb['xs_f32']
    x2s = sb['x2s']
    idn = sb['idn']

    svl = p.sem(tag='sw')
    svecs = ['s1c', 's2c', 'h1sc', 'h2sc', 's3c']

    @p.o("g")
    def _(g):
        for nm in svecs:
            g.dma_start(out=sb['svec_' + nm][:, 0:h[nm].shape[1]], in_=h[nm][:, :]).then_inc(svl.h, 16)
    svl.v += 80
    svlf = svl.v
    sv = {nm: sb['svec_' + nm] for nm in svecs}

    u = p.sem()
    ln_done = p.sem()
    sxs = [p.sem(tag='hw'), p.sem(tag='hw')]
    sos = [p.sem(tag='hw'), p.sem(tag='hw')]
    tokb = h.get('x_tokb')
    sob = [p.sem(tag='hw') for _ in range(4)]
    b_sob = [s.v for s in sob]
    ln_marks = []
    prev_tail = []
    ev_w1 = p.sem(); ev_a = p.sem(); ev_b = p.sem(); ev_c = p.sem()
    ev_w3 = p.sem(); ev_tr = p.sem()
    sh_w1 = (p.sem(), [p.sem(tag='hw'), p.sem(tag='hw')])
    sh_w2 = (p.sem(), [p.sem(tag='hw'), p.sem(tag='hw')])
    sh_h1 = (p.sem(), [p.sem(tag='hw'), p.sem(tag='hw'), p.sem(tag='hw')])
    sh_h2 = (p.sem(), [p.sem(tag='hw'), p.sem(tag='hw'), p.sem(tag='hw')])
    sh_w3 = (p.sem(), [p.sem(tag='hw'), p.sem(tag='hw'), p.sem(tag='hw')])

    for tt in range(NTILE):
        t0 = tt * TT

        def evac_w1(fc, ps, waits, ev):
            @p.o("s")
            def _(s, w=list(waits), fc=fc, ps=ps):
                for sm, val in w:
                    s.wait_ge(sm.h, val)
                s.wait_ge(svl.h, svlf)
                s.activation(a[:, fc * TT:(fc + 1) * TT], ps,
                             AF.Silu, scale=sv['s1c'][:, fc:fc + 1]).then_inc(ev.h, 1)
        fm_proj(p, w_dram=lambda fc: h['w1c'][fc], n_fc=32, n_k=8,
                x_rhs=lambda k, o, n, t0=t0: x_fm[:, k * 2048 + t0 + o: k * 2048 + t0 + o + n],
                ttile=TT, wbufs=w8[0:2], psum=psum, slot_banks=[0, 1] if TT == 512 else [0, 2],
                evac=evac_w1, ev=ev_w1, w_hw=True, shared=sh_w1,
                w_guard=(wc_done['w1c'] if (wc_done and tt == 0) else []),
                x_waits=[x_ready] + [w for w in prev_tail if w[0].h is not None][:2])

        def evac_w2(fc, ps, waits, ev, ew0=ev_w1.v - 32):
            @p.o("v")
            def _(v, w=list(waits), fc=fc, ps=ps, ew0=ew0):
                for sm, val in w:
                    v.wait_ge(sm.h, val)
                v.wait_ge(svl.h, svlf)
                v.wait_ge(ev_w1.h, ew0 + fc + 1)
                v.scalar_tensor_tensor(a[:, fc * TT:(fc + 1) * TT], ps,
                                       sv['s2c'][:, fc:fc + 1],
                                       a[:, fc * TT:(fc + 1) * TT],
                                       ALU.mult, ALU.mult).then_inc(ev.h, 1)
        ea_base = ev_a.v
        fm_proj(p, w_dram=lambda fc: h['w2c'][fc], n_fc=32, n_k=8,
                x_rhs=lambda k, o, n, t0=t0: x_fm[:, k * 2048 + t0 + o: k * 2048 + t0 + o + n],
                ttile=TT, wbufs=w8[2:4], psum=psum, slot_banks=[2, 3] if TT == 512 else [4, 6],
                evac=evac_w2, ev=ev_a, w_hw=True, shared=sh_w2,
                w_guard=(wc_done['w2c'] if (wc_done and tt == 0) else []),
                x_waits=[x_ready] + prev_tail[-1:])

        def evac_h1(fc, ps, waits, ev):
            @p.o("s")
            def _(s, w=list(waits), fc=fc, ps=ps):
                for sm, val in w:
                    s.wait_ge(sm.h, val)
                s.wait_ge(svl.h, svlf)
                s.activation(bb[:, fc * TT:(fc + 1) * TT], ps, AF.Gelu,
                             scale=sv['h1sc'][:, fc:fc + 1]).then_inc(ev.h, 1)
        fm_proj(p, w_dram=lambda fc: h['h1c'][fc], n_fc=32, n_k=32,
                x_rhs=lambda k, o, n: a[:, k * TT + o: k * TT + o + n],
                ttile=TT, wbufs=w32, psum=psum, slot_banks=[4, 5] if TT == 512 else [0, 2],
                evac=evac_h1, ev=ev_b, w_hw=True, shared=sh_h1,
                x_waits=[(ev_a, ea_base + 32)],
                w_guard=(([(ev_w3, ev_w3.v)] if tt > 0 else []) +
                         (wc_done['h1c'] if (wc_done and tt == 0) else [])))

        eb_base = ev_b.v - 32
        def evac_h2(fc, ps, waits, ev):
            @p.o("s")
            def _(s, w=list(waits), fc=fc, ps=ps):
                for sm, val in w:
                    s.wait_ge(sm.h, val)
                s.wait_ge(svl.h, svlf)
                s.activation(a[:, fc * TT:(fc + 1) * TT], ps, AF.Gelu,
                             scale=sv['h2sc'][:, fc:fc + 1]).then_inc(ev.h, 1)
        fm_proj(p, w_dram=lambda fc: h['h2c'][fc], n_fc=32, n_k=32,
                x_rhs=lambda k, o, n: bb[:, k * TT + o: k * TT + o + n],
                ttile=TT, wbufs=w32, psum=psum, slot_banks=[6, 7] if TT == 512 else [4, 6],
                evac=evac_h2, ev=ev_c, w_hw=True, shared=sh_h2,
                x_waits=[(ev_b, eb_base + 32)],
                w_guard=[(ev_b, eb_base + 32)] +
                        (wc_done['h2c'] if (wc_done and tt == 0) else []))

        ec_base = ev_c.v - 32
        def evac_w3(fc, ps, waits, ev):
            @p.o("v")
            def _(v, w=list(waits), fc=fc, ps=ps):
                for sm, val in w:
                    v.wait_ge(sm.h, val)
                v.wait_ge(svl.h, svlf)
                v.tensor_scalar(w3o[:, fc * TT:(fc + 1) * TT], ps,
                                sv['s3c'][:, fc:fc + 1], None, ALU.mult).then_inc(ev.h, 1)
        fm_proj(p, w_dram=lambda fc: h['w3c'][fc], n_fc=8, n_k=32,
                x_rhs=lambda k, o, n: a[:, k * TT + o: k * TT + o + n],
                ttile=TT, wbufs=w32, psum=psum, slot_banks=[0, 1],
                evac=evac_w3, ev=ev_w3, w_hw=True, shared=sh_w3,
                x_waits=[(ev_c, ec_base + 32)] + ([(ev_tr, ev_tr.v)] if tt > 0 else []),
                w_guard=[(ev_c, ec_base + 32)] +
                        (wc_done['w3c'] if (wc_done and tt == 0) else []))

        ew3_0 = ev_w3.v
        n_tr = 8 * NCH
        transpose_tiles(
            p,
            src=lambda i: w3o[:, (i % 8) * TT + (i // 8) * 128: (i % 8) * TT + (i // 8) * 128 + 128],
            dst=lambda i: w3tok[:, (i // 8) * 1024 + (i % 8) * 128: (i // 8) * 1024 + (i % 8) * 128 + 128],
            n=n_tr, idn=idn, psum_bf=psum_bf, slots=TRAN_SLOTS,
            waits=[(ev_w3, ew3_0)] + ([(ln_done, ln_done.v)] if tt > 0 else []), ev=ev_tr)
        etr_base = ev_tr.v - n_tr

        for c8 in range(NCH):
            tc = tt * NCH + c8
            sx = sxs[tc % 2]
            @p.o("y")
            def _(y, tc=tc, need=(ln_marks[tc - 2] if tc >= 2 else 0), sx=sx):
                if tc >= 2:
                    y.wait_ge(ln_done.h, need)
                y.dma_start(out=xs_f32[:, (tc % 2) * 1024:(tc % 2) * 1024 + 1024],
                            in_=h['x_tok'][tc * 128:(tc + 1) * 128, :]).then_inc(sx.h, 16)
            sx.v += 16
            d_before = ln_done.v
            ln_chunk(p,
                     a_ap=xs_f32[:, (tc % 2) * 1024:(tc % 2) * 1024 + 1024],
                     b_ap=w3tok[:, c8 * 1024:(c8 + 1) * 1024],
                     tmp=sb['ln_tmp'][:, :], trash_bf=sb['ln_tr'][:, :], sc=sb['ln_sc'],
                     n_f=1024,
                     waits=[(sx, sx.v), (ev_tr, etr_base + n_tr)] +
                           ([(sos[tc % 2], sos[tc % 2].v)] if tc >= 2 else []) +
                           ([(sob[tc % 4], sob[tc % 4].v)]
                            if (tokb is not None and tc >= 4) else []),
                     u=u, done=ln_done,
                     out_f32=x2s[:, (tc % 2) * 1024:(tc % 2) * 1024 + 1024],
                     out_bf=(sb['x2b'][:, (tc % 4) * 1024:(tc % 4) * 1024 + 1024]
                             if (x2fm is not None or tokb is not None) else None))
            so = sos[tc % 2]
            @p.o("y")
            def _(y, tc=tc, w=d_before + 1, so=so):
                y.wait_ge(ln_done.h, w)
                y.dma_start(out=h['x2_tok'][tc * 128:(tc + 1) * 128, :],
                            in_=x2s[:, (tc % 2) * 1024:(tc % 2) * 1024 + 1024]).then_inc(so.h, 16)
            so.v += 16
            if tokb is not None:
                @p.o("y")
                def _(y, tc=tc, w=d_before + 2, so2=sob[tc % 4]):
                    y.wait_ge(ln_done.h, w)
                    y.dma_start(out=tokb[tc * 128:(tc + 1) * 128, :],
                                in_=sb['x2b'][:, (tc % 4) * 1024:(tc % 4) * 1024 + 1024]
                                ).then_inc(so2.h, 16)
                sob[tc % 4].v += 16
            ln_marks.append(ln_done.v)
            if x2fm is not None:
                x2_fm_t, ev_x2 = x2fm
                transpose_tiles(
                    p,
                    src=lambda i, tc=tc: sb['x2b'][:, (tc % 2) * 1024 + i * 128:(tc % 2) * 1024 + i * 128 + 128],
                    dst=lambda i, tc=tc: x2_fm_t[:, i * 2048 + tc * 128: i * 2048 + tc * 128 + 128],
                    n=8, idn=idn, psum_bf=psum_bf, slots=TRAN_SLOTS,
                    waits=[(ln_done, ln_done.v)], ev=ev_x2)
        prev_tail = [(ev_w3, ev_w3.v), (ev_tr, ev_tr.v)]
        if v_hook is not None:
            v_hook(tt)

    @p.o("y")
    def _(y, w0=sos[0].v, w1=sos[1].v):
        y.wait_ge(sos[0].h, w0)
        y.wait_ge(sos[1].h, w1)
    return ln_done, sos, (sob, b_sob)


from concourse.bass_isa import ReduceOp
from concourse.library_config import attnmlp as LIB_ATTNMLP


def ffn1_wprep(p, h, sb, psum_bf, done):
    wt = sb['pw_a']
    sts = [sb['pw_c0'], sb['pw_c1']]
    scs = sb['pw_s']
    idn = sb['idn']
    ld = p.sem(tag='hw')
    stv = p.sem()
    sto = [p.sem(tag='hw'), p.sem(tag='hw')]
    tk = p.sem()
    te = p.sem()
    bld, bsv, btk, bte = ld.v, stv.v, tk.v, te.v
    bst = [sto[0].v, sto[1].v]
    NI = 6

    def _store1(g, fc):
        dst = h['w1q_blk'][fc].rearrange("(kc pp) j -> pp kc j", pp=128)
        src2 = sts[fc % 2][:, 1024:2048].rearrange("pp (kc j) -> pp kc j", j=128)
        g.dma_start(out=dst, in_=src2).then_inc(sto[fc % 2].h, 16)

    @p.o("y")
    def _(g):
        for fc in range(32):
            if fc:
                g.wait_ge(stv.h, bsv + NI * fc)
            if fc >= 2:
                g.wait_ge(te.h, bte + 8 * (fc - 1))
                _store1(g, fc - 2)
            g.dma_start(out=wt[:, 0:1024],
                        in_=h['W1nT'][fc * 128:(fc + 1) * 128, :]).then_inc(ld.h, 16)
        g.wait_ge(te.h, bte + 8 * 31)
        _store1(g, 30)
        g.wait_ge(te.h, bte + 8 * 32)
        _store1(g, 31)
    ld.v += 16 * 32

    @p.o("v")
    def _(v):
        for fc in range(32):
            v.wait_ge(ld.h, bld + 16 * (fc + 1))
            if fc >= 2:
                v.wait_ge(tk.h, btk + 8 * (fc - 1))
            v.reduce_max(scs[:, 36:37], wt[:, 0:1024], axis=AX.X,
                         apply_absolute_value=True).then_inc(stv.h, 1)
            v.wait_ge(stv.h, bsv + NI * fc + 1)
            v.tensor_scalar(scs[:, 36:37], scs[:, 36:37], 1e-8, None, ALU.max).then_inc(stv.h, 1)
            v.wait_ge(stv.h, bsv + NI * fc + 2)
            v.reciprocal(scs[:, 37:38], scs[:, 36:37]).then_inc(stv.h, 1)
            v.wait_ge(stv.h, bsv + NI * fc + 3)
            v.tensor_scalar(scs[:, fc:fc + 1], scs[:, 37:38], 127.0, None, ALU.mult).then_inc(stv.h, 1)
            v.wait_ge(stv.h, bsv + NI * fc + 4)
            v.tensor_scalar(wt[:, 0:1024], wt[:, 0:1024], scs[:, fc:fc + 1], MAGIC,
                            ALU.mult, ALU.add).then_inc(stv.h, 1)
            v.wait_ge(stv.h, bsv + NI * fc + 5)
            v.tensor_scalar(sts[fc % 2][:, 0:1024], wt[:, 0:1024], MAGIC, None,
                            ALU.subtract).then_inc(stv.h, 1)
    stv.v += NI * 32

    @p.o("t")
    def _(t):
        for fc in range(32):
            t.wait_ge(stv.h, bsv + NI * (fc + 1))
            for kc in range(8):
                i = fc * 8 + kc
                if i >= 4:
                    t.wait_ge(te.h, bte + i - 3)
                ps = psum_bf[:, TRAN_SLOTS[i % 4]: TRAN_SLOTS[i % 4] + 128]
                t.transpose(ps, sts[fc % 2][:, kc * 128: kc * 128 + 128], idn[:, :]).then_inc(tk.h, 1)
    tk.v += 32 * 8

    @p.o("s")
    def _(s):
        for fc in range(32):
            for kc in range(8):
                i = fc * 8 + kc
                s.wait_ge(tk.h, btk + i + 1)
                if fc >= 2 and kc == 0:
                    s.wait_ge(sto[fc % 2].h, bst[fc % 2] + 16 * (fc // 2))
                ps = psum_bf[:, TRAN_SLOTS[i % 4]: TRAN_SLOTS[i % 4] + 128]
                s.copy(sts[fc % 2][:, 1024 + kc * 128: 1024 + kc * 128 + 128],
                       ps).then_inc(te.h, 1)
    te.v += 32 * 8

    sto[0].v += 16 * 16
    sto[1].v += 16 * 16

    rm = scs[:, 38:39]
    rr = scs[:, 39:40]
    B2 = stv.v
    ld2 = ld.v
    st2 = [sto[0].v, sto[1].v]

    def _store2(g, k):
        dst = h['w2q_blk'][:, k * 128:(k + 1) * 128, :].rearrange("fc pp j -> pp fc j")
        src2 = sts[k % 2][:, 0:1024].rearrange("pp (fc j) -> pp fc j", j=128)
        g.dma_start(out=dst, in_=src2).then_inc(sto[k % 2].h, 16)

    @p.o("y")
    def _(g):
        for k in range(32):
            if k:
                g.wait_ge(stv.h, B2 + 6 * k)
                _store2(g, k - 1)
            g.dma_start(out=wt[:, 0:1024], in_=h['W2n'][k * 128:(k + 1) * 128, :]).then_inc(ld.h, 16)
        g.wait_ge(stv.h, B2 + 6 * 32)
        _store2(g, 31)
    ld.v += 16 * 32

    @p.o("v")
    def _(v, tev=te.v):
        for k in range(32):
            v.wait_ge(ld.h, ld2 + 16 * (k + 1))
            if k >= 2:
                v.wait_ge(sto[k % 2].h, st2[k % 2] + 16 * (k // 2))
            else:
                v.wait_ge(sto[k % 2].h, st2[k % 2])
                v.wait_ge(te.h, tev)
            v.reduce_max(rm, wt[:, 0:1024], axis=AX.X, apply_absolute_value=True).then_inc(stv.h, 1)
            v.wait_ge(stv.h, B2 + 6 * k + 1)
            v.tensor_scalar(rm, rm, 1e-8, None, ALU.max).then_inc(stv.h, 1)
            v.wait_ge(stv.h, B2 + 6 * k + 2)
            v.reciprocal(rr, rm).then_inc(stv.h, 1)
            v.wait_ge(stv.h, B2 + 6 * k + 3)
            v.tensor_scalar(wt[:, 0:1024], wt[:, 0:1024], rr, 127.0, ALU.mult, ALU.mult).then_inc(stv.h, 1)
            v.wait_ge(stv.h, B2 + 6 * k + 4)
            v.tensor_scalar(wt[:, 0:1024], wt[:, 0:1024], MAGIC, MAGIC, ALU.add, ALU.subtract).then_inc(stv.h, 1)
            v.wait_ge(stv.h, B2 + 6 * k + 5)
            v.tensor_scalar(sts[k % 2][:, 0:1024], wt[:, 0:1024], rm, 1.0 / 127.0,
                            ALU.mult, ALU.mult).then_inc(stv.h, 1)
    stv.v = B2 + 6 * 32

    @p.o("y")
    def _(g, a0=sto[0].v + 16 * 16, a1=sto[1].v + 16 * 16, b=stv.v):
        g.wait_ge(sto[0].h, a0)
        g.wait_ge(sto[1].h, a1)
        g.wait_ge(stv.h, b)
        g.dma_start(out=h['prepflag'][0:1, 0:1], in_=scs[0:1, 0:1]).then_inc(done.h, 16)
    sto[0].v += 16 * 16
    sto[1].v += 16 * 16
    done.v += 16


def ffn1_wprep_quant(p, h, sb, done):
    wt = sb['pw_a'][:, 0:1024]
    sts = [sb['pw_c0'], sb['pw_c1']]
    scs = sb['pw_s']
    ld = p.sem(tag='sw')
    stv = p.sem()
    sto = [p.sem(tag='sw'), p.sem(tag='sw')]
    bld, bsv = ld.v, stv.v
    bst = [sto[0].v, sto[1].v]
    NI = 6

    def _emit_g1():
        @p.o("g")
        def _(g):
            for fc in range(32):
                if fc:
                    g.wait_ge(stv.h, bsv + NI * fc)
                if fc >= 2:
                    g.dma_start(out=h['w1qn'][fc - 2],
                                in_=sts[fc % 2][:, 0:1024]).then_inc(sto[fc % 2].h, 16)
                g.dma_start(out=wt[:, :],
                            in_=h['W1nT'][fc * 128:(fc + 1) * 128, :]).then_inc(ld.h, 16)
            g.wait_ge(stv.h, bsv + NI * 31)
            g.dma_start(out=h['w1qn'][30], in_=sts[0][:, 0:1024]).then_inc(sto[0].h, 16)
            g.wait_ge(stv.h, bsv + NI * 32)
            g.dma_start(out=h['w1qn'][31], in_=sts[1][:, 0:1024]).then_inc(sto[1].h, 16)
        ld.v += 16 * 32
        sto[0].v += 16 * 16
        sto[1].v += 16 * 16

    def _piece1(lo, hi):
        @p.o("v")
        def _(v):
            for fc in range(lo, hi):
                v.wait_ge(ld.h, bld + 16 * (fc + 1))
                if fc >= 2:
                    v.wait_ge(sto[fc % 2].h, bst[fc % 2] + 16 * (fc // 2))
                v.reduce_max(scs[:, 36:37], wt[:, :], axis=AX.X,
                             apply_absolute_value=True).then_inc(stv.h, 1)
                v.wait_ge(stv.h, bsv + NI * fc + 1)
                v.tensor_scalar(scs[:, 36:37], scs[:, 36:37], 1e-8, None, ALU.max).then_inc(stv.h, 1)
                v.wait_ge(stv.h, bsv + NI * fc + 2)
                v.reciprocal(scs[:, 37:38], scs[:, 36:37]).then_inc(stv.h, 1)
                v.wait_ge(stv.h, bsv + NI * fc + 3)
                v.tensor_scalar(scs[:, fc:fc + 1], scs[:, 37:38], 127.0, None, ALU.mult).then_inc(stv.h, 1)
                v.wait_ge(stv.h, bsv + NI * fc + 4)
                v.tensor_scalar(wt[:, :], wt[:, :], scs[:, fc:fc + 1], MAGIC,
                                ALU.mult, ALU.add).then_inc(stv.h, 1)
                v.wait_ge(stv.h, bsv + NI * fc + 5)
                v.tensor_scalar(sts[fc % 2][:, 0:1024], wt[:, :], MAGIC, None,
                                ALU.subtract).then_inc(stv.h, 1)
    stv.v += NI * 32

    rm = scs[:, 38:39]
    rr = scs[:, 39:40]
    B2 = bsv + NI * 32
    ld2 = bld + 16 * 32
    st2 = [bst[0] + 16 * 16, bst[1] + 16 * 16]

    def _store2(g, k):
        dst = h['w2q_blk'][:, :, k * 128:(k + 1) * 128].rearrange("fc pp j -> pp fc j")
        src2 = sts[k % 2][:, 0:1024].rearrange("pp (fc j) -> pp fc j", j=128)
        g.dma_start(out=dst, in_=src2).then_inc(sto[k % 2].h, 16)

    def _emit_g2():
        @p.o("g")
        def _(g):
            for k in range(32):
                if k:
                    g.wait_ge(stv.h, B2 + 6 * k)
                    _store2(g, k - 1)
                g.dma_start(out=wt[:, :], in_=h['W2n'][k * 128:(k + 1) * 128, :]).then_inc(ld.h, 16)
            g.wait_ge(stv.h, B2 + 6 * 32)
            _store2(g, 31)
        ld.v += 16 * 32

    def _piece2(lo, hi):
        @p.o("v")
        def _(v):
            for k in range(lo, hi):
                v.wait_ge(ld.h, ld2 + 16 * (k + 1))
                if k >= 2:
                    v.wait_ge(sto[k % 2].h, st2[k % 2] + 16 * (k // 2))
                else:
                    v.wait_ge(sto[k % 2].h, st2[k % 2])
                v.reduce_max(rm, wt[:, :], axis=AX.X, apply_absolute_value=True).then_inc(stv.h, 1)
                v.wait_ge(stv.h, B2 + 6 * k + 1)
                v.tensor_scalar(rm, rm, 1e-8, None, ALU.max).then_inc(stv.h, 1)
                v.wait_ge(stv.h, B2 + 6 * k + 2)
                v.reciprocal(rr, rm).then_inc(stv.h, 1)
                v.wait_ge(stv.h, B2 + 6 * k + 3)
                v.tensor_scalar(wt[:, :], wt[:, :], rr, 127.0, ALU.mult, ALU.mult).then_inc(stv.h, 1)
                v.wait_ge(stv.h, B2 + 6 * k + 4)
                v.tensor_scalar(wt[:, :], wt[:, :], MAGIC, MAGIC, ALU.add, ALU.subtract).then_inc(stv.h, 1)
                v.wait_ge(stv.h, B2 + 6 * k + 5)
                v.tensor_scalar(sts[k % 2][:, 0:1024], wt[:, :], rm, 1.0 / 127.0,
                                ALU.mult, ALU.mult).then_inc(stv.h, 1)
    stv.v = B2 + 6 * 32

    def _emit_sentinel():
        @p.o("g")
        def _(g, a0=bst[0] + 16 * 32, a1=bst[1] + 16 * 32, b=B2 + 6 * 32):
            g.wait_ge(sto[0].h, a0)
            g.wait_ge(sto[1].h, a1)
            g.wait_ge(stv.h, b)
            g.dma_start(out=h['prepflag'][0:1, 0:1], in_=scs[0:1, 0:1]).then_inc(done.h, 16)
        sto[0].v += 16 * 16
        sto[1].v += 16 * 16
        done.v += 16

    def emit_g():
        _emit_g1()
        _emit_g2()
        _emit_sentinel()
    return ([lambda: _piece1(0, 16), lambda: _piece1(16, 32),
             lambda: _piece2(0, 16), lambda: _piece2(16, 32)], emit_g)


def ffn1_blockify(p, h, sb, psum_bf, idn, quant_done, done):
    stv2 = sb['pw_b'][:, 0:2048].bitcast(BF16)
    sts = [stv2[:, 0:1024], stv2[:, 1024:2048]]
    evb = sb['pw_b'][:, 2048:3072].bitcast(BF16)
    evs = [evb[:, 0:1024], evb[:, 1024:2048]]
    ld = [p.sem(tag='sw'), p.sem(tag='sw')]
    sto = [p.sem(tag='sw'), p.sem(tag='sw')]
    tk = p.sem()
    te = p.sem()
    bld = [ld[0].v, ld[1].v]
    btk, bte = tk.v, te.v
    bst = [sto[0].v, sto[1].v]

    @p.o("g")
    def _(g, qd=(quant_done, quant_done.v)):
        g.wait_ge(qd[0].h, qd[1])
        for fc in range(32):
            if fc >= 2:
                g.wait_ge(tk.h, btk + 8 * (fc - 1))
            g.dma_start(out=sts[fc % 2][:, :],
                        in_=h['w1qn'][fc]).then_inc(ld[fc % 2].h, 16)
            if fc >= 2:
                g.wait_ge(te.h, bte + 8 * (fc - 1))
                g.dma_start(out=h['w1q_blk'][fc - 2],
                            in_=evs[fc % 2][:, :]).then_inc(sto[fc % 2].h, 16)
        g.wait_ge(te.h, bte + 8 * 31)
        g.dma_start(out=h['w1q_blk'][30], in_=evs[0][:, :]).then_inc(sto[0].h, 16)
        g.wait_ge(te.h, bte + 8 * 32)
        g.dma_start(out=h['w1q_blk'][31], in_=evs[1][:, :]).then_inc(sto[1].h, 16)
        g.wait_ge(sto[0].h, bst[0] + 16 * 16)
        g.wait_ge(sto[1].h, bst[1] + 16 * 16)
        g.dma_start(out=h['prepflag'][0:1, 2:3], in_=sb['pw_s'][0:1, 0:1]).then_inc(done.h, 16)
    ld[0].v += 16 * 16
    ld[1].v += 16 * 16

    @p.o("t")
    def _(t):
        for fc in range(32):
            t.wait_ge(ld[fc % 2].h, bld[fc % 2] + 16 * (fc // 2 + 1))
            for kc in range(8):
                i = fc * 8 + kc
                if i >= 4:
                    t.wait_ge(te.h, bte + i - 3)
                ps = psum_bf[:, TRAN_SLOTS[i % 4]: TRAN_SLOTS[i % 4] + 128]
                t.transpose(ps, sts[fc % 2][:, kc * 128: kc * 128 + 128],
                            idn[:, :]).then_inc(tk.h, 1)
    tk.v += 32 * 8

    @p.o("s")
    def _(s):
        for fc in range(32):
            for kc in range(8):
                i = fc * 8 + kc
                s.wait_ge(tk.h, btk + i + 1)
                if fc >= 2 and kc == 0:
                    s.wait_ge(sto[fc % 2].h, bst[fc % 2] + 16 * (fc // 2))
                ps = psum_bf[:, TRAN_SLOTS[i % 4]: TRAN_SLOTS[i % 4] + 128]
                s.copy(evs[fc % 2][:, kc * 128: kc * 128 + 128], ps).then_inc(te.h, 1)
    te.v += 32 * 8

    sto[0].v += 16 * 16
    sto[1].v += 16 * 16
    done.v += 16


def allreduce_max_scalar(p, *, local_11, bounce_in, bounce_out, rg, out_11,
                         tmp18, waits, done):
    vz = p.sem()
    cc = p.sem(tag='co')
    dmx = p.sem(tag='sw')
    b_vz, b_cc, b_dmx = vz.v, cc.v, dmx.v

    @p.o("v")
    def _(v, w=list(waits)):
        for s, val in w:
            v.wait_ge(s.h, val)
        v.tensor_copy(tmp18[0:1, 0:8], local_11.broadcast_to([1, 8])).then_inc(vz.h, 1)
    vz.v += 1

    @p.o("g")
    def _(g):
        g.wait_ge(vz.h, b_vz + 1)
        g.dma_start(out=bounce_in[:, :], in_=tmp18[0:1, 0:8]).then_inc(dmx.h, 16)
        g.wait_ge(dmx.h, b_dmx + 16)
        g.collective_compute("AllReduce", ALU.max, replica_groups=rg,
                             ins=[bounce_in.ap().opt()],
                             outs=[bounce_out.ap().opt()]).then_inc(cc.h, 1)
        g.wait_ge(cc.h, b_cc + 1)
        g.dma_start(out=out_11, in_=bounce_out[0:1, 0:1]).then_inc(done.h, 16)
    dmx.v += 16
    cc.v += 1
    done.v += 16


def bcast_row(p, *, onesf, row, out, psum, bank, waits, done):
    mmv = p.sem()
    b_mm = mmv.v

    @p.o("t")
    def _(t, w=list(waits)):
        for s, val in w:
            t.wait_ge(s.h, val)
        t.matmul(psum[:, bank * 512: bank * 512 + row.shape[1]], onesf, row,
                 start=True, stop=True).then_inc(mmv.h, 1)
    mmv.v += 1

    @p.o("v")
    def _(v):
        v.wait_ge(mmv.h, b_mm + 1)
        v.tensor_copy(out, psum[:, bank * 512: bank * 512 + row.shape[1]]).then_inc(done.h, 1)
    done.v += 1


def colmax_to_scalar(p, *, col, dram_flat, flat128, out_11, waits, done):
    dmx = p.sem(tag='sw')
    b_dmx = dmx.v

    @p.o("g")
    def _(g, w=list(waits)):
        for s, val in w:
            g.wait_ge(s.h, val)
        g.dma_start(out=dram_flat[0:128], in_=col,
                    allow_slow_non_contiguous=True).then_inc(dmx.h, 16)
        g.wait_ge(dmx.h, b_dmx + 16)
        g.dma_start(out=flat128[0:1, 0:128], in_=dram_flat[0:128]).then_inc(dmx.h, 16)
    dmx.v += 32

    @p.o("v")
    def _(v):
        v.wait_ge(dmx.h, b_dmx + 32)
        v.reduce_max(out_11, flat128[0:1, 0:128], axis=AX.X).then_inc(done.h, 1)
    done.v += 1


def ffn1_phase(p, h, sb, psum, psum_bf, x_ready, rg, prep_done, TT=512, xamax=None):
    NTILE = 2048 // TT
    NCH = TT // 128
    x3f = sb['x3f']
    fhid = sb['fhid']
    w8 = sb['w8']
    w32 = sb['w32']
    wo = sb['w3o']
    wotok = sb['w3tok']
    xs_f32 = sb['xs_f32']
    x2s = sb['x2s']
    idn = sb['idn']
    qs = sb['qs']
    q1 = sb['q1']
    scs = sb['pw_s']
    ep1 = sb['ep1']
    binc = sb['binc']
    boutc = sb['boutc']
    hmv = sb['hmv']
    onesf = sb['onesf']

    svl = p.sem(tag='sw')

    @p.o("g")
    def _(g):
        g.dma_start(out=binc[:, :], in_=h['b_in_c'][:, :]).then_inc(svl.h, 16)
        g.dma_start(out=boutc[:, :], in_=h['b_out_c'][:, :]).then_inc(svl.h, 16)
    svl.v += 32
    svlf = svl.v

    cz = p.sem()

    @p.o("v")
    def _(v):
        v.memset(onesf[0:1, 0:128], 1.0).then_inc(cz.h, 1)
    cz.v += 1
    czf = cz.v

    if xamax is None:
        am = p.sem()
        b_am = am.v

        @p.o("v")
        def _(v, xw=[x_ready]):
            for s, val in xw:
                v.wait_ge(s.h, val)
            for kc in range(8):
                v.reduce_max(qs[:, 8 + kc:9 + kc], x3f[:, kc * 2048:(kc + 1) * 2048],
                             axis=AX.X, apply_absolute_value=True).then_inc(am.h, 1)
            v.wait_ge(am.h, b_am + 8)
            v.reduce_max(qs[:, 0:1], qs[:, 8:16], axis=AX.X).then_inc(am.h, 1)
        am.v += 9

        colmax_to_scalar(p, col=qs[:, 0:1], dram_flat=h['flat_sc'][0],
                         flat128=sb['q1x'], out_11=q1[0:1, 0:1],
                         waits=[(am, am.v)], done=am)
        arx = p.sem(tag='sw')
        allreduce_max_scalar(p, local_11=q1[0:1, 0:1], bounce_in=h['arm1'],
                             bounce_out=h['arm1o'], rg=rg, out_11=q1[0:1, 1:2],
                             tmp18=sb['tmp18'], waits=[(am, am.v)], done=arx)
        amx_src = None
    else:
        amx_src, arx = xamax

    sq = p.sem()
    b_sq = sq.v
    off = 1 if amx_src is not None else 0

    @p.o("v")
    def _(v, w=arx.v, amx_src=amx_src):
        v.wait_ge(arx.h, w)
        if amx_src is not None:
            v.tensor_copy(q1[0:1, 1:2], amx_src).then_inc(sq.h, 1)
            v.wait_ge(sq.h, b_sq + 1)
        v.tensor_scalar(q1[0:1, 1:2], q1[0:1, 1:2], 1e-8, None, ALU.max).then_inc(sq.h, 1)
        v.wait_ge(sq.h, b_sq + off + 1)
        v.reciprocal(q1[0:1, 2:3], q1[0:1, 1:2]).then_inc(sq.h, 1)
        v.wait_ge(sq.h, b_sq + off + 2)
        v.tensor_scalar(q1[0:1, 2:3], q1[0:1, 2:3], 32767.0, None, ALU.mult).then_inc(sq.h, 1)
        v.wait_ge(sq.h, b_sq + off + 3)
        v.tensor_scalar(q1[0:1, 3:4], q1[0:1, 1:2], 1.0 / 32767.0, None, ALU.mult).then_inc(sq.h, 1)
    sq.v = b_sq + off + 4

    bc1 = p.sem()
    bcast_row(p, onesf=onesf[0:1, 0:128], row=q1[0:1, 2:4], out=qs[:, 2:4],
              psum=psum, bank=0, waits=[(sq, sq.v), (cz, czf)], done=bc1)

    @p.o("v")
    def _(v, w=prep_done.v, b=bc1.v, xw=[x_ready]):
        v.wait_ge(prep_done.h, w)
        v.wait_ge(bc1.h, b)
        for s, val in xw:
            v.wait_ge(s.h, val)
        v.reciprocal(ep1[:, :], scs[:, 0:32]).then_inc(sq.h, 1)
        v.wait_ge(sq.h, b_sq + off + 5)
        v.tensor_scalar(ep1[:, :], ep1[:, :], qs[:, 3:4], None, ALU.mult).then_inc(sq.h, 1)
        tmpf = sb['pw_a']
        for kc in range(8):
            v.wait_ge(sq.h, b_sq + off + 6 + 2 * kc)
            v.tensor_scalar(tmpf[:, 0:2048], x3f[:, kc * 2048:(kc + 1) * 2048],
                            qs[:, 2:3], MAGIC, ALU.mult, ALU.add).then_inc(sq.h, 1)
            v.wait_ge(sq.h, b_sq + off + 7 + 2 * kc)
            v.tensor_scalar(x3f[:, kc * 2048:(kc + 1) * 2048], tmpf[:, 0:2048],
                            MAGIC, None, ALU.subtract).then_inc(sq.h, 1)
    sq.v = b_sq + off + 6 + 16
    xq_ready = (sq, sq.v)

    u = p.sem()
    ln_done = p.sem()
    sxs = [p.sem(tag='hw'), p.sem(tag='hw')]
    sos = [p.sem(tag='hw'), p.sem(tag='hw')]
    hsp_st = [p.sem(tag='hw'), p.sem(tag='hw')]
    ln_marks = []
    ev_h = p.sem()
    ev_wo = p.sem()
    ev_tr = p.sem()
    hmx = p.sem()
    sh_s1 = (p.sem(), [p.sem(tag='hw'), p.sem(tag='hw')])
    sh_s2 = (p.sem(), [p.sem(tag='hw'), p.sem(tag='hw'), p.sem(tag='hw')])
    prev_tail = []

    for tt in range(NTILE):
        t0 = tt * TT

        def evac_g(fc, ps, waits, ev, tt=tt):
            @p.o("s")
            def _(s, w=list(waits), fc=fc, ps=ps):
                for sm, val in w:
                    s.wait_ge(sm.h, val)
                s.wait_ge(svl.h, svlf)
                s.activation(fhid[:, fc * TT:(fc + 1) * TT], ps, AF.Gelu,
                             scale=ep1[:, fc:fc + 1], bias=binc[:, fc:fc + 1]).then_inc(ev.h, 1)

            @p.o("v")
            def _(v, w=ev_h.v + (fc + 1), fc=fc):
                v.wait_ge(ev_h.h, w)
                v.reduce_max(hmv[:, fc:fc + 1], fhid[:, fc * TT:(fc + 1) * TT],
                             axis=AX.X, apply_absolute_value=True).then_inc(hmx.h, 1)
        fm_proj(p, w_dram=lambda fc: h['w1q_blk'][fc],
                w_hw=True, shared=sh_s1, n_fc=32, n_k=8,
                w_guard=([(prep_done, prep_done.v)] if tt == 0 else []),
                x_rhs=lambda k, o, n, t0=t0: x3f[:, k * 2048 + t0 + o: k * 2048 + t0 + o + n],
                ttile=TT, wbufs=w8[0:2], psum=psum, slot_banks=[0, 1] if TT == 512 else [0, 2],
                evac=evac_g, ev=ev_h, x_waits=[xq_ready] + prev_tail)
        hb = hmx.v
        hmx.v += 32

        @p.o("v")
        def _(v, hb=hb, tt=tt):
            v.wait_ge(hmx.h, hb + 32)
            if tt == 0:
                v.reduce_max(qs[:, 8:9], hmv[:, :], axis=AX.X).then_inc(hmx.h, 1)
            else:
                v.reduce_max(qs[:, 9:10], hmv[:, :], axis=AX.X).then_inc(hmx.h, 1)
                v.wait_ge(hmx.h, hb + 33)
                v.tensor_tensor(qs[:, 8:9], qs[:, 8:9], qs[:, 9:10], ALU.max).then_inc(hmx.h, 1)
        hmx.v += 1 if tt == 0 else 2

        @p.o("y")
        def _(y, w=ev_h.v, tt=tt):
            y.wait_ge(ev_h.h, w)
            y.dma_start(out=h['hsp'][tt], in_=fhid[:, :]).then_inc(hsp_st[tt % 2].h, 16)
        hsp_st[tt % 2].v += 16
        prev_tail = [(hsp_st[tt % 2], hsp_st[tt % 2].v), (hmx, hmx.v)]

    colmax_to_scalar(p, col=qs[:, 8:9], dram_flat=h['flat_sc'][1],
                     flat128=sb['q1x'], out_11=q1[0:1, 4:5],
                     waits=[(hmx, hmx.v)], done=hmx)
    arh = p.sem(tag='sw')
    allreduce_max_scalar(p, local_11=q1[0:1, 4:5], bounce_in=h['arm2'],
                         bounce_out=h['arm2o'], rg=rg, out_11=q1[0:1, 5:6],
                         tmp18=sb['tmp18'], waits=[(hmx, hmx.v)], done=arh)

    @p.o("v")
    def _(v, w=arh.v, b=sq.v):
        v.wait_ge(arh.h, w)
        v.tensor_scalar(q1[0:1, 5:6], q1[0:1, 5:6], 1e-8, None, ALU.max).then_inc(sq.h, 1)
        v.wait_ge(sq.h, b + 1)
        v.reciprocal(q1[0:1, 6:7], q1[0:1, 5:6]).then_inc(sq.h, 1)
        v.wait_ge(sq.h, b + 2)
        v.tensor_scalar(q1[0:1, 6:7], q1[0:1, 6:7], 32767.0, None, ALU.mult).then_inc(sq.h, 1)
        v.wait_ge(sq.h, b + 3)
        v.tensor_scalar(q1[0:1, 7:8], q1[0:1, 5:6], 1.0 / 32767.0, None, ALU.mult).then_inc(sq.h, 1)
    sq.v += 4
    bc2 = p.sem()
    bcast_row(p, onesf=onesf[0:1, 0:128], row=q1[0:1, 6:8], out=qs[:, 6:8],
              psum=psum, bank=0, waits=[(sq, sq.v)], done=bc2)
    s_h_ready = (bc2, bc2.v)

    for tt in range(NTILE):
        rb = p.sem(tag='hw')

        @p.o("y")
        def _(y, tt=tt, w=(ev_wo.v if tt else 0), rb=rb,
              spill_w=[(s, s.v) for s in hsp_st] + [(hmx, hmx.v)]):
            if tt:
                y.wait_ge(ev_wo.h, w)
            else:
                for s, val in spill_w:
                    y.wait_ge(s.h, val)
            y.dma_start(out=fhid[:, :], in_=h['hsp'][tt]).then_inc(rb.h, 16)
        rb.v += 16

        NKC = (32 * TT) // 2048

        @p.o("v")
        def _(v, rb=rb, rbb_=rb.v, b=sq.v, shr=s_h_ready):
            v.wait_ge(rb.h, rbb_)
            v.wait_ge(shr[0].h, shr[1])
            tmpf = sb['pw_a']
            for kc in range(NKC):
                v.wait_ge(sq.h, b + 2 * kc)
                v.tensor_scalar(tmpf[:, 0:2048], fhid[:, kc * 2048:(kc + 1) * 2048],
                                qs[:, 6:7], MAGIC, ALU.mult, ALU.add).then_inc(sq.h, 1)
                v.wait_ge(sq.h, b + 2 * kc + 1)
                v.tensor_scalar(fhid[:, kc * 2048:(kc + 1) * 2048], tmpf[:, 0:2048],
                                MAGIC, None, ALU.subtract).then_inc(sq.h, 1)
        hq_ready = (sq, sq.v + 2 * NKC)
        sq.v += 2 * NKC

        def evac_o(fc, ps, waits, ev):
            @p.o("v")
            def _(v, w=list(waits), fc=fc, ps=ps):
                for sm, val in w:
                    v.wait_ge(sm.h, val)
                v.wait_ge(svl.h, svlf)
                v.tensor_scalar(wo[:, fc * TT:(fc + 1) * TT], ps, qs[:, 7:8],
                                boutc[:, fc:fc + 1], ALU.mult, ALU.add).then_inc(ev.h, 1)
        fm_proj(p, w_dram=lambda fc: h['w2q_blk'][fc],
                w_hw=True, shared=sh_s2, n_fc=8, n_k=32,
                w_guard=([(prep_done, prep_done.v)] if tt == 0 else []),
                x_rhs=lambda k, o, n: fhid[:, k * TT + o: k * TT + o + n],
                ttile=TT, wbufs=w32, psum=psum, slot_banks=[0, 1] if TT == 512 else [0, 2],
                evac=evac_o, ev=ev_wo, x_waits=[hq_ready])

        ewo0 = ev_wo.v
        n_tr = 8 * NCH
        transpose_tiles(
            p,
            src=lambda i: wo[:, (i % 8) * TT + (i // 8) * 128: (i % 8) * TT + (i // 8) * 128 + 128],
            dst=lambda i: wotok[:, (i // 8) * 1024 + (i % 8) * 128: (i // 8) * 1024 + (i % 8) * 128 + 128],
            n=n_tr, idn=idn, psum_bf=psum_bf, slots=TRAN_SLOTS,
            waits=[(ev_wo, ewo0)] + ([(ln_done, ln_done.v)] if tt else []), ev=ev_tr)
        etr_base = ev_tr.v - n_tr

        for c8 in range(NCH):
            tc = tt * NCH + c8
            sx = sxs[tc % 2]

            @p.o("y")
            def _(y, tc=tc, need=(ln_marks[tc - 2] if tc >= 2 else 0), sx=sx):
                if tc >= 2:
                    y.wait_ge(ln_done.h, need)
                y.dma_start(out=xs_f32[:, (tc % 2) * 1024:(tc % 2) * 1024 + 1024],
                            in_=h['x_tok'][tc * 128:(tc + 1) * 128, :]).then_inc(sx.h, 16)
            sx.v += 16
            d_before = ln_done.v
            ln_chunk(p,
                     a_ap=xs_f32[:, (tc % 2) * 1024:(tc % 2) * 1024 + 1024],
                     b_ap=wotok[:, c8 * 1024:(c8 + 1) * 1024],
                     tmp=sb['ln_tmp'][:, :], trash_bf=sb['ln_tr'][:, :], sc=sb['ln_sc'],
                     n_f=1024,
                     waits=[(sx, sx.v), (ev_tr, etr_base + n_tr)] +
                           ([(sos[tc % 2], sos[tc % 2].v)] if tc >= 2 else []),
                     u=u, done=ln_done,
                     out_f32=x2s[:, (tc % 2) * 1024:(tc % 2) * 1024 + 1024])
            so = sos[tc % 2]

            @p.o("y")
            def _(y, tc=tc, w=d_before + 1, so=so):
                y.wait_ge(ln_done.h, w)
                y.dma_start(out=h['x4_tok'][tc * 128:(tc + 1) * 128, :],
                            in_=x2s[:, (tc % 2) * 1024:(tc % 2) * 1024 + 1024]).then_inc(so.h, 16)
            so.v += 16
            ln_marks.append(ln_done.v)

    @p.o("y")
    def _(y, w0=sos[0].v, w1=sos[1].v):
        y.wait_ge(sos[0].h, w0)
        y.wait_ge(sos[1].h, w1)
    return ln_done, sos


def attn1_phase(p, h, sb, psum, psum_bf, x_ready, rg, absmax_col=None):
    x2f = sb['x2f']
    xt2 = sb['xt2']
    qf = sb['qf']
    kf = sb['kf']
    vaug = sb['vaug']
    aout = sb['aout']
    pbuf = sb['pbuf']
    tau = sb['tau']
    rrow = sb['rrow']
    wq3 = sb['wq3']
    wv = sb['wv']
    wos = sb['wos']
    onesf = sb['onesf']

    agv = p.sem(tag='sw')
    cc = p.sem(tag='co')
    b_agv = agv.v

    @p.o("g")
    def _(g, xw=[x_ready]):
        for s, val in xw:
            g.wait_ge(s.h, val)
        g.dma_start(out=h['agx_in'].ap().rearrange("(fc pp t) -> pp fc t", fc=8, pp=128),
                    in_=x2f[:, :].rearrange("pp (fc t) -> pp fc t", t=2048)).then_inc(agv.h, 16)
        g.wait_ge(agv.h, b_agv + 16)
        g.collective_compute("AllGather", ALU.bypass, replica_groups=rg,
                             ins=[h['agx_in'].ap().opt()],
                             outs=[h['agx_out'].ap().opt()]).then_inc(cc.h, 1)
    agv.v += 16
    cc.v += 1

    cl = p.sem(tag='sw')

    @p.o("g")
    def _(g, agw=agv.v):
        g.wait_ge(agv.h, agw)
        for j in range(4):
            g.dma_start(out=tau[:, j * 512:(j + 1) * 512], in_=h['tauj'][j]).then_inc(cl.h, 16)
        for i in range(4):
            dst = wq3[:, i * 1024: i * 1024 + 1024].rearrange("pp (kc j) -> pp kc j", j=128)
            g.dma_start(out=dst, in_=h['Wqb'][i].rearrange("(kc pp) j -> pp kc j", pp=128)).then_inc(cl.h, 16)
        for i in range(4):
            dst = wq3[:, (4 + i) * 1024: (4 + i) * 1024 + 1024].rearrange("pp (kc j) -> pp kc j", j=128)
            g.dma_start(out=dst, in_=h['Wkb'][i].rearrange("(kc pp) j -> pp kc j", pp=128)).then_inc(cl.h, 16)
        g.dma_start(out=wv[:, :].rearrange("pp (kc j) -> pp kc j", j=512),
                    in_=h['Wvb'].rearrange("(kc pp) j -> pp kc j", pp=128)).then_inc(cl.h, 16)
        g.dma_start(out=sb['bo_c'][:, :], in_=h['bo_c'][:, :]).then_inc(cl.h, 16)
    cl.v += 16 * 14

    vz = p.sem()
    b_vz = vz.v

    @p.o("v")
    def _(v):
        v.memset(vaug[:, :], 0.0).then_inc(vz.h, 1)
        v.wait_ge(vz.h, b_vz + 1)
        ones_ap = vaug[:, :].rearrange("pp (tc hh s) -> pp tc hh s", tc=32, hh=8)[:, :, :, 64:65]
        v.memset(ones_ap, 1.0).then_inc(vz.h, 1)
    vz.v += 2

    xl = [p.sem(tag='sw'), p.sem(tag='sw')]
    mmq = p.sem()
    evq = p.sem()
    b_xl = [xl[0].v, xl[1].v]
    b_mmq, b_evq = mmq.v, evq.v

    @p.o("g")
    def _(g, ccv=cc.v):
        g.wait_ge(cc.h, ccv)
        for tt in range(4):
            if tt >= 2:
                g.wait_ge(mmq.h, b_mmq + 16 * (tt - 1))
            src = h['agx_out'].ap().rearrange("r (fc pp t) -> r fc pp t", fc=8, pp=128)
            blk, tloc = (tt * 1024) // 2048, (tt * 1024) % 2048
            g.dma_start(out=xt2[tt % 2][:, :].rearrange("pp (fc t) -> pp fc t", t=1024),
                        in_=src[blk, :, :, tloc:tloc + 1024].rearrange("fc pp t -> pp fc t")
                        ).then_inc(xl[tt % 2].h, 16)
    xl[0].v += 32
    xl[1].v += 32

    @p.o("t")
    def _(t, clv=cl.v, vzv=vz.v):
        t.wait_ge(cl.h, clv)
        t.wait_ge(vz.h, vzv)
        n = 0
        for tt in range(4):
            t.wait_ge(xl[tt % 2].h, b_xl[tt % 2] + 16 * (tt // 2 + 1))
            x2t = xt2[tt % 2]
            for pi in range(2):
                for fc in range(4):
                    if n >= 2:
                        t.wait_ge(evq.h, b_evq + n - 1)
                    b0 = (n % 2) * 1024
                    for nt in range(2):
                        ps = psum[:, b0 + nt * 512: b0 + nt * 512 + 512]
                        for k in range(8):
                            mi = t.matmul(ps, wq3[:, (pi * 4 + fc) * 1024 + k * 128:
                                                  (pi * 4 + fc) * 1024 + k * 128 + 128],
                                          x2t[:, k * 1024 + nt * 512: k * 1024 + nt * 512 + 512],
                                          start=(k == 0), stop=(k == 7))
                        if nt == 1:
                            mi.then_inc(mmq.h, 1)
                    n += 1
            for tc8 in range(8):
                if n >= 2:
                    t.wait_ge(evq.h, b_evq + n - 1)
                b0 = (n % 2) * 1024
                ps = psum[:, b0: b0 + 512]
                for k in range(8):
                    mi = t.matmul(ps, x2t[:, k * 1024 + tc8 * 128: k * 1024 + tc8 * 128 + 128],
                                  wv[:, k * 512: k * 512 + 512],
                                  start=(k == 0), stop=(k == 7))
                mi.then_inc(mmq.h, 1)
                n += 1
    mmq.v += 64

    n = 0
    for tt in range(4):
        for pi in range(2):
            for fc in range(4):
                @p.o("v")
                def _(v, n=n, tt=tt, pi=pi, fc=fc):
                    v.wait_ge(mmq.h, b_mmq + n + 1)
                    dstbuf = qf if pi == 0 else kf
                    v.tensor_copy(dstbuf[:, fc * 4096 + tt * 1024: fc * 4096 + tt * 1024 + 1024],
                                  psum[:, (n % 2) * 1024: (n % 2) * 1024 + 1024]).then_inc(evq.h, 1)
                n += 1
        for tc8 in range(8):
            @p.o("v")
            def _(v, n=n, tt=tt, tc8=tc8):
                v.wait_ge(mmq.h, b_mmq + n + 1)
                tc = tt * 8 + tc8
                dst = vaug[:, tc * 528: tc * 528 + 528].rearrange("pp (hh s) -> pp hh s", s=66)[:, :, 0:64]
                v.tensor_copy(dst, psum[:, (n % 2) * 1024: (n % 2) * 1024 + 512]
                              .rearrange("pp (hh d) -> pp hh d", d=64)).then_inc(evq.h, 1)
            n += 1
    evq.v += 64
    qkv_done = (evq, evq.v)

    sc_mm = p.sem()
    sc_ex = p.sem()
    sc_ms = p.sem()
    sc_av = p.sem()
    sc_ep = p.sem()
    ep_bc = p.sem()
    ep_rr = p.sem()
    b_smm, b_sex, b_sav, b_sep = sc_mm.v, sc_ex.v, sc_av.v, sc_ep.v
    b_epbc = ep_bc.v

    step = 0
    epn = 0
    AVB = [4, 5, 6]
    pend_av = [None]
    pend_ep = [None]

    def emit_pend_av():
        if pend_av[0] is not None:
            p.o("t")(pend_av[0])
            pend_av[0] = None

    def emit_pend_ep():
        if pend_ep[0] is not None:
            p.o("t")(pend_ep[0])
            pend_ep[0] = None

    for h8 in range(8):
        for qt in range(8):
            avb = AVB[(h8 * 8 + qt) % 3]
            first = True
            for g_ in range(qt + 1):
                for half in range(2):
                    st = step % 2
                    b0 = st * 1024
                    kt0 = g_ * 4 + half * 2

                    @p.o("t")
                    def _(t, h8=h8, qt=qt, st=st, b0=b0, kt0=kt0,
                          step=step, qkv=qkv_done):
                        if step == 0:
                            t.wait_ge(qkv[0].h, qkv[1])
                        if step >= 2:
                            t.wait_ge(sc_ex.h, b_sex + step - 1)
                        for i in range(2):
                            kt = kt0 + i
                            ps = psum[:, b0 + i * 512: b0 + i * 512 + 512]
                            mi = t.matmul(ps,
                                          kf[(h8 % 2) * 64:(h8 % 2) * 64 + 64,
                                             (h8 // 2) * 4096 + kt * 128:(h8 // 2) * 4096 + kt * 128 + 128],
                                          qf[(h8 % 2) * 64:(h8 % 2) * 64 + 64,
                                             (h8 // 2) * 4096 + qt * 512:(h8 // 2) * 4096 + qt * 512 + 512],
                                          start=True, stop=True)
                            if i == 1:
                                mi.then_inc(sc_mm.h, 1)
                    sc_mm.v += 1
                    emit_pend_av()
                    emit_pend_ep()

                    @p.o("s")
                    def _(s, step=step, st=st, b0=b0):
                        s.wait_ge(sc_mm.h, b_smm + step + 1)
                        if step >= 2:
                            s.wait_ge(sc_av.h, b_sav + step - 1)
                        s.activation(pbuf[:, (step % 2) * 1024:(step % 2) * 1024 + 1024],
                                     psum[:, b0: b0 + 1024], AF.Exp,
                                     scale=0.125).then_inc(sc_ex.h, 1)
                    sc_ex.v += 1

                    if g_ == qt:
                        @p.o("v")
                        def _(v, step=step, half=half):
                            v.wait_ge(sc_ex.h, b_sex + step + 1)
                            for i in range(2):
                                v.tensor_tensor(pbuf[:, (step % 2) * 1024 + i * 512:
                                                     (step % 2) * 1024 + i * 512 + 512],
                                                pbuf[:, (step % 2) * 1024 + i * 512:
                                                     (step % 2) * 1024 + i * 512 + 512],
                                                tau[:, (half * 2 + i) * 512:(half * 2 + i) * 512 + 512],
                                                ALU.mult).then_inc(sc_ms.h, 1)
                        sc_ms.v += 2

                    def _av(t, step=step, avb=avb, h8=h8, first=first,
                            msv=sc_ms.v, exv=sc_ex.v, kt0=kt0, epn=epn):
                        t.wait_ge(sc_ex.h, exv)
                        t.wait_ge(sc_ms.h, msv)
                        if first and epn >= 3:
                            t.wait_ge(sc_ep.h, b_sep + 2 * epn - 4)
                        for i in range(2):
                            tc = kt0 + i
                            mi = t.matmul(psum[0:66, avb * 512: avb * 512 + 512],
                                          vaug[:, tc * 528 + (h8 % 8) * 66: tc * 528 + (h8 % 8) * 66 + 66],
                                          pbuf[:, (step % 2) * 1024 + i * 512:
                                               (step % 2) * 1024 + i * 512 + 512],
                                          start=(first and i == 0), stop=False,
                                          skip_group_check=True)
                            if i == 1:
                                mi.then_inc(sc_av.h, 1)
                    pend_av[0] = _av
                    sc_av.v += 1
                    first = False
                    step += 1

            @p.o("v")
            def _(v, avb=avb, avv=sc_av.v, epn=epn):
                v.wait_ge(sc_av.h, avv)
                v.wait_ge(ep_bc.h, b_epbc + epn)
                v.reciprocal(rrow[0:1, 0:512],
                             psum[64:65, avb * 512: avb * 512 + 512]
                             ).then_inc(ep_rr.h, 1)
            ep_rr.v += 1

            def _bc(t, rrv=ep_rr.v, epn=epn):
                t.wait_ge(ep_rr.h, rrv)
                if epn > 0:
                    t.wait_ge(sc_ep.h, b_sep + 2 * epn - 1)
                t.matmul(psum[0:64, 7 * 512: 7 * 512 + 512], onesf[0:1, 0:64],
                         rrow[0:1, 0:512], start=True, stop=True).then_inc(ep_bc.h, 1)
            pend_ep[0] = _bc
            ep_bc.v += 1

            @p.o("v")
            def _(v, avb=avb, bcv=ep_bc.v, h8=h8, qt=qt, sv=sc_ep.v):
                v.wait_ge(ep_bc.h, bcv)
                v.wait_ge(sc_ep.h, sv)
                v.tensor_copy(sb['rbb'][0:64, 0:512],
                              psum[0:64, 7 * 512: 7 * 512 + 512]).then_inc(sc_ep.h, 1)
                v.wait_ge(sc_ep.h, sv + 1)
                v.tensor_tensor(aout[(h8 % 2) * 64:(h8 % 2) * 64 + 64,
                                     (h8 // 2) * 4096 + qt * 512:(h8 // 2) * 4096 + qt * 512 + 512],
                                psum[0:64, avb * 512: avb * 512 + 512],
                                sb['rbb'][0:64, 0:512], ALU.mult).then_inc(sc_ep.h, 1)
            sc_ep.v += 2
            epn += 1
    emit_pend_av()
    emit_pend_ep()

    attn_done = (sc_ep, sc_ep.v)

    wl = p.sem(tag='sw')

    @p.o("g")
    def _(g):
        for fc in range(8):
            dst = wos[:, fc * 512: fc * 512 + 512].rearrange("pp (kc j) -> pp kc j", j=128)
            g.dma_start(out=dst, in_=h['Wob'][fc].rearrange("(kc pp) j -> pp kc j", pp=128)).then_inc(wl.h, 16)
    wl.v += 16 * 8

    mmo = p.sem()
    evo = p.sem()
    sto = [p.sem(tag='hw'), p.sem(tag='hw')]
    b_mmo, b_evo = mmo.v, evo.v
    wst = sb['wst']

    @p.o("t")
    def _(t, wlv=wl.v, ad=attn_done):
        t.wait_ge(wl.h, wlv)
        t.wait_ge(ad[0].h, ad[1])
        n = 0
        for tt in range(4):
            for fc in range(8):
                if n >= 2:
                    t.wait_ge(evo.h, b_evo + n - 1)
                b0 = (n % 2) * 1024
                for nt in range(2):
                    ps = psum[:, b0 + nt * 512: b0 + nt * 512 + 512]
                    for k in range(4):
                        mi = t.matmul(ps, wos[:, fc * 512 + k * 128: fc * 512 + k * 128 + 128],
                                      aout[:, k * 4096 + tt * 1024 + nt * 512:
                                           k * 4096 + tt * 1024 + nt * 512 + 512],
                                      start=(k == 0), stop=(k == 3))
                    if nt == 1:
                        mi.then_inc(mmo.h, 1)
                n += 1
    mmo.v += 32

    n = 0
    for tt in range(4):
        for fc in range(8):
            @p.o("v")
            def _(v, n=n, stw=(sto[n % 2], sto[n % 2].v)):
                v.wait_ge(mmo.h, b_mmo + n + 1)
                if n >= 2:
                    v.wait_ge(stw[0].h, stw[1])
                v.tensor_copy(wst[:, (n % 2) * 1024:(n % 2) * 1024 + 1024],
                              psum[:, (n % 2) * 1024:(n % 2) * 1024 + 1024]).then_inc(evo.h, 1)

            @p.o("y")
            def _(y, n=n, tt=tt, fc=fc):
                y.wait_ge(evo.h, b_evo + n + 1)
                blk, tloc = (tt * 1024) // 2048, (tt * 1024) % 2048
                y.dma_start(out=h['rs_in'][blk, fc * 128:(fc + 1) * 128, tloc:tloc + 1024],
                            in_=wst[:, (n % 2) * 1024:(n % 2) * 1024 + 1024]).then_inc(sto[n % 2].h, 16)
            sto[n % 2].v += 16
            n += 1
    evo.v += 32

    ccr = p.sem(tag='co')

    @p.o("g")
    def _(g, a=sto[0].v, b=sto[1].v):
        g.wait_ge(sto[0].h, a)
        g.wait_ge(sto[1].h, b)
        g.collective_compute("ReduceScatter", ALU.add, replica_groups=rg,
                             ins=[h['rs_in'].ap().opt()],
                             outs=[h['rs_out'].ap().opt()]).then_inc(ccr.h, 1)
    ccr.v += 1

    rl = p.sem(tag='hw')
    bov = p.sem()
    wotok = sb['wotok']
    wofm = sb['wofm']

    @p.o("y")
    def _(y, c=ccr.v):
        y.wait_ge(ccr.h, c)
        y.dma_start(out=wofm[:, :].rearrange("pp (fc t) -> pp fc t", t=2048),
                    in_=h['rs_out'].ap().rearrange("(fc pp t) -> pp fc t", fc=8, pp=128)
                    ).then_inc(rl.h, 16)
    rl.v += 16

    @p.o("v")
    def _(v, r=rl.v, c=cl.v):
        v.wait_ge(rl.h, r)
        v.wait_ge(cl.h, c)
        for fc in range(8):
            v.tensor_scalar(wofm[:, fc * 2048:(fc + 1) * 2048], wofm[:, fc * 2048:(fc + 1) * 2048],
                            sb['bo_c'][:, fc:fc + 1], None, ALU.add).then_inc(bov.h, 1)
    bov.v += 8

    ev_tr = p.sem()
    transpose_tiles(
        p,
        src=lambda i: wofm[:, (i % 8) * 2048 + (i // 8) * 128: (i % 8) * 2048 + (i // 8) * 128 + 128],
        dst=lambda i: wotok[:, (i // 8) * 1024 + (i % 8) * 128: (i // 8) * 1024 + (i % 8) * 128 + 128],
        n=128, idn=sb['idn'], psum_bf=psum_bf, slots=TRAN_SLOTS,
        waits=[(bov, bov.v)], ev=ev_tr)

    u = p.sem()
    ln_done = p.sem()
    sxs = [p.sem(tag='hw'), p.sem(tag='hw')]
    sos = [p.sem(tag='hw'), p.sem(tag='hw')]
    tokb = h.get('x_tokb')
    bfb = sb['aout'][:, 0:4096]
    sob = [p.sem(tag='hw') for _ in range(4)]
    b_sob = [s.v for s in sob]
    ln_marks = []
    for tc in range(16):
        sx = sxs[tc % 2]

        @p.o("y")
        def _(y, tc=tc, need=(ln_marks[tc - 2] if tc >= 2 else 0), sx=sx):
            if tc >= 2:
                y.wait_ge(ln_done.h, need)
            y.dma_start(out=sb['xs_f32'][:, (tc % 2) * 1024:(tc % 2) * 1024 + 1024],
                        in_=h['x_tok'][tc * 128:(tc + 1) * 128, :]).then_inc(sx.h, 16)
        sx.v += 16
        d_before = ln_done.v
        ln_chunk(p,
                 a_ap=sb['xs_f32'][:, (tc % 2) * 1024:(tc % 2) * 1024 + 1024],
                 b_ap=wotok[:, tc * 1024:(tc + 1) * 1024],
                 tmp=sb['ln_tmp'][:, :], trash_bf=sb['ln_tr'][:, :], sc=sb['ln_sc'],
                 n_f=1024,
                 waits=[(sx, sx.v), (ev_tr, ev_tr.v)] +
                       ([(sos[tc % 2], sos[tc % 2].v)] if tc >= 2 else []) +
                       ([(sob[tc % 4], sob[tc % 4].v)]
                        if (tokb is not None and tc >= 4) else []),
                 u=u, done=ln_done,
                 out_f32=sb['x2s'][:, (tc % 2) * 1024:(tc % 2) * 1024 + 1024],
                 out_bf=(bfb[:, (tc % 4) * 1024:(tc % 4) * 1024 + 1024]
                         if tokb is not None else None),
                 absmax=(absmax_col[:, tc:tc + 1] if absmax_col is not None else None))
        so = sos[tc % 2]

        @p.o("y")
        def _(y, tc=tc, w=d_before + 1, so=so):
            y.wait_ge(ln_done.h, w)
            y.dma_start(out=h['x3_tok'][tc * 128:(tc + 1) * 128, :],
                        in_=sb['x2s'][:, (tc % 2) * 1024:(tc % 2) * 1024 + 1024]).then_inc(so.h, 16)
        so.v += 16
        if tokb is not None:
            @p.o("y")
            def _(y, tc=tc, w=d_before + 2, so2=sob[tc % 4]):
                y.wait_ge(ln_done.h, w)
                y.dma_start(out=tokb[tc * 128:(tc + 1) * 128, :],
                            in_=bfb[:, (tc % 4) * 1024:(tc % 4) * 1024 + 1024]
                            ).then_inc(so2.h, 16)
            sob[tc % 4].v += 16
        ln_marks.append(ln_done.v)

    @p.o("y")
    def _(y, w0=sos[0].v, w1=sos[1].v):
        y.wait_ge(sos[0].h, w0)
        y.wait_ge(sos[1].h, w1)
    return ln_done, sos, (sob, b_sob)


def mamba_phase(p, h, sb, psum, psum_bf, rg, g_hook=None):
    nc = p.nc
    xt0 = sb['xt0']
    xbcp = sb['xbcp']
    xtok = sb['xtok']
    ztok = sb['ztok']
    bcf = sb['bcf']
    vfull = sb['vfull']
    dtraw = sb['dtraw']
    dwork = sb['dwork']
    cumf = sb['cumf']
    w16 = sb['w16']
    mbig = sb['mbig']
    m0sb = sb['m0sb']
    btok = sb['btok']
    hst = sb['hst']
    hbb = sb['hbb']
    xtt = sb['xtt']
    szb = sb['szb']
    yt1 = sb['yt1']
    ssq = sb['ssq']
    wdg = sb['wdg']
    dide = sb['dide']
    bmask = sb['bmask']
    bigl = sb['bigl']
    idtl = sb['idtl']
    trilm = sb['trilm']
    onesf = sb['onesf']
    idn = sb['idn']
    iom = sb['iom']
    io_i32 = sb['io_i32']
    BIG = 32768.0

    cz = p.sem()
    cl = p.sem(tag='sw')

    @p.o("g")
    def _(g):
        for nm, d in [('cwx', 'cwx'), ('cwbc', 'cwbc'), ('cbx', 'cbx'), ('cbbc', 'cbbc'),
                      ('dtb', 'dtb'), ('alog', 'alog'), ('drep', 'drep'), ('nwc', 'nwc')]:
            g.dma_start(out=sb[nm][:, 0:h[d].shape[-1]], in_=h[d][:, :]).then_inc(cl.h, 16)
    cl.v += 16 * 8

    b_cz = cz.v

    @p.o("v")
    def _(v, clv=cl.v):
        v.tensor_single_scalar(sb['trf'][:, 0:128], io_i32[:, :], 0, ALU.is_ge).then_inc(cz.h, 1)
        v.wait_ge(cz.h, b_cz + 1)
        v.tensor_copy(trilm[:, :], sb['trf'][:, 0:128]).then_inc(cz.h, 1)
        v.wait_ge(cz.h, b_cz + 2)
        v.tensor_single_scalar(sb['trf'][:, 128:256], io_i32[:, :], 0, ALU.is_le).then_inc(cz.h, 1)
        v.wait_ge(cz.h, b_cz + 3)
        v.tensor_scalar(bigl[:, :], sb['trf'][:, 128:256], BIG, None, ALU.mult).then_inc(cz.h, 1)
        v.wait_ge(cz.h, b_cz + 4)
        v.tensor_copy(idtl[:, :].rearrange("pp (r j) -> pp r j", r=16),
                      idn[:, :].unsqueeze(1).broadcast_to([128, 16, 128])).then_inc(cz.h, 1)
        v.wait_ge(cz.h, b_cz + 5)
        v.wait_ge(cl.h, clv)
        for hh in range(16):
            v.tensor_scalar(dide[:, hh * 128:(hh + 1) * 128], idn[:, :],
                            sb['drep'][:, hh:hh + 1], None, ALU.mult).then_inc(cz.h, 1)
        v.wait_ge(cz.h, b_cz + 21)
        for fc in range(8):
            for tap in range(4):
                v.tensor_scalar(wdg[:, (fc * 4 + tap) * 128:(fc * 4 + tap) * 128 + 128],
                                iom[:, :], sb['cwx'][:, fc * 4 + tap:fc * 4 + tap + 1],
                                None, ALU.mult).then_inc(cz.h, 1)
        for fc in range(2):
            for tap in range(4):
                v.tensor_scalar(wdg[:, (32 + fc * 4 + tap) * 128:(32 + fc * 4 + tap) * 128 + 128],
                                iom[:, :], sb['cwbc'][:, fc * 4 + tap:fc * 4 + tap + 1],
                                None, ALU.mult).then_inc(cz.h, 1)
        v.wait_ge(cz.h, b_cz + 61)
        v.memset(xbcp[:, :].rearrange("pp (fc j) -> pp fc j", j=1027)[:, :, 0:3], 0.0).then_inc(cz.h, 1)
        v.wait_ge(cz.h, b_cz + 62)
        v.memset(hst[:, :], 0.0).then_inc(cz.h, 1)
        v.wait_ge(cz.h, b_cz + 63)
        v.memset(hbb[:, :], 0.0).then_inc(cz.h, 1)
        v.wait_ge(cz.h, b_cz + 64)
        v.memset(sb['dones'][0:16, 0:128], 1.0).then_inc(cz.h, 1)
        v.wait_ge(cz.h, b_cz + 65)
        v.tensor_single_scalar(sb['w16'][:, 80:81], io_i32[:, 0:1], -127,
                               ALU.is_equal).then_inc(cz.h, 1)
        v.wait_ge(cz.h, b_cz + 66)
        v.memset(cumf[0:1, :], 0.0).then_inc(cz.h, 1)
    cz.v += 67

    @p.o("g")
    def _(g, w=cz.v):
        g.wait_ge(cz.h, w)
        g.iota(sb['bio'][0:16, 0:2048], pattern=[[1, 2048]], base=0,
               channel_multiplier=-128).then_inc(cz.h, 1)
    cz.v += 1

    @p.o("v")
    def _(v, w=cz.v):
        v.wait_ge(cz.h, w)
        v.tensor_single_scalar(bmask[0:16, :], sb['bio'][0:16, 0:2048], 0, ALU.is_ge).then_inc(cz.h, 1)
        v.wait_ge(cz.h, w + 1)
        v.tensor_single_scalar(sb['trf'][0:16, 0:2048], sb['bio'][0:16, 0:2048], 128,
                               ALU.is_lt).then_inc(cz.h, 1)
        v.wait_ge(cz.h, w + 2)
        v.tensor_tensor(bmask[0:16, :], bmask[0:16, :], sb['trf'][0:16, 0:2048],
                        ALU.mult).then_inc(cz.h, 1)
        v.wait_ge(cz.h, w + 3)
        v.engine_nop().then_inc(cz.h, 1)
    cz.v += 4

    sA = p.sem()

    @p.o("s")
    def _(s, clv=cl.v):
        s.wait_ge(cl.h, clv)
        s.activation(sb['acol'][0:16, 0:1], sb['alog'][0:16, 0:1], AF.Exp).then_inc(sA.h, 1)
    sA.v += 1

    @p.o("v")
    def _(v, w=sA.v):
        v.wait_ge(sA.h, w)
        v.tensor_scalar(sb['acol'][0:16, 0:1], sb['acol'][0:16, 0:1], -1.0, None,
                        ALU.mult).then_inc(sA.h, 1)
    sA.v += 1
    consts_done = (cz, cz.v)

    wzl = p.sem(tag='sw')

    @p.o("g")
    def _(g):
        for k in range(8):
            src_ = h['wzb'].ap().rearrange("fc (kc pp) j -> kc pp fc j", pp=128)[k]
            g.dma_start(out=sb['wzr'][:, k * 1024:(k + 1) * 1024]
                        .rearrange("pp (fc j) -> pp fc j", j=128), in_=src_).then_inc(wzl.h, 16)
        for i in range(10):
            hsrc = h['wxsb'][i] if i < 8 else h['wbcb'][i - 8]
            dst = sb['wxsbcr'][:, i * 1024:(i + 1) * 1024].rearrange("pp (kc j) -> pp kc j", j=128)
            g.dma_start(out=dst, in_=hsrc.rearrange("(kc pp) j -> pp kc j", pp=128)).then_inc(wzl.h, 16)
        for k in range(8):
            g.dma_start(out=sb['wdtr'][:, k * 16:(k + 1) * 16],
                        in_=h['wdt'][k * 128:(k + 1) * 128, :]).then_inc(wzl.h, 16)
    wzl.v += 16 * 26
    weights_ready = (wzl, wzl.v)

    x0l = p.sem(tag='sw')
    mmw = p.sem()
    evw = p.sem()
    hsm = p.sem()
    mmc = p.sem()
    evc = p.sem()
    tkx = p.sem()
    tex = p.sem()
    dtv = p.sem()
    mmk = p.sem()
    ckv = p.sem()
    cks = p.sem()
    spz = p.sem(tag='hw')
    vsp = p.sem(tag='hw')

    ssq_marks = []
    prev_tile_guard = [consts_done]
    tile_cks = []

    for tt in range(4):
        T0 = tt * 1024

        @p.o("g")
        def _(g, tt=tt, T0=T0, w=mmw.v, wz=weights_ready):
            if tt == 0:
                g.wait_ge(wz[0].h, wz[1])
            else:
                g.wait_ge(mmw.h, w)
            g.dma_start(out=xt0[:, :].rearrange("pp (kc t) -> pp kc t", t=1024),
                        in_=h['x0fm'].ap().rearrange("(kc pp) t -> pp kc t", pp=128)
                        [:, :, 3 + T0: 3 + T0 + 1024]).then_inc(x0l.h, 16)
        x0l.v += 16
        if g_hook is not None:
            g_hook(tt)

        if tt > 0:
            @p.o("v")
            def _(v, w=hsm.v, tt=tt):
                v.wait_ge(hsm.h, w)
                v.tensor_copy(xbcp[:, :].rearrange("pp (fc j) -> pp fc j", j=1027)[:, :, 0:3],
                              sb['hstash'][:, (tt % 2) * 30:(tt % 2) * 30 + 30]
                              .rearrange("pp (fc j) -> pp fc j", j=3)).then_inc(hsm.h, 1)
            hsm.v += 1

        base_mm = mmw.v
        base_ev = evw.v

        @p.o("t")
        def _(t, tt=tt, xv=x0l.v, hv=hsm.v, base_ev=base_ev, guards=list(prev_tile_guard)):
            t.wait_ge(x0l.h, xv)
            t.wait_ge(hsm.h, hv)
            for s_, val in guards:
                t.wait_ge(s_.h, val)
            n = 0
            for fc in range(10):
                if base_ev + n - 1 > 0:
                    t.wait_ge(evw.h, base_ev + n - 1)
                b0 = (n % 2) * 1024
                for nt in range(2):
                    ps = psum[:, b0 + nt * 512: b0 + nt * 512 + 512]
                    for k in range(8):
                        mi = t.matmul(ps, sb['wxsbcr'][:, fc * 1024 + k * 128: fc * 1024 + k * 128 + 128],
                                      xt0[:, k * 1024 + nt * 512: k * 1024 + nt * 512 + 512],
                                      start=(k == 0), stop=(k == 7))
                    if nt == 1:
                        mi.then_inc(mmw.h, 1)
                n += 1
            for tc in range(8):
                t.wait_ge(evw.h, base_ev + n - 1)
                b0 = (n % 2) * 1024
                for nt in range(2):
                    ps = psum[:, b0 + nt * 512: b0 + nt * 512 + 512]
                    for k in range(8):
                        mi = t.matmul(ps, xt0[:, k * 1024 + tc * 128: k * 1024 + tc * 128 + 128],
                                      sb['wzr'][:, k * 1024 + nt * 512: k * 1024 + nt * 512 + 512],
                                      start=(k == 0), stop=(k == 7))
                    if nt == 1:
                        mi.then_inc(mmw.h, 1)
                n += 1
            t.wait_ge(evw.h, base_ev + n - 1)
            b0 = (n % 2) * 1024
            for nt in range(2):
                ps = psum[0:16, b0 + nt * 512: b0 + nt * 512 + 512]
                for k in range(8):
                    mi = t.matmul(ps, sb['wdtr'][:, k * 16:(k + 1) * 16],
                                  xt0[:, k * 1024 + nt * 512: k * 1024 + nt * 512 + 512],
                                  start=(k == 0), stop=(k == 7))
                if nt == 1:
                    mi.then_inc(mmw.h, 1)
        mmw.v += 19

        for n in range(19):
            if n < 10:
                @p.o("v")
                def _(v, n=n, tt=tt, w=base_mm + n + 1, sw=hsm.v):
                    v.wait_ge(mmw.h, w)
                    v.wait_ge(hsm.h, sw)
                    v.tensor_copy(xbcp[:, n * 1027 + 3: n * 1027 + 3 + 1024],
                                  psum[:, (n % 2) * 1024:(n % 2) * 1024 + 1024]).then_inc(evw.h, 1)

                @p.o("v")
                def _(v, n=n, tt=tt, w=base_ev + n + 1):
                    v.wait_ge(evw.h, w)
                    v.tensor_copy(sb['hstash'][:, ((tt + 1) % 2) * 30 + n * 3: ((tt + 1) % 2) * 30 + n * 3 + 3],
                                  xbcp[:, n * 1027 + 1024: n * 1027 + 1027]).then_inc(hsm.h, 1)
            elif n < 18:
                @p.o("v")
                def _(v, n=n, w=base_mm + n + 1):
                    v.wait_ge(mmw.h, w)
                    v.tensor_copy(ztok[:, (n - 10) * 1024:(n - 9) * 1024],
                                  psum[:, (n % 2) * 1024:(n % 2) * 1024 + 1024]).then_inc(evw.h, 1)
            else:
                @p.o("v")
                def _(v, n=n, tt=tt, w=base_mm + n + 1,
                      ckw=(tile_cks[tt - 2] if tt >= 2 else 0)):
                    v.wait_ge(mmw.h, w)
                    if tt >= 2:
                        v.wait_ge(cks.h, ckw)
                    v.tensor_copy(dtraw[0:16, (tt % 2) * 1024:(tt % 2) * 1024 + 1024],
                                  psum[0:16, (n % 2) * 1024:(n % 2) * 1024 + 1024]).then_inc(evw.h, 1)
        evw.v += 19
        hsm.v += 10

        base_cm = mmc.v
        base_ce = evc.v

        @p.o("t")
        def _(t, tt=tt, base_ev=base_ev, base_ce=base_ce, sw=hsm.v):
            for fc in range(10):
                t.wait_ge(evw.h, base_ev + fc + 1)
                t.wait_ge(hsm.h, sw)
                if base_ce + fc - 1 > 0:
                    t.wait_ge(evc.h, base_ce + fc - 1)
                b0 = 4 * 512 + (fc % 2) * 1024
                for nt in range(2):
                    ps = psum[:, b0 + nt * 512: b0 + nt * 512 + 512]
                    for tap in range(4):
                        mi = t.matmul(ps, wdg[:, (fc * 4 + tap) * 128:(fc * 4 + tap) * 128 + 128],
                                      xbcp[:, fc * 1027 + nt * 512 + tap: fc * 1027 + nt * 512 + tap + 512],
                                      start=(tap == 0), stop=(tap == 3))
                    if nt == 1:
                        mi.then_inc(mmc.h, 1)
        mmc.v += 10

        for fc in range(10):
            @p.o("s")
            def _(s, fc=fc, tt=tt, w=base_cm + fc + 1, clv=cl.v):
                s.wait_ge(mmc.h, w)
                s.wait_ge(cl.h, clv)
                b0 = 4 * 512 + (fc % 2) * 1024
                bias = (sb['cbx'][:, fc:fc + 1] if fc < 8 else sb['cbbc'][:, fc - 8:fc - 7])
                s.activation(xbcp[:, fc * 1027 + 3: fc * 1027 + 3 + 1024] if fc < 8 else
                             bcf[:, (fc - 8) * 1024: (fc - 8) * 1024 + 1024],
                             psum[:, b0: b0 + 1024], AF.Silu, bias=bias).then_inc(evc.h, 1)
        evc.v += 10

        base_tk = tkx.v
        base_te = tex.v

        @p.o("t")
        def _(t, tt=tt, w=base_ev + 19, base_ce=base_ce, base_te=base_te):
            t.wait_ge(evw.h, w)
            for i in range(64):
                fc, tc = i // 8, i % 8
                t.wait_ge(evc.h, base_ce + 10)
                if i >= 4:
                    t.wait_ge(tex.h, base_te + i - 3)
                ps = psum_bf[:, TRAN_SLOTS[i % 4]: TRAN_SLOTS[i % 4] + 128]
                t.transpose(ps, xbcp[:, fc * 1027 + 3 + tc * 128: fc * 1027 + 3 + tc * 128 + 128],
                            idn[:, :]).then_inc(tkx.h, 1)
        tkx.v += 64

        for i in range(64):
            fc, tc = i // 8, i % 8

            @p.o("v")
            def _(v, i=i, fc=fc, tc=tc, w=base_tk + i + 1):
                v.wait_ge(tkx.h, w)
                ps = psum_bf[:, TRAN_SLOTS[i % 4]: TRAN_SLOTS[i % 4] + 128]
                v.tensor_copy(xtok[:, tc * 1024 + fc * 128: tc * 1024 + fc * 128 + 128],
                              ps).then_inc(tex.h, 1)
        tex.v += 64
        xs_t_done = (tex, tex.v)

        for c8 in range(8):
            c = tt * 8 + c8
            _mamba_chunk(p, h, sb, psum, psum_bf, tt, c8, c,
                         mmk, ckv, cks, spz, dtv, xs_t_done,
                         (evc, base_ce + 10), (evw, base_ev + 19), BIG)

        tile_cks.append(cks.v)

        @p.o("y")
        def _(y, tt=tt, w=ckv.v):
            y.wait_ge(ckv.h, w)
            y.dma_start(out=h['vsp'][tt], in_=vfull[:, :]).then_inc(vsp.h, 16)
        vsp.v += 16
        ssq_marks.append(ckv.v)

        prev_tile_guard = [(tex, tex.v), (evc, evc.v), (ckv, ckv.v), (vsp, vsp.v)]

    chunks_done = (ckv, ckv.v)
    return dict(chunks_done=chunks_done, ssq_marks=ssq_marks, vsp=vsp, spz=spz,
                ckv=ckv, cks=cks, mmk=mmk)


def _mamba_chunk(p, h, sb, psum, psum_bf, tt, c8, c, mmk, ckv, cks, spz, dtv,
                 xs_t_done, conv_done, win_done, BIG):
    dwork = sb['dwork']
    dtraw = sb['dtraw']
    w16 = sb['w16']
    wbt = sb['wbt']
    mbig = sb['mbig']
    m0sb = sb['m0sb']
    btok = sb['btok']
    xtt = sb['xtt']
    szb = sb['szb']
    yt1 = sb['yt1']
    vtrash = sb['vtrash']
    bcf = sb['bcf']
    xtok = sb['xtok']
    ztok = sb['ztok']
    hst = sb['hst']
    hbb = sb['hbb']
    vfull = sb['vfull']
    ssq = sb['ssq']
    cumf = sb['cumf']
    cum2048 = sb['cum2048']
    onesf = sb['onesf']
    idn = sb['idn']
    sl = 0

    V0 = ckv.v
    S0 = cks.v
    M0v = mmk.v
    D0 = dtv.v
    Z0 = spz.v

    @p.o("s")
    def _(s, c=c, S0=S0, V0=V0, wd=win_done):
        if c > 0:
            s.wait_ge(mmk.h, 10 * c - 3)
        s.wait_ge(wd[0].h, wd[1])
        s.activation(dwork[0:16, 0:128],
                     dtraw[0:16, (tt % 2) * 1024 + c8 * 128:(tt % 2) * 1024 + c8 * 128 + 128],
                     AF.Exp, bias=sb['dtb'][0:16, 0:1]).then_inc(cks.h, 1)
    cks.v += 1

    @p.o("v")
    def _(v, S0=S0):
        v.wait_ge(cks.h, S0 + 1)
        v.tensor_scalar(dwork[0:16, 0:128], dwork[0:16, 0:128], 1.0, None,
                        ALU.add).then_inc(cks.h, 1)
    cks.v += 1

    @p.o("s")
    def _(s, S0=S0):
        s.wait_ge(cks.h, S0 + 2)
        s.activation(dwork[0:16, 0:128], dwork[0:16, 0:128], AF.Ln).then_inc(cks.h, 1)
        s.wait_ge(cks.h, S0 + 3)
        s.activation(dwork[0:16, 128:256], dwork[0:16, 0:128], AF.Ln).then_inc(cks.h, 1)
    cks.v += 2

    @p.o("v")
    def _(v, S0=S0, V0=V0, sa=None):
        v.wait_ge(cks.h, S0 + 4)
        v.tensor_scalar(dwork[0:16, 128:256], dwork[0:16, 128:256], -60.0, None,
                        ALU.max).then_inc(ckv.h, 1)
        v.wait_ge(ckv.h, V0 + 1)
        v.tensor_scalar(dwork[0:16, 256:384], dwork[0:16, 0:128],
                        sb['acol'][0:16, 0:1], None, ALU.mult).then_inc(ckv.h, 1)
        v.wait_ge(ckv.h, V0 + 2)
        v.tensor_tensor_scan(dwork[0:16, 384:512], sb['dones'][0:16, 0:128],
                             dwork[0:16, 256:384], 0.0, ALU.mult, ALU.add).then_inc(ckv.h, 1)
        v.wait_ge(ckv.h, V0 + 3)
        v.tensor_sub(dwork[0:16, 512:640], dwork[0:16, 128:256],
                     dwork[0:16, 384:512]).then_inc(ckv.h, 1)
        v.wait_ge(ckv.h, V0 + 4)
        v.tensor_scalar(dwork[0:16, 640:768], dwork[0:16, 512:640], -BIG, None,
                        ALU.add).then_inc(ckv.h, 1)
    ckv.v += 5

    @p.o("t")
    def _(t, V0=V0, M0v=M0v, xtd=xs_t_done, c=c):
        t.wait_ge(ckv.h, V0 + 5)
        if c > 0:
            t.wait_ge(ckv.h, V0 - 17)
        t.transpose(psum[:, 1024:1040], dwork[0:16, 384:512], sb['iom'][0:16, 0:16]).then_inc(mmk.h, 1)
        t.transpose(psum[:, 1040:1056], dwork[0:16, 512:640], sb['iom'][0:16, 0:16]).then_inc(mmk.h, 1)
    mmk.v += 2

    @p.o("v")
    def _(v, M0v=M0v, V0=V0, S0=S0, c=c):
        v.wait_ge(mmk.h, M0v + 2)
        if c > 0:
            v.wait_ge(cks.h, S0 - 4)
        v.tensor_copy(w16[:, 0:16], psum[:, 1024:1040]).then_inc(ckv.h, 1)
        v.tensor_copy(w16[:, 16:32], psum[:, 1040:1056]).then_inc(ckv.h, 1)
    ckv.v += 2

    @p.o("t")
    def _(t, V0=V0):
        t.wait_ge(ckv.h, V0 + 6)
        t.matmul(psum[0:1, 1552:1568], w16[:, 80:81], w16[:, 0:16],
                 start=True, stop=True).then_inc(mmk.h, 1)
    mmk.v += 1

    @p.o("v")
    def _(v, M0v=M0v, V0=V0):
        v.tensor_tensor(cum2048[0:16, :].rearrange("p (hh t) -> p hh t", t=128),
                        sb['bmask'][0:16, :].rearrange("p (hh t) -> p hh t", t=128),
                        dwork[0:16, 384:512].unsqueeze(1).broadcast_to([16, 16, 128]),
                        ALU.mult).then_inc(ckv.h, 1)
        v.wait_ge(mmk.h, M0v + 3)
        v.tensor_copy(cumf[0:1, 0:16], psum[0:1, 1552:1568]).then_inc(ckv.h, 1)
    ckv.v += 2

    @p.o("t")
    def _(t, V0=V0):
        t.wait_ge(ckv.h, V0 + 9)
        t.matmul(psum[:, 1536:1552], onesf[0:1, 0:128], cumf[0:1, 0:16],
                 start=True, stop=True).then_inc(mmk.h, 1)
    mmk.v += 1

    @p.o("v")
    def _(v, M0v=M0v, V0=V0):
        v.wait_ge(mmk.h, M0v + 4)
        v.tensor_copy(w16[:, 32:48], psum[:, 1536:1552]).then_inc(ckv.h, 1)
        v.wait_ge(ckv.h, V0 + 10)
        v.tensor_add(w16[:, 64:80], w16[:, 16:32], w16[:, 32:48]).then_inc(ckv.h, 1)
    ckv.v += 2

    @p.o("s")
    def _(s, V0=V0, S0=S0):
        s.wait_ge(ckv.h, V0 + 11)
        s.activation(wbt[:, 0:16], w16[:, 0:16], AF.Exp).then_inc(cks.h, 1)
        s.activation(w16[:, 48:64], w16[:, 32:48], AF.Exp).then_inc(cks.h, 1)
        s.activation(wbt[:, 16:32], w16[:, 64:80], AF.Exp).then_inc(cks.h, 1)
    cks.v += 3

    @p.o("t")
    def _(t, c=c, M0v=M0v, V0=V0, cd=conv_done):
        t.wait_ge(cd[0].h, cd[1])
        if c > 0:
            t.wait_ge(ckv.h, V0)
        t.matmul(psum[:, 0:128], bcf[:, (c % 8) * 128:((c % 8) + 1) * 128],
                 bcf[:, 1024 + (c % 8) * 128: 1024 + (c % 8) * 128 + 128],
                 start=True, stop=True).then_inc(mmk.h, 1)
        t.transpose(psum_bf[:, 2 * 1024 + 512: 2 * 1024 + 512 + 128],
                    bcf[:, (c % 8) * 128:((c % 8) + 1) * 128], idn[:, :]).then_inc(mmk.h, 1)
    mmk.v += 2

    @p.o("v")
    def _(v, M0v=M0v, V0=V0, sl=sl):
        v.wait_ge(mmk.h, M0v + 5)
        v.tensor_tensor(m0sb[:, sl * 128:(sl + 1) * 128], psum[:, 0:128],
                        sb['trilm'][:, :], ALU.mult).then_inc(ckv.h, 1)
        v.wait_ge(mmk.h, M0v + 6)
        v.tensor_copy(btok[:, sl * 128:(sl + 1) * 128],
                      psum_bf[:, 2 * 1024 + 512: 2 * 1024 + 512 + 128]).then_inc(ckv.h, 1)
    ckv.v += 2

    @p.o("t")
    def _(t, V0=V0, M0v=M0v):
        t.wait_ge(ckv.h, V0 + 13)
        for nt in range(4):
            t.matmul(psum[:, nt * 512:(nt + 1) * 512], sb['dones'][0:16, 0:128],
                     cum2048[0:16, nt * 512:(nt + 1) * 512], start=True, stop=False,
                     skip_group_check=True)
            t.matmul(psum[:, nt * 512:(nt + 1) * 512], dwork[0:16, 640:768],
                     sb['bmask'][0:16, nt * 512:(nt + 1) * 512], start=False, stop=False,
                     skip_group_check=True)
            mi = t.matmul(psum[:, nt * 512:(nt + 1) * 512], sb['bigl'][:, :],
                          sb['idtl'][:, nt * 512:(nt + 1) * 512], start=False, stop=True,
                          skip_group_check=True)
            if nt == 3:
                mi.then_inc(mmk.h, 1)
    mmk.v += 1

    @p.o("s")
    def _(s, M0v=M0v, S0=S0, sl=sl, c=c, V0=V0):
        s.wait_ge(mmk.h, M0v + 7)
        if c > 0:
            s.wait_ge(mmk.h, 10 * c - 1)
        s.activation(mbig[:, sl * 2048:(sl + 1) * 2048], psum[:, 0:2048], AF.Exp).then_inc(cks.h, 1)
    cks.v += 1

    @p.o("v")
    def _(v, S0=S0, V0=V0, sl=sl, c=c):
        v.wait_ge(cks.h, S0 + 8)
        v.wait_ge(ckv.h, V0 + 12)
        v.tensor_tensor(mbig[:, sl * 2048:(sl + 1) * 2048].rearrange("pp (hh j) -> pp hh j", j=128),
                        mbig[:, sl * 2048:(sl + 1) * 2048].rearrange("pp (hh j) -> pp hh j", j=128),
                        m0sb[:, sl * 128:(sl + 1) * 128].unsqueeze(1).broadcast_to([128, 16, 128]),
                        ALU.mult).then_inc(ckv.h, 1)
        v.wait_ge(ckv.h, V0 + 14)
        v.tensor_tensor(mbig[:, sl * 2048:(sl + 1) * 2048], mbig[:, sl * 2048:(sl + 1) * 2048],
                        sb['dide'][:, :], ALU.add).then_inc(ckv.h, 1)
        v.wait_ge(cks.h, S0 + 7)
        if c > 0:
            v.wait_ge(mmk.h, 10 * c)
        v.tensor_tensor(xtt[:, sl * 1024:(sl + 1) * 1024].rearrange("pp (hh d) -> pp hh d", d=64),
                        xtok[:, (c % 8) * 1024:((c % 8) + 1) * 1024].rearrange("pp (hh d) -> pp hh d", d=64),
                        wbt[:, 16:32].unsqueeze(2).broadcast_to([128, 16, 64]),
                        ALU.mult).then_inc(ckv.h, 1)
    ckv.v += 3

    @p.o("s")
    def _(s, S0=S0, c=c, V0=V0, sl=sl, wd=win_done):
        if c > 0:
            s.wait_ge(ckv.h, V0)
        s.wait_ge(wd[0].h, wd[1])
        s.activation(szb[:, sl * 1024:(sl + 1) * 1024],
                     ztok[:, (c % 8) * 1024:((c % 8) + 1) * 1024], AF.Silu).then_inc(cks.h, 1)
    cks.v += 1

    @p.o("t")
    def _(t, V0=V0, M0v=M0v, sl=sl, c=c, xtd=xs_t_done):
        t.wait_ge(xtd[0].h, xtd[1])
        t.wait_ge(ckv.h, V0 + 13)
        for nt in range(2):
            mi = t.matmul(psum[:, (6 + nt) * 512:(7 + nt) * 512],
                          bcf[:, 1024 + (c % 8) * 128: 1024 + (c % 8) * 128 + 128],
                          hbb[:, nt * 512:(nt + 1) * 512], start=True, stop=True,
                          skip_group_check=True)
        mi.then_inc(mmk.h, 1)
        t.wait_ge(ckv.h, V0 + 15)
        for hh in range(16):
            mi = t.matmul(psum[:, 2048 + hh * 64: 2048 + hh * 64 + 64],
                          mbig[:, sl * 2048 + hh * 128: sl * 2048 + hh * 128 + 128],
                          xtok[:, (c % 8) * 1024 + hh * 64:(c % 8) * 1024 + hh * 64 + 64],
                          start=(hh == 0 or hh == 8), stop=(hh == 7 or hh == 15),
                          skip_group_check=True)
        mi.then_inc(mmk.h, 1)
        t.wait_ge(ckv.h, V0 + 16)
        for nt in range(2):
            mi = t.matmul(psum[:, nt * 512:(nt + 1) * 512], btok[:, sl * 128:(sl + 1) * 128],
                          xtt[:, sl * 1024 + nt * 512: sl * 1024 + nt * 512 + 512],
                          start=True, stop=True, skip_group_check=True)
        mi.then_inc(mmk.h, 1)
    mmk.v += 3

    @p.o("v")
    def _(v, V0=V0, M0v=M0v, S0=S0, sl=sl, c=c, c8=c8):
        v.wait_ge(mmk.h, M0v + 9)
        v.wait_ge(cks.h, S0 + 5)
        v.tensor_tensor(yt1[:, :].rearrange("pp (hh d) -> pp hh d", d=64),
                        psum[:, 3072:4096].rearrange("pp (hh d) -> pp hh d", d=64),
                        wbt[:, 0:16].unsqueeze(2).broadcast_to([128, 16, 64]),
                        ALU.mult).then_inc(ckv.h, 1)
        v.wait_ge(ckv.h, V0 + 17)
        v.tensor_tensor(yt1[:, :], yt1[:, :], psum[:, 2048:3072], ALU.add).then_inc(ckv.h, 1)
        v.wait_ge(ckv.h, V0 + 18)
        v.wait_ge(cks.h, S0 + 9)
        v.tensor_tensor(vfull[:, c8 * 1024:(c8 + 1) * 1024], yt1[:, :],
                        szb[:, sl * 1024:(sl + 1) * 1024], ALU.mult).then_inc(ckv.h, 1)
        v.wait_ge(ckv.h, V0 + 19)
        v.tensor_tensor(sb['vtrash'][:, :], vfull[:, c8 * 1024:(c8 + 1) * 1024],
                        vfull[:, c8 * 1024:(c8 + 1) * 1024], ALU.mult).then_inc(ckv.h, 1)
        v.wait_ge(ckv.h, V0 + 20)
        v.reduce_sum(sb['ssq'][:, c:c + 1], sb['vtrash'][:, :], axis=AX.X).then_inc(ckv.h, 1)
        v.wait_ge(cks.h, S0 + 6)
        v.tensor_tensor(yt1[:, :].rearrange("pp (hh d) -> pp hh d", d=64),
                        hst[:, :].rearrange("pp (hh d) -> pp hh d", d=64),
                        w16[:, 48:64].unsqueeze(2).broadcast_to([128, 16, 64]),
                        ALU.mult).then_inc(ckv.h, 1)
        v.wait_ge(ckv.h, V0 + 22)
        v.wait_ge(mmk.h, M0v + 10)
        v.tensor_tensor(hst[:, :], yt1[:, :], psum[:, 0:1024], ALU.add).then_inc(ckv.h, 1)
        v.wait_ge(ckv.h, V0 + 23)
        v.tensor_copy(hbb[:, :], hst[:, :]).then_inc(ckv.h, 1)
    ckv.v += 8


def mamba_tail(p, h, sb, psum, psum_bf, rg, mres):
    ckv = mres['ckv']
    vsp = mres['vsp']
    ssq = sb['ssq']
    vfull = sb['vfull']
    idn = sb['idn']
    onesf = sb['onesf']

    ag = p.sem(tag='sw')
    cc = p.sem(tag='co')
    b_ag, b_cc = ag.v, cc.v

    @p.o("g")
    def _(g, w=mres['chunks_done']):
        g.wait_ge(w[0].h, w[1])
        g.dma_start(out=h['ag2_in'].ap().rearrange("(c s) -> s c", s=128),
                    in_=ssq[:, 0:32], allow_slow_non_contiguous=True).then_inc(ag.h, 16)
        g.wait_ge(ag.h, b_ag + 16)
        g.collective_compute("AllGather", ALU.bypass, replica_groups=rg,
                             ins=[h['ag2_in'].ap().opt()],
                             outs=[h['ag2_out'].ap().opt()]).then_inc(cc.h, 1)
        g.wait_ge(cc.h, b_cc + 1)
        g.dma_start(out=ssq[:, 32:64], in_=h['ag2_out'].ap()[0].rearrange("(c s) -> s c", s=128),
                    allow_slow_non_contiguous=True).then_inc(ag.h, 16)
        g.dma_start(out=ssq[:, 64:96], in_=h['ag2_out'].ap()[1].rearrange("(c s) -> s c", s=128),
                    allow_slow_non_contiguous=True).then_inc(ag.h, 16)
    ag.v += 48
    cc.v += 1

    rq = p.sem()
    b_rq = rq.v

    @p.o("v")
    def _(v, w=ag.v):
        v.wait_ge(ag.h, w)
        v.tensor_add(ssq[:, 0:32], ssq[:, 32:64], ssq[:, 64:96]).then_inc(rq.h, 1)
        v.wait_ge(rq.h, b_rq + 1)
        v.tensor_scalar(ssq[:, 0:32], ssq[:, 0:32], 1.0 / 2048.0, 1e-5, ALU.mult,
                        ALU.add).then_inc(rq.h, 1)
    rq.v += 2

    @p.o("s")
    def _(s, w=rq.v):
        s.wait_ge(rq.h, w)
        s.activation(ssq[:, 32:64], ssq[:, 0:32], AF.Sqrt).then_inc(rq.h, 1)
    rq.v += 1

    @p.o("v")
    def _(v, w=rq.v):
        v.wait_ge(rq.h, w)
        v.reciprocal(ssq[:, 32:64], ssq[:, 32:64]).then_inc(rq.h, 1)
    rq.v += 1
    rsq_ready = (rq, rq.v)

    wol = p.sem(tag='sw')

    @p.o("g")
    def _(g):
        for fc in range(8):
            dst = sb['wob'][:, fc * 1024:(fc + 1) * 1024].rearrange("pp (kc j) -> pp kc j", j=128)
            g.dma_start(out=dst, in_=h['woutb'][fc].rearrange("(kc pp) j -> pp kc j", pp=128)).then_inc(wol.h, 16)
    wol.v += 16 * 8

    vrl = p.sem(tag='hw')
    vnv = p.sem()
    tk2 = p.sem()
    te2 = p.sem()
    mm2 = p.sem()
    ev2 = p.sem()
    st2 = [p.sem(tag='hw'), p.sem(tag='hw')]

    for tt in range(4):
        @p.o("y")
        def _(y, tt=tt, w=vsp.v, w2=(te2, te2.v)):
            y.wait_ge(vsp.h, w)
            if tt:
                y.wait_ge(w2[0].h, w2[1])
            y.dma_start(out=vfull[:, :], in_=h['vsp'][tt]).then_inc(vrl.h, 16)
        vrl.v += 16

        @p.o("v")
        def _(v, tt=tt, w=vrl.v, rv=rsq_ready):
            v.wait_ge(vrl.h, w)
            v.wait_ge(rv[0].h, rv[1])
            for c8 in range(8):
                v.tensor_scalar(vfull[:, c8 * 1024:(c8 + 1) * 1024],
                                vfull[:, c8 * 1024:(c8 + 1) * 1024],
                                ssq[:, 32 + tt * 8 + c8: 33 + tt * 8 + c8], None,
                                ALU.mult).then_inc(vnv.h, 1)
        vnv.v += 8

        base_tk = tk2.v
        base_te = te2.v

        @p.o("t")
        def _(t, tt=tt, w=vnv.v, base_te=base_te):
            t.wait_ge(vnv.h, w)
            for i in range(64):
                c8, fc = i // 8, i % 8
                if i >= 4:
                    t.wait_ge(te2.h, base_te + i - 3)
                ps = psum_bf[:, TRAN_SLOTS[i % 4]: TRAN_SLOTS[i % 4] + 128]
                t.transpose(ps, vfull[:, c8 * 1024 + fc * 128: c8 * 1024 + fc * 128 + 128],
                            idn[:, :]).then_inc(tk2.h, 1)
        tk2.v += 64

        for i in range(64):
            c8, fc = i // 8, i % 8

            @p.o("v")
            def _(v, i=i, c8=c8, fc=fc, w=base_tk + i + 1, clv=None):
                v.wait_ge(tk2.h, w)
                ps = psum_bf[:, TRAN_SLOTS[i % 4]: TRAN_SLOTS[i % 4] + 128]
                v.tensor_scalar(sb['vfm'][:, fc * 1024 + c8 * 128: fc * 1024 + c8 * 128 + 128],
                                ps, sb['nwc'][:, fc:fc + 1], None, ALU.mult).then_inc(te2.h, 1)
        te2.v += 64

        base_mm = mm2.v
        base_ev = ev2.v

        @p.o("t")
        def _(t, tt=tt, w=te2.v, wv=wol.v, base_ev=base_ev):
            t.wait_ge(te2.h, w)
            t.wait_ge(wol.h, wv)
            for fc in range(8):
                if base_ev + fc - 1 > 0:
                    t.wait_ge(ev2.h, base_ev + fc - 1)
                b0 = (fc % 2) * 1024
                for nt in range(2):
                    ps = psum[:, b0 + nt * 512: b0 + nt * 512 + 512]
                    for k in range(8):
                        mi = t.matmul(ps, sb['wob'][:, fc * 1024 + k * 128: fc * 1024 + k * 128 + 128],
                                      sb['vfm'][:, k * 1024 + nt * 512: k * 1024 + nt * 512 + 512],
                                      start=(k == 0), stop=(k == 7))
                    if nt == 1:
                        mi.then_inc(mm2.h, 1)
        mm2.v += 8

        for fc in range(8):
            @p.o("v")
            def _(v, fc=fc, w=base_mm + fc + 1, sw=(st2[fc % 2], st2[fc % 2].v)):
                v.wait_ge(mm2.h, w)
                v.wait_ge(sw[0].h, sw[1])
                v.tensor_copy(sb['wst2'][:, (fc % 2) * 1024:(fc % 2) * 1024 + 1024],
                              psum[:, (fc % 2) * 1024:(fc % 2) * 1024 + 1024]).then_inc(ev2.h, 1)

            @p.o("y")
            def _(y, fc=fc, tt=tt, w=base_ev + fc + 1):
                y.wait_ge(ev2.h, w)
                blk, tloc = (tt * 1024) // 2048, (tt * 1024) % 2048
                y.dma_start(out=h['rs2_in'][blk, fc * 128:(fc + 1) * 128, tloc:tloc + 1024],
                            in_=sb['wst2'][:, (fc % 2) * 1024:(fc % 2) * 1024 + 1024]).then_inc(st2[fc % 2].h, 16)
            st2[fc % 2].v += 16
        ev2.v += 8

    ccr = p.sem(tag='co')

    @p.o("g")
    def _(g, a=st2[0].v, b=st2[1].v):
        g.wait_ge(st2[0].h, a)
        g.wait_ge(st2[1].h, b)
        g.collective_compute("ReduceScatter", ALU.add, replica_groups=rg,
                             ins=[h['rs2_in'].ap().opt()],
                             outs=[h['rs2_out'].ap().opt()]).then_inc(ccr.h, 1)
    ccr.v += 1

    rl = p.sem(tag='hw')

    @p.o("y")
    def _(y, c=ccr.v):
        y.wait_ge(ccr.h, c)
        y.dma_start(out=sb['ofm'][:, :].rearrange("pp (fc t) -> pp fc t", t=2048),
                    in_=h['rs2_out'].ap().rearrange("(fc pp t) -> pp fc t", fc=8, pp=128)
                    ).then_inc(rl.h, 16)
    rl.v += 16

    ev_tr = p.sem()
    transpose_tiles(
        p,
        src=lambda i: sb['ofm'][:, (i % 8) * 2048 + (i // 8) * 128: (i % 8) * 2048 + (i // 8) * 128 + 128],
        dst=lambda i: sb['outok'][:, (i // 8) * 1024 + (i % 8) * 128: (i // 8) * 1024 + (i % 8) * 128 + 128],
        n=128, idn=idn, psum_bf=psum_bf, slots=TRAN_SLOTS,
        waits=[(rl, rl.v)], ev=ev_tr)

    u = p.sem()
    ln_done = p.sem()
    rms2 = p.sem()
    sxs = [p.sem(tag='hw'), p.sem(tag='hw')]
    sos = [p.sem(tag='hw'), p.sem(tag='hw')]
    tokb = h.get('x_tokb')
    bfb = sb['ofm'][:, 0:4096]
    sob = [p.sem(tag='hw') for _ in range(4)]
    b_sob = [s.v for s in sob]
    ln_marks = []
    for tc in range(16):
        sx = sxs[tc % 2]

        @p.o("y")
        def _(y, tc=tc, need=(ln_marks[tc - 2] if tc >= 2 else 0), sx=sx):
            if tc >= 2:
                y.wait_ge(ln_done.h, need)
            y.dma_start(out=sb['xs_f32'][:, (tc % 2) * 1024:(tc % 2) * 1024 + 1024],
                        in_=h['x0tokh'][tc * 128:(tc + 1) * 128, :]).then_inc(sx.h, 16)
        sx.v += 16

        R0 = rms2.v

        @p.o("v")
        def _(v, tc=tc, w=ev_tr.v, R0=R0, sxv=sx.v, sx=sx):
            v.wait_ge(ev_tr.h, w)
            v.wait_ge(sx.h, sxv)
            ot = sb['outok'][:, tc * 1024:(tc + 1) * 1024]
            v.tensor_tensor(sb['vtrash'][:, :], ot, ot, ALU.mult).then_inc(rms2.h, 1)
            v.wait_ge(rms2.h, R0 + 1)
            v.reduce_sum(sb['ln_sc'][:, 4:5], sb['vtrash'][:, :], axis=AX.X).then_inc(rms2.h, 1)
            v.wait_ge(rms2.h, R0 + 2)
            v.tensor_scalar(sb['ln_sc'][:, 4:5], sb['ln_sc'][:, 4:5], 1.0 / 1024.0, 1e-5,
                            ALU.mult, ALU.add).then_inc(rms2.h, 1)
        rms2.v += 3

        @p.o("s")
        def _(s, R0=R0):
            s.wait_ge(rms2.h, R0 + 3)
            s.activation(sb['ln_sc'][:, 5:6], sb['ln_sc'][:, 4:5], AF.Sqrt).then_inc(rms2.h, 1)
        rms2.v += 1

        @p.o("v")
        def _(v, tc=tc, R0=R0, uv=u.v):
            v.wait_ge(rms2.h, R0 + 4)
            v.reciprocal(sb['ln_sc'][:, 5:6], sb['ln_sc'][:, 5:6]).then_inc(rms2.h, 1)
            v.wait_ge(rms2.h, R0 + 5)
            v.wait_ge(u.h, uv)
            ot = sb['outok'][:, tc * 1024:(tc + 1) * 1024]
            v.scalar_tensor_tensor(sb['ln_tmp'][:, :], ot, sb['ln_sc'][:, 5:6],
                                   sb['xs_f32'][:, (tc % 2) * 1024:(tc % 2) * 1024 + 1024],
                                   ALU.mult, ALU.add).then_inc(rms2.h, 1)
        rms2.v += 2

        d_before = ln_done.v
        ln_chunk(p,
                 a_ap=sb['ln_tmp'][:, :],
                 b_ap=sb['xs_f32'][:, (tc % 2) * 1024:(tc % 2) * 1024 + 1024],
                 tmp=sb['ln_tmp2'][:, :], trash_bf=sb['ln_tr'][:, :], sc=sb['ln_sc'],
                 n_f=1024,
                 waits=[(rms2, rms2.v)] +
                       ([(sos[tc % 2], sos[tc % 2].v)] if tc >= 2 else []) +
                       ([(sob[tc % 4], sob[tc % 4].v)]
                        if (tokb is not None and tc >= 4) else []),
                 u=u, done=ln_done,
                 out_f32=sb['x2s'][:, (tc % 2) * 1024:(tc % 2) * 1024 + 1024],
                 out_bf=(bfb[:, (tc % 4) * 1024:(tc % 4) * 1024 + 1024]
                         if tokb is not None else None))
        so = sos[tc % 2]

        @p.o("y")
        def _(y, tc=tc, w=d_before + 1, so=so):
            y.wait_ge(ln_done.h, w)
            y.dma_start(out=h['x1_tok'][tc * 128:(tc + 1) * 128, :],
                        in_=sb['x2s'][:, (tc % 2) * 1024:(tc % 2) * 1024 + 1024]).then_inc(so.h, 16)
        so.v += 16
        if tokb is not None:
            @p.o("y")
            def _(y, tc=tc, w=d_before + 2, so2=sob[tc % 4]):
                y.wait_ge(ln_done.h, w)
                y.dma_start(out=tokb[tc * 128:(tc + 1) * 128, :],
                            in_=bfb[:, (tc % 4) * 1024:(tc % 4) * 1024 + 1024]
                            ).then_inc(so2.h, 16)
            sob[tc % 4].v += 16
        ln_marks.append(ln_done.v)

    @p.o("y")
    def _(y, w0=sos[0].v, w1=sos[1].v):
        y.wait_ge(sos[0].h, w0)
        y.wait_ge(sos[1].h, w1)
    return ln_done, sos, (sob, b_sob)


def barrier(p, waits):
    for eng in ("g", "t", "v", "s", "y"):
        @p.o(eng)
        def _(q, waits=list(waits)):
            for s, v in waits:
                q.wait_ge(s.h, v)


def fm_from_tok(p, *, tok_dram, stage, xfm, psum_bf, idn, waits, slots=None,
                tok_bf=None, bf_waits=()):
    if slots is None:
        slots = TRAN_SLOTS
    idn = idn[:, :]
    tk = p.sem()
    te = p.sem()
    b_tk, b_te = tk.v, te.v

    if tok_bf is not None:
        lds = [p.sem(tag='hw') for _ in range(4)]
        b_ld = [s.v for s in lds]

        @p.o("y")
        def _(y, waits=list(waits), bf_waits=list(bf_waits)):
            for s, v in waits:
                y.wait_ge(s.h, v)
            for g in range(4):
                for s, v in bf_waits[g] if bf_waits else []:
                    y.wait_ge(s.h, v)
                for j in range(4):
                    tc = g * 4 + j
                    y.dma_start(out=stage[:, tc * 1024:(tc + 1) * 1024],
                                in_=tok_bf[tc * 128:(tc + 1) * 128, :]).then_inc(lds[g].h, 16)
        for g in range(4):
            lds[g].v += 64

        @p.o("t")
        def _(t, waits=list(waits)):
            for s, v in waits:
                t.wait_ge(s.h, v)
            for i in range(128):
                tc, kc = i // 8, i % 8
                if kc == 0:
                    t.wait_ge(lds[tc // 4].h, b_ld[tc // 4] + 64)
                if i >= 4:
                    t.wait_ge(te.h, b_te + i - 3)
                ps = psum_bf[:, slots[i % 4]: slots[i % 4] + 128]
                t.transpose(ps, stage[:, tc * 1024 + kc * 128: tc * 1024 + kc * 128 + 128],
                            idn).then_inc(tk.h, 1)
        tk.v += 128
    else:
        ld = p.sem(tag='sw')

        @p.o("g")
        def _(g, waits=list(waits)):
            for s, v in waits:
                g.wait_ge(s.h, v)
            for tc in range(16):
                g.dma_start(out=stage[:, tc * 1024:(tc + 1) * 1024],
                            in_=tok_dram[tc * 128:(tc + 1) * 128, :]).then_inc(ld.h, 16)
        ld.v += 16 * 16

        @p.o("t")
        def _(t, w=ld.v, waits=list(waits)):
            for s, v in waits:
                t.wait_ge(s.h, v)
            t.wait_ge(ld.h, w)
            for i in range(128):
                tc, kc = i // 8, i % 8
                if i >= 4:
                    t.wait_ge(te.h, b_te + i - 3)
                ps = psum_bf[:, slots[i % 4]: slots[i % 4] + 128]
                t.transpose(ps, stage[:, tc * 1024 + kc * 128: tc * 1024 + kc * 128 + 128],
                            idn).then_inc(tk.h, 1)
        tk.v += 128

    for i in range(128):
        tc, kc = i // 8, i % 8

        @p.o("v")
        def _(v, i=i, tc=tc, kc=kc, w=b_tk + i + 1):
            v.wait_ge(tk.h, w)
            ps = psum_bf[:, slots[i % 4]: slots[i % 4] + 128]
            v.tensor_copy(xfm[:, kc * 2048 + tc * 128: kc * 2048 + tc * 128 + 128],
                          ps).then_inc(te.h, 1)
    te.v += 128
    return (te, te.v)


N_CORES = 8
POOL_N = 104832


class Carver:
    def __init__(self, pool):
        self.pool = pool
        self.o = 0

    def take(self, n_el, dt=BF16, parts=128):
        mult = 1 if dt == BF16 else 2
        nb = n_el * mult
        o = self.o
        self.o = o + ((nb + 31) // 32) * 32
        assert self.o <= POOL_N, f"pool overflow {self.o}"
        sl = self.pool[:, o:o + nb]
        if dt == BF16:
            v = sl
        else:
            v = sl.bitcast(dt)
        if parts != 128:
            v = v[0:parts, :]
        return v


def carve_mamba(pool):
    c = Carver(pool)
    sb = {}
    A1 = c.take(20608)
    sb['xt0'] = A1[:, 0:8192]
    sb['xbcp'] = A1[:, 8192:18462]
    sb['ofm'] = A1[:, 0:16384]
    sb['wst2'] = A1[:, 18560:20608]
    A2 = c.take(18432)
    sb['wzr'] = A2[:, 0:8192]
    sb['wxsbcr'] = A2[:, 8192:18432]
    sb['wob'] = A2[:, 0:8192]
    sb['vfm'] = A2[:, 8192:16384]
    sb['outok'] = A2[:, 0:16384]
    sb['wdtr'] = c.take(128)
    xtok_t = c.take(8192)
    sb['xtok'] = xtok_t
    xv = xtok_t.bitcast(F32)
    sb['xs_f32'] = xv[:, 0:2048]
    sb['x2s'] = xv[:, 2048:4096]
    ztok_t = c.take(8192)
    sb['ztok'] = ztok_t
    zv = ztok_t.bitcast(F32)
    sb['ln_tmp'] = zv[:, 0:1024]
    sb['ln_tmp2'] = zv[:, 1024:2048]
    sb['ln_tr'] = ztok_t[:, 4096:5120]
    sb['bcf'] = c.take(2048)
    vfull_t = c.take(8192)
    sb['vfull'] = vfull_t
    sb['trf'] = vfull_t.bitcast(F32)
    sb['bio'] = vfull_t.bitcast(I32)[0:16, 0:2048]
    sb['dtraw'] = c.take(2048, parts=16)
    sb['wcst'] = c.take(2048)
    sb['dwork'] = c.take(768, F32, parts=16)
    sb['cumf'] = c.take(2048, F32, parts=1)
    sb['cum2048'] = c.take(2048, F32, parts=16)
    sb['w16'] = c.take(96, F32)
    sb['wbt'] = c.take(32)
    sb['mbig'] = c.take(2048)
    sb['m0sb'] = c.take(128)
    sb['btok'] = c.take(128)
    sb['hst'] = c.take(1024, F32)
    sb['hbb'] = c.take(1024)
    sb['xtt'] = c.take(1024)
    sb['szb'] = c.take(1024)
    sb['yt1'] = c.take(1024, F32)
    sb['vtrash'] = c.take(1024)
    sb['ssq'] = c.take(96, F32)
    sb['wdg'] = c.take(5120)
    sb['dide'] = c.take(2048)
    sb['bmask'] = c.take(2048, F32, parts=16)
    sb['bigl'] = c.take(128)
    sb['idtl'] = c.take(2048)
    sb['trilm'] = c.take(128)
    sb['hstash'] = c.take(64)[:, 0:60]
    sb['acol'] = c.take(16, F32, parts=16)[:, 0:1]
    sb['dones'] = c.take(128, F32, parts=16)
    sb['ln_sc'] = c.take(8, F32)
    for nm, shp in [('cwx', (128, 32)), ('cwbc', (128, 8)), ('cbx', (128, 8)),
                    ('cbbc', (128, 2)), ('dtb', (16, 1)), ('alog', (16, 1)),
                    ('drep', (128, 16)), ('nwc', (128, 8))]:
        sb[nm] = c.take(max(shp[1], 8), F32, parts=shp[0])[:, 0:shp[1]]
    return sb, c.o


def carve_ffn0(pool, TT):
    c = Carver(pool)
    sb = {}
    sb['x_fm'] = c.take(16384)
    sb['ff_a'] = c.take(32 * TT)
    sb['ff_b'] = c.take(32 * TT)
    sb['ff_a1'] = c.take(2 * TT)
    sb['w8'] = [c.take(8 * 128) for _ in range(4)]
    sb['w32'] = [c.take(32 * 128) for _ in range(3)]
    sb['w3o'] = c.take(8 * TT)
    sb['w3tok'] = c.take((TT // 128) * 1024)
    sb['xs_f32'] = c.take(2 * 1024, F32)
    sb['ln_tmp'] = c.take(1024, F32)
    sb['ln_sc'] = c.take(8, F32)
    sb['ln_tr'] = c.take(1024)
    sb['x2s'] = c.take(2 * 1024, F32)
    sb['x2b'] = c.take(4 * 1024)
    for nm, w in [('s1c', 32), ('s2c', 32), ('h1sc', 32), ('h2sc', 32), ('s3c', 8)]:
        sb['svec_' + nm] = c.take(w, F32)
    sb['pw_a'] = c.take(1024, F32)
    sb['pw_c0'] = c.take(1024)
    sb['pw_c1'] = c.take(1024)
    return sb, c.o


def carve_attn1(pool):
    c = Carver(pool)
    sb = {}
    sb['x2f'] = c.take(8 * 2048)
    att_a = c.take(16384)
    sb['xt2'] = [att_a[:, 0:8192], att_a[:, 8192:16384]]
    sb['aout'] = att_a
    sb['qf'] = c.take(4 * 4096)
    sb['kf'] = c.take(4 * 4096)
    sb['vaug'] = c.take(32 * 528)
    sb['pbuf'] = c.take(2 * 1024)
    sb['tau'] = c.take(4 * 512)
    sb['rrow'] = c.take(512, F32, parts=1)
    sb['rbb'] = c.take(512, parts=64)
    sb['wq3'] = sb['x2f'][:, 0:8192]
    sb['wv'] = sb['x2f'][:, 8192:12288]
    sb['wos'] = c.take(8 * 512)
    sb['wst'] = sb['pbuf']
    sb['wofm'] = sb['qf']
    sb['wotok'] = sb['kf']
    sb['bo_c'] = c.take(8, F32)
    sb['xs_f32'] = c.take(2 * 1024, F32)
    sb['ln_tmp'] = c.take(1024, F32)
    sb['ln_sc'] = c.take(8, F32)
    sb['ln_tr'] = c.take(1024)
    sb['x2s'] = c.take(2 * 1024, F32)
    return sb, c.o


def carve_ffn1(pool, TT):
    c = Carver(pool)
    sb = {}
    sb['x3f'] = c.take(8 * 2048)
    sb['fhid'] = c.take(32 * TT)
    sb['w8'] = [c.take(8 * 128) for _ in range(2)]
    sb['w32'] = [c.take(32 * 128) for _ in range(3)]
    sb['w3o'] = c.take(8 * TT)
    sb['w3tok'] = c.take((TT // 128) * 1024)
    sb['xs_f32'] = c.take(2 * 1024, F32)
    sb['ln_tmp'] = c.take(1024, F32)
    sb['ln_sc'] = c.take(8, F32)
    sb['ln_tr'] = c.take(1024)
    sb['x2s'] = c.take(2 * 1024, F32)
    sb['pw_a'] = c.take(4096, F32)
    sb['pw_b'] = c.take(4096, F32)
    sb['pw_c0'] = c.take(4096)
    sb['pw_c1'] = c.take(4096)
    sb['pw_s'] = c.take(40, F32)
    sb['qs'] = c.take(16, F32)
    sb['ep1'] = c.take(32, F32)
    sb['binc'] = c.take(32, F32)
    sb['boutc'] = c.take(8, F32)
    sb['hmv'] = c.take(32, F32)
    sb['tmp18'] = c.take(8, F32, parts=1)
    sb['q1'] = c.take(16, F32, parts=1)
    sb['q1x'] = c.take(128, F32, parts=1)
    return sb, c.o


def build(two_core=False, TT0=512, TT1=512):
    rgp = [[0, 1]] if two_core else [[0, 1], [2, 3], [4, 5], [6, 7]]
    rg8 = [[0, 1]] if two_core else [list(range(N_CORES))]
    nc = bass.Bass("TRN2", target_bir_lowering=False)
    d = {}
    EI, EO, IN = "ExternalInput", "ExternalOutput", "Internal"
    d['x0fm'] = nc.dram_tensor("x0fm", [1024, 4099], F32, kind=EI)
    d['x0tokh'] = nc.dram_tensor("x0tokh", [2048, 1024], F32, kind=EI)
    for nm, shp in [('wzb', [8, 1024, 128]), ('wxsb', [8, 1024, 128]),
                    ('wbcb', [2, 1024, 128]), ('wdt', [1024, 16]),
                    ('cwx', [128, 32]), ('cwbc', [128, 8]), ('cbx', [128, 8]),
                    ('cbbc', [128, 2]), ('dtb', [16, 1]), ('alog', [16, 1]),
                    ('drep', [128, 16]), ('nwc', [128, 8]),
                    ('woutb', [8, 1024, 128])]:
        d[nm] = nc.dram_tensor(nm, shp, F32, kind=EI)
    d['x1_tok'] = nc.dram_tensor("x1_tok", [2048, 1024], F32, kind=IN)
    d['x1_tokb'] = nc.dram_tensor("x1_tokb", [2048, 1024], BF16, kind=IN)
    d['wcflag'] = nc.dram_tensor("wcflag", [1, 4], F32, kind=IN)
    d['x2_tokb'] = nc.dram_tensor("x2_tokb", [2048, 1024], BF16, kind=IN)
    d['x3_tokb'] = nc.dram_tensor("x3_tokb", [2048, 1024], BF16, kind=IN)
    d['cfl'] = nc.dram_tensor("cfl", [2048], F32, kind=IN)
    d['vsp'] = nc.dram_tensor("vsp", [4, 128, 8192], BF16, kind=IN)
    d['ag2_in'] = nc.dram_tensor("ag2_in", [4096], F32, kind=IN)
    d['ag2_out'] = nc.dram_tensor("ag2_out", [2, 4096], F32, kind=IN)
    d['rs2_in'] = nc.dram_tensor("rs2_in", [2, 1024, 2048], BF16, kind=IN)
    d['rs2_out'] = nc.dram_tensor("rs2_out", [1024 * 2048], BF16, kind=IN)
    for nm, shp in [('w1b', [32, 1024, 128]), ('w2b', [32, 1024, 128]),
                    ('h1b', [32, 4096, 128]), ('h2b', [32, 4096, 128]),
                    ('w3b', [8, 4096, 128]), ('s1c', [128, 32]), ('s2c', [128, 32]),
                    ('h1sc', [128, 32]), ('h2sc', [128, 32]), ('s3c', [128, 8])]:
        d[nm] = nc.dram_tensor(nm, shp, F32, kind=EI)
    d['x2_tok'] = nc.dram_tensor("x2_tok", [2048, 1024], F32, kind=IN)
    d['w1c'] = nc.dram_tensor("w1c", [32, 128, 1024], BF16, kind=IN)
    d['w2c'] = nc.dram_tensor("w2c", [32, 128, 1024], BF16, kind=IN)
    d['h1c'] = nc.dram_tensor("h1c", [32, 128, 4096], BF16, kind=IN)
    d['h2c'] = nc.dram_tensor("h2c", [32, 128, 4096], BF16, kind=IN)
    d['w3c'] = nc.dram_tensor("w3c", [8, 128, 4096], BF16, kind=IN)
    for nm, shp in [('Wqb', [4, 1024, 128]), ('Wkb', [4, 1024, 128]),
                    ('Wvb', [1024, 512]), ('Wob', [8, 512, 128]),
                    ('bo_c', [128, 8]), ('tauj', [4, 128, 512])]:
        d[nm] = nc.dram_tensor(nm, shp, F32, kind=EI)
    d['x3_tok'] = nc.dram_tensor("x3_tok", [2048, 1024], F32, kind=IN)
    d['agx_in'] = nc.dram_tensor("agx_in", [1024 * 2048], BF16, kind=IN)
    d['agx_out'] = nc.dram_tensor("agx_out", [2, 1024 * 2048], BF16, kind=IN)
    d['rs_in'] = nc.dram_tensor("rs_in", [2, 1024, 2048], BF16, kind=IN)
    d['rs_out'] = nc.dram_tensor("rs_out", [1024 * 2048], BF16, kind=IN)
    for nm, shp in [('W1nT', [4096, 1024]), ('W2n', [4096, 1024]),
                    ('b_in_c', [128, 32]), ('b_out_c', [128, 8])]:
        d[nm] = nc.dram_tensor(nm, shp, F32, kind=EI)
    d['x4_tok'] = nc.dram_tensor("x4_tok", [2048, 1024], F32, kind=EO)
    d['w1q_blk'] = nc.dram_tensor("w1q_blk", [32, 128, 1024], BF16, kind=IN)
    d['w1qn'] = nc.dram_tensor("w1qn", [32, 128, 1024], BF16, kind=IN)
    d['w2q_blk'] = nc.dram_tensor("w2q_blk", [8, 128, 4096], BF16, kind=IN)
    d['hsp'] = nc.dram_tensor("hsp", [2048 // TT1, 128, 32 * TT1], BF16, kind=IN)
    d['flat_sc'] = nc.dram_tensor("flat_sc", [2, 128], F32, kind=IN)
    d['prepflag'] = nc.dram_tensor("prepflag", [1, 4], F32, kind=IN)
    for nm in ('arm1', 'arm1o', 'arm2', 'arm2o'):
        d[nm] = nc.dram_tensor(nm, [1, 8], F32, kind=IN)

    p = P(nc)
    pool = nc.alloc_sbuf_tensor("pool", [128, POOL_N], BF16)
    amx = nc.alloc_sbuf_tensor("amx", [128, 18], F32)
    q1p = nc.alloc_sbuf_tensor("q1p", [1, 12], F32)
    q1xp = nc.alloc_sbuf_tensor("q1xp", [1, 128], F32)
    scsP = nc.alloc_sbuf_tensor("scsP", [128, 40], F32)
    idn = nc.alloc_sbuf_tensor("idn", [128, 128], BF16)
    io_i32 = nc.alloc_sbuf_tensor("io_i32", [128, 128], I32)
    iom = nc.alloc_sbuf_tensor("iom", [128, 128], F32)
    onesf = nc.alloc_sbuf_tensor("onesf", [1, 128], F32)
    psum = nc.alloc_psum_tensor("ps", [128, 4096], F32)
    psum_bf = psum[:, :].bitcast(BF16)

    cst = p.sem()
    build_identity(p, idn, io_i32, iom, cst)
    ov = p.sem()

    @p.o("v")
    def _(v, cv=cst.v):
        v.wait_ge(cst.h, cv)
        v.memset(onesf[0:1, 0:128], 1.0).then_inc(ov.h, 1)
    ov.v += 1

    sbM, oM = carve_mamba(pool)
    sbM['onesf'] = onesf
    sbM['idn'] = idn
    sbM['iom'] = iom
    sbM['io_i32'] = io_i32
    hM = {k: d[k] for k in ('x0fm', 'x0tokh', 'wzb', 'wxsb', 'wbcb', 'wdt',
                            'cwx', 'cwbc', 'cbx', 'cbbc', 'dtb', 'alog', 'drep',
                            'nwc', 'woutb', 'x1_tok', 'cfl', 'vsp', 'ag2_in',
                            'ag2_out', 'rs2_in', 'rs2_out')}
    hM['x_tokb'] = d['x1_tokb']

    wc_stage = sbM['wcst']
    wc_sg = [p.sem(tag='sw'), p.sem(tag='sw')]
    wc_ss = [p.sem(tag='hw'), p.sem(tag='hw')]
    wc_bg = [s.v for s in wc_sg]
    wc_bs = [s.v for s in wc_ss]

    def _wc_jobs(src_t, dst_t, nblk, nkc):
        out = []
        for fc in range(nblk):
            for q in range(nkc // 8):
                out.append((src_t[fc].rearrange("(kc pp) j -> pp kc j", pp=128)
                            [:, q * 8:(q + 1) * 8, :],
                            dst_t[fc][:, q * 1024:(q + 1) * 1024]))
        return out
    wc_sets = [_wc_jobs(d['w1b'], d['w1c'], 32, 8) + _wc_jobs(d['w2b'], d['w2c'], 32, 8),
               _wc_jobs(d['h1b'], d['h1c'], 32, 32),
               _wc_jobs(d['h2b'], d['h2c'], 32, 32),
               _wc_jobs(d['w3b'], d['w3c'], 8, 32)]
    wc_state = {'ji': 0}
    wc_done = {}

    def wc_hook(tt):
        jobs = wc_sets[tt]
        if not jobs:
            return
        base = wc_state['ji']

        @p.o("g")
        def _(g, jobs=jobs, base=base):
            for k, (srcA, dstA) in enumerate(jobs):
                i = base + k
                sl = i % 2
                if i >= 2:
                    g.wait_ge(wc_ss[sl].h, wc_bs[sl] + 16 * (i // 2))
                g.dma_start(out=wc_stage[:, sl * 1024:(sl + 1) * 1024]
                            .rearrange("pp (kc j) -> pp kc j", j=128),
                            in_=srcA).then_inc(wc_sg[sl].h, 16)

        @p.o("y")
        def _(y, jobs=jobs, base=base):
            for k, (srcA, dstA) in enumerate(jobs):
                i = base + k
                sl = i % 2
                y.wait_ge(wc_sg[sl].h, wc_bg[sl] + 16 * (i // 2 + 1))
                y.dma_start(out=dstA,
                            in_=wc_stage[:, sl * 1024:(sl + 1) * 1024]).then_inc(wc_ss[sl].h, 16)
        for k in range(len(jobs)):
            i = base + k
            wc_sg[i % 2].v += 16
            wc_ss[i % 2].v += 16
        wc_state['ji'] += len(jobs)
        sd = p.sem(tag='hw')

        @p.o("y")
        def _(y, w0=wc_ss[0].v, w1=wc_ss[1].v, sd=sd, tt=tt):
            y.wait_ge(wc_ss[0].h, w0)
            y.wait_ge(wc_ss[1].h, w1)
            y.dma_start(out=d['wcflag'][0:1, tt:tt + 1],
                        in_=onesf[0:1, 0:1]).then_inc(sd.h, 16)
        sd.v += 16
        names = [('w1c', 'w2c'), ('h1c',), ('h2c',), ('w3c',)][tt]
        for nm in names:
            wc_done[nm] = [(sd, sd.v)]
        wc_state.setdefault('sds', []).append(sd)

    mres = mamba_phase(p, hM, sbM, psum, psum_bf, rgp, g_hook=wc_hook)
    lnM, sosM, bfM = mamba_tail(p, hM, sbM, psum, psum_bf, rgp, mres)
    barM = ([(lnM, lnM.v), (sosM[0], sosM[0].v), (sosM[1], sosM[1].v)] +
            [(s, s.v) for s in bfM[0]] +
            [(s, s.v) for s in wc_state['sds']])
    barrier(p, barM)
    p.recycle()

    sbF0, oF0 = carve_ffn0(pool, TT0)
    sbF0['idn'] = idn
    hF0 = {'x_tok': d['x1_tok'], 'x2_tok': d['x2_tok']}
    for k in ('w1b', 'w2b', 'h1b', 'h2b', 'w3b', 's1c', 's2c', 'h1sc', 'h2sc', 's3c',
              'w1c', 'w2c', 'h1c', 'h2c', 'w3c'):
        hF0[k] = d[k]
    xr0 = fm_from_tok(p, tok_dram=d['x1_tok'], stage=sbF0['ff_a'][:, 0:16384],
                         xfm=sbF0['x_fm'], psum_bf=psum_bf, idn=idn, waits=barM,
                         tok_bf=d['x1_tokb'],
                         bf_waits=[[(bfM[0][s], bfM[1][s] + 16 * (g + 1))
                      for s in range(4)] for g in range(4)])
    hF0['x_tokb'] = d['x2_tokb']
    sbF0['pw_s'] = scsP
    hQ = {k: d[k] for k in ('w1qn', 'W1nT', 'W2n', 'w2q_blk', 'prepflag')}
    done_q = p.sem(tag='sw')
    q_pieces, q_emit_g = ffn1_wprep_quant(p, hQ, sbF0, done_q)
    ln0, sos0, bf0 = ffn0_phase(p, hF0, sbF0, psum, psum_bf, xr0, TT=TT0, x2fm=None,
                                   wc_done=wc_done,
                                   v_hook=lambda tt: q_pieces[tt]())
    q_emit_g()
    bar0 = ([(ln0, ln0.v), (sos0[0], sos0[0].v), (sos0[1], sos0[1].v)] +
            [(s, s.v) for s in bf0[0]] + [(done_q, done_q.v)])
    barrier(p, bar0)
    p.recycle()

    sbA, oA = carve_attn1(pool)
    sbA['idn'] = idn
    sbA['onesf'] = onesf
    hA = {'x_tok': d['x2_tok'], 'x3_tok': d['x3_tok']}
    for k in ('Wqb', 'Wkb', 'Wvb', 'Wob', 'bo_c', 'tauj', 'agx_in', 'agx_out',
              'rs_in', 'rs_out'):
        hA[k] = d[k]
    xr1 = fm_from_tok(p, tok_dram=d['x2_tok'], stage=sbA['qf'][:, 0:16384],
                         xfm=sbA['x2f'], psum_bf=psum_bf, idn=idn, waits=bar0,
                         tok_bf=d['x2_tokb'],
                         bf_waits=[[(bf0[0][s], bf0[1][s] + 16 * (g + 1))
                      for s in range(4)] for g in range(4)])
    hA['x_tokb'] = d['x3_tokb']
    lnA, sosA, bfA = attn1_phase(p, hA, sbA, psum, psum_bf, xr1, rgp,
                               absmax_col=amx[:, 0:16])

    amr = p.sem()
    b_amr = amr.v

    @p.o("v")
    def _(v, w=lnA.v):
        v.wait_ge(lnA.h, w)
        v.reduce_max(amx[:, 16:17], amx[:, 0:16], axis=mybir.AxisListType.X).then_inc(amr.h, 1)
    amr.v += 1
    amc = p.sem(tag='sw')
    b_amc = amc.v

    @p.o("g")
    def _(g, w=amr.v):
        g.wait_ge(amr.h, w)
        g.dma_start(out=d['flat_sc'][0], in_=amx[:, 16:17],
                    allow_slow_non_contiguous=True).then_inc(amc.h, 16)
        g.wait_ge(amc.h, b_amc + 16)
        g.dma_start(out=q1xp[0:1, 0:128], in_=d['flat_sc'][0]).then_inc(amc.h, 16)
    amc.v += 32

    @p.o("v")
    def _(v, w=amc.v, b=b_amr):
        v.wait_ge(amc.h, w)
        v.reduce_max(q1p[0:1, 0:1], q1xp[0:1, 0:128],
                     axis=mybir.AxisListType.X).then_inc(amr.h, 1)
        v.wait_ge(amr.h, b + 2)
        v.tensor_copy(q1p[0:1, 2:10],
                      q1p[0:1, 0:1].broadcast_to([1, 8])).then_inc(amr.h, 1)
    amr.v += 2
    cc1 = p.sem(tag='co')

    @p.o("g")
    def _(g, w=amr.v, b=b_amc):
        g.wait_ge(amr.h, w)
        g.dma_start(out=d['arm1'].ap(), in_=q1p[0:1, 2:10]).then_inc(amc.h, 16)
        g.wait_ge(amc.h, b + 48)
        g.collective_compute("AllReduce", mybir.AluOpType.max, replica_groups=rg8,
                             ins=[d['arm1'].ap().opt()],
                             outs=[d['arm1o'].ap().opt()]).then_inc(cc1.h, 1)
    amc.v += 16
    cc1.v += 1

    barA = ([(lnA, lnA.v), (sosA[0], sosA[0].v), (sosA[1], sosA[1].v)] +
            [(s, s.v) for s in bfA[0]])
    barrier(p, barA)
    p.recycle(keep=[amr, amc, cc1])

    sbF1, oF1 = carve_ffn1(pool, TT1)
    sbF1['idn'] = idn
    sbF1['onesf'] = onesf
    hF1 = {'x_tok': d['x3_tok'], 'x4_tok': d['x4_tok']}
    for k in ('W1nT', 'W2n', 'b_in_c', 'b_out_c', 'w1q_blk', 'w1qn', 'w2q_blk',
              'hsp', 'flat_sc', 'prepflag', 'arm1', 'arm1o', 'arm2', 'arm2o'):
        hF1[k] = d[k]
    xr2 = fm_from_tok(p, tok_dram=d['x3_tok'], stage=sbF1['fhid'][:, 0:16384],
                         xfm=sbF1['x3f'], psum_bf=psum_bf, idn=idn, waits=barA,
                         slots=[0, 1024, 2048, 3072],
                         tok_bf=d['x3_tokb'],
                         bf_waits=[[(bfA[0][s], bfA[1][s] + 16 * (g + 1))
                      for s in range(4)] for g in range(4)])
    sbF1['pw_s'] = scsP
    prep_done = p.sem(tag='sw')
    ffn1_blockify(p, hF1, sbF1, psum_bf, idn, done_q, prep_done)

    arxF = p.sem(tag='hw')

    @p.o("y")
    def _(y, w=cc1.v):
        y.wait_ge(cc1.h, w)
        y.dma_start(out=q1p[0:1, 1:2],
                    in_=d['arm1o'].ap()[0:1, 0:1]).then_inc(arxF.h, 16)
    arxF.v += 16
    ln1, sos1 = ffn1_phase(p, hF1, sbF1, psum, psum_bf, xr2, rg8, prep_done,
                              TT=TT1, xamax=(q1p[0:1, 1:2], arxF))
    p.finalize()
    return nc


def colsplit(s):
    return np.ascontiguousarray(np.asarray(s, np.float32).reshape(-1, 128).T)


def blocked(wT, n_fc):
    d_in, d_out = wT.shape
    return np.ascontiguousarray(wT.reshape(d_in, n_fc, 128).transpose(1, 0, 2))


def stage_core(ins, c):
    b, hh = c // 2, c % 2
    x = np.asarray(ins['L0_attn_x'][b])
    W_in = np.asarray(ins['L0_attn_W_in'])
    conv_w = np.asarray(ins['L0_attn_conv_w'])
    conv_b = np.asarray(ins['L0_attn_conv_b'])
    DI = 2048
    zcols = W_in[:, hh * 1024:(hh + 1) * 1024]
    xscols = W_in[:, DI + hh * 1024: DI + (hh + 1) * 1024]
    bccols = W_in[:, 2 * DI: 2 * DI + 256]
    dtcols = W_in[:, 4352 + hh * 16: 4352 + (hh + 1) * 16]
    cw_xs = conv_w[hh * 1024:(hh + 1) * 1024]
    cw_bc = conv_w[2048:2304]
    cwx = cw_xs.reshape(8, 128, 4).transpose(1, 0, 2).reshape(128, 32)
    cwbc = cw_bc.reshape(2, 128, 4).transpose(1, 0, 2).reshape(128, 8)
    m = {
        'x0fm': np.concatenate([np.zeros((1024, 3), np.float32),
                                np.ascontiguousarray(x.T)], axis=1),
        'x0tokh': np.ascontiguousarray(x[hh * 2048:(hh + 1) * 2048]),
        'wzb': blocked(zcols, 8), 'wxsb': blocked(xscols, 8),
        'wbcb': blocked(bccols, 2), 'wdt': np.ascontiguousarray(dtcols),
        'cwx': np.ascontiguousarray(cwx), 'cwbc': np.ascontiguousarray(cwbc),
        'cbx': colsplit(conv_b[hh * 1024:(hh + 1) * 1024]),
        'cbbc': colsplit(conv_b[2048:2304]),
        'dtb': np.asarray(ins['L0_attn_dt_bias'])[hh * 16:(hh + 1) * 16].reshape(16, 1),
        'alog': np.asarray(ins['L0_attn_A_log'])[hh * 16:(hh + 1) * 16].reshape(16, 1),
        'drep': np.tile(np.asarray(ins['L0_attn_D'])[hh * 16:(hh + 1) * 16], (128, 1)),
        'nwc': colsplit(np.asarray(ins['L0_attn_norm_w'])[hh * 1024:(hh + 1) * 1024]),
        'woutb': blocked(np.asarray(ins['L0_attn_W_out'])[hh * 1024:(hh + 1) * 1024, :], 8),
    }
    m.update({
        'w1b': blocked(np.asarray(ins['L0_ffn_w1_w']).T, 32),
        'w2b': blocked(np.asarray(ins['L0_ffn_w2_w']).T, 32),
        'h1b': blocked(np.asarray(ins['L0_ffn_h1_w']).T, 32),
        'h2b': blocked(np.asarray(ins['L0_ffn_h2_w']).T, 32),
        'w3b': blocked(np.asarray(ins['L0_ffn_w3_w']).T, 8),
        's1c': colsplit(ins['L0_ffn_w1_s']), 's2c': colsplit(ins['L0_ffn_w2_s']),
        'h1sc': colsplit(ins['L0_ffn_h1_s']), 'h2sc': colsplit(ins['L0_ffn_h2_s']),
        's3c': colsplit(ins['L0_ffn_w3_s']),
    })
    Wq = np.asarray(ins['L1_attn_Wq'])
    Wk = np.asarray(ins['L1_attn_Wk'])
    Wv = np.asarray(ins['L1_attn_Wv'])
    Wo = np.asarray(ins['L1_attn_Wo'])
    tauj = np.zeros((4, 128, 512), np.float32)
    for j in range(4):
        for kk in range(128):
            tauj[j, kk, 128 * j + kk:] = 1.0
    m.update({
        'Wqb': blocked(Wq[:, hh * 512:(hh + 1) * 512], 4),
        'Wkb': blocked(Wk[:, hh * 512:(hh + 1) * 512], 4),
        'Wvb': np.ascontiguousarray(Wv[:, hh * 512:(hh + 1) * 512]),
        'Wob': blocked(Wo[hh * 512:(hh + 1) * 512, :], 8),
        'bo_c': colsplit(ins['L1_attn_bo']),
        'tauj': tauj,
    })
    m.update({
        'W1nT': np.ascontiguousarray(np.asarray(ins['L1_ffn_W_in']).T),
        'W2n': np.ascontiguousarray(np.asarray(ins['L1_ffn_W_out'])),
        'b_in_c': colsplit(ins['L1_ffn_b_in']),
        'b_out_c': colsplit(ins['L1_ffn_b_out']),
    })
    return m




_NC_CACHE = {}


def _get_nc():
    if 'nc' not in _NC_CACHE:
        _NC_CACHE['nc'] = build(two_core=False)
    return _NC_CACHE['nc']


def kernel(**inputs):
    ins = {k: np.asarray(v) for k, v in inputs.items()}
    in_maps = [stage_core(ins, c) for c in range(N_CORES)]
    nc = _get_nc()
    res = run_bass_kernel_spmd(nc, in_maps, core_ids=list(range(N_CORES)))
    out = np.empty((4, 4096, 1024), np.float32)
    for c in range(N_CORES):
        b, hh = c // 2, c % 2
        out[b, hh * 2048:(hh + 1) * 2048] = res.results[c]['x4_tok']
    return out


# version 14
# speedup vs baseline: 1.0134x; 1.0006x over previous
import contextlib
import numpy as np
import concourse.bass as bass
from concourse.bass_utils import run_bass_kernel_spmd
import concourse.mybir as mybir

F32 = mybir.dt.float32
BF16 = mybir.dt.bfloat16
I32 = mybir.dt.int32
AF = mybir.ActivationFunctionType
ALU = mybir.AluOpType
AX = mybir.AxisListType

MAGIC = 12582912.0


class Sem:
    __slots__ = ("h", "v")
    def __init__(self, h):
        self.h = h
        self.v = 0


class P:
    ENGS = ("g", "t", "v", "s", "y")

    def __init__(self, nc):
        self.nc = nc
        self.qs = {e: [] for e in self.ENGS}
        self.stack = contextlib.ExitStack()
        self._semno = 0
        self._all_sems = {}
        self._free = {}

    def sem(self, name=None, tag="c"):
        free = self._free.setdefault(tag, [])
        if free:
            return free.pop()
        self._semno += 1
        h = self.stack.enter_context(self.nc.semaphore(name or f"sm{self._semno}"))
        s = Sem(h)
        self._all_sems.setdefault(tag, []).append(s)
        return s

    def recycle(self, keep=()):
        keep = set(id(k) for k in keep)
        self._free = {t: [s for s in lst if id(s) not in keep]
                      for t, lst in self._all_sems.items()}

    def o(self, eng):
        def reg(fn):
            self.qs[eng].append(fn)
        return reg

    def finalize(self):
        with self.nc.Block() as block:
            block.gpsimd(lambda e: [f(e) for f in self.qs["g"]])
            block.tensor(lambda e: [f(e) for f in self.qs["t"]])
            block.vector(lambda e: [f(e) for f in self.qs["v"]])
            block.scalar(lambda e: [f(e) for f in self.qs["s"]])
            block.sync(lambda e: [f(e) for f in self.qs["y"]])
        self.stack.close()


def cdiv(a, b):
    return (a + b - 1) // b


def fm_proj(p, *, w_dram, n_fc, n_k, x_rhs, ttile, wbufs, psum, slot_banks,
            evac, ev, x_waits=(), rank1=None, w_guard=(), tee=None, w_hw=False,
            w_hw_rearr=False, shared=None):
    if w_hw:
        wtag = ['hw', 'hw', 'hw', 'hw']
        wq = ['y', 'y', 'y', 'y']
    else:
        wtag = ['sw'] * 4
        wq = ['g'] * 4
    nslot = len(slot_banks)
    nw = len(wbufs)
    if shared is None:
        mmk = p.sem()
        wls = [p.sem(tag=wtag[i]) for i in range(nw)]
    else:
        mmk, wls = shared
    mb = mmk.v
    wtag = wtag[:nw]
    wq = wq[:nw]
    wlb = [w.v for w in wls]
    n_nt = cdiv(ttile, 512)
    ev0 = ev.v
    tes = [p.sem(tag='hw') for _ in range(nw)] if tee is not None else None
    teb = [t.v for t in tes] if tes else None

    for qname in sorted(set(wq)):
        slots_q = [i for i in range(nw) if wq[i] == qname]

        @p.o(qname)
        def _(g, wg=list(w_guard), slots_q=slots_q):
            for s, val in wg:
                g.wait_ge(s.h, val)
            for fc in range(n_fc):
                if fc % nw not in slots_q:
                    continue
                if fc - nw >= 0:
                    g.wait_ge(mmk.h, mb + fc - nw + 1)
                if tes is not None and fc >= nw:
                    g.wait_ge(tes[fc % nw].h, teb[fc % nw] + 16 * (fc // nw))
                if w_hw:
                    src = w_dram(fc)
                    dst = wbufs[fc % nw][:, 0:n_k * 128]
                    if w_hw_rearr:
                        dst = dst.rearrange("pp (kc j) -> pp kc j", j=128)
                else:
                    src = w_dram(fc).rearrange("(kc pp) j -> pp kc j", pp=128)
                    dst = wbufs[fc % nw][:, 0:n_k * 128].rearrange("pp (kc j) -> pp kc j", j=128)
                g.dma_start(out=dst, in_=src).then_inc(wls[fc % nw].h, 16)
    for i in range(nw):
        wls[i].v += 16 * ((n_fc + nw - 1 - i) // nw)

    if tee is not None:
        @p.o("y")
        def _(y):
            for fc in range(n_fc):
                y.wait_ge(wls[fc % nw].h, wlb[fc % nw] + 16 * (fc // nw + 1))
                y.dma_start(out=tee[fc],
                            in_=wbufs[fc % nw][:, 0:n_k * 128]).then_inc(tes[fc % nw].h, 16)
        for i in range(nw):
            tes[i].v += 16 * ((n_fc + nw - 1 - i) // nw)
    tee_done = ([(t, t.v) for t in tes] if tes is not None else None)

    @p.o("t")
    def _(t, xw=list(x_waits)):
        for s, val in xw:
            t.wait_ge(s.h, val)
        for fc in range(n_fc):
            t.wait_ge(wls[fc % nw].h, wlb[fc % nw] + 16 * (fc // nw + 1))
            if ev0 + fc - nslot + 1 > 0:
                t.wait_ge(ev.h, ev0 + fc - nslot + 1)
            wb = wbufs[fc % nw]
            b0 = slot_banks[fc % nslot] * 512
            for nt in range(n_nt):
                tn = min(512, ttile - nt * 512)
                last_nt = (nt == n_nt - 1)
                ps = psum[:, b0 + nt * 512: b0 + nt * 512 + tn]
                for k in range(n_k):
                    mi = t.matmul(ps, wb[:, k * 128:(k + 1) * 128], x_rhs(k, nt * 512, tn),
                                  start=(k == 0), stop=(k == n_k - 1 and rank1 is None))
                    if last_nt and rank1 is None and k == n_k - 1:
                        mi.then_inc(mmk.h, 1)
                if rank1 is not None:
                    mi = t.matmul(ps, lhs1 := rank1[0], rank1[1](fc, nt * 512, tn), start=False, stop=True)
                    if last_nt:
                        mi.then_inc(mmk.h, 1)
    mmk.v += n_fc

    for fc in range(n_fc):
        b0 = slot_banks[fc % nslot] * 512
        ps = psum[:, b0: b0 + ttile]
        evac(fc, ps, [(mmk, mb + fc + 1)], ev)
    ev.v = ev0 + n_fc
    return tee_done


def tok_proj(p, *, w_dram, d_in, d_out, x_lhs, n_tc, wbuf, psum, slot_banks,
             evac, ev, x_waits=(), wprep=None):
    wl = p.sem(tag='sw')
    mmk = p.sem()
    mb = mmk.v
    n_k = d_in // 128
    n_nt = cdiv(d_out, 512)
    nslot = len(slot_banks)
    ev0 = ev.v

    @p.o("g")
    def _(g):
        for k in range(n_k):
            g.dma_start(out=wbuf[:, k * d_out:(k + 1) * d_out],
                        in_=w_dram[k * 128:(k + 1) * 128, :]).then_inc(wl.h, 16)
    wl.v += 16 * n_k
    wready = [(wl, wl.v)]
    if wprep is not None:
        wready = wprep(wready)

    @p.o("t")
    def _(t, xw=list(x_waits) + wready):
        for s, val in xw:
            t.wait_ge(s.h, val)
        for tc in range(n_tc):
            if tc >= nslot:
                t.wait_ge(ev.h, ev0 + tc - nslot + 1)
            b0 = slot_banks[tc % nslot] * 512
            for k in range(n_k):
                for nt in range(n_nt):
                    w = min(512, d_out - nt * 512)
                    ps = psum[:, b0 + nt * 512: b0 + nt * 512 + w]
                    mi = t.matmul(ps, x_lhs(k, tc), wbuf[:, k * d_out + nt * 512: k * d_out + nt * 512 + w],
                                  start=(k == 0), stop=(k == n_k - 1))
                    if k == n_k - 1 and nt == n_nt - 1:
                        mi.then_inc(mmk.h, 1)
    mmk.v += n_tc

    for tc in range(n_tc):
        b0 = slot_banks[tc % nslot] * 512
        ps = psum[:, b0: b0 + d_out]
        evac(tc, ps, [(mmk, mb + tc + 1)], ev)
    ev.v = ev0 + n_tc


def build_identity(p, idn_bf, io_i32, iom_f32, done):
    @p.o("g")
    def _(g):
        g.iota(io_i32[:, :], pattern=[[1, 128]], base=0, channel_multiplier=-1).then_inc(done.h, 1)
    done.v += 1

    @p.o("v")
    def _(v, w=done.v):
        v.wait_ge(done.h, w)
        v.tensor_single_scalar(iom_f32[:, :], io_i32[:, :], 0.0, ALU.is_equal).then_inc(done.h, 1)
        v.wait_ge(done.h, w + 1)
        v.tensor_copy(idn_bf[:, :], iom_f32[:, :]).then_inc(done.h, 1)
    done.v += 2


def ln_chunk(p, *, a_ap, b_ap, tmp, trash_bf, sc, n_f, waits, u, done,
             out_f32=None, out_bf=None, eps=1e-5, absmax=None):
    b = u.v

    @p.o("v")
    def _(v, w=list(waits) + [(done, done.v)], b=b):
        for s, val in w:
            v.wait_ge(s.h, val)
        if b_ap is not None:
            v.tensor_tensor(tmp, a_ap, b_ap, ALU.add).then_inc(u.h, 1)
        else:
            v.tensor_copy(tmp, a_ap).then_inc(u.h, 1)
        v.wait_ge(u.h, b + 1)
        v.reduce_sum(sc[:, 0:1], tmp, axis=AX.X).then_inc(u.h, 1)
        v.wait_ge(u.h, b + 2)
        v.tensor_scalar(sc[:, 1:2], sc[:, 0:1], 1.0 / n_f, None, ALU.mult).then_inc(u.h, 1)
        v.wait_ge(u.h, b + 3)
        v.tensor_scalar(tmp, tmp, sc[:, 1:2], None, ALU.subtract).then_inc(u.h, 1)
        v.wait_ge(u.h, b + 4)
        v.tensor_tensor(trash_bf, tmp, tmp, ALU.mult).then_inc(u.h, 1)
        v.wait_ge(u.h, b + 5)
        v.reduce_sum(sc[:, 2:3], trash_bf, axis=AX.X).then_inc(u.h, 1)
        v.wait_ge(u.h, b + 6)
        v.tensor_scalar(sc[:, 2:3], sc[:, 2:3], 1.0 / n_f, eps, ALU.mult, ALU.add).then_inc(u.h, 1)

    @p.o("s")
    def _(s, b=b):
        s.wait_ge(u.h, b + 7)
        s.activation(sc[:, 3:4], sc[:, 2:3], AF.Sqrt).then_inc(u.h, 1)

    d0 = done.v
    nout = (out_f32 is not None) + (out_bf is not None)

    @p.o("v")
    def _(v, b=b, d0=d0, nout=nout):
        v.wait_ge(u.h, b + 8)
        v.reciprocal(sc[:, 3:4], sc[:, 3:4]).then_inc(u.h, 1)
        v.wait_ge(u.h, b + 9)
        if out_f32 is not None:
            v.tensor_scalar(out_f32, tmp, sc[:, 3:4], None, ALU.mult).then_inc(done.h, 1)
        if out_bf is not None:
            v.tensor_scalar(out_bf, tmp, sc[:, 3:4], None, ALU.mult).then_inc(done.h, 1)
        if absmax is not None:
            v.wait_ge(done.h, d0 + nout)
            v.reduce_max(absmax, out_f32 if out_f32 is not None else out_bf,
                         axis=AX.X, apply_absolute_value=True).then_inc(done.h, 1)
    u.v = b + 9
    done.v = d0 + nout + (absmax is not None)


def transpose_tiles(p, *, src, dst, n, idn, psum_bf, slots, waits, ev):
    tk = p.sem()
    b_tk = tk.v
    nslot = len(slots)
    ev0 = ev.v

    @p.o("t")
    def _(t, w=list(waits)):
        for s, val in w:
            t.wait_ge(s.h, val)
        for i in range(n):
            if i >= nslot - 1:
                t.wait_ge(ev.h, ev0 + i - nslot + 2)
            ps = psum_bf[:, slots[i % nslot]: slots[i % nslot] + 128]
            t.transpose(ps, src(i), idn[:, :]).then_inc(tk.h, 1)
    tk.v += n

    @p.o("v")
    def _(v):
        for i in range(n):
            v.wait_ge(tk.h, b_tk + i + 1)
            ps = psum_bf[:, slots[i % nslot]: slots[i % nslot] + 128]
            v.tensor_copy(dst(i), ps).then_inc(ev.h, 1)
    ev.v = ev0 + n


TRAN_SLOTS = [4 * 1024, 5 * 1024, 6 * 1024, 7 * 1024]


def ffn0_phase(p, h, sb, psum, psum_bf, x_ready, TT=512, x2fm=None, wc_done=None,
               v_hook=None):
    NTILE = 2048 // TT
    NCH = TT // 128
    x_fm = sb['x_fm']
    a = sb['ff_a']
    bb = sb['ff_b']
    a1 = sb['ff_a1']
    w8 = sb['w8']
    w32 = sb['w32']
    w3o = sb['w3o']
    w3tok = sb['w3tok']
    xs_f32 = sb['xs_f32']
    x2s = sb['x2s']
    idn = sb['idn']

    svl = p.sem(tag='sw')
    svecs = ['s1c', 's2c', 'h1sc', 'h2sc', 's3c']

    @p.o("g")
    def _(g):
        for nm in svecs:
            g.dma_start(out=sb['svec_' + nm][:, 0:h[nm].shape[1]], in_=h[nm][:, :]).then_inc(svl.h, 16)
    svl.v += 80
    svlf = svl.v
    sv = {nm: sb['svec_' + nm] for nm in svecs}

    u = p.sem()
    ln_done = p.sem()
    sxs = [p.sem(tag='hw'), p.sem(tag='hw')]
    sos = [p.sem(tag='hw'), p.sem(tag='hw')]
    tokb = h.get('x_tokb')
    sob = [p.sem(tag='hw') for _ in range(4)]
    b_sob = [s.v for s in sob]
    ln_marks = []
    prev_tail = []
    ev_w1 = p.sem(); ev_a = p.sem(); ev_b = p.sem(); ev_c = p.sem()
    ev_w3 = p.sem(); ev_tr = p.sem()
    sh_w1 = (p.sem(), [p.sem(tag='hw'), p.sem(tag='hw')])
    sh_w2 = (p.sem(), [p.sem(tag='hw'), p.sem(tag='hw')])
    sh_h1 = (p.sem(), [p.sem(tag='hw'), p.sem(tag='hw'), p.sem(tag='hw')])
    sh_h2 = (p.sem(), [p.sem(tag='hw'), p.sem(tag='hw'), p.sem(tag='hw')])
    sh_w3 = (p.sem(), [p.sem(tag='hw'), p.sem(tag='hw'), p.sem(tag='hw')])

    for tt in range(NTILE):
        t0 = tt * TT

        def evac_w1(fc, ps, waits, ev):
            @p.o("s")
            def _(s, w=list(waits), fc=fc, ps=ps):
                for sm, val in w:
                    s.wait_ge(sm.h, val)
                s.wait_ge(svl.h, svlf)
                s.activation(a[:, fc * TT:(fc + 1) * TT], ps,
                             AF.Silu, scale=sv['s1c'][:, fc:fc + 1]).then_inc(ev.h, 1)
        fm_proj(p, w_dram=lambda fc: h['w1c'][fc], n_fc=32, n_k=8,
                x_rhs=lambda k, o, n, t0=t0: x_fm[:, k * 2048 + t0 + o: k * 2048 + t0 + o + n],
                ttile=TT, wbufs=w8[0:2], psum=psum, slot_banks=[0, 1] if TT == 512 else [0, 2],
                evac=evac_w1, ev=ev_w1, w_hw=True, shared=sh_w1,
                w_guard=(wc_done['w1c'] if (wc_done and tt == 0) else []),
                x_waits=[x_ready] + [w for w in prev_tail if w[0].h is not None][:2])

        def evac_w2(fc, ps, waits, ev, ew0=ev_w1.v - 32):
            @p.o("v")
            def _(v, w=list(waits), fc=fc, ps=ps, ew0=ew0):
                for sm, val in w:
                    v.wait_ge(sm.h, val)
                v.wait_ge(svl.h, svlf)
                v.wait_ge(ev_w1.h, ew0 + fc + 1)
                v.scalar_tensor_tensor(a[:, fc * TT:(fc + 1) * TT], ps,
                                       sv['s2c'][:, fc:fc + 1],
                                       a[:, fc * TT:(fc + 1) * TT],
                                       ALU.mult, ALU.mult).then_inc(ev.h, 1)
        ea_base = ev_a.v
        fm_proj(p, w_dram=lambda fc: h['w2c'][fc], n_fc=32, n_k=8,
                x_rhs=lambda k, o, n, t0=t0: x_fm[:, k * 2048 + t0 + o: k * 2048 + t0 + o + n],
                ttile=TT, wbufs=w8[2:4], psum=psum, slot_banks=[2, 3] if TT == 512 else [4, 6],
                evac=evac_w2, ev=ev_a, w_hw=True, shared=sh_w2,
                w_guard=(wc_done['w2c'] if (wc_done and tt == 0) else []),
                x_waits=[x_ready] + prev_tail[-1:])

        def evac_h1(fc, ps, waits, ev):
            @p.o("s")
            def _(s, w=list(waits), fc=fc, ps=ps):
                for sm, val in w:
                    s.wait_ge(sm.h, val)
                s.wait_ge(svl.h, svlf)
                s.activation(bb[:, fc * TT:(fc + 1) * TT], ps, AF.Gelu,
                             scale=sv['h1sc'][:, fc:fc + 1]).then_inc(ev.h, 1)
        fm_proj(p, w_dram=lambda fc: h['h1c'][fc], n_fc=32, n_k=32,
                x_rhs=lambda k, o, n: a[:, k * TT + o: k * TT + o + n],
                ttile=TT, wbufs=w32, psum=psum, slot_banks=[4, 5] if TT == 512 else [0, 2],
                evac=evac_h1, ev=ev_b, w_hw=True, shared=sh_h1,
                x_waits=[(ev_a, ea_base + 32)],
                w_guard=(([(ev_w3, ev_w3.v)] if tt > 0 else []) +
                         (wc_done['h1c'] if (wc_done and tt == 0) else [])))

        eb_base = ev_b.v - 32
        def evac_h2(fc, ps, waits, ev):
            @p.o("s")
            def _(s, w=list(waits), fc=fc, ps=ps):
                for sm, val in w:
                    s.wait_ge(sm.h, val)
                s.wait_ge(svl.h, svlf)
                s.activation(a[:, fc * TT:(fc + 1) * TT], ps, AF.Gelu,
                             scale=sv['h2sc'][:, fc:fc + 1]).then_inc(ev.h, 1)
        fm_proj(p, w_dram=lambda fc: h['h2c'][fc], n_fc=32, n_k=32,
                x_rhs=lambda k, o, n: bb[:, k * TT + o: k * TT + o + n],
                ttile=TT, wbufs=w32, psum=psum, slot_banks=[6, 7] if TT == 512 else [4, 6],
                evac=evac_h2, ev=ev_c, w_hw=True, shared=sh_h2,
                x_waits=[(ev_b, eb_base + 32)],
                w_guard=[(ev_b, eb_base + 32)] +
                        (wc_done['h2c'] if (wc_done and tt == 0) else []))

        ec_base = ev_c.v - 32
        def evac_w3(fc, ps, waits, ev):
            @p.o("v")
            def _(v, w=list(waits), fc=fc, ps=ps):
                for sm, val in w:
                    v.wait_ge(sm.h, val)
                v.wait_ge(svl.h, svlf)
                v.tensor_scalar(w3o[:, fc * TT:(fc + 1) * TT], ps,
                                sv['s3c'][:, fc:fc + 1], None, ALU.mult).then_inc(ev.h, 1)
        fm_proj(p, w_dram=lambda fc: h['w3c'][fc], n_fc=8, n_k=32,
                x_rhs=lambda k, o, n: a[:, k * TT + o: k * TT + o + n],
                ttile=TT, wbufs=w32, psum=psum, slot_banks=[0, 1],
                evac=evac_w3, ev=ev_w3, w_hw=True, shared=sh_w3,
                x_waits=[(ev_c, ec_base + 32)] + ([(ev_tr, ev_tr.v)] if tt > 0 else []),
                w_guard=[(ev_c, ec_base + 32)] +
                        (wc_done['w3c'] if (wc_done and tt == 0) else []))

        ew3_0 = ev_w3.v
        n_tr = 8 * NCH
        transpose_tiles(
            p,
            src=lambda i: w3o[:, (i % 8) * TT + (i // 8) * 128: (i % 8) * TT + (i // 8) * 128 + 128],
            dst=lambda i: w3tok[:, (i // 8) * 1024 + (i % 8) * 128: (i // 8) * 1024 + (i % 8) * 128 + 128],
            n=n_tr, idn=idn, psum_bf=psum_bf, slots=TRAN_SLOTS,
            waits=[(ev_w3, ew3_0)] + ([(ln_done, ln_done.v)] if tt > 0 else []), ev=ev_tr)
        etr_base = ev_tr.v - n_tr

        for c8 in range(NCH):
            tc = tt * NCH + c8
            sx = sxs[tc % 2]
            @p.o("y")
            def _(y, tc=tc, need=(ln_marks[tc - 2] if tc >= 2 else 0), sx=sx):
                if tc >= 2:
                    y.wait_ge(ln_done.h, need)
                y.dma_start(out=xs_f32[:, (tc % 2) * 1024:(tc % 2) * 1024 + 1024],
                            in_=h['x_tok'][tc * 128:(tc + 1) * 128, :]).then_inc(sx.h, 16)
            sx.v += 16
            d_before = ln_done.v
            ln_chunk(p,
                     a_ap=xs_f32[:, (tc % 2) * 1024:(tc % 2) * 1024 + 1024],
                     b_ap=w3tok[:, c8 * 1024:(c8 + 1) * 1024],
                     tmp=sb['ln_tmp'][:, :], trash_bf=sb['ln_tr'][:, :], sc=sb['ln_sc'],
                     n_f=1024,
                     waits=[(sx, sx.v), (ev_tr, etr_base + n_tr)] +
                           ([(sos[tc % 2], sos[tc % 2].v)] if tc >= 2 else []) +
                           ([(sob[tc % 4], sob[tc % 4].v)]
                            if (tokb is not None and tc >= 4) else []),
                     u=u, done=ln_done,
                     out_f32=x2s[:, (tc % 2) * 1024:(tc % 2) * 1024 + 1024],
                     out_bf=(sb['x2b'][:, (tc % 4) * 1024:(tc % 4) * 1024 + 1024]
                             if (x2fm is not None or tokb is not None) else None))
            so = sos[tc % 2]
            @p.o("y")
            def _(y, tc=tc, w=d_before + 1, so=so):
                y.wait_ge(ln_done.h, w)
                y.dma_start(out=h['x2_tok'][tc * 128:(tc + 1) * 128, :],
                            in_=x2s[:, (tc % 2) * 1024:(tc % 2) * 1024 + 1024]).then_inc(so.h, 16)
            so.v += 16
            if tokb is not None:
                @p.o("y")
                def _(y, tc=tc, w=d_before + 2, so2=sob[tc % 4]):
                    y.wait_ge(ln_done.h, w)
                    y.dma_start(out=tokb[tc * 128:(tc + 1) * 128, :],
                                in_=sb['x2b'][:, (tc % 4) * 1024:(tc % 4) * 1024 + 1024]
                                ).then_inc(so2.h, 16)
                sob[tc % 4].v += 16
            ln_marks.append(ln_done.v)
            if x2fm is not None:
                x2_fm_t, ev_x2 = x2fm
                transpose_tiles(
                    p,
                    src=lambda i, tc=tc: sb['x2b'][:, (tc % 2) * 1024 + i * 128:(tc % 2) * 1024 + i * 128 + 128],
                    dst=lambda i, tc=tc: x2_fm_t[:, i * 2048 + tc * 128: i * 2048 + tc * 128 + 128],
                    n=8, idn=idn, psum_bf=psum_bf, slots=TRAN_SLOTS,
                    waits=[(ln_done, ln_done.v)], ev=ev_x2)
        prev_tail = [(ev_w3, ev_w3.v), (ev_tr, ev_tr.v)]
        if v_hook is not None:
            v_hook(tt)

    @p.o("y")
    def _(y, w0=sos[0].v, w1=sos[1].v):
        y.wait_ge(sos[0].h, w0)
        y.wait_ge(sos[1].h, w1)
    return ln_done, sos, (sob, b_sob)


from concourse.bass_isa import ReduceOp
from concourse.library_config import attnmlp as LIB_ATTNMLP


def ffn1_wprep(p, h, sb, psum_bf, done):
    wt = sb['pw_a']
    sts = [sb['pw_c0'], sb['pw_c1']]
    scs = sb['pw_s']
    idn = sb['idn']
    ld = p.sem(tag='hw')
    stv = p.sem()
    sto = [p.sem(tag='hw'), p.sem(tag='hw')]
    tk = p.sem()
    te = p.sem()
    bld, bsv, btk, bte = ld.v, stv.v, tk.v, te.v
    bst = [sto[0].v, sto[1].v]
    NI = 6

    def _store1(g, fc):
        dst = h['w1q_blk'][fc].rearrange("(kc pp) j -> pp kc j", pp=128)
        src2 = sts[fc % 2][:, 1024:2048].rearrange("pp (kc j) -> pp kc j", j=128)
        g.dma_start(out=dst, in_=src2).then_inc(sto[fc % 2].h, 16)

    @p.o("y")
    def _(g):
        for fc in range(32):
            if fc:
                g.wait_ge(stv.h, bsv + NI * fc)
            if fc >= 2:
                g.wait_ge(te.h, bte + 8 * (fc - 1))
                _store1(g, fc - 2)
            g.dma_start(out=wt[:, 0:1024],
                        in_=h['W1nT'][fc * 128:(fc + 1) * 128, :]).then_inc(ld.h, 16)
        g.wait_ge(te.h, bte + 8 * 31)
        _store1(g, 30)
        g.wait_ge(te.h, bte + 8 * 32)
        _store1(g, 31)
    ld.v += 16 * 32

    @p.o("v")
    def _(v):
        for fc in range(32):
            v.wait_ge(ld.h, bld + 16 * (fc + 1))
            if fc >= 2:
                v.wait_ge(tk.h, btk + 8 * (fc - 1))
            v.reduce_max(scs[:, 36:37], wt[:, 0:1024], axis=AX.X,
                         apply_absolute_value=True).then_inc(stv.h, 1)
            v.wait_ge(stv.h, bsv + NI * fc + 1)
            v.tensor_scalar(scs[:, 36:37], scs[:, 36:37], 1e-8, None, ALU.max).then_inc(stv.h, 1)
            v.wait_ge(stv.h, bsv + NI * fc + 2)
            v.reciprocal(scs[:, 37:38], scs[:, 36:37]).then_inc(stv.h, 1)
            v.wait_ge(stv.h, bsv + NI * fc + 3)
            v.tensor_scalar(scs[:, fc:fc + 1], scs[:, 37:38], 127.0, None, ALU.mult).then_inc(stv.h, 1)
            v.wait_ge(stv.h, bsv + NI * fc + 4)
            v.tensor_scalar(wt[:, 0:1024], wt[:, 0:1024], scs[:, fc:fc + 1], MAGIC,
                            ALU.mult, ALU.add).then_inc(stv.h, 1)
            v.wait_ge(stv.h, bsv + NI * fc + 5)
            v.tensor_scalar(sts[fc % 2][:, 0:1024], wt[:, 0:1024], MAGIC, None,
                            ALU.subtract).then_inc(stv.h, 1)
    stv.v += NI * 32

    @p.o("t")
    def _(t):
        for fc in range(32):
            t.wait_ge(stv.h, bsv + NI * (fc + 1))
            for kc in range(8):
                i = fc * 8 + kc
                if i >= 4:
                    t.wait_ge(te.h, bte + i - 3)
                ps = psum_bf[:, TRAN_SLOTS[i % 4]: TRAN_SLOTS[i % 4] + 128]
                t.transpose(ps, sts[fc % 2][:, kc * 128: kc * 128 + 128], idn[:, :]).then_inc(tk.h, 1)
    tk.v += 32 * 8

    @p.o("s")
    def _(s):
        for fc in range(32):
            for kc in range(8):
                i = fc * 8 + kc
                s.wait_ge(tk.h, btk + i + 1)
                if fc >= 2 and kc == 0:
                    s.wait_ge(sto[fc % 2].h, bst[fc % 2] + 16 * (fc // 2))
                ps = psum_bf[:, TRAN_SLOTS[i % 4]: TRAN_SLOTS[i % 4] + 128]
                s.copy(sts[fc % 2][:, 1024 + kc * 128: 1024 + kc * 128 + 128],
                       ps).then_inc(te.h, 1)
    te.v += 32 * 8

    sto[0].v += 16 * 16
    sto[1].v += 16 * 16

    rm = scs[:, 38:39]
    rr = scs[:, 39:40]
    B2 = stv.v
    ld2 = ld.v
    st2 = [sto[0].v, sto[1].v]

    def _store2(g, k):
        dst = h['w2q_blk'][:, k * 128:(k + 1) * 128, :].rearrange("fc pp j -> pp fc j")
        src2 = sts[k % 2][:, 0:1024].rearrange("pp (fc j) -> pp fc j", j=128)
        g.dma_start(out=dst, in_=src2).then_inc(sto[k % 2].h, 16)

    @p.o("y")
    def _(g):
        for k in range(32):
            if k:
                g.wait_ge(stv.h, B2 + 6 * k)
                _store2(g, k - 1)
            g.dma_start(out=wt[:, 0:1024], in_=h['W2n'][k * 128:(k + 1) * 128, :]).then_inc(ld.h, 16)
        g.wait_ge(stv.h, B2 + 6 * 32)
        _store2(g, 31)
    ld.v += 16 * 32

    @p.o("v")
    def _(v, tev=te.v):
        for k in range(32):
            v.wait_ge(ld.h, ld2 + 16 * (k + 1))
            if k >= 2:
                v.wait_ge(sto[k % 2].h, st2[k % 2] + 16 * (k // 2))
            else:
                v.wait_ge(sto[k % 2].h, st2[k % 2])
                v.wait_ge(te.h, tev)
            v.reduce_max(rm, wt[:, 0:1024], axis=AX.X, apply_absolute_value=True).then_inc(stv.h, 1)
            v.wait_ge(stv.h, B2 + 6 * k + 1)
            v.tensor_scalar(rm, rm, 1e-8, None, ALU.max).then_inc(stv.h, 1)
            v.wait_ge(stv.h, B2 + 6 * k + 2)
            v.reciprocal(rr, rm).then_inc(stv.h, 1)
            v.wait_ge(stv.h, B2 + 6 * k + 3)
            v.tensor_scalar(wt[:, 0:1024], wt[:, 0:1024], rr, 127.0, ALU.mult, ALU.mult).then_inc(stv.h, 1)
            v.wait_ge(stv.h, B2 + 6 * k + 4)
            v.tensor_scalar(wt[:, 0:1024], wt[:, 0:1024], MAGIC, MAGIC, ALU.add, ALU.subtract).then_inc(stv.h, 1)
            v.wait_ge(stv.h, B2 + 6 * k + 5)
            v.tensor_scalar(sts[k % 2][:, 0:1024], wt[:, 0:1024], rm, 1.0 / 127.0,
                            ALU.mult, ALU.mult).then_inc(stv.h, 1)
    stv.v = B2 + 6 * 32

    @p.o("y")
    def _(g, a0=sto[0].v + 16 * 16, a1=sto[1].v + 16 * 16, b=stv.v):
        g.wait_ge(sto[0].h, a0)
        g.wait_ge(sto[1].h, a1)
        g.wait_ge(stv.h, b)
        g.dma_start(out=h['prepflag'][0:1, 0:1], in_=scs[0:1, 0:1]).then_inc(done.h, 16)
    sto[0].v += 16 * 16
    sto[1].v += 16 * 16
    done.v += 16


def ffn1_wprep_quant(p, h, sb, done):
    wt = sb['pw_a'][:, 0:1024]
    sts = [sb['pw_c0'], sb['pw_c1']]
    scs = sb['pw_s']
    ld = p.sem(tag='sw')
    stv = p.sem()
    sto = [p.sem(tag='sw'), p.sem(tag='sw')]
    bld, bsv = ld.v, stv.v
    bst = [sto[0].v, sto[1].v]
    NI = 6

    def _emit_g1():
        @p.o("g")
        def _(g):
            for fc in range(32):
                if fc:
                    g.wait_ge(stv.h, bsv + NI * fc)
                if fc >= 2:
                    g.dma_start(out=h['w1qn'][fc - 2],
                                in_=sts[fc % 2][:, 0:1024]).then_inc(sto[fc % 2].h, 16)
                g.dma_start(out=wt[:, :],
                            in_=h['W1nT'][fc * 128:(fc + 1) * 128, :]).then_inc(ld.h, 16)
            g.wait_ge(stv.h, bsv + NI * 31)
            g.dma_start(out=h['w1qn'][30], in_=sts[0][:, 0:1024]).then_inc(sto[0].h, 16)
            g.wait_ge(stv.h, bsv + NI * 32)
            g.dma_start(out=h['w1qn'][31], in_=sts[1][:, 0:1024]).then_inc(sto[1].h, 16)
        ld.v += 16 * 32
        sto[0].v += 16 * 16
        sto[1].v += 16 * 16

    def _piece1(lo, hi):
        @p.o("v")
        def _(v):
            for fc in range(lo, hi):
                v.wait_ge(ld.h, bld + 16 * (fc + 1))
                if fc >= 2:
                    v.wait_ge(sto[fc % 2].h, bst[fc % 2] + 16 * (fc // 2))
                v.reduce_max(scs[:, 36:37], wt[:, :], axis=AX.X,
                             apply_absolute_value=True).then_inc(stv.h, 1)
                v.wait_ge(stv.h, bsv + NI * fc + 1)
                v.tensor_scalar(scs[:, 36:37], scs[:, 36:37], 1e-8, None, ALU.max).then_inc(stv.h, 1)
                v.wait_ge(stv.h, bsv + NI * fc + 2)
                v.reciprocal(scs[:, 37:38], scs[:, 36:37]).then_inc(stv.h, 1)
                v.wait_ge(stv.h, bsv + NI * fc + 3)
                v.tensor_scalar(scs[:, fc:fc + 1], scs[:, 37:38], 127.0, None, ALU.mult).then_inc(stv.h, 1)
                v.wait_ge(stv.h, bsv + NI * fc + 4)
                v.tensor_scalar(wt[:, :], wt[:, :], scs[:, fc:fc + 1], MAGIC,
                                ALU.mult, ALU.add).then_inc(stv.h, 1)
                v.wait_ge(stv.h, bsv + NI * fc + 5)
                v.tensor_scalar(sts[fc % 2][:, 0:1024], wt[:, :], MAGIC, None,
                                ALU.subtract).then_inc(stv.h, 1)
    stv.v += NI * 32

    rm = scs[:, 38:39]
    rr = scs[:, 39:40]
    B2 = bsv + NI * 32
    ld2 = bld + 16 * 32
    st2 = [bst[0] + 16 * 16, bst[1] + 16 * 16]

    def _store2(g, k):
        dst = h['w2q_blk'][:, :, k * 128:(k + 1) * 128].rearrange("fc pp j -> pp fc j")
        src2 = sts[k % 2][:, 0:1024].rearrange("pp (fc j) -> pp fc j", j=128)
        g.dma_start(out=dst, in_=src2).then_inc(sto[k % 2].h, 16)

    def _emit_g2():
        @p.o("g")
        def _(g):
            for k in range(32):
                if k:
                    g.wait_ge(stv.h, B2 + 6 * k)
                    _store2(g, k - 1)
                g.dma_start(out=wt[:, :], in_=h['W2n'][k * 128:(k + 1) * 128, :]).then_inc(ld.h, 16)
            g.wait_ge(stv.h, B2 + 6 * 32)
            _store2(g, 31)
        ld.v += 16 * 32

    def _piece2(lo, hi):
        @p.o("v")
        def _(v):
            for k in range(lo, hi):
                v.wait_ge(ld.h, ld2 + 16 * (k + 1))
                if k >= 2:
                    v.wait_ge(sto[k % 2].h, st2[k % 2] + 16 * (k // 2))
                else:
                    v.wait_ge(sto[k % 2].h, st2[k % 2])
                v.reduce_max(rm, wt[:, :], axis=AX.X, apply_absolute_value=True).then_inc(stv.h, 1)
                v.wait_ge(stv.h, B2 + 6 * k + 1)
                v.tensor_scalar(rm, rm, 1e-8, None, ALU.max).then_inc(stv.h, 1)
                v.wait_ge(stv.h, B2 + 6 * k + 2)
                v.reciprocal(rr, rm).then_inc(stv.h, 1)
                v.wait_ge(stv.h, B2 + 6 * k + 3)
                v.tensor_scalar(wt[:, :], wt[:, :], rr, 127.0, ALU.mult, ALU.mult).then_inc(stv.h, 1)
                v.wait_ge(stv.h, B2 + 6 * k + 4)
                v.tensor_scalar(wt[:, :], wt[:, :], MAGIC, MAGIC, ALU.add, ALU.subtract).then_inc(stv.h, 1)
                v.wait_ge(stv.h, B2 + 6 * k + 5)
                v.tensor_scalar(sts[k % 2][:, 0:1024], wt[:, :], rm, 1.0 / 127.0,
                                ALU.mult, ALU.mult).then_inc(stv.h, 1)
    stv.v = B2 + 6 * 32

    def _emit_sentinel():
        @p.o("g")
        def _(g, a0=bst[0] + 16 * 32, a1=bst[1] + 16 * 32, b=B2 + 6 * 32):
            g.wait_ge(sto[0].h, a0)
            g.wait_ge(sto[1].h, a1)
            g.wait_ge(stv.h, b)
            g.dma_start(out=h['prepflag'][0:1, 0:1], in_=scs[0:1, 0:1]).then_inc(done.h, 16)
        sto[0].v += 16 * 16
        sto[1].v += 16 * 16
        done.v += 16

    def emit_g():
        _emit_g1()
        _emit_g2()
        _emit_sentinel()
    return ([lambda: _piece1(0, 16), lambda: _piece1(16, 32),
             lambda: _piece2(0, 16), lambda: _piece2(16, 32)], emit_g)


def ffn1_blockify(p, h, sb, psum_bf, idn, quant_done, done):
    stv2 = sb['pw_b'][:, 0:2048].bitcast(BF16)
    sts = [stv2[:, 0:1024], stv2[:, 1024:2048]]
    evb = sb['pw_b'][:, 2048:3072].bitcast(BF16)
    evs = [evb[:, 0:1024], evb[:, 1024:2048]]
    ld = [p.sem(tag='sw'), p.sem(tag='sw')]
    sto = [p.sem(tag='sw'), p.sem(tag='sw')]
    tk = p.sem()
    te = p.sem()
    bld = [ld[0].v, ld[1].v]
    btk, bte = tk.v, te.v
    bst = [sto[0].v, sto[1].v]

    @p.o("g")
    def _(g, qd=(quant_done, quant_done.v)):
        g.wait_ge(qd[0].h, qd[1])
        for fc in range(32):
            if fc >= 2:
                g.wait_ge(tk.h, btk + 8 * (fc - 1))
            g.dma_start(out=sts[fc % 2][:, :],
                        in_=h['w1qn'][fc]).then_inc(ld[fc % 2].h, 16)
            if fc >= 2:
                g.wait_ge(te.h, bte + 8 * (fc - 1))
                g.dma_start(out=h['w1q_blk'][fc - 2],
                            in_=evs[fc % 2][:, :]).then_inc(sto[fc % 2].h, 16)
        g.wait_ge(te.h, bte + 8 * 31)
        g.dma_start(out=h['w1q_blk'][30], in_=evs[0][:, :]).then_inc(sto[0].h, 16)
        g.wait_ge(te.h, bte + 8 * 32)
        g.dma_start(out=h['w1q_blk'][31], in_=evs[1][:, :]).then_inc(sto[1].h, 16)
        g.wait_ge(sto[0].h, bst[0] + 16 * 16)
        g.wait_ge(sto[1].h, bst[1] + 16 * 16)
        g.dma_start(out=h['prepflag'][0:1, 2:3], in_=sb['pw_s'][0:1, 0:1]).then_inc(done.h, 16)
    ld[0].v += 16 * 16
    ld[1].v += 16 * 16

    @p.o("t")
    def _(t):
        for fc in range(32):
            t.wait_ge(ld[fc % 2].h, bld[fc % 2] + 16 * (fc // 2 + 1))
            for kc in range(8):
                i = fc * 8 + kc
                if i >= 4:
                    t.wait_ge(te.h, bte + i - 3)
                ps = psum_bf[:, TRAN_SLOTS[i % 4]: TRAN_SLOTS[i % 4] + 128]
                t.transpose(ps, sts[fc % 2][:, kc * 128: kc * 128 + 128],
                            idn[:, :]).then_inc(tk.h, 1)
    tk.v += 32 * 8

    @p.o("s")
    def _(s):
        for fc in range(32):
            for kc in range(8):
                i = fc * 8 + kc
                s.wait_ge(tk.h, btk + i + 1)
                if fc >= 2 and kc == 0:
                    s.wait_ge(sto[fc % 2].h, bst[fc % 2] + 16 * (fc // 2))
                ps = psum_bf[:, TRAN_SLOTS[i % 4]: TRAN_SLOTS[i % 4] + 128]
                s.copy(evs[fc % 2][:, kc * 128: kc * 128 + 128], ps).then_inc(te.h, 1)
    te.v += 32 * 8

    sto[0].v += 16 * 16
    sto[1].v += 16 * 16
    done.v += 16


def allreduce_max_scalar(p, *, local_11, bounce_in, bounce_out, rg, out_11,
                         tmp18, waits, done):
    vz = p.sem()
    cc = p.sem(tag='co')
    dmx = p.sem(tag='sw')
    b_vz, b_cc, b_dmx = vz.v, cc.v, dmx.v

    @p.o("v")
    def _(v, w=list(waits)):
        for s, val in w:
            v.wait_ge(s.h, val)
        v.tensor_copy(tmp18[0:1, 0:8], local_11.broadcast_to([1, 8])).then_inc(vz.h, 1)
    vz.v += 1

    @p.o("g")
    def _(g):
        g.wait_ge(vz.h, b_vz + 1)
        g.dma_start(out=bounce_in[:, :], in_=tmp18[0:1, 0:8]).then_inc(dmx.h, 16)
        g.wait_ge(dmx.h, b_dmx + 16)
        g.collective_compute("AllReduce", ALU.max, replica_groups=rg,
                             ins=[bounce_in.ap().opt()],
                             outs=[bounce_out.ap().opt()]).then_inc(cc.h, 1)
        g.wait_ge(cc.h, b_cc + 1)
        g.dma_start(out=out_11, in_=bounce_out[0:1, 0:1]).then_inc(done.h, 16)
    dmx.v += 16
    cc.v += 1
    done.v += 16


def bcast_row(p, *, onesf, row, out, psum, bank, waits, done):
    mmv = p.sem()
    b_mm = mmv.v

    @p.o("t")
    def _(t, w=list(waits)):
        for s, val in w:
            t.wait_ge(s.h, val)
        t.matmul(psum[:, bank * 512: bank * 512 + row.shape[1]], onesf, row,
                 start=True, stop=True).then_inc(mmv.h, 1)
    mmv.v += 1

    @p.o("v")
    def _(v):
        v.wait_ge(mmv.h, b_mm + 1)
        v.tensor_copy(out, psum[:, bank * 512: bank * 512 + row.shape[1]]).then_inc(done.h, 1)
    done.v += 1


def colmax_to_scalar(p, *, col, dram_flat, flat128, out_11, waits, done):
    dmx = p.sem(tag='sw')
    b_dmx = dmx.v

    @p.o("g")
    def _(g, w=list(waits)):
        for s, val in w:
            g.wait_ge(s.h, val)
        g.dma_start(out=dram_flat[0:128], in_=col,
                    allow_slow_non_contiguous=True).then_inc(dmx.h, 16)
        g.wait_ge(dmx.h, b_dmx + 16)
        g.dma_start(out=flat128[0:1, 0:128], in_=dram_flat[0:128]).then_inc(dmx.h, 16)
    dmx.v += 32

    @p.o("v")
    def _(v):
        v.wait_ge(dmx.h, b_dmx + 32)
        v.reduce_max(out_11, flat128[0:1, 0:128], axis=AX.X).then_inc(done.h, 1)
    done.v += 1


def ffn1_phase(p, h, sb, psum, psum_bf, x_ready, rg, prep_done, TT=512, xamax=None):
    NTILE = 2048 // TT
    NCH = TT // 128
    x3f = sb['x3f']
    fhid = sb['fhid']
    w8 = sb['w8']
    w32 = sb['w32']
    wo = sb['w3o']
    wotok = sb['w3tok']
    xs_f32 = sb['xs_f32']
    x2s = sb['x2s']
    idn = sb['idn']
    qs = sb['qs']
    q1 = sb['q1']
    scs = sb['pw_s']
    ep1 = sb['ep1']
    binc = sb['binc']
    boutc = sb['boutc']
    hmv = sb['hmv']
    onesf = sb['onesf']

    svl = p.sem(tag='sw')

    @p.o("g")
    def _(g):
        g.dma_start(out=binc[:, :], in_=h['b_in_c'][:, :]).then_inc(svl.h, 16)
        g.dma_start(out=boutc[:, :], in_=h['b_out_c'][:, :]).then_inc(svl.h, 16)
    svl.v += 32
    svlf = svl.v

    cz = p.sem()

    @p.o("v")
    def _(v):
        v.memset(onesf[0:1, 0:128], 1.0).then_inc(cz.h, 1)
    cz.v += 1
    czf = cz.v

    if xamax is None:
        am = p.sem()
        b_am = am.v

        @p.o("v")
        def _(v, xw=[x_ready]):
            for s, val in xw:
                v.wait_ge(s.h, val)
            for kc in range(8):
                v.reduce_max(qs[:, 8 + kc:9 + kc], x3f[:, kc * 2048:(kc + 1) * 2048],
                             axis=AX.X, apply_absolute_value=True).then_inc(am.h, 1)
            v.wait_ge(am.h, b_am + 8)
            v.reduce_max(qs[:, 0:1], qs[:, 8:16], axis=AX.X).then_inc(am.h, 1)
        am.v += 9

        colmax_to_scalar(p, col=qs[:, 0:1], dram_flat=h['flat_sc'][0],
                         flat128=sb['q1x'], out_11=q1[0:1, 0:1],
                         waits=[(am, am.v)], done=am)
        arx = p.sem(tag='sw')
        allreduce_max_scalar(p, local_11=q1[0:1, 0:1], bounce_in=h['arm1'],
                             bounce_out=h['arm1o'], rg=rg, out_11=q1[0:1, 1:2],
                             tmp18=sb['tmp18'], waits=[(am, am.v)], done=arx)
        amx_src = None
    else:
        amx_src, arx = xamax

    sq = p.sem()
    b_sq = sq.v
    off = 1 if amx_src is not None else 0

    @p.o("v")
    def _(v, w=arx.v, amx_src=amx_src):
        v.wait_ge(arx.h, w)
        if amx_src is not None:
            v.tensor_copy(q1[0:1, 1:2], amx_src).then_inc(sq.h, 1)
            v.wait_ge(sq.h, b_sq + 1)
        v.tensor_scalar(q1[0:1, 1:2], q1[0:1, 1:2], 1e-8, None, ALU.max).then_inc(sq.h, 1)
        v.wait_ge(sq.h, b_sq + off + 1)
        v.reciprocal(q1[0:1, 2:3], q1[0:1, 1:2]).then_inc(sq.h, 1)
        v.wait_ge(sq.h, b_sq + off + 2)
        v.tensor_scalar(q1[0:1, 2:3], q1[0:1, 2:3], 32767.0, None, ALU.mult).then_inc(sq.h, 1)
        v.wait_ge(sq.h, b_sq + off + 3)
        v.tensor_scalar(q1[0:1, 3:4], q1[0:1, 1:2], 1.0 / 32767.0, None, ALU.mult).then_inc(sq.h, 1)
    sq.v = b_sq + off + 4

    bc1 = p.sem()
    bcast_row(p, onesf=onesf[0:1, 0:128], row=q1[0:1, 2:4], out=qs[:, 2:4],
              psum=psum, bank=0, waits=[(sq, sq.v), (cz, czf)], done=bc1)

    @p.o("v")
    def _(v, w=prep_done.v, b=bc1.v, xw=[x_ready]):
        v.wait_ge(prep_done.h, w)
        v.wait_ge(bc1.h, b)
        for s, val in xw:
            v.wait_ge(s.h, val)
        v.reciprocal(ep1[:, :], scs[:, 0:32]).then_inc(sq.h, 1)
        v.wait_ge(sq.h, b_sq + off + 5)
        v.tensor_scalar(ep1[:, :], ep1[:, :], qs[:, 3:4], None, ALU.mult).then_inc(sq.h, 1)
        tmpf = sb['pw_a']
        for kc in range(8):
            v.wait_ge(sq.h, b_sq + off + 6 + 2 * kc)
            v.tensor_scalar(tmpf[:, 0:2048], x3f[:, kc * 2048:(kc + 1) * 2048],
                            qs[:, 2:3], MAGIC, ALU.mult, ALU.add).then_inc(sq.h, 1)
            v.wait_ge(sq.h, b_sq + off + 7 + 2 * kc)
            v.tensor_scalar(x3f[:, kc * 2048:(kc + 1) * 2048], tmpf[:, 0:2048],
                            MAGIC, None, ALU.subtract).then_inc(sq.h, 1)
    sq.v = b_sq + off + 6 + 16
    xq_ready = (sq, sq.v)

    u = p.sem()
    ln_done = p.sem()
    sxs = [p.sem(tag='hw'), p.sem(tag='hw')]
    sos = [p.sem(tag='hw'), p.sem(tag='hw')]
    hsp_st = [p.sem(tag='hw'), p.sem(tag='hw')]
    ln_marks = []
    ev_h = p.sem()
    ev_wo = p.sem()
    ev_tr = p.sem()
    hmx = p.sem()
    sh_s1 = (p.sem(), [p.sem(tag='hw'), p.sem(tag='hw')])
    sh_s2 = (p.sem(), [p.sem(tag='hw'), p.sem(tag='hw'), p.sem(tag='hw')])
    prev_tail = []

    for tt in range(NTILE):
        t0 = tt * TT

        def evac_g(fc, ps, waits, ev, tt=tt):
            @p.o("s")
            def _(s, w=list(waits), fc=fc, ps=ps):
                for sm, val in w:
                    s.wait_ge(sm.h, val)
                s.wait_ge(svl.h, svlf)
                s.activation(fhid[:, fc * TT:(fc + 1) * TT], ps, AF.Gelu,
                             scale=ep1[:, fc:fc + 1], bias=binc[:, fc:fc + 1]).then_inc(ev.h, 1)

            @p.o("v")
            def _(v, w=ev_h.v + (fc + 1), fc=fc):
                v.wait_ge(ev_h.h, w)
                v.reduce_max(hmv[:, fc:fc + 1], fhid[:, fc * TT:(fc + 1) * TT],
                             axis=AX.X, apply_absolute_value=True).then_inc(hmx.h, 1)
        fm_proj(p, w_dram=lambda fc: h['w1q_blk'][fc],
                w_hw=True, shared=sh_s1, n_fc=32, n_k=8,
                w_guard=([(prep_done, prep_done.v)] if tt == 0 else []),
                x_rhs=lambda k, o, n, t0=t0: x3f[:, k * 2048 + t0 + o: k * 2048 + t0 + o + n],
                ttile=TT, wbufs=w8[0:2], psum=psum, slot_banks=[0, 1] if TT == 512 else [0, 2],
                evac=evac_g, ev=ev_h, x_waits=[xq_ready] + prev_tail)
        hb = hmx.v
        hmx.v += 32

        @p.o("v")
        def _(v, hb=hb, tt=tt):
            v.wait_ge(hmx.h, hb + 32)
            if tt == 0:
                v.reduce_max(qs[:, 8:9], hmv[:, :], axis=AX.X).then_inc(hmx.h, 1)
            else:
                v.reduce_max(qs[:, 9:10], hmv[:, :], axis=AX.X).then_inc(hmx.h, 1)
                v.wait_ge(hmx.h, hb + 33)
                v.tensor_tensor(qs[:, 8:9], qs[:, 8:9], qs[:, 9:10], ALU.max).then_inc(hmx.h, 1)
        hmx.v += 1 if tt == 0 else 2

        @p.o("y")
        def _(y, w=ev_h.v, tt=tt):
            y.wait_ge(ev_h.h, w)
            y.dma_start(out=h['hsp'][tt], in_=fhid[:, :]).then_inc(hsp_st[tt % 2].h, 16)
        hsp_st[tt % 2].v += 16
        prev_tail = [(hsp_st[tt % 2], hsp_st[tt % 2].v), (hmx, hmx.v)]

    colmax_to_scalar(p, col=qs[:, 8:9], dram_flat=h['flat_sc'][1],
                     flat128=sb['q1x'], out_11=q1[0:1, 4:5],
                     waits=[(hmx, hmx.v)], done=hmx)
    arh = p.sem(tag='sw')
    allreduce_max_scalar(p, local_11=q1[0:1, 4:5], bounce_in=h['arm2'],
                         bounce_out=h['arm2o'], rg=rg, out_11=q1[0:1, 5:6],
                         tmp18=sb['tmp18'], waits=[(hmx, hmx.v)], done=arh)

    @p.o("v")
    def _(v, w=arh.v, b=sq.v):
        v.wait_ge(arh.h, w)
        v.tensor_scalar(q1[0:1, 5:6], q1[0:1, 5:6], 1e-8, None, ALU.max).then_inc(sq.h, 1)
        v.wait_ge(sq.h, b + 1)
        v.reciprocal(q1[0:1, 6:7], q1[0:1, 5:6]).then_inc(sq.h, 1)
        v.wait_ge(sq.h, b + 2)
        v.tensor_scalar(q1[0:1, 6:7], q1[0:1, 6:7], 32767.0, None, ALU.mult).then_inc(sq.h, 1)
        v.wait_ge(sq.h, b + 3)
        v.tensor_scalar(q1[0:1, 7:8], q1[0:1, 5:6], 1.0 / 32767.0, None, ALU.mult).then_inc(sq.h, 1)
    sq.v += 4
    bc2 = p.sem()
    bcast_row(p, onesf=onesf[0:1, 0:128], row=q1[0:1, 6:8], out=qs[:, 6:8],
              psum=psum, bank=0, waits=[(sq, sq.v)], done=bc2)
    s_h_ready = (bc2, bc2.v)

    for tt in range(NTILE):
        rb = p.sem(tag='hw')

        @p.o("y")
        def _(y, tt=tt, w=(ev_wo.v if tt else 0), rb=rb,
              spill_w=[(s, s.v) for s in hsp_st] + [(hmx, hmx.v)]):
            if tt:
                y.wait_ge(ev_wo.h, w)
            else:
                for s, val in spill_w:
                    y.wait_ge(s.h, val)
            y.dma_start(out=fhid[:, :], in_=h['hsp'][tt]).then_inc(rb.h, 16)
        rb.v += 16

        NKC = (32 * TT) // 2048

        @p.o("v")
        def _(v, rb=rb, rbb_=rb.v, b=sq.v, shr=s_h_ready):
            v.wait_ge(rb.h, rbb_)
            v.wait_ge(shr[0].h, shr[1])
            tmpf = sb['pw_a']
            for kc in range(NKC):
                v.wait_ge(sq.h, b + 2 * kc)
                v.tensor_scalar(tmpf[:, 0:2048], fhid[:, kc * 2048:(kc + 1) * 2048],
                                qs[:, 6:7], MAGIC, ALU.mult, ALU.add).then_inc(sq.h, 1)
                v.wait_ge(sq.h, b + 2 * kc + 1)
                v.tensor_scalar(fhid[:, kc * 2048:(kc + 1) * 2048], tmpf[:, 0:2048],
                                MAGIC, None, ALU.subtract).then_inc(sq.h, 1)
        hq_ready = (sq, sq.v + 2 * NKC)
        sq.v += 2 * NKC

        def evac_o(fc, ps, waits, ev):
            @p.o("v")
            def _(v, w=list(waits), fc=fc, ps=ps):
                for sm, val in w:
                    v.wait_ge(sm.h, val)
                v.wait_ge(svl.h, svlf)
                v.tensor_scalar(wo[:, fc * TT:(fc + 1) * TT], ps, qs[:, 7:8],
                                boutc[:, fc:fc + 1], ALU.mult, ALU.add).then_inc(ev.h, 1)
        fm_proj(p, w_dram=lambda fc: h['w2q_blk'][fc],
                w_hw=True, shared=sh_s2, n_fc=8, n_k=32,
                w_guard=([(prep_done, prep_done.v)] if tt == 0 else []),
                x_rhs=lambda k, o, n: fhid[:, k * TT + o: k * TT + o + n],
                ttile=TT, wbufs=w32, psum=psum, slot_banks=[0, 1] if TT == 512 else [0, 2],
                evac=evac_o, ev=ev_wo, x_waits=[hq_ready])

        ewo0 = ev_wo.v
        n_tr = 8 * NCH
        transpose_tiles(
            p,
            src=lambda i: wo[:, (i % 8) * TT + (i // 8) * 128: (i % 8) * TT + (i // 8) * 128 + 128],
            dst=lambda i: wotok[:, (i // 8) * 1024 + (i % 8) * 128: (i // 8) * 1024 + (i % 8) * 128 + 128],
            n=n_tr, idn=idn, psum_bf=psum_bf, slots=TRAN_SLOTS,
            waits=[(ev_wo, ewo0)] + ([(ln_done, ln_done.v)] if tt else []), ev=ev_tr)
        etr_base = ev_tr.v - n_tr

        for c8 in range(NCH):
            tc = tt * NCH + c8
            sx = sxs[tc % 2]

            @p.o("y")
            def _(y, tc=tc, need=(ln_marks[tc - 2] if tc >= 2 else 0), sx=sx):
                if tc >= 2:
                    y.wait_ge(ln_done.h, need)
                y.dma_start(out=xs_f32[:, (tc % 2) * 1024:(tc % 2) * 1024 + 1024],
                            in_=h['x_tok'][tc * 128:(tc + 1) * 128, :]).then_inc(sx.h, 16)
            sx.v += 16
            d_before = ln_done.v
            ln_chunk(p,
                     a_ap=xs_f32[:, (tc % 2) * 1024:(tc % 2) * 1024 + 1024],
                     b_ap=wotok[:, c8 * 1024:(c8 + 1) * 1024],
                     tmp=sb['ln_tmp'][:, :], trash_bf=sb['ln_tr'][:, :], sc=sb['ln_sc'],
                     n_f=1024,
                     waits=[(sx, sx.v), (ev_tr, etr_base + n_tr)] +
                           ([(sos[tc % 2], sos[tc % 2].v)] if tc >= 2 else []),
                     u=u, done=ln_done,
                     out_f32=x2s[:, (tc % 2) * 1024:(tc % 2) * 1024 + 1024])
            so = sos[tc % 2]

            @p.o("y")
            def _(y, tc=tc, w=d_before + 1, so=so):
                y.wait_ge(ln_done.h, w)
                y.dma_start(out=h['x4_tok'][tc * 128:(tc + 1) * 128, :],
                            in_=x2s[:, (tc % 2) * 1024:(tc % 2) * 1024 + 1024]).then_inc(so.h, 16)
            so.v += 16
            ln_marks.append(ln_done.v)

    @p.o("y")
    def _(y, w0=sos[0].v, w1=sos[1].v):
        y.wait_ge(sos[0].h, w0)
        y.wait_ge(sos[1].h, w1)
    return ln_done, sos


def attn1_phase(p, h, sb, psum, psum_bf, x_ready, rg, absmax_col=None):
    x2f = sb['x2f']
    xt2 = sb['xt2']
    qf = sb['qf']
    kf = sb['kf']
    vaug = sb['vaug']
    aout = sb['aout']
    pbuf = sb['pbuf']
    tau = sb['tau']
    rrow = sb['rrow']
    wq3 = sb['wq3']
    wv = sb['wv']
    wos = sb['wos']
    onesf = sb['onesf']

    agv = p.sem(tag='sw')
    cc = p.sem(tag='co')
    b_agv = agv.v

    @p.o("g")
    def _(g, xw=[x_ready]):
        for s, val in xw:
            g.wait_ge(s.h, val)
        g.dma_start(out=h['agx_in'].ap().rearrange("(fc pp t) -> pp fc t", fc=8, pp=128),
                    in_=x2f[:, :].rearrange("pp (fc t) -> pp fc t", t=2048)).then_inc(agv.h, 16)
        g.wait_ge(agv.h, b_agv + 16)
        g.collective_compute("AllGather", ALU.bypass, replica_groups=rg,
                             ins=[h['agx_in'].ap().opt()],
                             outs=[h['agx_out'].ap().opt()]).then_inc(cc.h, 1)
    agv.v += 16
    cc.v += 1

    cl = p.sem(tag='sw')

    @p.o("g")
    def _(g, agw=agv.v):
        g.wait_ge(agv.h, agw)
        for j in range(4):
            g.dma_start(out=tau[:, j * 512:(j + 1) * 512], in_=h['tauj'][j]).then_inc(cl.h, 16)
        for i in range(4):
            dst = wq3[:, i * 1024: i * 1024 + 1024].rearrange("pp (kc j) -> pp kc j", j=128)
            g.dma_start(out=dst, in_=h['Wqb'][i].rearrange("(kc pp) j -> pp kc j", pp=128)).then_inc(cl.h, 16)
        for i in range(4):
            dst = wq3[:, (4 + i) * 1024: (4 + i) * 1024 + 1024].rearrange("pp (kc j) -> pp kc j", j=128)
            g.dma_start(out=dst, in_=h['Wkb'][i].rearrange("(kc pp) j -> pp kc j", pp=128)).then_inc(cl.h, 16)
        g.dma_start(out=wv[:, :].rearrange("pp (kc j) -> pp kc j", j=512),
                    in_=h['Wvb'].rearrange("(kc pp) j -> pp kc j", pp=128)).then_inc(cl.h, 16)
        g.dma_start(out=sb['bo_c'][:, :], in_=h['bo_c'][:, :]).then_inc(cl.h, 16)
    cl.v += 16 * 14

    vz = p.sem()
    b_vz = vz.v

    @p.o("v")
    def _(v):
        v.memset(vaug[:, :], 0.0).then_inc(vz.h, 1)
        v.wait_ge(vz.h, b_vz + 1)
        ones_ap = vaug[:, :].rearrange("pp (tc hh s) -> pp tc hh s", tc=32, hh=8)[:, :, :, 64:65]
        v.memset(ones_ap, 1.0).then_inc(vz.h, 1)
    vz.v += 2

    xl = [p.sem(tag='sw'), p.sem(tag='sw')]
    mmq = p.sem()
    evq = p.sem()
    b_xl = [xl[0].v, xl[1].v]
    b_mmq, b_evq = mmq.v, evq.v

    @p.o("g")
    def _(g, ccv=cc.v):
        g.wait_ge(cc.h, ccv)
        for tt in range(4):
            if tt >= 2:
                g.wait_ge(mmq.h, b_mmq + 16 * (tt - 1))
            src = h['agx_out'].ap().rearrange("r (fc pp t) -> r fc pp t", fc=8, pp=128)
            blk, tloc = (tt * 1024) // 2048, (tt * 1024) % 2048
            g.dma_start(out=xt2[tt % 2][:, :].rearrange("pp (fc t) -> pp fc t", t=1024),
                        in_=src[blk, :, :, tloc:tloc + 1024].rearrange("fc pp t -> pp fc t")
                        ).then_inc(xl[tt % 2].h, 16)
    xl[0].v += 32
    xl[1].v += 32

    @p.o("t")
    def _(t, clv=cl.v, vzv=vz.v):
        t.wait_ge(cl.h, clv)
        t.wait_ge(vz.h, vzv)
        n = 0
        for tt in range(4):
            t.wait_ge(xl[tt % 2].h, b_xl[tt % 2] + 16 * (tt // 2 + 1))
            x2t = xt2[tt % 2]
            for pi in range(2):
                for fc in range(4):
                    if n >= 2:
                        t.wait_ge(evq.h, b_evq + n - 1)
                    b0 = (n % 2) * 1024
                    for nt in range(2):
                        ps = psum[:, b0 + nt * 512: b0 + nt * 512 + 512]
                        for k in range(8):
                            mi = t.matmul(ps, wq3[:, (pi * 4 + fc) * 1024 + k * 128:
                                                  (pi * 4 + fc) * 1024 + k * 128 + 128],
                                          x2t[:, k * 1024 + nt * 512: k * 1024 + nt * 512 + 512],
                                          start=(k == 0), stop=(k == 7))
                        if nt == 1:
                            mi.then_inc(mmq.h, 1)
                    n += 1
            for tc8 in range(8):
                if n >= 2:
                    t.wait_ge(evq.h, b_evq + n - 1)
                b0 = (n % 2) * 1024
                ps = psum[:, b0: b0 + 512]
                for k in range(8):
                    mi = t.matmul(ps, x2t[:, k * 1024 + tc8 * 128: k * 1024 + tc8 * 128 + 128],
                                  wv[:, k * 512: k * 512 + 512],
                                  start=(k == 0), stop=(k == 7))
                mi.then_inc(mmq.h, 1)
                n += 1
    mmq.v += 64

    n = 0
    for tt in range(4):
        for pi in range(2):
            for fc in range(4):
                @p.o("v")
                def _(v, n=n, tt=tt, pi=pi, fc=fc):
                    v.wait_ge(mmq.h, b_mmq + n + 1)
                    dstbuf = qf if pi == 0 else kf
                    v.tensor_copy(dstbuf[:, fc * 4096 + tt * 1024: fc * 4096 + tt * 1024 + 1024],
                                  psum[:, (n % 2) * 1024: (n % 2) * 1024 + 1024]).then_inc(evq.h, 1)
                n += 1
        for tc8 in range(8):
            @p.o("v")
            def _(v, n=n, tt=tt, tc8=tc8):
                v.wait_ge(mmq.h, b_mmq + n + 1)
                tc = tt * 8 + tc8
                dst = vaug[:, tc * 528: tc * 528 + 528].rearrange("pp (hh s) -> pp hh s", s=66)[:, :, 0:64]
                v.tensor_copy(dst, psum[:, (n % 2) * 1024: (n % 2) * 1024 + 512]
                              .rearrange("pp (hh d) -> pp hh d", d=64)).then_inc(evq.h, 1)
            n += 1
    evq.v += 64
    qkv_done = (evq, evq.v)

    sc_mm = p.sem()
    sc_ex = p.sem()
    sc_ms = p.sem()
    sc_av = p.sem()
    sc_ep = p.sem()
    ep_bc = p.sem()
    ep_rr = p.sem()
    b_smm, b_sex, b_sav, b_sep = sc_mm.v, sc_ex.v, sc_av.v, sc_ep.v
    b_epbc = ep_bc.v

    step = 0
    epn = 0
    AVB = [4, 5, 6]
    pend_av = [None]
    pend_ep = [None]

    def emit_pend_av():
        if pend_av[0] is not None:
            p.o("t")(pend_av[0])
            pend_av[0] = None

    def emit_pend_ep():
        if pend_ep[0] is not None:
            p.o("t")(pend_ep[0])
            pend_ep[0] = None

    for h8 in range(8):
        for qt in range(8):
            avb = AVB[(h8 * 8 + qt) % 3]
            first = True
            for g_ in range(qt + 1):
                for half in range(2):
                    st = step % 2
                    b0 = st * 1024
                    kt0 = g_ * 4 + half * 2

                    @p.o("t")
                    def _(t, h8=h8, qt=qt, st=st, b0=b0, kt0=kt0,
                          step=step, qkv=qkv_done):
                        if step == 0:
                            t.wait_ge(qkv[0].h, qkv[1])
                        if step >= 2:
                            t.wait_ge(sc_ex.h, b_sex + step - 1)
                        for i in range(2):
                            kt = kt0 + i
                            ps = psum[:, b0 + i * 512: b0 + i * 512 + 512]
                            mi = t.matmul(ps,
                                          kf[(h8 % 2) * 64:(h8 % 2) * 64 + 64,
                                             (h8 // 2) * 4096 + kt * 128:(h8 // 2) * 4096 + kt * 128 + 128],
                                          qf[(h8 % 2) * 64:(h8 % 2) * 64 + 64,
                                             (h8 // 2) * 4096 + qt * 512:(h8 // 2) * 4096 + qt * 512 + 512],
                                          start=True, stop=True)
                            if i == 1:
                                mi.then_inc(sc_mm.h, 1)
                    sc_mm.v += 1
                    emit_pend_av()
                    emit_pend_ep()

                    @p.o("s")
                    def _(s, step=step, st=st, b0=b0):
                        s.wait_ge(sc_mm.h, b_smm + step + 1)
                        if step >= 2:
                            s.wait_ge(sc_av.h, b_sav + step - 1)
                        s.activation(pbuf[:, (step % 2) * 1024:(step % 2) * 1024 + 1024],
                                     psum[:, b0: b0 + 1024], AF.Exp,
                                     scale=0.125).then_inc(sc_ex.h, 1)
                    sc_ex.v += 1

                    if g_ == qt:
                        @p.o("v")
                        def _(v, step=step, half=half):
                            v.wait_ge(sc_ex.h, b_sex + step + 1)
                            for i in range(2):
                                v.tensor_tensor(pbuf[:, (step % 2) * 1024 + i * 512:
                                                     (step % 2) * 1024 + i * 512 + 512],
                                                pbuf[:, (step % 2) * 1024 + i * 512:
                                                     (step % 2) * 1024 + i * 512 + 512],
                                                tau[:, (half * 2 + i) * 512:(half * 2 + i) * 512 + 512],
                                                ALU.mult).then_inc(sc_ms.h, 1)
                        sc_ms.v += 2

                    def _av(t, step=step, avb=avb, h8=h8, first=first,
                            msv=sc_ms.v, exv=sc_ex.v, kt0=kt0, epn=epn):
                        t.wait_ge(sc_ex.h, exv)
                        t.wait_ge(sc_ms.h, msv)
                        if first and epn >= 3:
                            t.wait_ge(sc_ep.h, b_sep + 2 * epn - 4)
                        for i in range(2):
                            tc = kt0 + i
                            mi = t.matmul(psum[0:66, avb * 512: avb * 512 + 512],
                                          vaug[:, tc * 528 + (h8 % 8) * 66: tc * 528 + (h8 % 8) * 66 + 66],
                                          pbuf[:, (step % 2) * 1024 + i * 512:
                                               (step % 2) * 1024 + i * 512 + 512],
                                          start=(first and i == 0), stop=False,
                                          skip_group_check=True)
                            if i == 1:
                                mi.then_inc(sc_av.h, 1)
                    pend_av[0] = _av
                    sc_av.v += 1
                    first = False
                    step += 1

            @p.o("v")
            def _(v, avb=avb, avv=sc_av.v, epn=epn):
                v.wait_ge(sc_av.h, avv)
                if epn >= 2:
                    v.wait_ge(ep_bc.h, b_epbc + epn - 1)
                v.reciprocal(rrow[0:1, (epn % 2) * 512:(epn % 2) * 512 + 512],
                             psum[64:65, avb * 512: avb * 512 + 512]
                             ).then_inc(ep_rr.h, 1)
            ep_rr.v += 1

            def _bc(t, rrv=ep_rr.v, epn=epn):
                t.wait_ge(ep_rr.h, rrv)
                if epn > 0:
                    t.wait_ge(sc_ep.h, b_sep + 2 * epn - 1)
                t.matmul(psum[0:64, 7 * 512: 7 * 512 + 512], onesf[0:1, 0:64],
                         rrow[0:1, (epn % 2) * 512:(epn % 2) * 512 + 512],
                         start=True, stop=True).then_inc(ep_bc.h, 1)
            pend_ep[0] = _bc
            ep_bc.v += 1

            @p.o("v")
            def _(v, avb=avb, bcv=ep_bc.v, h8=h8, qt=qt, sv=sc_ep.v):
                v.wait_ge(ep_bc.h, bcv)
                v.wait_ge(sc_ep.h, sv)
                v.tensor_copy(sb['rbb'][0:64, 0:512],
                              psum[0:64, 7 * 512: 7 * 512 + 512]).then_inc(sc_ep.h, 1)
                v.wait_ge(sc_ep.h, sv + 1)
                v.tensor_tensor(aout[(h8 % 2) * 64:(h8 % 2) * 64 + 64,
                                     (h8 // 2) * 4096 + qt * 512:(h8 // 2) * 4096 + qt * 512 + 512],
                                psum[0:64, avb * 512: avb * 512 + 512],
                                sb['rbb'][0:64, 0:512], ALU.mult).then_inc(sc_ep.h, 1)
            sc_ep.v += 2
            epn += 1
    emit_pend_av()
    emit_pend_ep()

    attn_done = (sc_ep, sc_ep.v)

    wl = p.sem(tag='sw')

    @p.o("g")
    def _(g):
        for fc in range(8):
            dst = wos[:, fc * 512: fc * 512 + 512].rearrange("pp (kc j) -> pp kc j", j=128)
            g.dma_start(out=dst, in_=h['Wob'][fc].rearrange("(kc pp) j -> pp kc j", pp=128)).then_inc(wl.h, 16)
    wl.v += 16 * 8

    mmo = p.sem()
    evo = p.sem()
    sto = [p.sem(tag='hw'), p.sem(tag='hw')]
    b_mmo, b_evo = mmo.v, evo.v
    wst = sb['wst']

    @p.o("t")
    def _(t, wlv=wl.v, ad=attn_done):
        t.wait_ge(wl.h, wlv)
        t.wait_ge(ad[0].h, ad[1])
        n = 0
        for tt in range(4):
            for fc in range(8):
                if n >= 2:
                    t.wait_ge(evo.h, b_evo + n - 1)
                b0 = (n % 2) * 1024
                for nt in range(2):
                    ps = psum[:, b0 + nt * 512: b0 + nt * 512 + 512]
                    for k in range(4):
                        mi = t.matmul(ps, wos[:, fc * 512 + k * 128: fc * 512 + k * 128 + 128],
                                      aout[:, k * 4096 + tt * 1024 + nt * 512:
                                           k * 4096 + tt * 1024 + nt * 512 + 512],
                                      start=(k == 0), stop=(k == 3))
                    if nt == 1:
                        mi.then_inc(mmo.h, 1)
                n += 1
    mmo.v += 32

    n = 0
    for tt in range(4):
        for fc in range(8):
            @p.o("v")
            def _(v, n=n, stw=(sto[n % 2], sto[n % 2].v)):
                v.wait_ge(mmo.h, b_mmo + n + 1)
                if n >= 2:
                    v.wait_ge(stw[0].h, stw[1])
                v.tensor_copy(wst[:, (n % 2) * 1024:(n % 2) * 1024 + 1024],
                              psum[:, (n % 2) * 1024:(n % 2) * 1024 + 1024]).then_inc(evo.h, 1)

            @p.o("y")
            def _(y, n=n, tt=tt, fc=fc):
                y.wait_ge(evo.h, b_evo + n + 1)
                blk, tloc = (tt * 1024) // 2048, (tt * 1024) % 2048
                y.dma_start(out=h['rs_in'][blk, fc * 128:(fc + 1) * 128, tloc:tloc + 1024],
                            in_=wst[:, (n % 2) * 1024:(n % 2) * 1024 + 1024]).then_inc(sto[n % 2].h, 16)
            sto[n % 2].v += 16
            n += 1
    evo.v += 32

    ccr = p.sem(tag='co')

    @p.o("g")
    def _(g, a=sto[0].v, b=sto[1].v):
        g.wait_ge(sto[0].h, a)
        g.wait_ge(sto[1].h, b)
        g.collective_compute("ReduceScatter", ALU.add, replica_groups=rg,
                             ins=[h['rs_in'].ap().opt()],
                             outs=[h['rs_out'].ap().opt()]).then_inc(ccr.h, 1)
    ccr.v += 1

    rl = p.sem(tag='hw')
    bov = p.sem()
    wotok = sb['wotok']
    wofm = sb['wofm']

    @p.o("y")
    def _(y, c=ccr.v):
        y.wait_ge(ccr.h, c)
        y.dma_start(out=wofm[:, :].rearrange("pp (fc t) -> pp fc t", t=2048),
                    in_=h['rs_out'].ap().rearrange("(fc pp t) -> pp fc t", fc=8, pp=128)
                    ).then_inc(rl.h, 16)
    rl.v += 16

    @p.o("v")
    def _(v, r=rl.v, c=cl.v):
        v.wait_ge(rl.h, r)
        v.wait_ge(cl.h, c)
        for fc in range(8):
            v.tensor_scalar(wofm[:, fc * 2048:(fc + 1) * 2048], wofm[:, fc * 2048:(fc + 1) * 2048],
                            sb['bo_c'][:, fc:fc + 1], None, ALU.add).then_inc(bov.h, 1)
    bov.v += 8

    ev_tr = p.sem()
    transpose_tiles(
        p,
        src=lambda i: wofm[:, (i % 8) * 2048 + (i // 8) * 128: (i % 8) * 2048 + (i // 8) * 128 + 128],
        dst=lambda i: wotok[:, (i // 8) * 1024 + (i % 8) * 128: (i // 8) * 1024 + (i % 8) * 128 + 128],
        n=128, idn=sb['idn'], psum_bf=psum_bf, slots=TRAN_SLOTS,
        waits=[(bov, bov.v)], ev=ev_tr)

    u = p.sem()
    ln_done = p.sem()
    sxs = [p.sem(tag='hw'), p.sem(tag='hw')]
    sos = [p.sem(tag='hw'), p.sem(tag='hw')]
    tokb = h.get('x_tokb')
    bfb = sb['aout'][:, 0:4096]
    sob = [p.sem(tag='hw') for _ in range(4)]
    b_sob = [s.v for s in sob]
    ln_marks = []
    for tc in range(16):
        sx = sxs[tc % 2]

        @p.o("y")
        def _(y, tc=tc, need=(ln_marks[tc - 2] if tc >= 2 else 0), sx=sx):
            if tc >= 2:
                y.wait_ge(ln_done.h, need)
            y.dma_start(out=sb['xs_f32'][:, (tc % 2) * 1024:(tc % 2) * 1024 + 1024],
                        in_=h['x_tok'][tc * 128:(tc + 1) * 128, :]).then_inc(sx.h, 16)
        sx.v += 16
        d_before = ln_done.v
        ln_chunk(p,
                 a_ap=sb['xs_f32'][:, (tc % 2) * 1024:(tc % 2) * 1024 + 1024],
                 b_ap=wotok[:, tc * 1024:(tc + 1) * 1024],
                 tmp=sb['ln_tmp'][:, :], trash_bf=sb['ln_tr'][:, :], sc=sb['ln_sc'],
                 n_f=1024,
                 waits=[(sx, sx.v), (ev_tr, ev_tr.v)] +
                       ([(sos[tc % 2], sos[tc % 2].v)] if tc >= 2 else []) +
                       ([(sob[tc % 4], sob[tc % 4].v)]
                        if (tokb is not None and tc >= 4) else []),
                 u=u, done=ln_done,
                 out_f32=sb['x2s'][:, (tc % 2) * 1024:(tc % 2) * 1024 + 1024],
                 out_bf=(bfb[:, (tc % 4) * 1024:(tc % 4) * 1024 + 1024]
                         if tokb is not None else None),
                 absmax=(absmax_col[:, tc:tc + 1] if absmax_col is not None else None))
        so = sos[tc % 2]

        @p.o("y")
        def _(y, tc=tc, w=d_before + 1, so=so):
            y.wait_ge(ln_done.h, w)
            y.dma_start(out=h['x3_tok'][tc * 128:(tc + 1) * 128, :],
                        in_=sb['x2s'][:, (tc % 2) * 1024:(tc % 2) * 1024 + 1024]).then_inc(so.h, 16)
        so.v += 16
        if tokb is not None:
            @p.o("y")
            def _(y, tc=tc, w=d_before + 2, so2=sob[tc % 4]):
                y.wait_ge(ln_done.h, w)
                y.dma_start(out=tokb[tc * 128:(tc + 1) * 128, :],
                            in_=bfb[:, (tc % 4) * 1024:(tc % 4) * 1024 + 1024]
                            ).then_inc(so2.h, 16)
            sob[tc % 4].v += 16
        ln_marks.append(ln_done.v)

    @p.o("y")
    def _(y, w0=sos[0].v, w1=sos[1].v):
        y.wait_ge(sos[0].h, w0)
        y.wait_ge(sos[1].h, w1)
    return ln_done, sos, (sob, b_sob)


def mamba_phase(p, h, sb, psum, psum_bf, rg, g_hook=None):
    nc = p.nc
    xt0 = sb['xt0']
    xbcp = sb['xbcp']
    xtok = sb['xtok']
    ztok = sb['ztok']
    bcf = sb['bcf']
    vfull = sb['vfull']
    dtraw = sb['dtraw']
    dwork = sb['dwork']
    cumf = sb['cumf']
    w16 = sb['w16']
    mbig = sb['mbig']
    m0sb = sb['m0sb']
    btok = sb['btok']
    hst = sb['hst']
    hbb = sb['hbb']
    xtt = sb['xtt']
    szb = sb['szb']
    yt1 = sb['yt1']
    ssq = sb['ssq']
    wdg = sb['wdg']
    dide = sb['dide']
    bmask = sb['bmask']
    bigl = sb['bigl']
    idtl = sb['idtl']
    trilm = sb['trilm']
    onesf = sb['onesf']
    idn = sb['idn']
    iom = sb['iom']
    io_i32 = sb['io_i32']
    BIG = 32768.0

    cz = p.sem()
    cl = p.sem(tag='sw')

    @p.o("g")
    def _(g):
        for nm, d in [('cwx', 'cwx'), ('cwbc', 'cwbc'), ('cbx', 'cbx'), ('cbbc', 'cbbc'),
                      ('dtb', 'dtb'), ('alog', 'alog'), ('drep', 'drep'), ('nwc', 'nwc')]:
            g.dma_start(out=sb[nm][:, 0:h[d].shape[-1]], in_=h[d][:, :]).then_inc(cl.h, 16)
    cl.v += 16 * 8

    b_cz = cz.v

    @p.o("v")
    def _(v, clv=cl.v):
        v.tensor_single_scalar(sb['trf'][:, 0:128], io_i32[:, :], 0, ALU.is_ge).then_inc(cz.h, 1)
        v.wait_ge(cz.h, b_cz + 1)
        v.tensor_copy(trilm[:, :], sb['trf'][:, 0:128]).then_inc(cz.h, 1)
        v.wait_ge(cz.h, b_cz + 2)
        v.tensor_single_scalar(sb['trf'][:, 128:256], io_i32[:, :], 0, ALU.is_le).then_inc(cz.h, 1)
        v.wait_ge(cz.h, b_cz + 3)
        v.tensor_scalar(bigl[:, :], sb['trf'][:, 128:256], BIG, None, ALU.mult).then_inc(cz.h, 1)
        v.wait_ge(cz.h, b_cz + 4)
        v.tensor_copy(idtl[:, :].rearrange("pp (r j) -> pp r j", r=16),
                      idn[:, :].unsqueeze(1).broadcast_to([128, 16, 128])).then_inc(cz.h, 1)
        v.wait_ge(cz.h, b_cz + 5)
        v.wait_ge(cl.h, clv)
        for hh in range(16):
            v.tensor_scalar(dide[:, hh * 128:(hh + 1) * 128], idn[:, :],
                            sb['drep'][:, hh:hh + 1], None, ALU.mult).then_inc(cz.h, 1)
        v.wait_ge(cz.h, b_cz + 21)
        for fc in range(8):
            for tap in range(4):
                v.tensor_scalar(wdg[:, (fc * 4 + tap) * 128:(fc * 4 + tap) * 128 + 128],
                                iom[:, :], sb['cwx'][:, fc * 4 + tap:fc * 4 + tap + 1],
                                None, ALU.mult).then_inc(cz.h, 1)
        for fc in range(2):
            for tap in range(4):
                v.tensor_scalar(wdg[:, (32 + fc * 4 + tap) * 128:(32 + fc * 4 + tap) * 128 + 128],
                                iom[:, :], sb['cwbc'][:, fc * 4 + tap:fc * 4 + tap + 1],
                                None, ALU.mult).then_inc(cz.h, 1)
        v.wait_ge(cz.h, b_cz + 61)
        v.memset(xbcp[:, :].rearrange("pp (fc j) -> pp fc j", j=1027)[:, :, 0:3], 0.0).then_inc(cz.h, 1)
        v.wait_ge(cz.h, b_cz + 62)
        v.memset(hst[:, :], 0.0).then_inc(cz.h, 1)
        v.wait_ge(cz.h, b_cz + 63)
        v.memset(hbb[:, :], 0.0).then_inc(cz.h, 1)
        v.wait_ge(cz.h, b_cz + 64)
        v.memset(sb['dones'][0:16, 0:128], 1.0).then_inc(cz.h, 1)
        v.wait_ge(cz.h, b_cz + 65)
        v.tensor_single_scalar(sb['w16'][:, 80:81], io_i32[:, 0:1], -127,
                               ALU.is_equal).then_inc(cz.h, 1)
        v.wait_ge(cz.h, b_cz + 66)
        v.memset(cumf[0:1, :], 0.0).then_inc(cz.h, 1)
    cz.v += 67

    @p.o("g")
    def _(g, w=cz.v):
        g.wait_ge(cz.h, w)
        g.iota(sb['bio'][0:16, 0:2048], pattern=[[1, 2048]], base=0,
               channel_multiplier=-128).then_inc(cz.h, 1)
    cz.v += 1

    @p.o("v")
    def _(v, w=cz.v):
        v.wait_ge(cz.h, w)
        v.tensor_single_scalar(bmask[0:16, :], sb['bio'][0:16, 0:2048], 0, ALU.is_ge).then_inc(cz.h, 1)
        v.wait_ge(cz.h, w + 1)
        v.tensor_single_scalar(sb['trf'][0:16, 0:2048], sb['bio'][0:16, 0:2048], 128,
                               ALU.is_lt).then_inc(cz.h, 1)
        v.wait_ge(cz.h, w + 2)
        v.tensor_tensor(bmask[0:16, :], bmask[0:16, :], sb['trf'][0:16, 0:2048],
                        ALU.mult).then_inc(cz.h, 1)
        v.wait_ge(cz.h, w + 3)
        v.engine_nop().then_inc(cz.h, 1)
    cz.v += 4

    sA = p.sem()

    @p.o("s")
    def _(s, clv=cl.v):
        s.wait_ge(cl.h, clv)
        s.activation(sb['acol'][0:16, 0:1], sb['alog'][0:16, 0:1], AF.Exp).then_inc(sA.h, 1)
    sA.v += 1

    @p.o("v")
    def _(v, w=sA.v):
        v.wait_ge(sA.h, w)
        v.tensor_scalar(sb['acol'][0:16, 0:1], sb['acol'][0:16, 0:1], -1.0, None,
                        ALU.mult).then_inc(sA.h, 1)
    sA.v += 1
    consts_done = (cz, cz.v)

    wzl = p.sem(tag='sw')

    @p.o("g")
    def _(g):
        for k in range(8):
            src_ = h['wzb'].ap().rearrange("fc (kc pp) j -> kc pp fc j", pp=128)[k]
            g.dma_start(out=sb['wzr'][:, k * 1024:(k + 1) * 1024]
                        .rearrange("pp (fc j) -> pp fc j", j=128), in_=src_).then_inc(wzl.h, 16)
        for i in range(10):
            hsrc = h['wxsb'][i] if i < 8 else h['wbcb'][i - 8]
            dst = sb['wxsbcr'][:, i * 1024:(i + 1) * 1024].rearrange("pp (kc j) -> pp kc j", j=128)
            g.dma_start(out=dst, in_=hsrc.rearrange("(kc pp) j -> pp kc j", pp=128)).then_inc(wzl.h, 16)
        for k in range(8):
            g.dma_start(out=sb['wdtr'][:, k * 16:(k + 1) * 16],
                        in_=h['wdt'][k * 128:(k + 1) * 128, :]).then_inc(wzl.h, 16)
    wzl.v += 16 * 26
    weights_ready = (wzl, wzl.v)

    x0l = p.sem(tag='sw')
    mmw = p.sem()
    evw = p.sem()
    hsm = p.sem()
    mmc = p.sem()
    evc = p.sem()
    tkx = p.sem()
    tex = p.sem()
    dtv = p.sem()
    mmk = p.sem()
    ckv = p.sem()
    cks = p.sem()
    spz = p.sem(tag='hw')
    vsp = p.sem(tag='hw')

    ssq_marks = []
    prev_tile_guard = [consts_done]
    tile_cks = []

    for tt in range(4):
        T0 = tt * 1024

        @p.o("g")
        def _(g, tt=tt, T0=T0, w=mmw.v, wz=weights_ready):
            if tt == 0:
                g.wait_ge(wz[0].h, wz[1])
            else:
                g.wait_ge(mmw.h, w)
            g.dma_start(out=xt0[:, :].rearrange("pp (kc t) -> pp kc t", t=1024),
                        in_=h['x0fm'].ap().rearrange("(kc pp) t -> pp kc t", pp=128)
                        [:, :, 3 + T0: 3 + T0 + 1024]).then_inc(x0l.h, 16)
        x0l.v += 16
        if g_hook is not None:
            g_hook(tt)

        if tt > 0:
            @p.o("v")
            def _(v, w=hsm.v, tt=tt):
                v.wait_ge(hsm.h, w)
                v.tensor_copy(xbcp[:, :].rearrange("pp (fc j) -> pp fc j", j=1027)[:, :, 0:3],
                              sb['hstash'][:, (tt % 2) * 30:(tt % 2) * 30 + 30]
                              .rearrange("pp (fc j) -> pp fc j", j=3)).then_inc(hsm.h, 1)
            hsm.v += 1

        base_mm = mmw.v
        base_ev = evw.v

        @p.o("t")
        def _(t, tt=tt, xv=x0l.v, hv=hsm.v, base_ev=base_ev, guards=list(prev_tile_guard)):
            t.wait_ge(x0l.h, xv)
            t.wait_ge(hsm.h, hv)
            for s_, val in guards:
                t.wait_ge(s_.h, val)
            n = 0
            for fc in range(10):
                if base_ev + n - 1 > 0:
                    t.wait_ge(evw.h, base_ev + n - 1)
                b0 = (n % 2) * 1024
                for nt in range(2):
                    ps = psum[:, b0 + nt * 512: b0 + nt * 512 + 512]
                    for k in range(8):
                        mi = t.matmul(ps, sb['wxsbcr'][:, fc * 1024 + k * 128: fc * 1024 + k * 128 + 128],
                                      xt0[:, k * 1024 + nt * 512: k * 1024 + nt * 512 + 512],
                                      start=(k == 0), stop=(k == 7))
                    if nt == 1:
                        mi.then_inc(mmw.h, 1)
                n += 1
            for tc in range(8):
                t.wait_ge(evw.h, base_ev + n - 1)
                b0 = (n % 2) * 1024
                for nt in range(2):
                    ps = psum[:, b0 + nt * 512: b0 + nt * 512 + 512]
                    for k in range(8):
                        mi = t.matmul(ps, xt0[:, k * 1024 + tc * 128: k * 1024 + tc * 128 + 128],
                                      sb['wzr'][:, k * 1024 + nt * 512: k * 1024 + nt * 512 + 512],
                                      start=(k == 0), stop=(k == 7))
                    if nt == 1:
                        mi.then_inc(mmw.h, 1)
                n += 1
            t.wait_ge(evw.h, base_ev + n - 1)
            b0 = (n % 2) * 1024
            for nt in range(2):
                ps = psum[0:16, b0 + nt * 512: b0 + nt * 512 + 512]
                for k in range(8):
                    mi = t.matmul(ps, sb['wdtr'][:, k * 16:(k + 1) * 16],
                                  xt0[:, k * 1024 + nt * 512: k * 1024 + nt * 512 + 512],
                                  start=(k == 0), stop=(k == 7))
                if nt == 1:
                    mi.then_inc(mmw.h, 1)
        mmw.v += 19

        for n in range(19):
            if n < 10:
                @p.o("v")
                def _(v, n=n, tt=tt, w=base_mm + n + 1, sw=hsm.v):
                    v.wait_ge(mmw.h, w)
                    v.wait_ge(hsm.h, sw)
                    v.tensor_copy(xbcp[:, n * 1027 + 3: n * 1027 + 3 + 1024],
                                  psum[:, (n % 2) * 1024:(n % 2) * 1024 + 1024]).then_inc(evw.h, 1)

                @p.o("v")
                def _(v, n=n, tt=tt, w=base_ev + n + 1):
                    v.wait_ge(evw.h, w)
                    v.tensor_copy(sb['hstash'][:, ((tt + 1) % 2) * 30 + n * 3: ((tt + 1) % 2) * 30 + n * 3 + 3],
                                  xbcp[:, n * 1027 + 1024: n * 1027 + 1027]).then_inc(hsm.h, 1)
            elif n < 18:
                @p.o("v")
                def _(v, n=n, w=base_mm + n + 1):
                    v.wait_ge(mmw.h, w)
                    v.tensor_copy(ztok[:, (n - 10) * 1024:(n - 9) * 1024],
                                  psum[:, (n % 2) * 1024:(n % 2) * 1024 + 1024]).then_inc(evw.h, 1)
            else:
                @p.o("v")
                def _(v, n=n, tt=tt, w=base_mm + n + 1,
                      ckw=(tile_cks[tt - 2] if tt >= 2 else 0)):
                    v.wait_ge(mmw.h, w)
                    if tt >= 2:
                        v.wait_ge(cks.h, ckw)
                    v.tensor_copy(dtraw[0:16, (tt % 2) * 1024:(tt % 2) * 1024 + 1024],
                                  psum[0:16, (n % 2) * 1024:(n % 2) * 1024 + 1024]).then_inc(evw.h, 1)
        evw.v += 19
        hsm.v += 10

        base_cm = mmc.v
        base_ce = evc.v

        @p.o("t")
        def _(t, tt=tt, base_ev=base_ev, base_ce=base_ce, sw=hsm.v):
            for fc in range(10):
                t.wait_ge(evw.h, base_ev + fc + 1)
                t.wait_ge(hsm.h, sw)
                if base_ce + fc - 1 > 0:
                    t.wait_ge(evc.h, base_ce + fc - 1)
                b0 = 4 * 512 + (fc % 2) * 1024
                for nt in range(2):
                    ps = psum[:, b0 + nt * 512: b0 + nt * 512 + 512]
                    for tap in range(4):
                        mi = t.matmul(ps, wdg[:, (fc * 4 + tap) * 128:(fc * 4 + tap) * 128 + 128],
                                      xbcp[:, fc * 1027 + nt * 512 + tap: fc * 1027 + nt * 512 + tap + 512],
                                      start=(tap == 0), stop=(tap == 3))
                    if nt == 1:
                        mi.then_inc(mmc.h, 1)
        mmc.v += 10

        for fc in range(10):
            @p.o("s")
            def _(s, fc=fc, tt=tt, w=base_cm + fc + 1, clv=cl.v):
                s.wait_ge(mmc.h, w)
                s.wait_ge(cl.h, clv)
                b0 = 4 * 512 + (fc % 2) * 1024
                bias = (sb['cbx'][:, fc:fc + 1] if fc < 8 else sb['cbbc'][:, fc - 8:fc - 7])
                s.activation(xbcp[:, fc * 1027 + 3: fc * 1027 + 3 + 1024] if fc < 8 else
                             bcf[:, (fc - 8) * 1024: (fc - 8) * 1024 + 1024],
                             psum[:, b0: b0 + 1024], AF.Silu, bias=bias).then_inc(evc.h, 1)
        evc.v += 10

        base_tk = tkx.v
        base_te = tex.v

        @p.o("t")
        def _(t, tt=tt, w=base_ev + 19, base_ce=base_ce, base_te=base_te):
            t.wait_ge(evw.h, w)
            for i in range(64):
                fc, tc = i // 8, i % 8
                t.wait_ge(evc.h, base_ce + 10)
                if i >= 4:
                    t.wait_ge(tex.h, base_te + i - 3)
                ps = psum_bf[:, TRAN_SLOTS[i % 4]: TRAN_SLOTS[i % 4] + 128]
                t.transpose(ps, xbcp[:, fc * 1027 + 3 + tc * 128: fc * 1027 + 3 + tc * 128 + 128],
                            idn[:, :]).then_inc(tkx.h, 1)
        tkx.v += 64

        for i in range(64):
            fc, tc = i // 8, i % 8

            @p.o("v")
            def _(v, i=i, fc=fc, tc=tc, w=base_tk + i + 1):
                v.wait_ge(tkx.h, w)
                ps = psum_bf[:, TRAN_SLOTS[i % 4]: TRAN_SLOTS[i % 4] + 128]
                v.tensor_copy(xtok[:, tc * 1024 + fc * 128: tc * 1024 + fc * 128 + 128],
                              ps).then_inc(tex.h, 1)
        tex.v += 64
        xs_t_done = (tex, tex.v)

        for c8 in range(8):
            c = tt * 8 + c8
            _mamba_chunk(p, h, sb, psum, psum_bf, tt, c8, c,
                         mmk, ckv, cks, spz, dtv, xs_t_done,
                         (evc, base_ce + 10), (evw, base_ev + 19), BIG)

        tile_cks.append(cks.v)

        @p.o("y")
        def _(y, tt=tt, w=ckv.v):
            y.wait_ge(ckv.h, w)
            y.dma_start(out=h['vsp'][tt], in_=vfull[:, :]).then_inc(vsp.h, 16)
        vsp.v += 16
        ssq_marks.append(ckv.v)

        prev_tile_guard = [(tex, tex.v), (evc, evc.v), (ckv, ckv.v), (vsp, vsp.v)]

    chunks_done = (ckv, ckv.v)
    return dict(chunks_done=chunks_done, ssq_marks=ssq_marks, vsp=vsp, spz=spz,
                ckv=ckv, cks=cks, mmk=mmk)


def _mamba_chunk(p, h, sb, psum, psum_bf, tt, c8, c, mmk, ckv, cks, spz, dtv,
                 xs_t_done, conv_done, win_done, BIG):
    dwork = sb['dwork']
    dtraw = sb['dtraw']
    w16 = sb['w16']
    wbt = sb['wbt']
    mbig = sb['mbig']
    m0sb = sb['m0sb']
    btok = sb['btok']
    xtt = sb['xtt']
    szb = sb['szb']
    yt1 = sb['yt1']
    vtrash = sb['vtrash']
    bcf = sb['bcf']
    xtok = sb['xtok']
    ztok = sb['ztok']
    hst = sb['hst']
    hbb = sb['hbb']
    vfull = sb['vfull']
    ssq = sb['ssq']
    cumf = sb['cumf']
    cum2048 = sb['cum2048']
    onesf = sb['onesf']
    idn = sb['idn']
    sl = 0

    V0 = ckv.v
    S0 = cks.v
    M0v = mmk.v
    D0 = dtv.v
    Z0 = spz.v

    @p.o("s")
    def _(s, c=c, S0=S0, V0=V0, wd=win_done):
        if c > 0:
            s.wait_ge(mmk.h, 10 * c - 3)
        s.wait_ge(wd[0].h, wd[1])
        s.activation(dwork[0:16, 0:128],
                     dtraw[0:16, (tt % 2) * 1024 + c8 * 128:(tt % 2) * 1024 + c8 * 128 + 128],
                     AF.Exp, bias=sb['dtb'][0:16, 0:1]).then_inc(cks.h, 1)
    cks.v += 1

    @p.o("v")
    def _(v, S0=S0):
        v.wait_ge(cks.h, S0 + 1)
        v.tensor_scalar(dwork[0:16, 0:128], dwork[0:16, 0:128], 1.0, None,
                        ALU.add).then_inc(cks.h, 1)
    cks.v += 1

    @p.o("s")
    def _(s, S0=S0):
        s.wait_ge(cks.h, S0 + 2)
        s.activation(dwork[0:16, 0:128], dwork[0:16, 0:128], AF.Ln).then_inc(cks.h, 1)
        s.wait_ge(cks.h, S0 + 3)
        s.activation(dwork[0:16, 128:256], dwork[0:16, 0:128], AF.Ln).then_inc(cks.h, 1)
    cks.v += 2

    @p.o("v")
    def _(v, S0=S0, V0=V0, sa=None):
        v.wait_ge(cks.h, S0 + 4)
        v.tensor_scalar(dwork[0:16, 128:256], dwork[0:16, 128:256], -60.0, None,
                        ALU.max).then_inc(ckv.h, 1)
        v.wait_ge(ckv.h, V0 + 1)
        v.tensor_scalar(dwork[0:16, 256:384], dwork[0:16, 0:128],
                        sb['acol'][0:16, 0:1], None, ALU.mult).then_inc(ckv.h, 1)
        v.wait_ge(ckv.h, V0 + 2)
        v.tensor_tensor_scan(dwork[0:16, 384:512], sb['dones'][0:16, 0:128],
                             dwork[0:16, 256:384], 0.0, ALU.mult, ALU.add).then_inc(ckv.h, 1)
        v.wait_ge(ckv.h, V0 + 3)
        v.tensor_sub(dwork[0:16, 512:640], dwork[0:16, 128:256],
                     dwork[0:16, 384:512]).then_inc(ckv.h, 1)
        v.wait_ge(ckv.h, V0 + 4)
        v.tensor_scalar(dwork[0:16, 640:768], dwork[0:16, 512:640], -BIG, None,
                        ALU.add).then_inc(ckv.h, 1)
    ckv.v += 5

    @p.o("t")
    def _(t, V0=V0, M0v=M0v, xtd=xs_t_done, c=c):
        t.wait_ge(ckv.h, V0 + 5)
        if c > 0:
            t.wait_ge(ckv.h, V0 - 17)
        t.transpose(psum[:, 1024:1040], dwork[0:16, 384:512], sb['iom'][0:16, 0:16]).then_inc(mmk.h, 1)
        t.transpose(psum[:, 1040:1056], dwork[0:16, 512:640], sb['iom'][0:16, 0:16]).then_inc(mmk.h, 1)
    mmk.v += 2

    @p.o("v")
    def _(v, M0v=M0v, V0=V0, S0=S0, c=c):
        v.wait_ge(mmk.h, M0v + 2)
        if c > 0:
            v.wait_ge(cks.h, S0 - 4)
        v.tensor_copy(w16[:, 0:16], psum[:, 1024:1040]).then_inc(ckv.h, 1)
        v.tensor_copy(w16[:, 16:32], psum[:, 1040:1056]).then_inc(ckv.h, 1)
    ckv.v += 2

    @p.o("t")
    def _(t, V0=V0):
        t.wait_ge(ckv.h, V0 + 6)
        t.matmul(psum[0:1, 1552:1568], w16[:, 80:81], w16[:, 0:16],
                 start=True, stop=True).then_inc(mmk.h, 1)
    mmk.v += 1

    @p.o("v")
    def _(v, M0v=M0v, V0=V0):
        v.tensor_tensor(cum2048[0:16, :].rearrange("p (hh t) -> p hh t", t=128),
                        sb['bmask'][0:16, :].rearrange("p (hh t) -> p hh t", t=128),
                        dwork[0:16, 384:512].unsqueeze(1).broadcast_to([16, 16, 128]),
                        ALU.mult).then_inc(ckv.h, 1)
        v.wait_ge(mmk.h, M0v + 3)
        v.tensor_copy(cumf[0:1, 0:16], psum[0:1, 1552:1568]).then_inc(ckv.h, 1)
    ckv.v += 2

    @p.o("t")
    def _(t, V0=V0):
        t.wait_ge(ckv.h, V0 + 9)
        t.matmul(psum[:, 1536:1552], onesf[0:1, 0:128], cumf[0:1, 0:16],
                 start=True, stop=True).then_inc(mmk.h, 1)
    mmk.v += 1

    @p.o("v")
    def _(v, M0v=M0v, V0=V0):
        v.wait_ge(mmk.h, M0v + 4)
        v.tensor_copy(w16[:, 32:48], psum[:, 1536:1552]).then_inc(ckv.h, 1)
        v.wait_ge(ckv.h, V0 + 10)
        v.tensor_add(w16[:, 64:80], w16[:, 16:32], w16[:, 32:48]).then_inc(ckv.h, 1)
    ckv.v += 2

    @p.o("s")
    def _(s, V0=V0, S0=S0):
        s.wait_ge(ckv.h, V0 + 11)
        s.activation(wbt[:, 0:16], w16[:, 0:16], AF.Exp).then_inc(cks.h, 1)
        s.activation(w16[:, 48:64], w16[:, 32:48], AF.Exp).then_inc(cks.h, 1)
        s.activation(wbt[:, 16:32], w16[:, 64:80], AF.Exp).then_inc(cks.h, 1)
    cks.v += 3

    @p.o("t")
    def _(t, c=c, M0v=M0v, V0=V0, cd=conv_done):
        t.wait_ge(cd[0].h, cd[1])
        if c > 0:
            t.wait_ge(ckv.h, V0)
        t.matmul(psum[:, 0:128], bcf[:, (c % 8) * 128:((c % 8) + 1) * 128],
                 bcf[:, 1024 + (c % 8) * 128: 1024 + (c % 8) * 128 + 128],
                 start=True, stop=True).then_inc(mmk.h, 1)
        t.transpose(psum_bf[:, 2 * 1024 + 512: 2 * 1024 + 512 + 128],
                    bcf[:, (c % 8) * 128:((c % 8) + 1) * 128], idn[:, :]).then_inc(mmk.h, 1)
    mmk.v += 2

    @p.o("v")
    def _(v, M0v=M0v, V0=V0, sl=sl):
        v.wait_ge(mmk.h, M0v + 5)
        v.tensor_tensor(m0sb[:, sl * 128:(sl + 1) * 128], psum[:, 0:128],
                        sb['trilm'][:, :], ALU.mult).then_inc(ckv.h, 1)
        v.wait_ge(mmk.h, M0v + 6)
        v.tensor_copy(btok[:, sl * 128:(sl + 1) * 128],
                      psum_bf[:, 2 * 1024 + 512: 2 * 1024 + 512 + 128]).then_inc(ckv.h, 1)
    ckv.v += 2

    @p.o("t")
    def _(t, V0=V0, M0v=M0v):
        t.wait_ge(ckv.h, V0 + 13)
        for nt in range(4):
            t.matmul(psum[:, nt * 512:(nt + 1) * 512], sb['dones'][0:16, 0:128],
                     cum2048[0:16, nt * 512:(nt + 1) * 512], start=True, stop=False,
                     skip_group_check=True)
            t.matmul(psum[:, nt * 512:(nt + 1) * 512], dwork[0:16, 640:768],
                     sb['bmask'][0:16, nt * 512:(nt + 1) * 512], start=False, stop=False,
                     skip_group_check=True)
            mi = t.matmul(psum[:, nt * 512:(nt + 1) * 512], sb['bigl'][:, :],
                          sb['idtl'][:, nt * 512:(nt + 1) * 512], start=False, stop=True,
                          skip_group_check=True)
            if nt == 3:
                mi.then_inc(mmk.h, 1)
    mmk.v += 1

    @p.o("s")
    def _(s, M0v=M0v, S0=S0, sl=sl, c=c, V0=V0):
        s.wait_ge(mmk.h, M0v + 7)
        if c > 0:
            s.wait_ge(mmk.h, 10 * c - 1)
        s.activation(mbig[:, sl * 2048:(sl + 1) * 2048], psum[:, 0:2048], AF.Exp).then_inc(cks.h, 1)
    cks.v += 1

    @p.o("v")
    def _(v, S0=S0, V0=V0, sl=sl, c=c):
        v.wait_ge(cks.h, S0 + 8)
        v.wait_ge(ckv.h, V0 + 12)
        v.tensor_tensor(mbig[:, sl * 2048:(sl + 1) * 2048].rearrange("pp (hh j) -> pp hh j", j=128),
                        mbig[:, sl * 2048:(sl + 1) * 2048].rearrange("pp (hh j) -> pp hh j", j=128),
                        m0sb[:, sl * 128:(sl + 1) * 128].unsqueeze(1).broadcast_to([128, 16, 128]),
                        ALU.mult).then_inc(ckv.h, 1)
        v.wait_ge(ckv.h, V0 + 14)
        v.tensor_tensor(mbig[:, sl * 2048:(sl + 1) * 2048], mbig[:, sl * 2048:(sl + 1) * 2048],
                        sb['dide'][:, :], ALU.add).then_inc(ckv.h, 1)
        v.wait_ge(cks.h, S0 + 7)
        if c > 0:
            v.wait_ge(mmk.h, 10 * c)
        v.tensor_tensor(xtt[:, sl * 1024:(sl + 1) * 1024].rearrange("pp (hh d) -> pp hh d", d=64),
                        xtok[:, (c % 8) * 1024:((c % 8) + 1) * 1024].rearrange("pp (hh d) -> pp hh d", d=64),
                        wbt[:, 16:32].unsqueeze(2).broadcast_to([128, 16, 64]),
                        ALU.mult).then_inc(ckv.h, 1)
    ckv.v += 3

    @p.o("s")
    def _(s, S0=S0, c=c, V0=V0, sl=sl, wd=win_done):
        if c > 0:
            s.wait_ge(ckv.h, V0)
        s.wait_ge(wd[0].h, wd[1])
        s.activation(szb[:, sl * 1024:(sl + 1) * 1024],
                     ztok[:, (c % 8) * 1024:((c % 8) + 1) * 1024], AF.Silu).then_inc(cks.h, 1)
    cks.v += 1

    @p.o("t")
    def _(t, V0=V0, M0v=M0v, sl=sl, c=c, xtd=xs_t_done):
        t.wait_ge(xtd[0].h, xtd[1])
        t.wait_ge(ckv.h, V0 + 13)
        for nt in range(2):
            mi = t.matmul(psum[:, (6 + nt) * 512:(7 + nt) * 512],
                          bcf[:, 1024 + (c % 8) * 128: 1024 + (c % 8) * 128 + 128],
                          hbb[:, nt * 512:(nt + 1) * 512], start=True, stop=True,
                          skip_group_check=True)
        mi.then_inc(mmk.h, 1)
        t.wait_ge(ckv.h, V0 + 15)
        for hh in range(16):
            mi = t.matmul(psum[:, 2048 + hh * 64: 2048 + hh * 64 + 64],
                          mbig[:, sl * 2048 + hh * 128: sl * 2048 + hh * 128 + 128],
                          xtok[:, (c % 8) * 1024 + hh * 64:(c % 8) * 1024 + hh * 64 + 64],
                          start=(hh == 0 or hh == 8), stop=(hh == 7 or hh == 15),
                          skip_group_check=True)
        mi.then_inc(mmk.h, 1)
        t.wait_ge(ckv.h, V0 + 16)
        for nt in range(2):
            mi = t.matmul(psum[:, nt * 512:(nt + 1) * 512], btok[:, sl * 128:(sl + 1) * 128],
                          xtt[:, sl * 1024 + nt * 512: sl * 1024 + nt * 512 + 512],
                          start=True, stop=True, skip_group_check=True)
        mi.then_inc(mmk.h, 1)
    mmk.v += 3

    @p.o("v")
    def _(v, V0=V0, M0v=M0v, S0=S0, sl=sl, c=c, c8=c8):
        v.wait_ge(mmk.h, M0v + 9)
        v.wait_ge(cks.h, S0 + 5)
        v.tensor_tensor(yt1[:, :].rearrange("pp (hh d) -> pp hh d", d=64),
                        psum[:, 3072:4096].rearrange("pp (hh d) -> pp hh d", d=64),
                        wbt[:, 0:16].unsqueeze(2).broadcast_to([128, 16, 64]),
                        ALU.mult).then_inc(ckv.h, 1)
        v.wait_ge(ckv.h, V0 + 17)
        v.tensor_tensor(yt1[:, :], yt1[:, :], psum[:, 2048:3072], ALU.add).then_inc(ckv.h, 1)
        v.wait_ge(ckv.h, V0 + 18)
        v.wait_ge(cks.h, S0 + 9)
        v.tensor_tensor(vfull[:, c8 * 1024:(c8 + 1) * 1024], yt1[:, :],
                        szb[:, sl * 1024:(sl + 1) * 1024], ALU.mult).then_inc(ckv.h, 1)
        v.wait_ge(ckv.h, V0 + 19)
        v.tensor_tensor(sb['vtrash'][:, :], vfull[:, c8 * 1024:(c8 + 1) * 1024],
                        vfull[:, c8 * 1024:(c8 + 1) * 1024], ALU.mult).then_inc(ckv.h, 1)
        v.wait_ge(ckv.h, V0 + 20)
        v.reduce_sum(sb['ssq'][:, c:c + 1], sb['vtrash'][:, :], axis=AX.X).then_inc(ckv.h, 1)
        v.wait_ge(cks.h, S0 + 6)
        v.tensor_tensor(yt1[:, :].rearrange("pp (hh d) -> pp hh d", d=64),
                        hst[:, :].rearrange("pp (hh d) -> pp hh d", d=64),
                        w16[:, 48:64].unsqueeze(2).broadcast_to([128, 16, 64]),
                        ALU.mult).then_inc(ckv.h, 1)
        v.wait_ge(ckv.h, V0 + 22)
        v.wait_ge(mmk.h, M0v + 10)
        v.tensor_tensor(hst[:, :], yt1[:, :], psum[:, 0:1024], ALU.add).then_inc(ckv.h, 1)
        v.wait_ge(ckv.h, V0 + 23)
        v.tensor_copy(hbb[:, :], hst[:, :]).then_inc(ckv.h, 1)
    ckv.v += 8


def mamba_tail(p, h, sb, psum, psum_bf, rg, mres):
    ckv = mres['ckv']
    vsp = mres['vsp']
    ssq = sb['ssq']
    vfull = sb['vfull']
    idn = sb['idn']
    onesf = sb['onesf']

    ag = p.sem(tag='sw')
    cc = p.sem(tag='co')
    b_ag, b_cc = ag.v, cc.v

    @p.o("g")
    def _(g, w=mres['chunks_done']):
        g.wait_ge(w[0].h, w[1])
        g.dma_start(out=h['ag2_in'].ap().rearrange("(c s) -> s c", s=128),
                    in_=ssq[:, 0:32], allow_slow_non_contiguous=True).then_inc(ag.h, 16)
        g.wait_ge(ag.h, b_ag + 16)
        g.collective_compute("AllGather", ALU.bypass, replica_groups=rg,
                             ins=[h['ag2_in'].ap().opt()],
                             outs=[h['ag2_out'].ap().opt()]).then_inc(cc.h, 1)
        g.wait_ge(cc.h, b_cc + 1)
        g.dma_start(out=ssq[:, 32:64], in_=h['ag2_out'].ap()[0].rearrange("(c s) -> s c", s=128),
                    allow_slow_non_contiguous=True).then_inc(ag.h, 16)
        g.dma_start(out=ssq[:, 64:96], in_=h['ag2_out'].ap()[1].rearrange("(c s) -> s c", s=128),
                    allow_slow_non_contiguous=True).then_inc(ag.h, 16)
    ag.v += 48
    cc.v += 1

    rq = p.sem()
    b_rq = rq.v

    @p.o("v")
    def _(v, w=ag.v):
        v.wait_ge(ag.h, w)
        v.tensor_add(ssq[:, 0:32], ssq[:, 32:64], ssq[:, 64:96]).then_inc(rq.h, 1)
        v.wait_ge(rq.h, b_rq + 1)
        v.tensor_scalar(ssq[:, 0:32], ssq[:, 0:32], 1.0 / 2048.0, 1e-5, ALU.mult,
                        ALU.add).then_inc(rq.h, 1)
    rq.v += 2

    @p.o("s")
    def _(s, w=rq.v):
        s.wait_ge(rq.h, w)
        s.activation(ssq[:, 32:64], ssq[:, 0:32], AF.Sqrt).then_inc(rq.h, 1)
    rq.v += 1

    @p.o("v")
    def _(v, w=rq.v):
        v.wait_ge(rq.h, w)
        v.reciprocal(ssq[:, 32:64], ssq[:, 32:64]).then_inc(rq.h, 1)
    rq.v += 1
    rsq_ready = (rq, rq.v)

    wol = p.sem(tag='sw')

    @p.o("g")
    def _(g):
        for fc in range(8):
            dst = sb['wob'][:, fc * 1024:(fc + 1) * 1024].rearrange("pp (kc j) -> pp kc j", j=128)
            g.dma_start(out=dst, in_=h['woutb'][fc].rearrange("(kc pp) j -> pp kc j", pp=128)).then_inc(wol.h, 16)
    wol.v += 16 * 8

    vrl = p.sem(tag='hw')
    vnv = p.sem()
    tk2 = p.sem()
    te2 = p.sem()
    mm2 = p.sem()
    ev2 = p.sem()
    st2 = [p.sem(tag='hw'), p.sem(tag='hw')]

    for tt in range(4):
        @p.o("y")
        def _(y, tt=tt, w=vsp.v, w2=(te2, te2.v)):
            y.wait_ge(vsp.h, w)
            if tt:
                y.wait_ge(w2[0].h, w2[1])
            y.dma_start(out=vfull[:, :], in_=h['vsp'][tt]).then_inc(vrl.h, 16)
        vrl.v += 16

        @p.o("v")
        def _(v, tt=tt, w=vrl.v, rv=rsq_ready):
            v.wait_ge(vrl.h, w)
            v.wait_ge(rv[0].h, rv[1])
            for c8 in range(8):
                v.tensor_scalar(vfull[:, c8 * 1024:(c8 + 1) * 1024],
                                vfull[:, c8 * 1024:(c8 + 1) * 1024],
                                ssq[:, 32 + tt * 8 + c8: 33 + tt * 8 + c8], None,
                                ALU.mult).then_inc(vnv.h, 1)
        vnv.v += 8

        base_tk = tk2.v
        base_te = te2.v

        @p.o("t")
        def _(t, tt=tt, w=vnv.v, base_te=base_te):
            t.wait_ge(vnv.h, w)
            for i in range(64):
                c8, fc = i // 8, i % 8
                if i >= 4:
                    t.wait_ge(te2.h, base_te + i - 3)
                ps = psum_bf[:, TRAN_SLOTS[i % 4]: TRAN_SLOTS[i % 4] + 128]
                t.transpose(ps, vfull[:, c8 * 1024 + fc * 128: c8 * 1024 + fc * 128 + 128],
                            idn[:, :]).then_inc(tk2.h, 1)
        tk2.v += 64

        for i in range(64):
            c8, fc = i // 8, i % 8

            @p.o("v")
            def _(v, i=i, c8=c8, fc=fc, w=base_tk + i + 1, clv=None):
                v.wait_ge(tk2.h, w)
                ps = psum_bf[:, TRAN_SLOTS[i % 4]: TRAN_SLOTS[i % 4] + 128]
                v.tensor_scalar(sb['vfm'][:, fc * 1024 + c8 * 128: fc * 1024 + c8 * 128 + 128],
                                ps, sb['nwc'][:, fc:fc + 1], None, ALU.mult).then_inc(te2.h, 1)
        te2.v += 64

        base_mm = mm2.v
        base_ev = ev2.v

        @p.o("t")
        def _(t, tt=tt, w=te2.v, wv=wol.v, base_ev=base_ev):
            t.wait_ge(te2.h, w)
            t.wait_ge(wol.h, wv)
            for fc in range(8):
                if base_ev + fc - 1 > 0:
                    t.wait_ge(ev2.h, base_ev + fc - 1)
                b0 = (fc % 2) * 1024
                for nt in range(2):
                    ps = psum[:, b0 + nt * 512: b0 + nt * 512 + 512]
                    for k in range(8):
                        mi = t.matmul(ps, sb['wob'][:, fc * 1024 + k * 128: fc * 1024 + k * 128 + 128],
                                      sb['vfm'][:, k * 1024 + nt * 512: k * 1024 + nt * 512 + 512],
                                      start=(k == 0), stop=(k == 7))
                    if nt == 1:
                        mi.then_inc(mm2.h, 1)
        mm2.v += 8

        for fc in range(8):
            @p.o("v")
            def _(v, fc=fc, w=base_mm + fc + 1, sw=(st2[fc % 2], st2[fc % 2].v)):
                v.wait_ge(mm2.h, w)
                v.wait_ge(sw[0].h, sw[1])
                v.tensor_copy(sb['wst2'][:, (fc % 2) * 1024:(fc % 2) * 1024 + 1024],
                              psum[:, (fc % 2) * 1024:(fc % 2) * 1024 + 1024]).then_inc(ev2.h, 1)

            @p.o("y")
            def _(y, fc=fc, tt=tt, w=base_ev + fc + 1):
                y.wait_ge(ev2.h, w)
                blk, tloc = (tt * 1024) // 2048, (tt * 1024) % 2048
                y.dma_start(out=h['rs2_in'][blk, fc * 128:(fc + 1) * 128, tloc:tloc + 1024],
                            in_=sb['wst2'][:, (fc % 2) * 1024:(fc % 2) * 1024 + 1024]).then_inc(st2[fc % 2].h, 16)
            st2[fc % 2].v += 16
        ev2.v += 8

    ccr = p.sem(tag='co')

    @p.o("g")
    def _(g, a=st2[0].v, b=st2[1].v):
        g.wait_ge(st2[0].h, a)
        g.wait_ge(st2[1].h, b)
        g.collective_compute("ReduceScatter", ALU.add, replica_groups=rg,
                             ins=[h['rs2_in'].ap().opt()],
                             outs=[h['rs2_out'].ap().opt()]).then_inc(ccr.h, 1)
    ccr.v += 1

    rl = p.sem(tag='hw')

    @p.o("y")
    def _(y, c=ccr.v):
        y.wait_ge(ccr.h, c)
        y.dma_start(out=sb['ofm'][:, :].rearrange("pp (fc t) -> pp fc t", t=2048),
                    in_=h['rs2_out'].ap().rearrange("(fc pp t) -> pp fc t", fc=8, pp=128)
                    ).then_inc(rl.h, 16)
    rl.v += 16

    ev_tr = p.sem()
    transpose_tiles(
        p,
        src=lambda i: sb['ofm'][:, (i % 8) * 2048 + (i // 8) * 128: (i % 8) * 2048 + (i // 8) * 128 + 128],
        dst=lambda i: sb['outok'][:, (i // 8) * 1024 + (i % 8) * 128: (i // 8) * 1024 + (i % 8) * 128 + 128],
        n=128, idn=idn, psum_bf=psum_bf, slots=TRAN_SLOTS,
        waits=[(rl, rl.v)], ev=ev_tr)

    u = p.sem()
    ln_done = p.sem()
    rms2 = p.sem()
    sxs = [p.sem(tag='hw'), p.sem(tag='hw')]
    sos = [p.sem(tag='hw'), p.sem(tag='hw')]
    tokb = h.get('x_tokb')
    bfb = sb['ofm'][:, 0:4096]
    sob = [p.sem(tag='hw') for _ in range(4)]
    b_sob = [s.v for s in sob]
    ln_marks = []
    for tc in range(16):
        sx = sxs[tc % 2]

        @p.o("y")
        def _(y, tc=tc, need=(ln_marks[tc - 2] if tc >= 2 else 0), sx=sx):
            if tc >= 2:
                y.wait_ge(ln_done.h, need)
            y.dma_start(out=sb['xs_f32'][:, (tc % 2) * 1024:(tc % 2) * 1024 + 1024],
                        in_=h['x0tokh'][tc * 128:(tc + 1) * 128, :]).then_inc(sx.h, 16)
        sx.v += 16

        R0 = rms2.v

        @p.o("v")
        def _(v, tc=tc, w=ev_tr.v, R0=R0, sxv=sx.v, sx=sx):
            v.wait_ge(ev_tr.h, w)
            v.wait_ge(sx.h, sxv)
            ot = sb['outok'][:, tc * 1024:(tc + 1) * 1024]
            v.tensor_tensor(sb['vtrash'][:, :], ot, ot, ALU.mult).then_inc(rms2.h, 1)
            v.wait_ge(rms2.h, R0 + 1)
            v.reduce_sum(sb['ln_sc'][:, 4:5], sb['vtrash'][:, :], axis=AX.X).then_inc(rms2.h, 1)
            v.wait_ge(rms2.h, R0 + 2)
            v.tensor_scalar(sb['ln_sc'][:, 4:5], sb['ln_sc'][:, 4:5], 1.0 / 1024.0, 1e-5,
                            ALU.mult, ALU.add).then_inc(rms2.h, 1)
        rms2.v += 3

        @p.o("s")
        def _(s, R0=R0):
            s.wait_ge(rms2.h, R0 + 3)
            s.activation(sb['ln_sc'][:, 5:6], sb['ln_sc'][:, 4:5], AF.Sqrt).then_inc(rms2.h, 1)
        rms2.v += 1

        @p.o("v")
        def _(v, tc=tc, R0=R0, uv=u.v):
            v.wait_ge(rms2.h, R0 + 4)
            v.reciprocal(sb['ln_sc'][:, 5:6], sb['ln_sc'][:, 5:6]).then_inc(rms2.h, 1)
            v.wait_ge(rms2.h, R0 + 5)
            v.wait_ge(u.h, uv)
            ot = sb['outok'][:, tc * 1024:(tc + 1) * 1024]
            v.scalar_tensor_tensor(sb['ln_tmp'][:, :], ot, sb['ln_sc'][:, 5:6],
                                   sb['xs_f32'][:, (tc % 2) * 1024:(tc % 2) * 1024 + 1024],
                                   ALU.mult, ALU.add).then_inc(rms2.h, 1)
        rms2.v += 2

        d_before = ln_done.v
        ln_chunk(p,
                 a_ap=sb['ln_tmp'][:, :],
                 b_ap=sb['xs_f32'][:, (tc % 2) * 1024:(tc % 2) * 1024 + 1024],
                 tmp=sb['ln_tmp2'][:, :], trash_bf=sb['ln_tr'][:, :], sc=sb['ln_sc'],
                 n_f=1024,
                 waits=[(rms2, rms2.v)] +
                       ([(sos[tc % 2], sos[tc % 2].v)] if tc >= 2 else []) +
                       ([(sob[tc % 4], sob[tc % 4].v)]
                        if (tokb is not None and tc >= 4) else []),
                 u=u, done=ln_done,
                 out_f32=sb['x2s'][:, (tc % 2) * 1024:(tc % 2) * 1024 + 1024],
                 out_bf=(bfb[:, (tc % 4) * 1024:(tc % 4) * 1024 + 1024]
                         if tokb is not None else None))
        so = sos[tc % 2]

        @p.o("y")
        def _(y, tc=tc, w=d_before + 1, so=so):
            y.wait_ge(ln_done.h, w)
            y.dma_start(out=h['x1_tok'][tc * 128:(tc + 1) * 128, :],
                        in_=sb['x2s'][:, (tc % 2) * 1024:(tc % 2) * 1024 + 1024]).then_inc(so.h, 16)
        so.v += 16
        if tokb is not None:
            @p.o("y")
            def _(y, tc=tc, w=d_before + 2, so2=sob[tc % 4]):
                y.wait_ge(ln_done.h, w)
                y.dma_start(out=tokb[tc * 128:(tc + 1) * 128, :],
                            in_=bfb[:, (tc % 4) * 1024:(tc % 4) * 1024 + 1024]
                            ).then_inc(so2.h, 16)
            sob[tc % 4].v += 16
        ln_marks.append(ln_done.v)

    @p.o("y")
    def _(y, w0=sos[0].v, w1=sos[1].v):
        y.wait_ge(sos[0].h, w0)
        y.wait_ge(sos[1].h, w1)
    return ln_done, sos, (sob, b_sob)


def barrier(p, waits):
    for eng in ("g", "t", "v", "s", "y"):
        @p.o(eng)
        def _(q, waits=list(waits)):
            for s, v in waits:
                q.wait_ge(s.h, v)


def fm_from_tok(p, *, tok_dram, stage, xfm, psum_bf, idn, waits, slots=None,
                tok_bf=None, bf_waits=()):
    if slots is None:
        slots = TRAN_SLOTS
    idn = idn[:, :]
    tk = p.sem()
    te = p.sem()
    b_tk, b_te = tk.v, te.v

    if tok_bf is not None:
        lds = [p.sem(tag='hw') for _ in range(4)]
        b_ld = [s.v for s in lds]

        @p.o("y")
        def _(y, waits=list(waits), bf_waits=list(bf_waits)):
            for s, v in waits:
                y.wait_ge(s.h, v)
            for g in range(4):
                for s, v in bf_waits[g] if bf_waits else []:
                    y.wait_ge(s.h, v)
                for j in range(4):
                    tc = g * 4 + j
                    y.dma_start(out=stage[:, tc * 1024:(tc + 1) * 1024],
                                in_=tok_bf[tc * 128:(tc + 1) * 128, :]).then_inc(lds[g].h, 16)
        for g in range(4):
            lds[g].v += 64

        @p.o("t")
        def _(t, waits=list(waits)):
            for s, v in waits:
                t.wait_ge(s.h, v)
            for i in range(128):
                tc, kc = i // 8, i % 8
                if kc == 0:
                    t.wait_ge(lds[tc // 4].h, b_ld[tc // 4] + 64)
                if i >= 4:
                    t.wait_ge(te.h, b_te + i - 3)
                ps = psum_bf[:, slots[i % 4]: slots[i % 4] + 128]
                t.transpose(ps, stage[:, tc * 1024 + kc * 128: tc * 1024 + kc * 128 + 128],
                            idn).then_inc(tk.h, 1)
        tk.v += 128
    else:
        ld = p.sem(tag='sw')

        @p.o("g")
        def _(g, waits=list(waits)):
            for s, v in waits:
                g.wait_ge(s.h, v)
            for tc in range(16):
                g.dma_start(out=stage[:, tc * 1024:(tc + 1) * 1024],
                            in_=tok_dram[tc * 128:(tc + 1) * 128, :]).then_inc(ld.h, 16)
        ld.v += 16 * 16

        @p.o("t")
        def _(t, w=ld.v, waits=list(waits)):
            for s, v in waits:
                t.wait_ge(s.h, v)
            t.wait_ge(ld.h, w)
            for i in range(128):
                tc, kc = i // 8, i % 8
                if i >= 4:
                    t.wait_ge(te.h, b_te + i - 3)
                ps = psum_bf[:, slots[i % 4]: slots[i % 4] + 128]
                t.transpose(ps, stage[:, tc * 1024 + kc * 128: tc * 1024 + kc * 128 + 128],
                            idn).then_inc(tk.h, 1)
        tk.v += 128

    for i in range(128):
        tc, kc = i // 8, i % 8

        @p.o("v")
        def _(v, i=i, tc=tc, kc=kc, w=b_tk + i + 1):
            v.wait_ge(tk.h, w)
            ps = psum_bf[:, slots[i % 4]: slots[i % 4] + 128]
            v.tensor_copy(xfm[:, kc * 2048 + tc * 128: kc * 2048 + tc * 128 + 128],
                          ps).then_inc(te.h, 1)
    te.v += 128
    return (te, te.v)


N_CORES = 8
POOL_N = 104832


class Carver:
    def __init__(self, pool):
        self.pool = pool
        self.o = 0

    def take(self, n_el, dt=BF16, parts=128):
        mult = 1 if dt == BF16 else 2
        nb = n_el * mult
        o = self.o
        self.o = o + ((nb + 31) // 32) * 32
        assert self.o <= POOL_N, f"pool overflow {self.o}"
        sl = self.pool[:, o:o + nb]
        if dt == BF16:
            v = sl
        else:
            v = sl.bitcast(dt)
        if parts != 128:
            v = v[0:parts, :]
        return v


def carve_mamba(pool):
    c = Carver(pool)
    sb = {}
    A1 = c.take(20608)
    sb['xt0'] = A1[:, 0:8192]
    sb['xbcp'] = A1[:, 8192:18462]
    sb['ofm'] = A1[:, 0:16384]
    sb['wst2'] = A1[:, 18560:20608]
    A2 = c.take(18432)
    sb['wzr'] = A2[:, 0:8192]
    sb['wxsbcr'] = A2[:, 8192:18432]
    sb['wob'] = A2[:, 0:8192]
    sb['vfm'] = A2[:, 8192:16384]
    sb['outok'] = A2[:, 0:16384]
    sb['wdtr'] = c.take(128)
    xtok_t = c.take(8192)
    sb['xtok'] = xtok_t
    xv = xtok_t.bitcast(F32)
    sb['xs_f32'] = xv[:, 0:2048]
    sb['x2s'] = xv[:, 2048:4096]
    ztok_t = c.take(8192)
    sb['ztok'] = ztok_t
    zv = ztok_t.bitcast(F32)
    sb['ln_tmp'] = zv[:, 0:1024]
    sb['ln_tmp2'] = zv[:, 1024:2048]
    sb['ln_tr'] = ztok_t[:, 4096:5120]
    sb['bcf'] = c.take(2048)
    vfull_t = c.take(8192)
    sb['vfull'] = vfull_t
    sb['trf'] = vfull_t.bitcast(F32)
    sb['bio'] = vfull_t.bitcast(I32)[0:16, 0:2048]
    sb['dtraw'] = c.take(2048, parts=16)
    sb['wcst'] = c.take(2048)
    sb['dwork'] = c.take(768, F32, parts=16)
    sb['cumf'] = c.take(2048, F32, parts=1)
    sb['cum2048'] = c.take(2048, F32, parts=16)
    sb['w16'] = c.take(96, F32)
    sb['wbt'] = c.take(32)
    sb['mbig'] = c.take(2048)
    sb['m0sb'] = c.take(128)
    sb['btok'] = c.take(128)
    sb['hst'] = c.take(1024, F32)
    sb['hbb'] = c.take(1024)
    sb['xtt'] = c.take(1024)
    sb['szb'] = c.take(1024)
    sb['yt1'] = c.take(1024, F32)
    sb['vtrash'] = c.take(1024)
    sb['ssq'] = c.take(96, F32)
    sb['wdg'] = c.take(5120)
    sb['dide'] = c.take(2048)
    sb['bmask'] = c.take(2048, F32, parts=16)
    sb['bigl'] = c.take(128)
    sb['idtl'] = c.take(2048)
    sb['trilm'] = c.take(128)
    sb['hstash'] = c.take(64)[:, 0:60]
    sb['acol'] = c.take(16, F32, parts=16)[:, 0:1]
    sb['dones'] = c.take(128, F32, parts=16)
    sb['ln_sc'] = c.take(8, F32)
    for nm, shp in [('cwx', (128, 32)), ('cwbc', (128, 8)), ('cbx', (128, 8)),
                    ('cbbc', (128, 2)), ('dtb', (16, 1)), ('alog', (16, 1)),
                    ('drep', (128, 16)), ('nwc', (128, 8))]:
        sb[nm] = c.take(max(shp[1], 8), F32, parts=shp[0])[:, 0:shp[1]]
    return sb, c.o


def carve_ffn0(pool, TT):
    c = Carver(pool)
    sb = {}
    sb['x_fm'] = c.take(16384)
    sb['ff_a'] = c.take(32 * TT)
    sb['ff_b'] = c.take(32 * TT)
    sb['ff_a1'] = c.take(2 * TT)
    sb['w8'] = [c.take(8 * 128) for _ in range(4)]
    sb['w32'] = [c.take(32 * 128) for _ in range(3)]
    sb['w3o'] = c.take(8 * TT)
    sb['w3tok'] = c.take((TT // 128) * 1024)
    sb['xs_f32'] = c.take(2 * 1024, F32)
    sb['ln_tmp'] = c.take(1024, F32)
    sb['ln_sc'] = c.take(8, F32)
    sb['ln_tr'] = c.take(1024)
    sb['x2s'] = c.take(2 * 1024, F32)
    sb['x2b'] = c.take(4 * 1024)
    for nm, w in [('s1c', 32), ('s2c', 32), ('h1sc', 32), ('h2sc', 32), ('s3c', 8)]:
        sb['svec_' + nm] = c.take(w, F32)
    sb['pw_a'] = c.take(1024, F32)
    sb['pw_c0'] = c.take(1024)
    sb['pw_c1'] = c.take(1024)
    return sb, c.o


def carve_attn1(pool):
    c = Carver(pool)
    sb = {}
    sb['x2f'] = c.take(8 * 2048)
    att_a = c.take(16384)
    sb['xt2'] = [att_a[:, 0:8192], att_a[:, 8192:16384]]
    sb['aout'] = att_a
    sb['qf'] = c.take(4 * 4096)
    sb['kf'] = c.take(4 * 4096)
    sb['vaug'] = c.take(32 * 528)
    sb['pbuf'] = c.take(2 * 1024)
    sb['tau'] = c.take(4 * 512)
    sb['rrow'] = c.take(1024, F32, parts=1)
    sb['rbb'] = c.take(512, parts=64)
    sb['wq3'] = sb['x2f'][:, 0:8192]
    sb['wv'] = sb['x2f'][:, 8192:12288]
    sb['wos'] = c.take(8 * 512)
    sb['wst'] = sb['pbuf']
    sb['wofm'] = sb['qf']
    sb['wotok'] = sb['kf']
    sb['bo_c'] = c.take(8, F32)
    sb['xs_f32'] = c.take(2 * 1024, F32)
    sb['ln_tmp'] = c.take(1024, F32)
    sb['ln_sc'] = c.take(8, F32)
    sb['ln_tr'] = c.take(1024)
    sb['x2s'] = c.take(2 * 1024, F32)
    return sb, c.o


def carve_ffn1(pool, TT):
    c = Carver(pool)
    sb = {}
    sb['x3f'] = c.take(8 * 2048)
    sb['fhid'] = c.take(32 * TT)
    sb['w8'] = [c.take(8 * 128) for _ in range(2)]
    sb['w32'] = [c.take(32 * 128) for _ in range(3)]
    sb['w3o'] = c.take(8 * TT)
    sb['w3tok'] = c.take((TT // 128) * 1024)
    sb['xs_f32'] = c.take(2 * 1024, F32)
    sb['ln_tmp'] = c.take(1024, F32)
    sb['ln_sc'] = c.take(8, F32)
    sb['ln_tr'] = c.take(1024)
    sb['x2s'] = c.take(2 * 1024, F32)
    sb['pw_a'] = c.take(4096, F32)
    sb['pw_b'] = c.take(4096, F32)
    sb['pw_c0'] = c.take(4096)
    sb['pw_c1'] = c.take(4096)
    sb['pw_s'] = c.take(40, F32)
    sb['qs'] = c.take(16, F32)
    sb['ep1'] = c.take(32, F32)
    sb['binc'] = c.take(32, F32)
    sb['boutc'] = c.take(8, F32)
    sb['hmv'] = c.take(32, F32)
    sb['tmp18'] = c.take(8, F32, parts=1)
    sb['q1'] = c.take(16, F32, parts=1)
    sb['q1x'] = c.take(128, F32, parts=1)
    return sb, c.o


def build(two_core=False, TT0=512, TT1=512):
    rgp = [[0, 1]] if two_core else [[0, 1], [2, 3], [4, 5], [6, 7]]
    rg8 = [[0, 1]] if two_core else [list(range(N_CORES))]
    nc = bass.Bass("TRN2", target_bir_lowering=False)
    d = {}
    EI, EO, IN = "ExternalInput", "ExternalOutput", "Internal"
    d['x0fm'] = nc.dram_tensor("x0fm", [1024, 4099], F32, kind=EI)
    d['x0tokh'] = nc.dram_tensor("x0tokh", [2048, 1024], F32, kind=EI)
    for nm, shp in [('wzb', [8, 1024, 128]), ('wxsb', [8, 1024, 128]),
                    ('wbcb', [2, 1024, 128]), ('wdt', [1024, 16]),
                    ('cwx', [128, 32]), ('cwbc', [128, 8]), ('cbx', [128, 8]),
                    ('cbbc', [128, 2]), ('dtb', [16, 1]), ('alog', [16, 1]),
                    ('drep', [128, 16]), ('nwc', [128, 8]),
                    ('woutb', [8, 1024, 128])]:
        d[nm] = nc.dram_tensor(nm, shp, F32, kind=EI)
    d['x1_tok'] = nc.dram_tensor("x1_tok", [2048, 1024], F32, kind=IN)
    d['x1_tokb'] = nc.dram_tensor("x1_tokb", [2048, 1024], BF16, kind=IN)
    d['wcflag'] = nc.dram_tensor("wcflag", [1, 4], F32, kind=IN)
    d['x2_tokb'] = nc.dram_tensor("x2_tokb", [2048, 1024], BF16, kind=IN)
    d['x3_tokb'] = nc.dram_tensor("x3_tokb", [2048, 1024], BF16, kind=IN)
    d['cfl'] = nc.dram_tensor("cfl", [2048], F32, kind=IN)
    d['vsp'] = nc.dram_tensor("vsp", [4, 128, 8192], BF16, kind=IN)
    d['ag2_in'] = nc.dram_tensor("ag2_in", [4096], F32, kind=IN)
    d['ag2_out'] = nc.dram_tensor("ag2_out", [2, 4096], F32, kind=IN)
    d['rs2_in'] = nc.dram_tensor("rs2_in", [2, 1024, 2048], BF16, kind=IN)
    d['rs2_out'] = nc.dram_tensor("rs2_out", [1024 * 2048], BF16, kind=IN)
    for nm, shp in [('w1b', [32, 1024, 128]), ('w2b', [32, 1024, 128]),
                    ('h1b', [32, 4096, 128]), ('h2b', [32, 4096, 128]),
                    ('w3b', [8, 4096, 128]), ('s1c', [128, 32]), ('s2c', [128, 32]),
                    ('h1sc', [128, 32]), ('h2sc', [128, 32]), ('s3c', [128, 8])]:
        d[nm] = nc.dram_tensor(nm, shp, F32, kind=EI)
    d['x2_tok'] = nc.dram_tensor("x2_tok", [2048, 1024], F32, kind=IN)
    d['w1c'] = nc.dram_tensor("w1c", [32, 128, 1024], BF16, kind=IN)
    d['w2c'] = nc.dram_tensor("w2c", [32, 128, 1024], BF16, kind=IN)
    d['h1c'] = nc.dram_tensor("h1c", [32, 128, 4096], BF16, kind=IN)
    d['h2c'] = nc.dram_tensor("h2c", [32, 128, 4096], BF16, kind=IN)
    d['w3c'] = nc.dram_tensor("w3c", [8, 128, 4096], BF16, kind=IN)
    for nm, shp in [('Wqb', [4, 1024, 128]), ('Wkb', [4, 1024, 128]),
                    ('Wvb', [1024, 512]), ('Wob', [8, 512, 128]),
                    ('bo_c', [128, 8]), ('tauj', [4, 128, 512])]:
        d[nm] = nc.dram_tensor(nm, shp, F32, kind=EI)
    d['x3_tok'] = nc.dram_tensor("x3_tok", [2048, 1024], F32, kind=IN)
    d['agx_in'] = nc.dram_tensor("agx_in", [1024 * 2048], BF16, kind=IN)
    d['agx_out'] = nc.dram_tensor("agx_out", [2, 1024 * 2048], BF16, kind=IN)
    d['rs_in'] = nc.dram_tensor("rs_in", [2, 1024, 2048], BF16, kind=IN)
    d['rs_out'] = nc.dram_tensor("rs_out", [1024 * 2048], BF16, kind=IN)
    for nm, shp in [('W1nT', [4096, 1024]), ('W2n', [4096, 1024]),
                    ('b_in_c', [128, 32]), ('b_out_c', [128, 8])]:
        d[nm] = nc.dram_tensor(nm, shp, F32, kind=EI)
    d['x4_tok'] = nc.dram_tensor("x4_tok", [2048, 1024], F32, kind=EO)
    d['w1q_blk'] = nc.dram_tensor("w1q_blk", [32, 128, 1024], BF16, kind=IN)
    d['w1qn'] = nc.dram_tensor("w1qn", [32, 128, 1024], BF16, kind=IN)
    d['w2q_blk'] = nc.dram_tensor("w2q_blk", [8, 128, 4096], BF16, kind=IN)
    d['hsp'] = nc.dram_tensor("hsp", [2048 // TT1, 128, 32 * TT1], BF16, kind=IN)
    d['flat_sc'] = nc.dram_tensor("flat_sc", [2, 128], F32, kind=IN)
    d['prepflag'] = nc.dram_tensor("prepflag", [1, 4], F32, kind=IN)
    for nm in ('arm1', 'arm1o', 'arm2', 'arm2o'):
        d[nm] = nc.dram_tensor(nm, [1, 8], F32, kind=IN)

    p = P(nc)
    pool = nc.alloc_sbuf_tensor("pool", [128, POOL_N], BF16)
    amx = nc.alloc_sbuf_tensor("amx", [128, 18], F32)
    q1p = nc.alloc_sbuf_tensor("q1p", [1, 12], F32)
    q1xp = nc.alloc_sbuf_tensor("q1xp", [1, 128], F32)
    scsP = nc.alloc_sbuf_tensor("scsP", [128, 40], F32)
    idn = nc.alloc_sbuf_tensor("idn", [128, 128], BF16)
    io_i32 = nc.alloc_sbuf_tensor("io_i32", [128, 128], I32)
    iom = nc.alloc_sbuf_tensor("iom", [128, 128], F32)
    onesf = nc.alloc_sbuf_tensor("onesf", [1, 128], F32)
    psum = nc.alloc_psum_tensor("ps", [128, 4096], F32)
    psum_bf = psum[:, :].bitcast(BF16)

    cst = p.sem()
    build_identity(p, idn, io_i32, iom, cst)
    ov = p.sem()

    @p.o("v")
    def _(v, cv=cst.v):
        v.wait_ge(cst.h, cv)
        v.memset(onesf[0:1, 0:128], 1.0).then_inc(ov.h, 1)
    ov.v += 1

    sbM, oM = carve_mamba(pool)
    sbM['onesf'] = onesf
    sbM['idn'] = idn
    sbM['iom'] = iom
    sbM['io_i32'] = io_i32
    hM = {k: d[k] for k in ('x0fm', 'x0tokh', 'wzb', 'wxsb', 'wbcb', 'wdt',
                            'cwx', 'cwbc', 'cbx', 'cbbc', 'dtb', 'alog', 'drep',
                            'nwc', 'woutb', 'x1_tok', 'cfl', 'vsp', 'ag2_in',
                            'ag2_out', 'rs2_in', 'rs2_out')}
    hM['x_tokb'] = d['x1_tokb']

    wc_stage = sbM['wcst']
    wc_sg = [p.sem(tag='sw'), p.sem(tag='sw')]
    wc_ss = [p.sem(tag='hw'), p.sem(tag='hw')]
    wc_bg = [s.v for s in wc_sg]
    wc_bs = [s.v for s in wc_ss]

    def _wc_jobs(src_t, dst_t, nblk, nkc):
        out = []
        for fc in range(nblk):
            for q in range(nkc // 8):
                out.append((src_t[fc].rearrange("(kc pp) j -> pp kc j", pp=128)
                            [:, q * 8:(q + 1) * 8, :],
                            dst_t[fc][:, q * 1024:(q + 1) * 1024]))
        return out
    wc_sets = [_wc_jobs(d['w1b'], d['w1c'], 32, 8) + _wc_jobs(d['w2b'], d['w2c'], 32, 8),
               _wc_jobs(d['h1b'], d['h1c'], 32, 32),
               _wc_jobs(d['h2b'], d['h2c'], 32, 32),
               _wc_jobs(d['w3b'], d['w3c'], 8, 32)]
    wc_state = {'ji': 0}
    wc_done = {}

    def wc_hook(tt):
        jobs = wc_sets[tt]
        if not jobs:
            return
        base = wc_state['ji']

        @p.o("g")
        def _(g, jobs=jobs, base=base):
            for k, (srcA, dstA) in enumerate(jobs):
                i = base + k
                sl = i % 2
                if i >= 2:
                    g.wait_ge(wc_ss[sl].h, wc_bs[sl] + 16 * (i // 2))
                g.dma_start(out=wc_stage[:, sl * 1024:(sl + 1) * 1024]
                            .rearrange("pp (kc j) -> pp kc j", j=128),
                            in_=srcA).then_inc(wc_sg[sl].h, 16)

        @p.o("y")
        def _(y, jobs=jobs, base=base):
            for k, (srcA, dstA) in enumerate(jobs):
                i = base + k
                sl = i % 2
                y.wait_ge(wc_sg[sl].h, wc_bg[sl] + 16 * (i // 2 + 1))
                y.dma_start(out=dstA,
                            in_=wc_stage[:, sl * 1024:(sl + 1) * 1024]).then_inc(wc_ss[sl].h, 16)
        for k in range(len(jobs)):
            i = base + k
            wc_sg[i % 2].v += 16
            wc_ss[i % 2].v += 16
        wc_state['ji'] += len(jobs)
        sd = p.sem(tag='hw')

        @p.o("y")
        def _(y, w0=wc_ss[0].v, w1=wc_ss[1].v, sd=sd, tt=tt):
            y.wait_ge(wc_ss[0].h, w0)
            y.wait_ge(wc_ss[1].h, w1)
            y.dma_start(out=d['wcflag'][0:1, tt:tt + 1],
                        in_=onesf[0:1, 0:1]).then_inc(sd.h, 16)
        sd.v += 16
        names = [('w1c', 'w2c'), ('h1c',), ('h2c',), ('w3c',)][tt]
        for nm in names:
            wc_done[nm] = [(sd, sd.v)]
        wc_state.setdefault('sds', []).append(sd)

    mres = mamba_phase(p, hM, sbM, psum, psum_bf, rgp, g_hook=wc_hook)
    lnM, sosM, bfM = mamba_tail(p, hM, sbM, psum, psum_bf, rgp, mres)
    barM = ([(lnM, lnM.v), (sosM[0], sosM[0].v), (sosM[1], sosM[1].v)] +
            [(s, s.v) for s in bfM[0]] +
            [(s, s.v) for s in wc_state['sds']])
    barrier(p, barM)
    p.recycle()

    sbF0, oF0 = carve_ffn0(pool, TT0)
    sbF0['idn'] = idn
    hF0 = {'x_tok': d['x1_tok'], 'x2_tok': d['x2_tok']}
    for k in ('w1b', 'w2b', 'h1b', 'h2b', 'w3b', 's1c', 's2c', 'h1sc', 'h2sc', 's3c',
              'w1c', 'w2c', 'h1c', 'h2c', 'w3c'):
        hF0[k] = d[k]
    xr0 = fm_from_tok(p, tok_dram=d['x1_tok'], stage=sbF0['ff_a'][:, 0:16384],
                         xfm=sbF0['x_fm'], psum_bf=psum_bf, idn=idn, waits=barM,
                         tok_bf=d['x1_tokb'],
                         bf_waits=[[(bfM[0][s], bfM[1][s] + 16 * (g + 1))
                      for s in range(4)] for g in range(4)])
    hF0['x_tokb'] = d['x2_tokb']
    sbF0['pw_s'] = scsP
    hQ = {k: d[k] for k in ('w1qn', 'W1nT', 'W2n', 'w2q_blk', 'prepflag')}
    done_q = p.sem(tag='sw')
    q_pieces, q_emit_g = ffn1_wprep_quant(p, hQ, sbF0, done_q)
    ln0, sos0, bf0 = ffn0_phase(p, hF0, sbF0, psum, psum_bf, xr0, TT=TT0, x2fm=None,
                                   wc_done=wc_done,
                                   v_hook=lambda tt: q_pieces[tt]())
    q_emit_g()
    bar0 = ([(ln0, ln0.v), (sos0[0], sos0[0].v), (sos0[1], sos0[1].v)] +
            [(s, s.v) for s in bf0[0]] + [(done_q, done_q.v)])
    barrier(p, bar0)
    p.recycle()

    sbA, oA = carve_attn1(pool)
    sbA['idn'] = idn
    sbA['onesf'] = onesf
    hA = {'x_tok': d['x2_tok'], 'x3_tok': d['x3_tok']}
    for k in ('Wqb', 'Wkb', 'Wvb', 'Wob', 'bo_c', 'tauj', 'agx_in', 'agx_out',
              'rs_in', 'rs_out'):
        hA[k] = d[k]
    xr1 = fm_from_tok(p, tok_dram=d['x2_tok'], stage=sbA['qf'][:, 0:16384],
                         xfm=sbA['x2f'], psum_bf=psum_bf, idn=idn, waits=bar0,
                         tok_bf=d['x2_tokb'],
                         bf_waits=[[(bf0[0][s], bf0[1][s] + 16 * (g + 1))
                      for s in range(4)] for g in range(4)])
    hA['x_tokb'] = d['x3_tokb']
    lnA, sosA, bfA = attn1_phase(p, hA, sbA, psum, psum_bf, xr1, rgp,
                               absmax_col=amx[:, 0:16])

    amr = p.sem()
    b_amr = amr.v

    @p.o("v")
    def _(v, w=lnA.v):
        v.wait_ge(lnA.h, w)
        v.reduce_max(amx[:, 16:17], amx[:, 0:16], axis=mybir.AxisListType.X).then_inc(amr.h, 1)
    amr.v += 1
    amc = p.sem(tag='sw')
    b_amc = amc.v

    @p.o("g")
    def _(g, w=amr.v):
        g.wait_ge(amr.h, w)
        g.dma_start(out=d['flat_sc'][0], in_=amx[:, 16:17],
                    allow_slow_non_contiguous=True).then_inc(amc.h, 16)
        g.wait_ge(amc.h, b_amc + 16)
        g.dma_start(out=q1xp[0:1, 0:128], in_=d['flat_sc'][0]).then_inc(amc.h, 16)
    amc.v += 32

    @p.o("v")
    def _(v, w=amc.v, b=b_amr):
        v.wait_ge(amc.h, w)
        v.reduce_max(q1p[0:1, 0:1], q1xp[0:1, 0:128],
                     axis=mybir.AxisListType.X).then_inc(amr.h, 1)
        v.wait_ge(amr.h, b + 2)
        v.tensor_copy(q1p[0:1, 2:10],
                      q1p[0:1, 0:1].broadcast_to([1, 8])).then_inc(amr.h, 1)
    amr.v += 2
    cc1 = p.sem(tag='co')

    @p.o("g")
    def _(g, w=amr.v, b=b_amc):
        g.wait_ge(amr.h, w)
        g.dma_start(out=d['arm1'].ap(), in_=q1p[0:1, 2:10]).then_inc(amc.h, 16)
        g.wait_ge(amc.h, b + 48)
        g.collective_compute("AllReduce", mybir.AluOpType.max, replica_groups=rg8,
                             ins=[d['arm1'].ap().opt()],
                             outs=[d['arm1o'].ap().opt()]).then_inc(cc1.h, 1)
    amc.v += 16
    cc1.v += 1

    barA = ([(lnA, lnA.v), (sosA[0], sosA[0].v), (sosA[1], sosA[1].v)] +
            [(s, s.v) for s in bfA[0]])
    barrier(p, barA)
    p.recycle(keep=[amr, amc, cc1])

    sbF1, oF1 = carve_ffn1(pool, TT1)
    sbF1['idn'] = idn
    sbF1['onesf'] = onesf
    hF1 = {'x_tok': d['x3_tok'], 'x4_tok': d['x4_tok']}
    for k in ('W1nT', 'W2n', 'b_in_c', 'b_out_c', 'w1q_blk', 'w1qn', 'w2q_blk',
              'hsp', 'flat_sc', 'prepflag', 'arm1', 'arm1o', 'arm2', 'arm2o'):
        hF1[k] = d[k]
    xr2 = fm_from_tok(p, tok_dram=d['x3_tok'], stage=sbF1['fhid'][:, 0:16384],
                         xfm=sbF1['x3f'], psum_bf=psum_bf, idn=idn, waits=barA,
                         slots=[0, 1024, 2048, 3072],
                         tok_bf=d['x3_tokb'],
                         bf_waits=[[(bfA[0][s], bfA[1][s] + 16 * (g + 1))
                      for s in range(4)] for g in range(4)])
    sbF1['pw_s'] = scsP
    prep_done = p.sem(tag='sw')
    ffn1_blockify(p, hF1, sbF1, psum_bf, idn, done_q, prep_done)

    arxF = p.sem(tag='hw')

    @p.o("y")
    def _(y, w=cc1.v):
        y.wait_ge(cc1.h, w)
        y.dma_start(out=q1p[0:1, 1:2],
                    in_=d['arm1o'].ap()[0:1, 0:1]).then_inc(arxF.h, 16)
    arxF.v += 16
    ln1, sos1 = ffn1_phase(p, hF1, sbF1, psum, psum_bf, xr2, rg8, prep_done,
                              TT=TT1, xamax=(q1p[0:1, 1:2], arxF))
    p.finalize()
    return nc


def colsplit(s):
    return np.ascontiguousarray(np.asarray(s, np.float32).reshape(-1, 128).T)


def blocked(wT, n_fc):
    d_in, d_out = wT.shape
    return np.ascontiguousarray(wT.reshape(d_in, n_fc, 128).transpose(1, 0, 2))


def stage_core(ins, c):
    b, hh = c // 2, c % 2
    x = np.asarray(ins['L0_attn_x'][b])
    W_in = np.asarray(ins['L0_attn_W_in'])
    conv_w = np.asarray(ins['L0_attn_conv_w'])
    conv_b = np.asarray(ins['L0_attn_conv_b'])
    DI = 2048
    zcols = W_in[:, hh * 1024:(hh + 1) * 1024]
    xscols = W_in[:, DI + hh * 1024: DI + (hh + 1) * 1024]
    bccols = W_in[:, 2 * DI: 2 * DI + 256]
    dtcols = W_in[:, 4352 + hh * 16: 4352 + (hh + 1) * 16]
    cw_xs = conv_w[hh * 1024:(hh + 1) * 1024]
    cw_bc = conv_w[2048:2304]
    cwx = cw_xs.reshape(8, 128, 4).transpose(1, 0, 2).reshape(128, 32)
    cwbc = cw_bc.reshape(2, 128, 4).transpose(1, 0, 2).reshape(128, 8)
    m = {
        'x0fm': np.concatenate([np.zeros((1024, 3), np.float32),
                                np.ascontiguousarray(x.T)], axis=1),
        'x0tokh': np.ascontiguousarray(x[hh * 2048:(hh + 1) * 2048]),
        'wzb': blocked(zcols, 8), 'wxsb': blocked(xscols, 8),
        'wbcb': blocked(bccols, 2), 'wdt': np.ascontiguousarray(dtcols),
        'cwx': np.ascontiguousarray(cwx), 'cwbc': np.ascontiguousarray(cwbc),
        'cbx': colsplit(conv_b[hh * 1024:(hh + 1) * 1024]),
        'cbbc': colsplit(conv_b[2048:2304]),
        'dtb': np.asarray(ins['L0_attn_dt_bias'])[hh * 16:(hh + 1) * 16].reshape(16, 1),
        'alog': np.asarray(ins['L0_attn_A_log'])[hh * 16:(hh + 1) * 16].reshape(16, 1),
        'drep': np.tile(np.asarray(ins['L0_attn_D'])[hh * 16:(hh + 1) * 16], (128, 1)),
        'nwc': colsplit(np.asarray(ins['L0_attn_norm_w'])[hh * 1024:(hh + 1) * 1024]),
        'woutb': blocked(np.asarray(ins['L0_attn_W_out'])[hh * 1024:(hh + 1) * 1024, :], 8),
    }
    m.update({
        'w1b': blocked(np.asarray(ins['L0_ffn_w1_w']).T, 32),
        'w2b': blocked(np.asarray(ins['L0_ffn_w2_w']).T, 32),
        'h1b': blocked(np.asarray(ins['L0_ffn_h1_w']).T, 32),
        'h2b': blocked(np.asarray(ins['L0_ffn_h2_w']).T, 32),
        'w3b': blocked(np.asarray(ins['L0_ffn_w3_w']).T, 8),
        's1c': colsplit(ins['L0_ffn_w1_s']), 's2c': colsplit(ins['L0_ffn_w2_s']),
        'h1sc': colsplit(ins['L0_ffn_h1_s']), 'h2sc': colsplit(ins['L0_ffn_h2_s']),
        's3c': colsplit(ins['L0_ffn_w3_s']),
    })
    Wq = np.asarray(ins['L1_attn_Wq'])
    Wk = np.asarray(ins['L1_attn_Wk'])
    Wv = np.asarray(ins['L1_attn_Wv'])
    Wo = np.asarray(ins['L1_attn_Wo'])
    tauj = np.zeros((4, 128, 512), np.float32)
    for j in range(4):
        for kk in range(128):
            tauj[j, kk, 128 * j + kk:] = 1.0
    m.update({
        'Wqb': blocked(Wq[:, hh * 512:(hh + 1) * 512], 4),
        'Wkb': blocked(Wk[:, hh * 512:(hh + 1) * 512], 4),
        'Wvb': np.ascontiguousarray(Wv[:, hh * 512:(hh + 1) * 512]),
        'Wob': blocked(Wo[hh * 512:(hh + 1) * 512, :], 8),
        'bo_c': colsplit(ins['L1_attn_bo']),
        'tauj': tauj,
    })
    m.update({
        'W1nT': np.ascontiguousarray(np.asarray(ins['L1_ffn_W_in']).T),
        'W2n': np.ascontiguousarray(np.asarray(ins['L1_ffn_W_out'])),
        'b_in_c': colsplit(ins['L1_ffn_b_in']),
        'b_out_c': colsplit(ins['L1_ffn_b_out']),
    })
    return m




_NC_CACHE = {}


def _get_nc():
    if 'nc' not in _NC_CACHE:
        _NC_CACHE['nc'] = build(two_core=False)
    return _NC_CACHE['nc']


def kernel(**inputs):
    ins = {k: np.asarray(v) for k, v in inputs.items()}
    in_maps = [stage_core(ins, c) for c in range(N_CORES)]
    nc = _get_nc()
    res = run_bass_kernel_spmd(nc, in_maps, core_ids=list(range(N_CORES)))
    out = np.empty((4, 4096, 1024), np.float32)
    for c in range(N_CORES):
        b, hh = c // 2, c % 2
        out[b, hh * 2048:(hh + 1) * 2048] = res.results[c]['x4_tok']
    return out
